# version 05
# baseline (speedup 1.0000x reference)
import numpy as np
from contextlib import ExitStack
import concourse.bass as bass
import concourse.mybir as mybir
from concourse.bass import IndirectOffsetOnAxis
from concourse.bass_utils import run_bass_kernel_spmd

F32 = mybir.dt.float32
BF16 = mybir.dt.bfloat16
I32 = mybir.dt.int32
AF = mybir.ActivationFunctionType
AL = mybir.AluOpType
AX = mybir.AxisListType

ENGS = ("sync", "scalar", "vector", "tensor", "gpsimd")


class Tok:
    __slots__ = ("item", "sem", "val")
    def __init__(self, a=None, b=None, c=None):
        if isinstance(a, Item):
            self.item = a; self.sem = b; self.val = c
        else:
            self.item = None
            if c is not None:
                self.sem = b; self.val = c
            else:
                self.sem = a; self.val = b


class Item:
    __slots__ = ("eng", "waits", "fn", "marked", "val", "dma")
    def __init__(self, eng, waits, fn, dma):
        self.eng = eng; self.waits = waits; self.fn = fn
        self.marked = False; self.val = None; self.dma = dma


class Buf:
    __slots__ = ("name", "writer", "readers")
    def __init__(self, name=""):
        self.name = name; self.writer = None; self.readers = []


class Prog:
    def __init__(self, nc):
        self.nc = nc
        self.items = {e: [] for e in ENGS}
        self.chain = {}
        self.dma_sems = {}
        self._ctx = None
        self.extra_waits = {e: [] for e in ENGS}

    def setup(self, ctx):
        self._ctx = ctx
        for e in ENGS:
            self.chain[e] = ctx.enter_context(self.nc.semaphore(f"ch_{e}"))

    def dma_sem(self, name):
        if name not in self.dma_sems:
            self.dma_sems[name] = [self._ctx.enter_context(self.nc.semaphore(f"ds_{name}")), 0]
        return self.dma_sems[name]

    def emit(self, eng, fn, reads=(), writes=(), waits=(), dma=None):
        wl = []
        for b in reads:
            if b.writer: wl.append(b.writer)
        for b in writes:
            if b.writer: wl.append(b.writer)
            wl.extend(b.readers)
        wl.extend(waits)
        item = Item(eng, wl, fn, dma)
        if dma is not None:
            ds = self.dma_sem(dma)
            ds[1] += 16
            tok = Tok(None, ds[0], ds[1])
            item.marked = True
            item.val = (ds[0], ds[1])
        else:
            tok = Tok(item)
        self.items[eng].append(item)
        for b in reads: b.readers.append(tok)
        for b in writes:
            b.writer = tok; b.readers = []
        return tok

    def barrier(self, eng, toks):
        item = Item(eng, list(toks), None, None)
        self.items[eng].append(item)

    def full_barrier(self):
        toks = []
        for e in ENGS:
            for it in reversed(self.items[e]):
                if it.fn is not None and it.dma is None:
                    toks.append(Tok(it)); break
        toks += [Tok(None, s, c) for (s, c) in self.dma_sems.values() if c > 0]
        for e in ENGS:
            self.barrier(e, toks)

    def play(self, block):
        for e in ENGS:
            for it in self.items[e]:
                for t in it.waits:
                    if t.item is not None:
                        t.item.marked = True
        for e in ENGS:
            cnt = 0
            for it in self.items[e]:
                if it.fn is not None and it.dma is None and it.marked:
                    cnt += 1
                    it.val = (self.chain[e], cnt)
        def mk(eng_name):
            waited = {}
            def body(e):
                for it in self.items[eng_name]:
                    for t in it.waits:
                        if t.item is not None:
                            sem, val = t.item.val[0], t.item.val[1]
                        else:
                            sem, val = t.sem, t.val
                        kk = id(sem)
                        if waited.get(kk, -1) < val:
                            e.wait_ge(sem, val)
                            waited[kk] = val
                    if it.fn is not None:
                        inst = it.fn(e)
                        if it.dma is not None:
                            inst.then_inc(it.val[0], 16)
                        elif it.marked:
                            inst.then_inc(it.val[0], 1)
            return body
        for e in ENGS:
            if self.items[e]:
                getattr(block, e)(mk(e))


_UNIQ = [0]

def _uniquify(nc):
    orig_sb, orig_ps = nc.sbuf_tensor, nc.psum_tensor
    def sb(name, shape, dt, **kw):
        _UNIQ[0] += 1
        return orig_sb(f"{name}_u{_UNIQ[0]}", shape, dt, **kw)
    def ps(name, shape, dt=F32, **kw):
        _UNIQ[0] += 1
        return orig_ps(f"{name}_u{_UNIQ[0]}", shape, dt, **kw)
    nc.sbuf_tensor = sb
    nc.psum_tensor = ps

def sbuf(nc, ctx, name, shape, dt=F32):
    return ctx.enter_context(nc.sbuf_tensor(name, shape, dt)), Buf(name)


def psum(nc, ctx, name, shape, dt=F32):
    return ctx.enter_context(nc.psum_tensor(name, shape, dt)), Buf(name)


def emit_identity(nc, p, b_id, ap):
    p.emit("gpsimd", lambda g: g.memset(ap, 0.0), writes=[b_id])
    p.emit("gpsimd", lambda g: g.affine_select(out=ap, in_=ap,
        compare_op=AL.not_equal, fill=1.0, base=0,
        pattern=[[-1, ap.shape[0]]], channel_multiplier=1),
        reads=[b_id], writes=[b_id])


def emit_ln(p, x_ap, out_ap, st_ap, sq_ap, b_x, b_out, b_st, b_sq, D, out2_ap=None, b_out2=None):
    p.emit("vector", lambda e: e.reduce_sum(st_ap[:, 0:1], x_ap, axis=AX.X), reads=[b_x], writes=[b_st])
    p.emit("vector", lambda e: e.tensor_scalar_mul(st_ap[:, 1:2], st_ap[:, 0:1], -1.0 / D), reads=[b_st], writes=[b_st])
    p.emit("scalar", lambda e: e.activation(sq_ap, x_ap, AF.Square, bias=st_ap[:, 1:2], scale=1.0,
                                            accum_out=st_ap[:, 2:3]), reads=[b_x, b_st], writes=[b_sq, b_st])
    p.emit("vector", lambda e: e.tensor_scalar(st_ap[:, 3:4], st_ap[:, 2:3], 1.0 / D, 1e-5, op0=AL.mult, op1=AL.add), reads=[b_st], writes=[b_st])
    p.emit("vector", lambda e: e.reciprocal(st_ap[:, 2:3], st_ap[:, 3:4]), reads=[b_st], writes=[b_st])
    p.emit("scalar", lambda e: e.activation(st_ap[:, 3:4], st_ap[:, 2:3], AF.Sqrt), reads=[b_st], writes=[b_st])
    p.emit("vector", lambda e: e.tensor_scalar(out_ap, x_ap, st_ap[:, 1:2], st_ap[:, 3:4],
                                               op0=AL.add, op1=AL.mult), reads=[b_x, b_st], writes=[b_out])
    if out2_ap is not None:
        p.emit("scalar", lambda e: e.activation(out2_ap, out_ap, AF.Copy), reads=[b_out], writes=[b_out2])


def build_a1():
    nc = bass.Bass("TRN2", target_bir_lowering=False)
    p = Prog(nc)
    T = 2048
    x = nc.dram_tensor("x", [T, 1024], F32, kind="ExternalInput").ap()
    w_qkv = nc.dram_tensor("w_qkv", [3072, 1024], F32, kind="ExternalInput").ap()
    w_router = nc.dram_tensor("w_router", [1024], F32, kind="ExternalInput").ap()
    q_o = nc.dram_tensor("q_o", [128, 8 * T], BF16, kind="ExternalOutput").ap()
    k_o = nc.dram_tensor("k_o", [128, 8 * T], BF16, kind="ExternalOutput").ap()
    v_o = nc.dram_tensor("v_o", [T, 1024], BF16, kind="ExternalOutput").ap()
    r_o = nc.dram_tensor("r_o", [1, T], F32, kind="ExternalOutput").ap()
    ctx = ExitStack()
    with ctx:
        p.setup(ctx)
        ident, b_id = sbuf(nc, ctx, "ident", [128, 128], BF16)
        xt = []; b_xt = []
        for i in range(2):
            t, b = sbuf(nc, ctx, f"xt{i}", [128, 1024]); xt.append(t); b_xt.append(b)
        st, b_st = sbuf(nc, ctx, "st", [128, 8])
        sq, b_sq = sbuf(nc, ctx, "sq", [128, 1024])
        lnb, b_lnb = sbuf(nc, ctx, "lnb", [128, 1024], BF16)
        xn_fm, b_xn = sbuf(nc, ctx, "xn_fm", [128, 8 * T], BF16)
        wn, b_wn = sbuf(nc, ctx, "wn", [128, 2 * 1024], BF16)
        wT, b_wT = sbuf(nc, ctx, "wT", [128, 8 * 3072], BF16)
        wr_row, b_wrr = sbuf(nc, ctx, "wr_row", [1, 1024], BF16)
        wr, b_wr = sbuf(nc, ctx, "wr", [128, 8], BF16)
        qk_sb, b_qk = sbuf(nc, ctx, "qk_sb", [128, 512], BF16)
        v_sb, b_vsb = sbuf(nc, ctx, "v_sb", [128, 1024], BF16)
        r_sb, b_rsb = sbuf(nc, ctx, "r_sb", [1, T])
        ps_t = []; b_pst = []
        for i in range(2):
            t = ctx.enter_context(nc.psum_tensor(f"ps_t{i}", [128, 128], BF16))
            ps_t.append(t); b_pst.append(Buf(f"ps_t{i}"))
        ps_m = []; b_psm = []
        for i in range(2):
            t = ctx.enter_context(nc.psum_tensor(f"ps_m{i}", [128, 512], F32))
            ps_m.append(t); b_psm.append(Buf(f"ps_m{i}"))
        block = ctx.enter_context(nc.Block())

        emit_identity(nc, p, b_id, ident[:])

        for t in range(16):
            i = t % 2
            p.emit("sync", lambda e, t=t, i=i: e.dma_start(out=xt[i][:], in_=x[t*128:(t+1)*128, :]),
                   writes=[b_xt[i]], dma=f"x{i}")
            emit_ln(p, xt[i][:], lnb[:], st[:], sq[:], b_xt[i], b_lnb, b_st, b_sq, 1024)
            for c in range(8):
                j = c % 2
                p.emit("tensor", lambda e, c=c, j=j: e.transpose(ps_t[j][:], lnb[:, c*128:(c+1)*128], ident[:]),
                       reads=[b_lnb, b_id], writes=[b_pst[j]])
                p.emit("vector", lambda e, c=c, j=j, t=t: e.tensor_copy(xn_fm[:, c*T + t*128 : c*T + (t+1)*128], ps_t[j][:]),
                       reads=[b_pst[j]], writes=[b_xn])

        for t in range(24):
            i = t % 2
            p.emit("gpsimd", lambda e, t=t, i=i: e.dma_start(out=wn[:, i*1024:(i+1)*1024], in_=w_qkv[t*128:(t+1)*128, :]),
                   writes=[b_wn], dma=f"w{i}")
            for c in range(8):
                j = c % 2
                p.emit("tensor", lambda e, c=c, i=i, j=j: e.transpose(ps_t[j][:], wn[:, i*1024 + c*128 : i*1024 + (c+1)*128], ident[:]),
                       reads=[b_wn, b_id], writes=[b_pst[j]])
                p.emit("vector", lambda e, c=c, j=j, t=t: e.tensor_copy(wT[:, c*3072 + t*128 : c*3072 + (t+1)*128], ps_t[j][:]),
                       reads=[b_pst[j]], writes=[b_wT])

        p.emit("gpsimd", lambda e: e.dma_start(out=wr_row[:], in_=w_router[None, :]), writes=[b_wrr], dma="wr")
        for c in range(8):
            j = c % 2
            p.emit("tensor", lambda e, c=c, j=j: e.transpose(ps_t[j][0:128, 0:1], wr_row[0:1, c*128:(c+1)*128], ident[0:1, 0:1]),
                   reads=[b_wrr, b_id], writes=[b_pst[j]])
            p.emit("vector", lambda e, c=c, j=j: e.tensor_copy(wr[:, c:c+1], ps_t[j][0:128, 0:1]),
                   reads=[b_pst[j]], writes=[b_wr])

        for which, off, out_t in (("q", 0, q_o), ("k", 1024, k_o)):
            for f in range(8):
                for sc in range(4):
                    j = (f * 4 + sc) % 2
                    for ec in range(8):
                        p.emit("tensor", lambda e, f=f, sc=sc, ec=ec, j=j, off=off:
                               e.matmul(ps_m[j][:], wT[:, ec*3072 + off + f*128 : ec*3072 + off + (f+1)*128],
                                        xn_fm[:, ec*T + sc*512 : ec*T + (sc+1)*512],
                                        start=(ec == 0), stop=(ec == 7)),
                               reads=[b_wT, b_xn], writes=[b_psm[j]])
                    p.emit("vector", lambda e, j=j: e.tensor_copy(qk_sb[:], ps_m[j][:]),
                           reads=[b_psm[j]], writes=[b_qk])
                    p.emit("sync", lambda e, f=f, sc=sc, out_t=out_t:
                           e.dma_start(out=out_t[:, f*T + sc*512 : f*T + (sc+1)*512], in_=qk_sb[:]),
                           reads=[b_qk], dma="qk_o")

        for t in range(16):
            for vh in range(2):
                j = (t * 2 + vh) % 2
                for ec in range(8):
                    p.emit("tensor", lambda e, t=t, vh=vh, ec=ec, j=j:
                           e.matmul(ps_m[j][:], xn_fm[:, ec*T + t*128 : ec*T + (t+1)*128],
                                    wT[:, ec*3072 + 2048 + vh*512 : ec*3072 + 2048 + (vh+1)*512],
                                    start=(ec == 0), stop=(ec == 7)),
                           reads=[b_wT, b_xn], writes=[b_psm[j]])
                p.emit("vector", lambda e, vh=vh, j=j: e.tensor_copy(v_sb[:, vh*512:(vh+1)*512], ps_m[j][:]),
                       reads=[b_psm[j]], writes=[b_vsb])
            p.emit("sync", lambda e, t=t: e.dma_start(out=v_o[t*128:(t+1)*128, :], in_=v_sb[:]),
                   reads=[b_vsb], dma="v_o")

        for sc in range(4):
            j = sc % 2
            for ec in range(8):
                p.emit("tensor", lambda e, sc=sc, ec=ec, j=j:
                       e.matmul(ps_m[j][0:1, :], wr[:, ec:ec+1],
                                xn_fm[:, ec*T + sc*512 : ec*T + (sc+1)*512],
                                start=(ec == 0), stop=(ec == 7)),
                       reads=[b_wr, b_xn], writes=[b_psm[j]])
            p.emit("vector", lambda e, sc=sc, j=j: e.tensor_copy(r_sb[0:1, sc*512:(sc+1)*512], ps_m[j][0:1, :]),
                   reads=[b_psm[j]], writes=[b_rsb])
        p.emit("sync", lambda e: e.dma_start(out=r_o, in_=r_sb[:]), reads=[b_rsb], dma="r_o")

        fin = []
        for n in ("qk_o", "v_o", "r_o"):
            s, c = p.dma_sems[n]
            fin.append(Tok(s, c))
        p.barrier("sync", fin)
        p.play(block)
    return nc


def build_a2a():
    nc = bass.Bass("TRN2", target_bir_lowering=False)
    p = Prog(nc)
    T, S = 2048, 4096
    q = nc.dram_tensor("q", [128, 8 * T], BF16, kind="ExternalInput").ap()
    k = nc.dram_tensor("k", [128, 8 * S], BF16, kind="ExternalInput").ap()
    v = nc.dram_tensor("v", [S, 1024], BF16, kind="ExternalInput").ap()
    o = nc.dram_tensor("o", [T, 1024], BF16, kind="ExternalOutput").ap()
    ctx = ExitStack()
    with ctx:
        p.setup(ctx)
        ident, b_id = sbuf(nc, ctx, "ident", [128, 128], BF16)
        q_sb, b_q = sbuf(nc, ctx, "q_sb", [128, 8 * T], BF16)
        k_sb, b_k = sbuf(nc, ctx, "k_sb", [128, 8 * S], BF16)
        v_sb, b_v = sbuf(nc, ctx, "v_sb", [128, 32 * 1024], BF16)
        sc_sb, b_sc = sbuf(nc, ctx, "sc_sb", [128, S])
        pr_sb, b_pr = sbuf(nc, ctx, "pr_sb", [128, S], BF16)
        pt_sb, b_pt = sbuf(nc, ctx, "pt_sb", [128, 32 * 128], BF16)
        st, b_st = sbuf(nc, ctx, "st", [128, 8])
        o_sb = []; b_o = []
        for i in range(2):
            t, b = sbuf(nc, ctx, f"o_sb{i}", [128, 1024], BF16); o_sb.append(t); b_o.append(b)
        ps_s = []; b_ps = []
        for i in range(2):
            t = ctx.enter_context(nc.psum_tensor(f"ps_s{i}", [128, 512], F32))
            ps_s.append(t); b_ps.append(Buf(f"ps_s{i}"))
        ps_t = []; b_pst = []
        for i in range(2):
            t = ctx.enter_context(nc.psum_tensor(f"ps_t{i}", [128, 128], BF16))
            ps_t.append(t); b_pst.append(Buf(f"ps_t{i}"))
        ps_av = []; b_pav = []
        for i in range(2):
            t = ctx.enter_context(nc.psum_tensor(f"ps_av{i}", [128, 512], F32))
            ps_av.append(t); b_pav.append(Buf(f"ps_av{i}"))
        block = ctx.enter_context(nc.Block())

        emit_identity(nc, p, b_id, ident[:])
        p.emit("sync", lambda e: e.dma_start(out=q_sb[:], in_=q), writes=[b_q], dma="q")
        p.emit("sync", lambda e: e.dma_start(out=k_sb[:], in_=k), writes=[b_k], dma="k")
        p.emit("sync", lambda e: e.dma_start(out=v_sb[:].rearrange("p (t d) -> p t d", d=1024),
                                             in_=v.rearrange("(t p) d -> p t d", p=128)),
               writes=[b_v], dma="v")

        for qt in range(16):
            for kc in range(8):
                j = kc % 2
                for qf in range(8):
                    p.emit("tensor", lambda e, qt=qt, kc=kc, qf=qf, j=j:
                           e.matmul(ps_s[j][:], q_sb[:, qf*T + qt*128 : qf*T + (qt+1)*128],
                                    k_sb[:, qf*S + kc*512 : qf*S + (kc+1)*512],
                                    start=(qf == 0), stop=(qf == 7)),
                           reads=[b_q, b_k], writes=[b_ps[j]])
                p.emit("vector", lambda e, kc=kc, j=j: e.tensor_copy(sc_sb[:, kc*512:(kc+1)*512], ps_s[j][:]),
                       reads=[b_ps[j]], writes=[b_sc])
            p.emit("vector", lambda e: e.reduce_max(st[:, 0:1], sc_sb[:], axis=AX.X, negate=True),
                   reads=[b_sc], writes=[b_st])
            p.emit("vector", lambda e: e.tensor_scalar_mul(st[:, 1:2], st[:, 0:1], 0.125),
                   reads=[b_st], writes=[b_st])
            p.emit("scalar", lambda e: e.activation(pr_sb[:], sc_sb[:], AF.Exp, bias=st[:, 1:2],
                                                    scale=0.125, accum_out=st[:, 2:3]),
                   reads=[b_sc, b_st], writes=[b_pr, b_st])
            p.emit("vector", lambda e: e.reciprocal(st[:, 3:4], st[:, 2:3]), reads=[b_st], writes=[b_st])
            for kt in range(32):
                j = kt % 2
                p.emit("tensor", lambda e, kt=kt, j=j: e.transpose(ps_t[j][:], pr_sb[:, kt*128:(kt+1)*128], ident[:]),
                       reads=[b_pr, b_id], writes=[b_pst[j]])
                p.emit("vector", lambda e, kt=kt, j=j: e.tensor_copy(pt_sb[:, kt*128:(kt+1)*128], ps_t[j][:]),
                       reads=[b_pst[j]], writes=[b_pt])
            ob = qt % 2
            for vh in range(2):
                j = vh
                for kt in range(32):
                    p.emit("tensor", lambda e, kt=kt, vh=vh, j=j:
                           e.matmul(ps_av[j][:], pt_sb[:, kt*128:(kt+1)*128],
                                    v_sb[:, kt*1024 + vh*512 : kt*1024 + (vh+1)*512],
                                    start=(kt == 0), stop=(kt == 31)),
                           reads=[b_pt, b_v], writes=[b_pav[j]])
                p.emit("vector", lambda e, vh=vh, j=j, ob=ob:
                       e.tensor_scalar(o_sb[ob][:, vh*512:(vh+1)*512], ps_av[j][:], st[:, 3:4], None, op0=AL.mult),
                       reads=[b_pav[j], b_st], writes=[b_o[ob]])
            p.emit("sync", lambda e, qt=qt, ob=ob: e.dma_start(out=o[qt*128:(qt+1)*128, :], in_=o_sb[ob][:]),
                   reads=[b_o[ob]], dma="o")
        s_, c_ = p.dma_sems["o"]
        p.barrier("sync", [Tok(s_, c_)])
        p.play(block)
    return nc


def build_a2b(TOPK=3072, NIT=36):
    nc = bass.Bass("TRN2", target_bir_lowering=False)
    p = Prog(nc)
    T = 2048
    og = nc.dram_tensor("og", [T, 1024], BF16, kind="ExternalInput").ap()
    x_sig = nc.dram_tensor("x_sig", [T, 1024], F32, kind="ExternalInput").ap()
    rsc = nc.dram_tensor("rsc", [4096], F32, kind="ExternalInput").ap()
    rsc_sig = nc.dram_tensor("rsc_sig", [2048], F32, kind="ExternalInput").ap()
    w_proj = nc.dram_tensor("w_proj", [1024, 1024], F32, kind="ExternalInput").ap()
    b_proj = nc.dram_tensor("b_proj", [1024], F32, kind="ExternalInput").ap()
    w_block = nc.dram_tensor("w_block", [1024, 1024], F32, kind="ExternalInput").ap()
    x1_o = nc.dram_tensor("x1_o", [T, 1024], F32, kind="ExternalOutput").ap()
    ctx = ExitStack()
    with ctx:
        p.setup(ctx)
        ident, b_id = sbuf(nc, ctx, "ident", [128, 128], BF16)
        identf, b_idf = sbuf(nc, ctx, "identf", [128, 128])
        ones128, b_o128 = sbuf(nc, ctx, "ones128", [128, 128])
        og_sb, b_og = sbuf(nc, ctx, "og_sb", [128, 16 * 1024], BF16)
        wpn, b_wpn = sbuf(nc, ctx, "wpn", [128, 2 * 1024], BF16)
        wpT, b_wpT = sbuf(nc, ctx, "wpT", [128, 8 * 1024], BF16)
        wb_sb, b_wb = sbuf(nc, ctx, "wb_sb", [128, 8 * 1024], BF16)
        bp_row, b_bpr = sbuf(nc, ctx, "bp_row", [1, 1024])
        bpb, b_bpb = sbuf(nc, ctx, "bpb", [128, 1024])
        rs_sb, b_rs = sbuf(nc, ctx, "rs_sb", [128, 32])
        rsig_sb, b_rsig = sbuf(nc, ctx, "rsig_sb", [1, 2048])
        cmp_sb, b_cmp = sbuf(nc, ctx, "cmp_sb", [128, 32])
        thr, b_thr = sbuf(nc, ctx, "thr", [128, 1])
        sgn, b_sgn = sbuf(nc, ctx, "sgn", [128, 1])
        step_tab, b_stp = sbuf(nc, ctx, "step_tab", [128, NIT])
        st4, b_st4 = sbuf(nc, ctx, "st4", [128, 8])
        row_sc, b_rsc2 = sbuf(nc, ctx, "row_sc", [1, 128])
        mcol, b_mc = sbuf(nc, ctx, "mcol", [128, 2])
        xt, b_xt = sbuf(nc, ctx, "xt", [128, 1024])
        st, b_st = sbuf(nc, ctx, "st", [128, 8])
        sq, b_sq = sbuf(nc, ctx, "sq", [128, 1024])
        xn_f, b_xnf = sbuf(nc, ctx, "xn_f", [128, 1024])
        xn_b, b_xnb = sbuf(nc, ctx, "xn_b", [128, 1024], BF16)
        xnfm, b_xnfm = sbuf(nc, ctx, "xnfm", [128, 8 * 128], BF16)
        y_sb, b_y = sbuf(nc, ctx, "y_sb", [128, 1024])
        a3_sb, b_a3 = sbuf(nc, ctx, "a3_sb", [128, 1024])
        d_sb, b_d = sbuf(nc, ctx, "d_sb", [128, 1024])
        x1_sb, b_x1 = sbuf(nc, ctx, "x1_sb", [128, 1024])
        ps_y = ctx.enter_context(nc.psum_tensor("ps_y", [128, 1024], F32)); b_psy = Buf("ps_y")
        ps_p = ctx.enter_context(nc.psum_tensor("ps_p", [128, 1024], F32)); b_psp = Buf("ps_p")
        ps_c = ctx.enter_context(nc.psum_tensor("ps_c", [128, 128], F32)); b_psc = Buf("ps_c")
        ps_t = []; b_pst = []
        for i in range(2):
            t = ctx.enter_context(nc.psum_tensor(f"ps_t{i}", [128, 128], BF16))
            ps_t.append(t); b_pst.append(Buf(f"ps_t{i}"))
        block = ctx.enter_context(nc.Block())

        emit_identity(nc, p, b_id, ident[:])
        p.emit("gpsimd", lambda g: g.memset(identf[:], 0.0), writes=[b_idf])
        p.emit("gpsimd", lambda g: g.affine_select(out=identf[:], in_=identf[:],
            compare_op=AL.not_equal, fill=1.0, base=0, pattern=[[-1, 128]], channel_multiplier=1),
            reads=[b_idf], writes=[b_idf])
        p.emit("vector", lambda e: e.memset(ones128[:], 1.0), writes=[b_o128])

        p.emit("sync", lambda e: e.dma_start(out=og_sb[:].rearrange("p (t d) -> p t d", d=1024),
                                             in_=og.rearrange("(t p) d -> p t d", p=128)),
               writes=[b_og], dma="og")
        p.emit("gpsimd", lambda e: e.dma_start(out=wb_sb[:].rearrange("p (t d) -> p t d", d=1024),
                                               in_=w_block.rearrange("(t p) d -> p t d", p=128)),
               writes=[b_wb], dma="wb")
        p.emit("sync", lambda e: e.dma_start(out=rs_sb[:], in_=rsc.rearrange("(q j) -> q j", j=32)),
               writes=[b_rs], dma="rs")
        p.emit("sync", lambda e: e.dma_start(out=rsig_sb[:], in_=rsc_sig[None, :]), writes=[b_rsig], dma="rs")
        p.emit("sync", lambda e: e.dma_start(out=bp_row[:], in_=b_proj[None, :]), writes=[b_bpr], dma="rs")
        for t in range(8):
            i = t % 2
            p.emit("gpsimd", lambda e, t=t, i=i: e.dma_start(out=wpn[:, i*1024:(i+1)*1024],
                                                             in_=w_proj[t*128:(t+1)*128, :]),
                   writes=[b_wpn], dma=f"wp{i}")
            for c in range(8):
                j = c % 2
                p.emit("tensor", lambda e, c=c, i=i, j=j: e.transpose(ps_t[j][:], wpn[:, i*1024 + c*128 : i*1024 + (c+1)*128], ident[:]),
                       reads=[b_wpn, b_id], writes=[b_pst[j]])
                p.emit("vector", lambda e, c=c, j=j, t=t: e.tensor_copy(wpT[:, c*1024 + t*128 : c*1024 + (t+1)*128], ps_t[j][:]),
                       reads=[b_pst[j]], writes=[b_wpT])
        for eh in range(2):
            p.emit("tensor", lambda e, eh=eh: e.matmul(ps_p[:, eh*512:(eh+1)*512], ones128[0:1, :],
                                                       bp_row[0:1, eh*512:(eh+1)*512], start=True, stop=True),
                   reads=[b_o128, b_bpr], writes=[b_psp])
        p.emit("vector", lambda e: e.tensor_copy(bpb[:], ps_p[:]), reads=[b_psp], writes=[b_bpb])

        p.emit("vector", lambda e: e.reduce_max(st4[:, 0:1], rs_sb[:], axis=AX.X, apply_absolute_value=True),
               reads=[b_rs], writes=[b_st4])
        p.emit("tensor", lambda e: e.transpose(ps_c[0:1, 0:128], st4[:, 0:1], identf[:]),
               reads=[b_st4, b_idf], writes=[b_psc])
        p.emit("vector", lambda e: e.tensor_copy(row_sc[:], ps_c[0:1, 0:128]), reads=[b_psc], writes=[b_rsc2])
        p.emit("vector", lambda e: e.reduce_max(row_sc[0:1, 0:1], row_sc[:], axis=AX.X),
               reads=[b_rsc2], writes=[b_rsc2])
        p.emit("tensor", lambda e: e.matmul(ps_c[:, 0:1], ones128[0:1, :], row_sc[0:1, 0:1], start=True, stop=True),
               reads=[b_o128, b_rsc2], writes=[b_psc])
        p.emit("vector", lambda e: e.tensor_copy(st4[:, 1:2], ps_c[:, 0:1]), reads=[b_psc], writes=[b_st4])
        for i in range(NIT):
            p.emit("vector", lambda e, i=i: e.tensor_scalar_mul(step_tab[:, i:i+1], st4[:, 1:2], 2.0 ** (-i)),
                   reads=[b_st4], writes=[b_stp])
        p.emit("vector", lambda e: e.memset(thr[:], 0.0), writes=[b_thr])
        for i in range(NIT):
            p.emit("vector", lambda e: e.tensor_scalar(cmp_sb[:], rs_sb[:], thr[:, 0:1], None, op0=AL.is_gt),
                   reads=[b_rs, b_thr], writes=[b_cmp])
            p.emit("vector", lambda e: e.reduce_sum(st4[:, 2:3], cmp_sb[:], axis=AX.X),
                   reads=[b_cmp], writes=[b_st4])
            p.emit("tensor", lambda e: e.matmul(ps_c[:, 1:2], ones128[:], st4[:, 2:3], start=True, stop=True),
                   reads=[b_o128, b_st4], writes=[b_psc])
            p.emit("vector", lambda e: e.tensor_scalar(st4[:, 3:4], ps_c[:, 1:2], float(-TOPK), None, op0=AL.add),
                   reads=[b_psc], writes=[b_st4])
            p.emit("scalar", lambda e: e.activation(sgn[:], st4[:, 3:4], AF.Sign),
                   reads=[b_st4], writes=[b_sgn])
            p.emit("vector", lambda e, i=i: e.scalar_tensor_tensor(thr[:], sgn[:], step_tab[:, i:i+1], thr[:],
                                                                   op0=AL.mult, op1=AL.add),
                   reads=[b_sgn, b_stp, b_thr], writes=[b_thr])

        for t in range(16):
            gg, hc = t // 8, t % 8
            p.emit("sync", lambda e, t=t: e.dma_start(out=xt[:], in_=x_sig[t*128:(t+1)*128, :]),
                   writes=[b_xt], dma="xs")
            emit_ln(p, xt[:], xn_f[:], st[:], sq[:], b_xt, b_xnf, b_st, b_sq, 1024)
            p.emit("scalar", lambda e: e.activation(xn_b[:], xn_f[:], AF.Copy), reads=[b_xnf], writes=[b_xnb])
            for c in range(8):
                j = c % 2
                p.emit("tensor", lambda e, c=c, j=j: e.transpose(ps_t[j][:], xn_b[:, c*128:(c+1)*128], ident[:]),
                       reads=[b_xnb, b_id], writes=[b_pst[j]])
                p.emit("vector", lambda e, c=c, j=j: e.tensor_copy(xnfm[:, c*128:(c+1)*128], ps_t[j][:]),
                       reads=[b_pst[j]], writes=[b_xnfm])
            for eh in range(2):
                for dc in range(8):
                    p.emit("tensor", lambda e, eh=eh, dc=dc:
                           e.matmul(ps_y[:, eh*512:(eh+1)*512], xnfm[:, dc*128:(dc+1)*128],
                                    wb_sb[:, dc*1024 + eh*512 : dc*1024 + (eh+1)*512],
                                    start=(dc == 0), stop=(dc == 7)),
                           reads=[b_xnfm, b_wb], writes=[b_psy])
            p.emit("scalar", lambda e: e.activation(y_sb[:], ps_y[:], AF.Gelu_apprx_tanh),
                   reads=[b_psy], writes=[b_y])
            for eh in range(2):
                for jc in range(8):
                    p.emit("tensor", lambda e, eh=eh, jc=jc, gg=gg, hc=hc:
                           e.matmul(ps_p[:, eh*512:(eh+1)*512],
                                    og_sb[:, (8*gg + jc)*1024 + hc*128 : (8*gg + jc)*1024 + (hc+1)*128],
                                    wpT[:, jc*1024 + eh*512 : jc*1024 + (eh+1)*512],
                                    start=(jc == 0), stop=(jc == 7)),
                           reads=[b_og, b_wpT], writes=[b_psp])
            p.emit("vector", lambda e: e.tensor_add(a3_sb[:], ps_p[:], bpb[:]), reads=[b_psp, b_bpb], writes=[b_a3])
            p.emit("tensor", lambda e, t=t: e.transpose(ps_c[0:128, 2:3], rsig_sb[0:1, t*128:(t+1)*128], identf[0:1, 0:1]),
                   reads=[b_rsig, b_idf], writes=[b_psc])
            p.emit("vector", lambda e: e.tensor_copy(mcol[:, 0:1], ps_c[0:128, 2:3]), reads=[b_psc], writes=[b_mc])
            p.emit("vector", lambda e: e.tensor_scalar(mcol[:, 1:2], mcol[:, 0:1], thr[:, 0:1], None, op0=AL.is_gt),
                   reads=[b_mc, b_thr], writes=[b_mc])
            p.emit("vector", lambda e: e.tensor_sub(d_sb[:], y_sb[:], xn_f[:]), reads=[b_y, b_xnf], writes=[b_d])
            p.emit("vector", lambda e: e.scalar_tensor_tensor(d_sb[:], d_sb[:], mcol[:, 1:2], xn_f[:],
                                                              op0=AL.mult, op1=AL.add),
                   reads=[b_d, b_mc, b_xnf], writes=[b_d])
            p.emit("vector", lambda e: e.tensor_add(x1_sb[:], xt[:], a3_sb[:]), reads=[b_xt, b_a3], writes=[b_x1])
            p.emit("vector", lambda e: e.tensor_add(x1_sb[:], x1_sb[:], d_sb[:]), reads=[b_x1, b_d], writes=[b_x1])
            p.emit("sync", lambda e, t=t: e.dma_start(out=x1_o[t*128:(t+1)*128, :], in_=x1_sb[:]),
                   reads=[b_x1], dma="x1")
        s_, c_ = p.dma_sems["x1"]
        p.barrier("sync", [Tok(s_, c_)])
        p.play(block)
    return nc


def build_b1():
    nc = bass.Bass("TRN2", target_bir_lowering=False)
    p = Prog(nc)
    T = 2048
    x1 = nc.dram_tensor("x1", [T, 1024], F32, kind="ExternalInput").ap()
    norm_w = nc.dram_tensor("norm_w", [1024], F32, kind="ExternalInput").ap()
    norm_b = nc.dram_tensor("norm_b", [1024], F32, kind="ExternalInput").ap()
    Wq = nc.dram_tensor("Wq", [1024, 512], F32, kind="ExternalInput").ap()
    Wkv = nc.dram_tensor("Wkv", [1024, 128], F32, kind="ExternalInput").ap()
    xn1_tm_o = nc.dram_tensor("xn1_tm_o", [T, 1024], BF16, kind="ExternalOutput").ap()
    xn1_fm_o = nc.dram_tensor("xn1_fm_o", [128, 8 * T], BF16, kind="ExternalOutput").ap()
    q_o = nc.dram_tensor("q_o", [64, 8 * T], BF16, kind="ExternalOutput").ap()
    k_o = nc.dram_tensor("k_o", [64, T], BF16, kind="ExternalOutput").ap()
    v_o = nc.dram_tensor("v_o", [T, 64], BF16, kind="ExternalOutput").ap()
    ctx = ExitStack()
    with ctx:
        p.setup(ctx)
        ident, b_id = sbuf(nc, ctx, "ident", [128, 128], BF16)
        ones1, b_on = sbuf(nc, ctx, "ones1", [1, 128])
        nw_row, b_nwr = sbuf(nc, ctx, "nw_row", [1, 1024])
        nb_row, b_nbr = sbuf(nc, ctx, "nb_row", [1, 1024])
        nwb, b_nwb = sbuf(nc, ctx, "nwb", [128, 1024])
        nbb, b_nbb = sbuf(nc, ctx, "nbb", [128, 1024])
        wq_sb, b_wq = sbuf(nc, ctx, "wq_sb", [128, 8 * 512], BF16)
        wkv_sb, b_wkv = sbuf(nc, ctx, "wkv_sb", [128, 8 * 128], BF16)
        xt, b_xt = sbuf(nc, ctx, "xt", [128, 1024])
        st, b_st = sbuf(nc, ctx, "st", [128, 8])
        sq, b_sq = sbuf(nc, ctx, "sq", [128, 1024])
        xn1_f, b_xn1f = sbuf(nc, ctx, "xn1_f", [128, 1024])
        xn1_b, b_xn1b = sbuf(nc, ctx, "xn1_b", [128, 1024], BF16)
        xn1n, b_xn1n = sbuf(nc, ctx, "xn1n", [128, 1024])
        xa_b, b_xab = sbuf(nc, ctx, "xa_b", [128, 1024], BF16)
        xa_fm, b_xafm = sbuf(nc, ctx, "xa_fm", [128, 8 * T], BF16)
        xn1_fm, b_xn1fm = sbuf(nc, ctx, "xn1_fm", [128, 8 * T], BF16)
        ev_sb, b_ev = sbuf(nc, ctx, "ev_sb", [128, 512], BF16)
        v_sb, b_vsb = sbuf(nc, ctx, "v_sb", [128, 64], BF16)
        ps_m = []; b_psm = []
        for i in range(2):
            t = ctx.enter_context(nc.psum_tensor(f"ps_m{i}", [128, 512], F32))
            ps_m.append(t); b_psm.append(Buf(f"ps_m{i}"))
        ps_t = []; b_pst = []
        for i in range(2):
            t = ctx.enter_context(nc.psum_tensor(f"ps_t{i}", [128, 128], BF16))
            ps_t.append(t); b_pst.append(Buf(f"ps_t{i}"))
        ps_b = ctx.enter_context(nc.psum_tensor("ps_b", [128, 1024], F32)); b_psb = Buf("ps_b")
        block = ctx.enter_context(nc.Block())

        emit_identity(nc, p, b_id, ident[:])
        p.emit("vector", lambda e: e.memset(ones1[:], 1.0), writes=[b_on])
        p.emit("sync", lambda e: e.dma_start(out=nw_row[:], in_=norm_w[None, :]), writes=[b_nwr], dma="n")
        p.emit("sync", lambda e: e.dma_start(out=nb_row[:], in_=norm_b[None, :]), writes=[b_nbr], dma="n2")
        p.emit("gpsimd", lambda e: e.dma_start(out=wq_sb[:].rearrange("p (t d) -> p t d", d=512),
                                               in_=Wq.rearrange("(t p) d -> p t d", p=128)),
               writes=[b_wq], dma="wq")
        p.emit("gpsimd", lambda e: e.dma_start(out=wkv_sb[:].rearrange("p (t d) -> p t d", d=128),
                                               in_=Wkv.rearrange("(t p) d -> p t d", p=128)),
               writes=[b_wkv], dma="wkv")
        for eh in range(2):
            p.emit("tensor", lambda e, eh=eh: e.matmul(ps_b[:, eh*512:(eh+1)*512], ones1[:],
                                                       nw_row[0:1, eh*512:(eh+1)*512], start=True, stop=True),
                   reads=[b_on, b_nwr], writes=[b_psb])
        p.emit("vector", lambda e: e.tensor_copy(nwb[:], ps_b[:]), reads=[b_psb], writes=[b_nwb])
        for eh in range(2):
            p.emit("tensor", lambda e, eh=eh: e.matmul(ps_b[:, eh*512:(eh+1)*512], ones1[:],
                                                       nb_row[0:1, eh*512:(eh+1)*512], start=True, stop=True),
                   reads=[b_on, b_nbr], writes=[b_psb])
        p.emit("vector", lambda e: e.tensor_copy(nbb[:], ps_b[:]), reads=[b_psb], writes=[b_nbb])

        for t in range(16):
            p.emit("sync", lambda e, t=t: e.dma_start(out=xt[:], in_=x1[t*128:(t+1)*128, :]),
                   writes=[b_xt], dma="x")
            emit_ln(p, xt[:], xn1_f[:], st[:], sq[:], b_xt, b_xn1f, b_st, b_sq, 1024)
            p.emit("scalar", lambda e: e.activation(xn1_b[:], xn1_f[:], AF.Copy), reads=[b_xn1f], writes=[b_xn1b])
            p.emit("sync", lambda e, t=t: e.dma_start(out=xn1_tm_o[t*128:(t+1)*128, :], in_=xn1_b[:]),
                   reads=[b_xn1b], dma="xo")
            emit_ln(p, xn1_f[:], xn1n[:], st[:], sq[:], b_xn1f, b_xn1n, b_st, b_sq, 1024)
            p.emit("vector", lambda e: e.tensor_mul(xn1n[:], xn1n[:], nwb[:]), reads=[b_xn1n, b_nwb], writes=[b_xn1n])
            p.emit("vector", lambda e: e.tensor_add(xa_b[:], xn1n[:], nbb[:]), reads=[b_xn1n, b_nbb], writes=[b_xab])
            for c in range(8):
                j = c % 2
                p.emit("tensor", lambda e, c=c, j=j: e.transpose(ps_t[j][:], xa_b[:, c*128:(c+1)*128], ident[:]),
                       reads=[b_xab, b_id], writes=[b_pst[j]])
                p.emit("vector", lambda e, c=c, j=j, t=t: e.tensor_copy(xa_fm[:, c*T + t*128 : c*T + (t+1)*128], ps_t[j][:]),
                       reads=[b_pst[j]], writes=[b_xafm])
            for c in range(8):
                j = c % 2
                p.emit("tensor", lambda e, c=c, j=j: e.transpose(ps_t[j][:], xn1_b[:, c*128:(c+1)*128], ident[:]),
                       reads=[b_xn1b, b_id], writes=[b_pst[j]])
                p.emit("vector", lambda e, c=c, j=j, t=t: e.tensor_copy(xn1_fm[:, c*T + t*128 : c*T + (t+1)*128], ps_t[j][:]),
                       reads=[b_pst[j]], writes=[b_xn1fm])
        p.emit("sync", lambda e: e.dma_start(out=xn1_fm_o, in_=xn1_fm[:]), reads=[b_xn1fm], dma="xo")

        for ic in range(4):
            for sc in range(4):
                j = (ic * 4 + sc) % 2
                for dc in range(8):
                    p.emit("tensor", lambda e, ic=ic, sc=sc, dc=dc, j=j:
                           e.matmul(ps_m[j][:], wq_sb[:, dc*512 + ic*128 : dc*512 + (ic+1)*128],
                                    xa_fm[:, dc*T + sc*512 : dc*T + (sc+1)*512],
                                    start=(dc == 0), stop=(dc == 7)),
                           reads=[b_wq, b_xafm], writes=[b_psm[j]])
                for hh in range(2):
                    p.emit("vector", lambda e, hh=hh, j=j: e.tensor_copy(ev_sb[0:64, :], ps_m[j][hh*64:hh*64+64, :]),
                           reads=[b_psm[j]], writes=[b_ev])
                    p.emit("sync", lambda e, ic=ic, sc=sc, hh=hh:
                           e.dma_start(out=q_o[:, (2*ic+hh)*T + sc*512 : (2*ic+hh)*T + (sc+1)*512], in_=ev_sb[0:64, :]),
                           reads=[b_ev], dma="qo")
        for sc in range(4):
            j = sc % 2
            for dc in range(8):
                p.emit("tensor", lambda e, sc=sc, dc=dc, j=j:
                       e.matmul(ps_m[j][0:64, :], wkv_sb[:, dc*128 : dc*128 + 64],
                                xa_fm[:, dc*T + sc*512 : dc*T + (sc+1)*512],
                                start=(dc == 0), stop=(dc == 7)),
                       reads=[b_wkv, b_xafm], writes=[b_psm[j]])
            p.emit("vector", lambda e, j=j: e.tensor_copy(ev_sb[0:64, :], ps_m[j][0:64, :]),
                   reads=[b_psm[j]], writes=[b_ev])
            p.emit("sync", lambda e, sc=sc: e.dma_start(out=k_o[:, sc*512:(sc+1)*512], in_=ev_sb[0:64, :]),
                   reads=[b_ev], dma="ko")
        for t in range(16):
            j = t % 2
            for dc in range(8):
                p.emit("tensor", lambda e, t=t, dc=dc, j=j:
                       e.matmul(ps_m[j][:, 0:64], xa_fm[:, dc*T + t*128 : dc*T + (t+1)*128],
                                wkv_sb[:, dc*128 + 64 : dc*128 + 128],
                                start=(dc == 0), stop=(dc == 7)),
                       reads=[b_wkv, b_xafm], writes=[b_psm[j]])
            p.emit("vector", lambda e, j=j: e.tensor_copy(v_sb[:], ps_m[j][:, 0:64]),
                   reads=[b_psm[j]], writes=[b_vsb])
            p.emit("sync", lambda e, t=t: e.dma_start(out=v_o[t*128:(t+1)*128, :], in_=v_sb[:]),
                   reads=[b_vsb], dma="vo")
        fin = []
        for n in ("xo", "qo", "ko", "vo"):
            s_, c_ = p.dma_sems[n]
            fin.append(Tok(s_, c_))
        p.barrier("sync", fin)
        p.play(block)
    return nc


def build_b2(debug=False):
    nc = bass.Bass("TRN2", target_bir_lowering=False)
    p = Prog(nc)
    T, S = 2048, 4096
    q = nc.dram_tensor("q", [64, 8 * T], BF16, kind="ExternalInput").ap()
    k = nc.dram_tensor("k", [64, S], BF16, kind="ExternalInput").ap()
    v = nc.dram_tensor("v", [S, 64], BF16, kind="ExternalInput").ap()
    xn1_fm = nc.dram_tensor("xn1_fm", [128, 8 * T], BF16, kind="ExternalInput").ap()
    x1 = nc.dram_tensor("x1", [T, 1024], F32, kind="ExternalInput").ap()
    Wo = nc.dram_tensor("Wo", [512, 1024], F32, kind="ExternalInput").ap()
    onw = nc.dram_tensor("onw", [1024], F32, kind="ExternalInput").ap()
    onb = nc.dram_tensor("onb", [1024], F32, kind="ExternalInput").ap()
    gate_w = nc.dram_tensor("gate_w", [8, 1024], F32, kind="ExternalInput").ap()
    gate_b = nc.dram_tensor("gate_b", [8], F32, kind="ExternalInput").ap()
    expert_bias = nc.dram_tensor("expert_bias", [8], F32, kind="ExternalInput").ap()
    w1 = nc.dram_tensor("w1", [8, 1024, 4096], F32, kind="ExternalInput").ap()
    b1 = nc.dram_tensor("b1", [8, 4096], F32, kind="ExternalInput").ap()
    w2 = nc.dram_tensor("w2", [8, 4096, 1024], F32, kind="ExternalInput").ap()
    b2 = nc.dram_tensor("b2", [8, 1024], F32, kind="ExternalInput").ap()
    fin_o = nc.dram_tensor("fin_o", [T, 1024], F32, kind="ExternalOutput").ap()
    attn1_dr = nc.dram_tensor("attn1_dr", [T, 1024], BF16,
                              kind="ExternalOutput" if debug else "Internal").ap()
    if debug:
        moe_o = nc.dram_tensor("moe_o", [128, 16 * 1024], BF16, kind="ExternalOutput").ap()
        oh_o = nc.dram_tensor("oh_o", [128, 16 * 8], F32, kind="ExternalOutput").ap()
        o1_o = nc.dram_tensor("o1_o", [128, 4 * 2048], BF16, kind="ExternalOutput").ap()
    ctx = ExitStack()
    with ctx:
        p.setup(ctx)
        ident, b_id = sbuf(nc, ctx, "ident", [128, 128], BF16)
        identf, b_idf = sbuf(nc, ctx, "identf", [128, 128])
        ones1, b_on = sbuf(nc, ctx, "ones1", [1, 128])
        oh_tm, b_oht = sbuf(nc, ctx, "oh_tm", [128, 16 * 8])
        oh_em, b_ohe = sbuf(nc, ctx, "oh_em", [8, 16 * 128], BF16)
        moe_sb, b_moe = sbuf(nc, ctx, "moe_sb", [128, 16 * 1024], BF16)
        block = ctx.enter_context(nc.Block())
        emit_identity(nc, p, b_id, ident[:])
        p.emit("gpsimd", lambda g: g.memset(identf[:], 0.0), writes=[b_idf])
        p.emit("gpsimd", lambda g: g.affine_select(out=identf[:], in_=identf[:],
            compare_op=AL.not_equal, fill=1.0, base=0, pattern=[[-1, 128]], channel_multiplier=1),
            reads=[b_idf], writes=[b_idf])
        p.emit("vector", lambda e: e.memset(ones1[:], 1.0), writes=[b_on])

        with ExitStack() as c1:
            q_sb, b_q = sbuf(nc, c1, "q_sb", [64, 8 * T], BF16)
            k_sb, b_k = sbuf(nc, c1, "k_sb", [64, S], BF16)
            v_sb, b_v = sbuf(nc, c1, "v_sb", [128, 32 * 64], BF16)
            wo_sb, b_wo = sbuf(nc, c1, "wo_sb", [128, 4 * 1024], BF16)
            onwb, b_onwb = sbuf(nc, c1, "onwb", [128, 1024])
            onbb, b_onbb = sbuf(nc, c1, "onbb", [128, 1024])
            row_sb, b_row = sbuf(nc, c1, "row_sb", [1, 1024])
            sc_sb, b_sc = sbuf(nc, c1, "sc_sb", [128, S])
            pr_sb, b_pr = sbuf(nc, c1, "pr_sb", [128, S], BF16)
            pt_sb, b_pt = sbuf(nc, c1, "pt_sb", [128, 32 * 512], BF16)
            st, b_st = sbuf(nc, c1, "st", [128, 8])
            o1_sb, b_o1 = sbuf(nc, c1, "o1_sb", [128, 4 * T], BF16)
            z_sb, b_z = sbuf(nc, c1, "z_sb", [128, 1024])
            sq, b_sq = sbuf(nc, c1, "sq", [128, 1024])
            a1_sb, b_a1 = sbuf(nc, c1, "a1_sb", [128, 1024], BF16)
            ps_s = []; b_ps = []
            for i in range(2):
                t = c1.enter_context(nc.psum_tensor(f"ps_s{i}", [128, 512], F32))
                ps_s.append(t); b_ps.append(Buf(f"ps_s{i}"))
            ps_t = []; b_pst = []
            for i in range(2):
                t = c1.enter_context(nc.psum_tensor(f"ps_t{i}", [128, 128], BF16))
                ps_t.append(t); b_pst.append(Buf(f"ps_t{i}"))
            ps_o = c1.enter_context(nc.psum_tensor("ps_o", [128, 512], F32)); b_pso = Buf("ps_o")
            ps_z = c1.enter_context(nc.psum_tensor("ps_z", [128, 1024], F32)); b_psz = Buf("ps_z")

            p.emit("sync", lambda e: e.dma_start(out=q_sb[:], in_=q), writes=[b_q], dma="q")
            p.emit("sync", lambda e: e.dma_start(out=k_sb[:], in_=k), writes=[b_k], dma="kk")
            p.emit("sync", lambda e: e.dma_start(out=v_sb[:].rearrange("p (t d) -> p t d", d=64),
                                                 in_=v.rearrange("(t p) d -> p t d", p=128)),
                   writes=[b_v], dma="vv")
            p.emit("gpsimd", lambda e: e.dma_start(out=wo_sb[:].rearrange("p (t d) -> p t d", d=1024),
                                                   in_=Wo.rearrange("(t p) d -> p t d", p=128)),
                   writes=[b_wo], dma="wo")
            for nm, row_b, dst, b_dst in (("onw", onw, onwb, b_onwb), ("onb", onb, onbb, b_onbb)):
                p.emit("sync", lambda e, row_b=row_b: e.dma_start(out=row_sb[:], in_=row_b[None, :]),
                       writes=[b_row], dma="nr")
                for eh in range(2):
                    p.emit("tensor", lambda e, eh=eh: e.matmul(ps_z[:, eh*512:(eh+1)*512], ones1[:],
                                                               row_sb[0:1, eh*512:(eh+1)*512], start=True, stop=True),
                           reads=[b_on, b_row], writes=[b_psz])
                p.emit("vector", lambda e, dst=dst: e.tensor_copy(dst[:], ps_z[:]), reads=[b_psz], writes=[b_dst])

            for h in range(8):
                hb = 64 * (h % 2)
                for qg in range(4):
                    for q4 in range(4):
                        qt = qg * 4 + q4
                        for kc in range(8):
                            j = kc % 2
                            p.emit("tensor", lambda e, h=h, hb=hb, qt=qt, kc=kc, j=j:
                                   e.matmul(ps_s[j][:], q_sb[:, h*T + qt*128 : h*T + (qt+1)*128],
                                            k_sb[:, kc*512:(kc+1)*512], start=True, stop=True),
                                   reads=[b_q, b_k], writes=[b_ps[j]])
                            p.emit("vector", lambda e, kc=kc, j=j: e.tensor_copy(sc_sb[:, kc*512:(kc+1)*512], ps_s[j][:]),
                                   reads=[b_ps[j]], writes=[b_sc])
                        p.emit("vector", lambda e: e.reduce_max(st[:, 0:1], sc_sb[:], axis=AX.X, negate=True),
                               reads=[b_sc], writes=[b_st])
                        p.emit("vector", lambda e: e.tensor_scalar_mul(st[:, 1:2], st[:, 0:1], 0.125),
                               reads=[b_st], writes=[b_st])
                        p.emit("scalar", lambda e: e.activation(pr_sb[:], sc_sb[:], AF.Exp, bias=st[:, 1:2], scale=0.125,
                                                                accum_out=st[:, 2:3]),
                               reads=[b_sc, b_st], writes=[b_pr, b_st])
                        p.emit("vector", lambda e: e.reciprocal(st[:, 3:4], st[:, 2:3]), reads=[b_st], writes=[b_st])
                        p.emit("vector", lambda e: e.tensor_scalar(pr_sb[:], pr_sb[:], st[:, 3:4], None, op0=AL.mult),
                               reads=[b_pr, b_st], writes=[b_pr])
                        for kt in range(32):
                            j = kt % 2
                            p.emit("tensor", lambda e, kt=kt, j=j: e.transpose(ps_t[j][:], pr_sb[:, kt*128:(kt+1)*128], ident[:]),
                                   reads=[b_pr, b_id], writes=[b_pst[j]])
                            p.emit("vector", lambda e, kt=kt, j=j, q4=q4: e.tensor_copy(pt_sb[:, kt*512 + q4*128 : kt*512 + (q4+1)*128], ps_t[j][:]),
                                   reads=[b_pst[j]], writes=[b_pt])
                    for kt in range(32):
                        p.emit("tensor", lambda e, kt=kt:
                               e.matmul(ps_o[0:64, :], v_sb[:, kt*64:(kt+1)*64], pt_sb[:, kt*512:(kt+1)*512],
                                        start=(kt == 0), stop=(kt == 31)),
                               reads=[b_v, b_pt], writes=[b_pso])
                    p.emit("vector", lambda e, h=h, hb=hb, qg=qg:
                           e.tensor_copy(o1_sb[hb:hb+64, (h//2)*T + qg*512 : (h//2)*T + (qg+1)*512], ps_o[0:64, :]),
                           reads=[b_pso], writes=[b_o1])
            if debug:
                p.emit("sync", lambda e: e.dma_start(out=o1_o, in_=o1_sb[:]), reads=[b_o1], dma="dbg")
            for qt in range(16):
                for eh in range(2):
                    for ic in range(4):
                        p.emit("tensor", lambda e, qt=qt, eh=eh, ic=ic:
                               e.matmul(ps_z[:, eh*512:(eh+1)*512], o1_sb[:, ic*T + qt*128 : ic*T + (qt+1)*128],
                                        wo_sb[:, ic*1024 + eh*512 : ic*1024 + (eh+1)*512],
                                        start=(ic == 0), stop=(ic == 3)),
                               reads=[b_o1, b_wo], writes=[b_psz])
                p.emit("vector", lambda e: e.tensor_copy(z_sb[:], ps_z[:]), reads=[b_psz], writes=[b_z])
                emit_ln(p, z_sb[:], z_sb[:], st[:], sq[:], b_z, b_z, b_st, b_sq, 1024)
                p.emit("vector", lambda e: e.tensor_mul(z_sb[:], z_sb[:], onwb[:]), reads=[b_z, b_onwb], writes=[b_z])
                p.emit("vector", lambda e: e.tensor_add(a1_sb[:], z_sb[:], onbb[:]), reads=[b_z, b_onbb], writes=[b_a1])
                p.emit("sync", lambda e, qt=qt: e.dma_start(out=attn1_dr[qt*128:(qt+1)*128, :], in_=a1_sb[:]),
                       reads=[b_a1], dma="a1")
            p.full_barrier()

        with ExitStack() as c2:
            T2 = T
            xn_str = []; b_xns = []
            for i in range(2):
                t, b = sbuf(nc, c2, f"xn_str{i}", [128, 8 * 256], BF16); xn_str.append(t); b_xns.append(b)
            ps_g = c2.enter_context(nc.psum_tensor("ps_g", [128, 512], F32)); b_psg = Buf("ps_g")
            ps_h = []; b_psh = []
            for i in range(2):
                t = c2.enter_context(nc.psum_tensor(f"ps_h{i}", [128, 256], F32))
                ps_h.append(t); b_psh.append(Buf(f"ps_h{i}"))
            ps_y = []; b_psy = []
            for i in range(2):
                t = c2.enter_context(nc.psum_tensor(f"ps_y{i}", [128, 1024], F32))
                ps_y.append(t); b_psy.append(Buf(f"ps_y{i}"))

            with ExitStack() as c2g:
                gw_row, b_gwr = sbuf(nc, c2g, "gw_row", [8, 1024])
                gwT, b_gw = sbuf(nc, c2g, "gwT", [128, 64], BF16)
                gb_col, b_gb = sbuf(nc, c2g, "gb_col", [8, 2])
                lg_sb, b_lg = sbuf(nc, c2g, "lg_sb", [8, T])
                stm, b_stm = sbuf(nc, c2g, "stm", [128, 4])
                p.emit("sync", lambda e: e.dma_start(out=gw_row[:], in_=gate_w), writes=[b_gwr], dma="gw")
                for dc in range(8):
                    j = dc % 2
                    p.emit("tensor", lambda e, dc=dc, j=j: e.transpose(ps_h[j][0:128, 0:8], gw_row[0:8, dc*128:(dc+1)*128], identf[0:8, 0:8]),
                           reads=[b_gwr, b_idf], writes=[b_psh[j]])
                    p.emit("vector", lambda e, dc=dc, j=j: e.tensor_copy(gwT[:, dc*8:(dc+1)*8], ps_h[j][0:128, 0:8]),
                           reads=[b_psh[j]], writes=[b_gw])
                p.emit("sync", lambda e: e.dma_start(out=gb_col[:, 0:1], in_=gate_b[:, None]), writes=[b_gb], dma="gb")
                p.emit("sync", lambda e: e.dma_start(out=gb_col[:, 1:2], in_=expert_bias[:, None]), writes=[b_gb], dma="gb2")
                p.emit("vector", lambda e: e.tensor_add(gb_col[:, 0:1], gb_col[:, 0:1], gb_col[:, 1:2]),
                       reads=[b_gb], writes=[b_gb])
                for sc in range(4):
                    i = sc % 2
                    p.emit("sync", lambda e, sc=sc, i=i: e.dma_start(
                            out=xn_str[i][:].rearrange("p (dc s) -> p dc s", s=512),
                            in_=xn1_fm.rearrange("p (dc s) -> p dc s", s=T)[:, :, sc*512:(sc+1)*512]),
                           writes=[b_xns[i]], dma=f"xs{i}")
                    for dc in range(8):
                        p.emit("tensor", lambda e, sc=sc, dc=dc, i=i:
                               e.matmul(ps_g[0:8, :], gwT[:, dc*8:(dc+1)*8], xn_str[i][:, dc*512:(dc+1)*512],
                                        start=(dc == 0), stop=(dc == 7)),
                               reads=[b_gw, b_xns[i]], writes=[b_psg])
                    p.emit("vector", lambda e, sc=sc: e.tensor_scalar(lg_sb[0:8, sc*512:(sc+1)*512], ps_g[0:8, :],
                                                                      gb_col[0:8, 0:1], None, op0=AL.add),
                           reads=[b_psg, b_gb], writes=[b_lg])
                for t in range(16):
                    j = t % 2
                    p.emit("tensor", lambda e, t=t, j=j: e.transpose(ps_h[j][0:128, 0:8], lg_sb[0:8, t*128:(t+1)*128], identf[0:8, 0:8]),
                           reads=[b_lg, b_idf], writes=[b_psh[j]])
                    p.emit("vector", lambda e, t=t, j=j: e.tensor_copy(oh_tm[:, t*8:(t+1)*8], ps_h[j][0:128, 0:8]),
                           reads=[b_psh[j]], writes=[b_oht])
                    p.emit("vector", lambda e, t=t: e.reduce_max(stm[:, 0:1], oh_tm[:, t*8:(t+1)*8], axis=AX.X),
                           reads=[b_oht], writes=[b_stm])
                    p.emit("vector", lambda e, t=t: e.tensor_scalar(oh_tm[:, t*8:(t+1)*8], oh_tm[:, t*8:(t+1)*8],
                                                                    stm[:, 0:1], None, op0=AL.is_ge),
                           reads=[b_oht, b_stm], writes=[b_oht])
                    p.emit("tensor", lambda e, t=t, j=j: e.transpose(ps_h[j][0:8, 0:128], oh_tm[:, t*8:(t+1)*8], identf[:]),
                           reads=[b_oht, b_idf], writes=[b_psh[j]])
                    p.emit("vector", lambda e, t=t, j=j: e.tensor_copy(oh_em[0:8, t*128:(t+1)*128], ps_h[j][0:8, 0:128]),
                           reads=[b_psh[j]], writes=[b_ohe])
                p.full_barrier()

            w1e, b_w1e = sbuf(nc, c2, "w1e", [128, 8 * 4096], BF16)
            w2e, b_w2e = sbuf(nc, c2, "w2e", [128, 32 * 1024], BF16)
            b1row, b_b1r = sbuf(nc, c2, "b1row", [1, 4096], BF16)
            onesr, b_onr = sbuf(nc, c2, "onesr", [1, 512], BF16)
            h_sb, b_h = sbuf(nc, c2, "h_sb", [128, 32 * 256], BF16)
            p.emit("vector", lambda e: e.memset(onesr[:], 1.0), writes=[b_onr])
            p.emit("vector", lambda e: e.memset(moe_sb[:], 0.0), writes=[b_moe])
            for ex in range(8):
                p.emit("gpsimd", lambda e, ex=ex: e.dma_start(
                        out=w1e[:].rearrange("p (t d) -> p t d", d=4096),
                        in_=w1[ex].rearrange("(t p) d -> p t d", p=128)),
                       writes=[b_w1e], dma="w1")
                p.emit("gpsimd", lambda e, ex=ex: e.dma_start(
                        out=w2e[:].rearrange("p (t d) -> p t d", d=1024),
                        in_=w2[ex].rearrange("(t p) d -> p t d", p=128)),
                       writes=[b_w2e], dma="w2")
                p.emit("gpsimd", lambda e, ex=ex: e.dma_start(out=b1row[:], in_=b1[ex][None, :]),
                       writes=[b_b1r], dma="b1")
                for sb in range(8):
                    i = sb % 2
                    p.emit("sync", lambda e, sb=sb, i=i: e.dma_start(
                            out=xn_str[i][:, 0:8*256].rearrange("p (dc s) -> p dc s", s=256),
                            in_=xn1_fm.rearrange("p (dc s) -> p dc s", s=T)[:, :, sb*256:(sb+1)*256]),
                           writes=[b_xns[i]], dma=f"xs{i}")
                    for fc in range(32):
                        j = fc % 2
                        for dc in range(8):
                            p.emit("tensor", lambda e, sb=sb, fc=fc, dc=dc, j=j, i=i:
                                   e.matmul(ps_h[j][:], w1e[:, dc*4096 + fc*128 : dc*4096 + (fc+1)*128],
                                            xn_str[i][:, dc*256:(dc+1)*256],
                                            start=(dc == 0), stop=False),
                                   reads=[b_w1e, b_xns[i]], writes=[b_psh[j]])
                        p.emit("tensor", lambda e, fc=fc, j=j:
                               e.matmul(ps_h[j][:], b1row[0:1, fc*128:(fc+1)*128], onesr[0:1, 0:256],
                                        start=False, stop=True),
                               reads=[b_b1r, b_onr], writes=[b_psh[j]])
                        p.emit("scalar", lambda e, fc=fc, j=j: e.activation(h_sb[:, fc*256:(fc+1)*256], ps_h[j][:], AF.Relu),
                               reads=[b_psh[j]], writes=[b_h])
                    for st2 in range(2):
                        for fc in range(32):
                            for eh in range(2):
                                p.emit("tensor", lambda e, sb=sb, st2=st2, fc=fc, eh=eh:
                                       e.matmul(ps_y[st2][:, eh*512:(eh+1)*512],
                                                h_sb[:, fc*256 + st2*128 : fc*256 + (st2+1)*128],
                                                w2e[:, fc*1024 + eh*512 : fc*1024 + (eh+1)*512],
                                                start=(fc == 0), stop=(fc == 31)),
                                       reads=[b_h, b_w2e], writes=[b_psy[st2]])
                        tt = sb * 2 + st2
                        p.emit("vector", lambda e, tt=tt, st2=st2, ex=ex:
                               e.scalar_tensor_tensor(moe_sb[:, tt*1024:(tt+1)*1024], ps_y[st2][:],
                                                      oh_tm[:, tt*8 + ex : tt*8 + ex + 1],
                                                      moe_sb[:, tt*1024:(tt+1)*1024], op0=AL.mult, op1=AL.add),
                               reads=[b_psy[st2], b_oht, b_moe], writes=[b_moe])
            p.full_barrier()

        if debug:
            p.emit("sync", lambda e: e.dma_start(out=moe_o, in_=moe_sb[:]), reads=[b_moe], dma="dbg")
            p.emit("sync", lambda e: e.dma_start(out=oh_o, in_=oh_tm[:]), reads=[b_oht], dma="dbg")
        with ExitStack() as c3:
            b2_sb, b_b2 = sbuf(nc, c3, "b2_sb", [8, 1024], BF16)
            xt, b_xt = sbuf(nc, c3, "xt", [128, 1024])
            a1t, b_a1t = sbuf(nc, c3, "a1t", [128, 1024], BF16)
            f_sb, b_f = sbuf(nc, c3, "f_sb", [128, 1024])
            ps_f = c3.enter_context(nc.psum_tensor("ps_f", [128, 1024], F32)); b_psf = Buf("ps_f")
            p.emit("gpsimd", lambda e: e.dma_start(out=b2_sb[:], in_=b2), writes=[b_b2], dma="b2")
            for t in range(16):
                p.emit("sync", lambda e, t=t: e.dma_start(out=xt[:], in_=x1[t*128:(t+1)*128, :]),
                       writes=[b_xt], dma="x1")
                p.emit("sync", lambda e, t=t: e.dma_start(out=a1t[:], in_=attn1_dr[t*128:(t+1)*128, :]),
                       writes=[b_a1t], dma="a1l")
                for eh in range(2):
                    p.emit("tensor", lambda e, t=t, eh=eh:
                           e.matmul(ps_f[:, eh*512:(eh+1)*512], oh_em[0:8, t*128:(t+1)*128],
                                    b2_sb[0:8, eh*512:(eh+1)*512], start=True, stop=True),
                           reads=[b_ohe, b_b2], writes=[b_psf])
                p.emit("vector", lambda e: e.tensor_add(f_sb[:], xt[:], ps_f[:]), reads=[b_xt, b_psf], writes=[b_f])
                p.emit("vector", lambda e: e.tensor_add(f_sb[:], f_sb[:], a1t[:]), reads=[b_f, b_a1t], writes=[b_f])
                p.emit("vector", lambda e, t=t: e.tensor_add(f_sb[:], f_sb[:], moe_sb[:, t*1024:(t+1)*1024]),
                       reads=[b_f, b_moe], writes=[b_f])
                p.emit("sync", lambda e, t=t: e.dma_start(out=fin_o[t*128:(t+1)*128, :], in_=f_sb[:]),
                       reads=[b_f], dma="fo")
            s_, c_ = p.dma_sems["fo"]
            p.barrier("sync", [Tok(s_, c_)])
        p.play(block)
    return nc


def build_fused():
    nc = bass.Bass("TRN2", target_bir_lowering=False)
    _uniquify(nc)
    p = Prog(nc)
    S, T, NIT, TOPK = 4096, 2048, 36, 3072
    def ein(name, shape, dt=F32):
        return nc.dram_tensor(name, shape, dt, kind="ExternalInput").ap()
    x = ein("x", [S, 1024])
    w_qkv = ein("w_qkv", [3072, 1024]); w_router = ein("w_router", [1024])
    w_proj = ein("w_proj", [1024, 1024]); b_proj = ein("b_proj", [1024])
    w_block = ein("w_block", [1024, 1024])
    norm_w = ein("norm_w", [1024]); norm_b = ein("norm_b", [1024])
    Wq = ein("Wq", [1024, 512]); Wkv = ein("Wkv", [1024, 128]); Wo = ein("Wo", [512, 1024])
    onw = ein("onw", [1024]); onb = ein("onb", [1024])
    gate_w = ein("gate_w", [8, 1024]); gate_b = ein("gate_b", [8]); expert_bias = ein("expert_bias", [8])
    w1 = ein("w1", [8, 1024, 4096]); b1 = ein("b1", [8, 4096])
    w2 = ein("w2", [8, 4096, 1024]); b2 = ein("b2", [8, 1024])
    idx_xn = ein("idx_xn", [128, 8], I32)
    idx_q = ein("idx_q", [64, 1], I32)
    idx_x1 = ein("idx_x1", [128, 16], I32)
    fin_o = nc.dram_tensor("fin_o", [T, 1024], F32, kind="ExternalOutput").ap()
    q_dr = nc.dram_tensor("q_dr", [128, 8 * S], BF16).ap()
    k_dr = nc.dram_tensor("k_dr", [128, 8 * S], BF16).ap()
    v_dr = nc.dram_tensor("v_dr", [S, 1024], BF16).ap()
    rsc_dr = nc.dram_tensor("rsc_dr", [4096], F32).ap()
    out_dr = nc.dram_tensor("out_dr", [S, 1024], BF16).ap()
    x1_dr = nc.dram_tensor("x1_dr", [S, 1024], F32).ap()
    xn1_dr = nc.dram_tensor("xn1_dr", [128, 16 * 2048], BF16).ap()
    q1_dr = nc.dram_tensor("q1_dr", [64, 2 * 8 * 2048], BF16).ap()
    attn1_dr = nc.dram_tensor("attn1_dr", [T, 1024], BF16).ap()

    ctx = ExitStack()
    with ctx:
        p.setup(ctx)
        ident, b_id = sbuf(nc, ctx, "ident", [128, 128], BF16)
        identf, b_idf = sbuf(nc, ctx, "identf", [128, 128])
        ones1, b_on = sbuf(nc, ctx, "ones1", [1, 128])
        block = ctx.enter_context(nc.Block())
        emit_identity(nc, p, b_id, ident[:])
        p.emit("gpsimd", lambda g: g.memset(identf[:], 0.0), writes=[b_idf])
        p.emit("gpsimd", lambda g: g.affine_select(out=identf[:], in_=identf[:],
            compare_op=AL.not_equal, fill=1.0, base=0, pattern=[[-1, 128]], channel_multiplier=1),
            reads=[b_idf], writes=[b_idf])
        p.emit("vector", lambda e: e.memset(ones1[:], 1.0), writes=[b_on])

        def _phase0():
            with ExitStack() as c:
                xt = []; b_xt = []
                for i in range(2):
                    t, b = sbuf(nc, c, f"xt{i}", [128, 1024]); xt.append(t); b_xt.append(b)
                st, b_st = sbuf(nc, c, "st", [128, 8])
                sq, b_sq = sbuf(nc, c, "sq", [128, 1024])
                lnb, b_lnb = sbuf(nc, c, "lnb", [128, 1024], BF16)
                xn_fm, b_xn = sbuf(nc, c, "xn_fm", [128, 8 * S], BF16)
                wn, b_wn = sbuf(nc, c, "wn", [128, 2 * 1024], BF16)
                wT, b_wT = sbuf(nc, c, "wT", [128, 8 * 3072], BF16)
                wr_row, b_wrr = sbuf(nc, c, "wr_row", [1, 1024], BF16)
                wr, b_wr = sbuf(nc, c, "wr", [128, 8], BF16)
                qk_sb, b_qk = sbuf(nc, c, "qk_sb", [128, 512], BF16)
                v_sb, b_vsb = sbuf(nc, c, "v_sb", [128, 1024], BF16)
                r_sb, b_rsb = sbuf(nc, c, "r_sb", [1, S])
                ps_t = []; b_pst = []
                for i in range(2):
                    t = c.enter_context(nc.psum_tensor(f"ps_t{i}", [128, 128], BF16))
                    ps_t.append(t); b_pst.append(Buf(f"ps_t{i}"))
                ps_m = []; b_psm = []
                for i in range(2):
                    t = c.enter_context(nc.psum_tensor(f"ps_m{i}", [128, 512], F32))
                    ps_m.append(t); b_psm.append(Buf(f"ps_m{i}"))

                for t in range(32):
                    i = t % 2
                    p.emit("sync", lambda e, t=t, i=i: e.dma_start(out=xt[i][:], in_=x[t*128:(t+1)*128, :]),
                           writes=[b_xt[i]], dma=f"x{i}")
                    emit_ln(p, xt[i][:], lnb[:], st[:], sq[:], b_xt[i], b_lnb, b_st, b_sq, 1024)
                    for cc in range(8):
                        j = cc % 2
                        p.emit("tensor", lambda e, cc=cc, j=j: e.transpose(ps_t[j][:], lnb[:, cc*128:(cc+1)*128], ident[:]),
                               reads=[b_lnb, b_id], writes=[b_pst[j]])
                        p.emit("vector", lambda e, cc=cc, j=j, t=t: e.tensor_copy(xn_fm[:, cc*S + t*128 : cc*S + (t+1)*128], ps_t[j][:]),
                               reads=[b_pst[j]], writes=[b_xn])
                for t in range(24):
                    i = t % 2
                    p.emit("gpsimd", lambda e, t=t, i=i: e.dma_start(out=wn[:, i*1024:(i+1)*1024], in_=w_qkv[t*128:(t+1)*128, :]),
                           writes=[b_wn], dma=f"w{i}")
                    for cc in range(8):
                        j = cc % 2
                        p.emit("tensor", lambda e, cc=cc, i=i, j=j: e.transpose(ps_t[j][:], wn[:, i*1024 + cc*128 : i*1024 + (cc+1)*128], ident[:]),
                               reads=[b_wn, b_id], writes=[b_pst[j]])
                        p.emit("vector", lambda e, cc=cc, j=j, t=t: e.tensor_copy(wT[:, cc*3072 + t*128 : cc*3072 + (t+1)*128], ps_t[j][:]),
                               reads=[b_pst[j]], writes=[b_wT])
                p.emit("gpsimd", lambda e: e.dma_start(out=wr_row[:], in_=w_router[None, :]), writes=[b_wrr], dma="wr")
                for cc in range(8):
                    j = cc % 2
                    p.emit("tensor", lambda e, cc=cc, j=j: e.transpose(ps_t[j][0:128, 0:1], wr_row[0:1, cc*128:(cc+1)*128], ident[0:1, 0:1]),
                           reads=[b_wrr, b_id], writes=[b_pst[j]])
                    p.emit("vector", lambda e, cc=cc, j=j: e.tensor_copy(wr[:, cc:cc+1], ps_t[j][0:128, 0:1]),
                           reads=[b_pst[j]], writes=[b_wr])
                for which, off, out_t in (("q", 0, q_dr), ("k", 1024, k_dr)):
                    for f in range(8):
                        for sc in range(8):
                            j = (f * 8 + sc) % 2
                            for ec in range(8):
                                p.emit("tensor", lambda e, f=f, sc=sc, ec=ec, j=j, off=off:
                                       e.matmul(ps_m[j][:], wT[:, ec*3072 + off + f*128 : ec*3072 + off + (f+1)*128],
                                                xn_fm[:, ec*S + sc*512 : ec*S + (sc+1)*512],
                                                start=(ec == 0), stop=(ec == 7)),
                                       reads=[b_wT, b_xn], writes=[b_psm[j]])
                            p.emit("vector", lambda e, j=j: e.tensor_copy(qk_sb[:], ps_m[j][:]),
                                   reads=[b_psm[j]], writes=[b_qk])
                            p.emit("sync", lambda e, f=f, sc=sc, out_t=out_t:
                                   e.dma_start(out=out_t[:, f*S + sc*512 : f*S + (sc+1)*512], in_=qk_sb[:]),
                                   reads=[b_qk], dma="qk_o")
                for t in range(32):
                    for vh in range(2):
                        j = (t * 2 + vh) % 2
                        for ec in range(8):
                            p.emit("tensor", lambda e, t=t, vh=vh, ec=ec, j=j:
                                   e.matmul(ps_m[j][:], xn_fm[:, ec*S + t*128 : ec*S + (t+1)*128],
                                            wT[:, ec*3072 + 2048 + vh*512 : ec*3072 + 2048 + (vh+1)*512],
                                            start=(ec == 0), stop=(ec == 7)),
                                   reads=[b_wT, b_xn], writes=[b_psm[j]])
                        p.emit("vector", lambda e, vh=vh, j=j: e.tensor_copy(v_sb[:, vh*512:(vh+1)*512], ps_m[j][:]),
                               reads=[b_psm[j]], writes=[b_vsb])
                    p.emit("sync", lambda e, t=t: e.dma_start(out=v_dr[t*128:(t+1)*128, :], in_=v_sb[:]),
                           reads=[b_vsb], dma="v_o")
                for sc in range(8):
                    j = sc % 2
                    for ec in range(8):
                        p.emit("tensor", lambda e, sc=sc, ec=ec, j=j:
                               e.matmul(ps_m[j][0:1, :], wr[:, ec:ec+1],
                                        xn_fm[:, ec*S + sc*512 : ec*S + (sc+1)*512],
                                        start=(ec == 0), stop=(ec == 7)),
                               reads=[b_wr, b_xn], writes=[b_psm[j]])
                    p.emit("vector", lambda e, sc=sc, j=j: e.tensor_copy(r_sb[0:1, sc*512:(sc+1)*512], ps_m[j][0:1, :]),
                           reads=[b_psm[j]], writes=[b_rsb])
                p.emit("sync", lambda e: e.dma_start(out=rsc_dr[None, :], in_=r_sb[:]), reads=[b_rsb], dma="r_o")
                p.full_barrier()

        _phase0()

        def _phase1():
            with ExitStack() as c:
                qg_sb, b_q = sbuf(nc, c, "qg_sb", [128, 8 * 1024], BF16)
                k_sb, b_k = sbuf(nc, c, "k_sb", [128, 8 * S], BF16)
                v_sb2, b_v = sbuf(nc, c, "v_sb2", [128, 32 * 1024], BF16)
                sc_sb = []; b_sc = []
                pr_sb = []; b_pr = []
                for i in range(2):
                    t, b = sbuf(nc, c, f"sc_sb{i}", [128, S]); sc_sb.append(t); b_sc.append(b)
                    t, b = sbuf(nc, c, f"pr_sb{i}", [128, S], BF16); pr_sb.append(t); b_pr.append(b)
                pt_sb, b_pt = sbuf(nc, c, "pt_sb", [128, 32 * 128], BF16)
                st, b_st = sbuf(nc, c, "st", [128, 8])
                o_sb = []; b_o = []
                for i in range(2):
                    t, b = sbuf(nc, c, f"o_sb{i}", [128, 1024], BF16); o_sb.append(t); b_o.append(b)
                ps_s = []; b_ps = []
                for i in range(2):
                    t = c.enter_context(nc.psum_tensor(f"ps_s{i}", [128, 512], F32))
                    ps_s.append(t); b_ps.append(Buf(f"ps_s{i}"))
                ps_t = []; b_pst = []
                for i in range(2):
                    t = c.enter_context(nc.psum_tensor(f"ps_t{i}", [128, 128], BF16))
                    ps_t.append(t); b_pst.append(Buf(f"ps_t{i}"))
                ps_av = []; b_pav = []
                for i in range(2):
                    t = c.enter_context(nc.psum_tensor(f"ps_av{i}", [128, 512], F32))
                    ps_av.append(t); b_pav.append(Buf(f"ps_av{i}"))
                p.emit("sync", lambda e: e.dma_start(out=k_sb[:], in_=k_dr), writes=[b_k], dma="kL")
                p.emit("sync", lambda e: e.dma_start(out=v_sb2[:].rearrange("p (t d) -> p t d", d=1024),
                                                     in_=v_dr.rearrange("(t p) d -> p t d", p=128)),
                       writes=[b_v], dma="vL")
                def emit_scores(qt, sci):
                    for kc in range(8):
                        j = kc % 2
                        for qf in range(8):
                            p.emit("tensor", lambda e, qt=qt, kc=kc, qf=qf, j=j:
                                   e.matmul(ps_s[j][:], qg_sb[:, qf*1024 + (qt % 8)*128 : qf*1024 + (qt % 8 + 1)*128],
                                            k_sb[:, qf*S + kc*512 : qf*S + (kc+1)*512],
                                            start=(qf == 0), stop=(qf == 7)),
                                   reads=[b_q, b_k], writes=[b_ps[j]])
                        p.emit("vector", lambda e, kc=kc, j=j, sci=sci: e.tensor_copy(sc_sb[sci][:, kc*512:(kc+1)*512], ps_s[j][:]),
                               reads=[b_ps[j]], writes=[b_sc[sci]])
                    p.emit("vector", lambda e, sci=sci: e.reduce_max(st[:, 4*sci:4*sci+1], sc_sb[sci][:], axis=AX.X, negate=True),
                           reads=[b_sc[sci]], writes=[b_st])
                    p.emit("vector", lambda e, sci=sci: e.tensor_scalar_mul(st[:, 4*sci+1:4*sci+2], st[:, 4*sci:4*sci+1], 0.125),
                           reads=[b_st], writes=[b_st])
                    p.emit("scalar", lambda e, sci=sci: e.activation(pr_sb[sci][:], sc_sb[sci][:], AF.Exp, bias=st[:, 4*sci+1:4*sci+2],
                                                            scale=0.125, accum_out=st[:, 4*sci+2:4*sci+3]),
                           reads=[b_sc[sci], b_st], writes=[b_pr[sci], b_st])
                    p.emit("vector", lambda e, sci=sci: e.reciprocal(st[:, 4*sci+3:4*sci+4], st[:, 4*sci+2:4*sci+3]),
                           reads=[b_st], writes=[b_st])

                def emit_av(qt, sci):
                    for kt in range(32):
                        j = kt % 2
                        p.emit("tensor", lambda e, kt=kt, j=j, sci=sci: e.transpose(ps_t[j][:], pr_sb[sci][:, kt*128:(kt+1)*128], ident[:]),
                               reads=[b_pr[sci], b_id], writes=[b_pst[j]])
                        p.emit("vector", lambda e, kt=kt, j=j: e.tensor_copy(pt_sb[:, kt*128:(kt+1)*128], ps_t[j][:]),
                               reads=[b_pst[j]], writes=[b_pt])
                    ob = qt % 2
                    for vh in range(2):
                        j = vh
                        for kt in range(32):
                            p.emit("tensor", lambda e, kt=kt, vh=vh, j=j:
                                   e.matmul(ps_av[j][:], pt_sb[:, kt*128:(kt+1)*128],
                                            v_sb2[:, kt*1024 + vh*512 : kt*1024 + (vh+1)*512],
                                            start=(kt == 0), stop=(kt == 31)),
                                   reads=[b_pt, b_v], writes=[b_pav[j]])
                        p.emit("vector", lambda e, vh=vh, j=j, ob=ob, sci=sci:
                               e.tensor_scalar(o_sb[ob][:, vh*512:(vh+1)*512], ps_av[j][:], st[:, 4*sci+3:4*sci+4], None, op0=AL.mult),
                               reads=[b_pav[j], b_st], writes=[b_o[ob]])
                    p.emit("sync", lambda e, qt=qt, ob=ob: e.dma_start(out=out_dr[qt*128:(qt+1)*128, :], in_=o_sb[ob][:]),
                           reads=[b_o[ob]], dma="o_o")

                prev = None
                for g in range(4):
                    p.emit("sync", lambda e, g=g: e.dma_start(
                            out=qg_sb[:].rearrange("p (f s) -> p f s", s=1024),
                            in_=q_dr.rearrange("p (f s) -> p f s", s=S)[:, :, g*1024:(g+1)*1024]),
                           writes=[b_q], dma="qL")
                    for q4 in range(8):
                        qt = g * 8 + q4
                        emit_scores(qt, qt % 2)
                        if prev is not None:
                            emit_av(prev, prev % 2)
                        prev = qt
                emit_av(prev, prev % 2)
                p.full_barrier()

        _phase1()

        def _phase2():
            with ExitStack() as c:
                ogg_sb, b_og = sbuf(nc, c, "ogg_sb", [128, 8 * 1024], BF16)
                wpn, b_wpn = sbuf(nc, c, "wpn", [128, 2 * 1024], BF16)
                wpT, b_wpT = sbuf(nc, c, "wpT", [128, 8 * 1024], BF16)
                wb_sb, b_wb = sbuf(nc, c, "wb_sb", [128, 8 * 1024], BF16)
                bp_row, b_bpr = sbuf(nc, c, "bp_row", [1, 1024])
                bpb, b_bpb = sbuf(nc, c, "bpb", [128, 1024])
                rs_sb, b_rs = sbuf(nc, c, "rs_sb", [128, 32])
                rsig_sb, b_rsig = sbuf(nc, c, "rsig_sb", [1, 4096])
                cmp_sb, b_cmp = sbuf(nc, c, "cmp_sb", [128, 32])
                ones128, b_o128 = sbuf(nc, c, "ones128", [128, 128])
                thr, b_thr = sbuf(nc, c, "thr", [128, 1])
                sgn, b_sgn = sbuf(nc, c, "sgn", [128, 1])
                step_tab, b_stp = sbuf(nc, c, "step_tab", [128, NIT])
                st4, b_st4 = sbuf(nc, c, "st4", [128, 8])
                row_sc, b_rsc2 = sbuf(nc, c, "row_sc", [1, 128])
                mcol, b_mc = sbuf(nc, c, "mcol", [128, 2])
                xt, b_xt = sbuf(nc, c, "xt", [128, 1024])
                st, b_st = sbuf(nc, c, "st", [128, 8])
                sq, b_sq = sbuf(nc, c, "sq", [128, 1024])
                xn_f, b_xnf = sbuf(nc, c, "xn_f", [128, 1024])
                xn_b, b_xnb = sbuf(nc, c, "xn_b", [128, 1024], BF16)
                xnfm, b_xnfm = sbuf(nc, c, "xnfm", [128, 8 * 128], BF16)
                y_sb, b_y = sbuf(nc, c, "y_sb", [128, 1024])
                a3_sb, b_a3 = sbuf(nc, c, "a3_sb", [128, 1024])
                d_sb, b_d = sbuf(nc, c, "d_sb", [128, 1024])
                x1_sb, b_x1 = sbuf(nc, c, "x1_sb", [128, 1024])
                ps_y = c.enter_context(nc.psum_tensor("ps_y", [128, 1024], F32)); b_psy = Buf("ps_y")
                ps_p = c.enter_context(nc.psum_tensor("ps_p", [128, 1024], F32)); b_psp = Buf("ps_p")
                ps_c = c.enter_context(nc.psum_tensor("ps_c", [128, 128], F32)); b_psc = Buf("ps_c")
                ps_t = []; b_pst = []
                for i in range(2):
                    t = c.enter_context(nc.psum_tensor(f"ps_t{i}", [128, 128], BF16))
                    ps_t.append(t); b_pst.append(Buf(f"ps_t{i}"))
                p.emit("vector", lambda e: e.memset(ones128[:], 1.0), writes=[b_o128])
                p.emit("gpsimd", lambda e: e.dma_start(out=wb_sb[:].rearrange("p (t d) -> p t d", d=1024),
                                                       in_=w_block.rearrange("(t p) d -> p t d", p=128)),
                       writes=[b_wb], dma="wb")
                p.emit("sync", lambda e: e.dma_start(out=rs_sb[:], in_=rsc_dr.rearrange("(q j) -> q j", j=32)),
                       writes=[b_rs], dma="rs", waits=[Tok(p.dma_sems["r_o"][0], p.dma_sems["r_o"][1])])
                p.emit("sync", lambda e: e.dma_start(out=rsig_sb[:], in_=rsc_dr[None, :]), writes=[b_rsig], dma="rs")
                p.emit("sync", lambda e: e.dma_start(out=bp_row[:], in_=b_proj[None, :]), writes=[b_bpr], dma="rs")
                for t in range(8):
                    i = t % 2
                    p.emit("gpsimd", lambda e, t=t, i=i: e.dma_start(out=wpn[:, i*1024:(i+1)*1024],
                                                                     in_=w_proj[t*128:(t+1)*128, :]),
                           writes=[b_wpn], dma=f"wp{i}")
                    for cc in range(8):
                        j = cc % 2
                        p.emit("tensor", lambda e, cc=cc, i=i, j=j: e.transpose(ps_t[j][:], wpn[:, i*1024 + cc*128 : i*1024 + (cc+1)*128], ident[:]),
                               reads=[b_wpn, b_id], writes=[b_pst[j]])
                        p.emit("vector", lambda e, cc=cc, j=j, t=t: e.tensor_copy(wpT[:, cc*1024 + t*128 : cc*1024 + (t+1)*128], ps_t[j][:]),
                               reads=[b_pst[j]], writes=[b_wpT])
                for eh in range(2):
                    p.emit("tensor", lambda e, eh=eh: e.matmul(ps_p[:, eh*512:(eh+1)*512], ones1[:],
                                                               bp_row[0:1, eh*512:(eh+1)*512], start=True, stop=True),
                           reads=[b_on, b_bpr], writes=[b_psp])
                p.emit("vector", lambda e: e.tensor_copy(bpb[:], ps_p[:]), reads=[b_psp], writes=[b_bpb])
                p.emit("vector", lambda e: e.reduce_max(st4[:, 0:1], rs_sb[:], axis=AX.X, apply_absolute_value=True),
                       reads=[b_rs], writes=[b_st4])
                p.emit("tensor", lambda e: e.transpose(ps_c[0:1, 0:128], st4[:, 0:1], identf[:]),
                       reads=[b_st4, b_idf], writes=[b_psc])
                p.emit("vector", lambda e: e.tensor_copy(row_sc[:], ps_c[0:1, 0:128]), reads=[b_psc], writes=[b_rsc2])
                p.emit("vector", lambda e: e.reduce_max(row_sc[0:1, 0:1], row_sc[:], axis=AX.X),
                       reads=[b_rsc2], writes=[b_rsc2])
                p.emit("tensor", lambda e: e.matmul(ps_c[:, 0:1], ones1[:], row_sc[0:1, 0:1], start=True, stop=True),
                       reads=[b_on, b_rsc2], writes=[b_psc])
                p.emit("vector", lambda e: e.tensor_copy(st4[:, 1:2], ps_c[:, 0:1]), reads=[b_psc], writes=[b_st4])
                for i in range(NIT):
                    p.emit("vector", lambda e, i=i: e.tensor_scalar_mul(step_tab[:, i:i+1], st4[:, 1:2], 2.0 ** (-i)),
                           reads=[b_st4], writes=[b_stp])
                p.emit("vector", lambda e: e.memset(thr[:], 0.0), writes=[b_thr])
                for i in range(NIT):
                    p.emit("vector", lambda e: e.tensor_scalar(cmp_sb[:], rs_sb[:], thr[:, 0:1], None, op0=AL.is_gt),
                           reads=[b_rs, b_thr], writes=[b_cmp])
                    p.emit("vector", lambda e: e.reduce_sum(st4[:, 2:3], cmp_sb[:], axis=AX.X),
                           reads=[b_cmp], writes=[b_st4])
                    p.emit("tensor", lambda e: e.matmul(ps_c[:, 1:2], ones128[:], st4[:, 2:3], start=True, stop=True),
                           reads=[b_o128, b_st4], writes=[b_psc])
                    p.emit("vector", lambda e: e.tensor_scalar(st4[:, 3:4], ps_c[:, 1:2], float(-TOPK), None, op0=AL.add),
                           reads=[b_psc], writes=[b_st4])
                    p.emit("scalar", lambda e: e.activation(sgn[:], st4[:, 3:4], AF.Sign),
                           reads=[b_st4], writes=[b_sgn])
                    p.emit("vector", lambda e, i=i: e.scalar_tensor_tensor(thr[:], sgn[:], step_tab[:, i:i+1], thr[:],
                                                                           op0=AL.mult, op1=AL.add),
                           reads=[b_sgn, b_stp, b_thr], writes=[b_thr])
                for tt in range(32):
                    gg, hc = tt // 8, tt % 8
                    p.emit("sync", lambda e, gg=gg, hc=hc: e.dma_start(
                            out=xt[:], in_=x.rearrange("(a f) d -> f a d", f=4)[gg, hc*128:(hc+1)*128, :]),
                           writes=[b_xt], dma="xs")
                    emit_ln(p, xt[:], xn_f[:], st[:], sq[:], b_xt, b_xnf, b_st, b_sq, 1024)
                    p.emit("scalar", lambda e: e.activation(xn_b[:], xn_f[:], AF.Copy), reads=[b_xnf], writes=[b_xnb])
                    for cc in range(8):
                        j = cc % 2
                        p.emit("tensor", lambda e, cc=cc, j=j: e.transpose(ps_t[j][:], xn_b[:, cc*128:(cc+1)*128], ident[:]),
                               reads=[b_xnb, b_id], writes=[b_pst[j]])
                        p.emit("vector", lambda e, cc=cc, j=j: e.tensor_copy(xnfm[:, cc*128:(cc+1)*128], ps_t[j][:]),
                               reads=[b_pst[j]], writes=[b_xnfm])
                    for eh in range(2):
                        for dc in range(8):
                            p.emit("tensor", lambda e, eh=eh, dc=dc:
                                   e.matmul(ps_y[:, eh*512:(eh+1)*512], xnfm[:, dc*128:(dc+1)*128],
                                            wb_sb[:, dc*1024 + eh*512 : dc*1024 + (eh+1)*512],
                                            start=(dc == 0), stop=(dc == 7)),
                                   reads=[b_xnfm, b_wb], writes=[b_psy])
                    p.emit("scalar", lambda e: e.activation(y_sb[:], ps_y[:], AF.Gelu_apprx_tanh),
                           reads=[b_psy], writes=[b_y])
                    if hc == 0:
                        p.emit("sync", lambda e, gg=gg: e.dma_start(
                                out=ogg_sb[:].rearrange("p (t d) -> p t d", d=1024),
                                in_=out_dr.rearrange("(t p) d -> p t d", p=128)[:, 8*gg:8*gg+8, :]),
                               writes=[b_og], dma="og")
                    for eh in range(2):
                        for jc in range(8):
                            p.emit("tensor", lambda e, eh=eh, jc=jc, hc=hc:
                                   e.matmul(ps_p[:, eh*512:(eh+1)*512],
                                            ogg_sb[:, jc*1024 + hc*128 : jc*1024 + (hc+1)*128],
                                            wpT[:, jc*1024 + eh*512 : jc*1024 + (eh+1)*512],
                                            start=(jc == 0), stop=(jc == 7)),
                                   reads=[b_og, b_wpT], writes=[b_psp])
                    p.emit("vector", lambda e: e.tensor_add(a3_sb[:], ps_p[:], bpb[:]), reads=[b_psp, b_bpb], writes=[b_a3])
                    p.emit("tensor", lambda e, gg=gg, hc=hc: e.transpose(ps_c[0:128, 2:3],
                            rsig_sb[0:1, :].rearrange("o (a f) -> o f a", f=4)[:, gg, hc*128:(hc+1)*128],
                            identf[0:1, 0:1]),
                           reads=[b_rsig, b_idf], writes=[b_psc])
                    p.emit("vector", lambda e: e.tensor_copy(mcol[:, 0:1], ps_c[0:128, 2:3]), reads=[b_psc], writes=[b_mc])
                    p.emit("vector", lambda e: e.tensor_scalar(mcol[:, 1:2], mcol[:, 0:1], thr[:, 0:1], None, op0=AL.is_gt),
                           reads=[b_mc, b_thr], writes=[b_mc])
                    p.emit("vector", lambda e: e.tensor_sub(d_sb[:], y_sb[:], xn_f[:]), reads=[b_y, b_xnf], writes=[b_d])
                    p.emit("vector", lambda e: e.scalar_tensor_tensor(d_sb[:], d_sb[:], mcol[:, 1:2], xn_f[:],
                                                                      op0=AL.mult, op1=AL.add),
                           reads=[b_d, b_mc, b_xnf], writes=[b_d])
                    p.emit("vector", lambda e: e.tensor_add(x1_sb[:], xt[:], a3_sb[:]), reads=[b_xt, b_a3], writes=[b_x1])
                    p.emit("vector", lambda e: e.tensor_add(x1_sb[:], x1_sb[:], d_sb[:]), reads=[b_x1, b_d], writes=[b_x1])
                    p.emit("sync", lambda e, gg=gg, hc=hc: e.dma_start(
                            out=x1_dr.rearrange("(a f) d -> f a d", f=4)[gg, hc*128:(hc+1)*128, :], in_=x1_sb[:]),
                           reads=[b_x1], dma="x1w")
                p.full_barrier()

        _phase2()

        k1_sb, b_k1 = sbuf(nc, ctx, "k1_sb", [64, S], BF16)
        v1_sb, b_v1 = sbuf(nc, ctx, "v1_sb", [128, 32 * 64], BF16)
        def _phase3():
            with ExitStack() as c:
                nw_row, b_nwr = sbuf(nc, c, "nw_row", [1, 1024])
                nb_row, b_nbr = sbuf(nc, c, "nb_row", [1, 1024])
                nwb, b_nwb = sbuf(nc, c, "nwb", [128, 1024])
                nbb, b_nbb = sbuf(nc, c, "nbb", [128, 1024])
                wq_sb, b_wq = sbuf(nc, c, "wq_sb", [128, 8 * 512], BF16)
                wkv_sb, b_wkv = sbuf(nc, c, "wkv_sb", [128, 8 * 128], BF16)
                xt, b_xt = sbuf(nc, c, "xt", [128, 1024])
                st, b_st = sbuf(nc, c, "st", [128, 8])
                sq, b_sq = sbuf(nc, c, "sq", [128, 1024])
                xn1_f, b_xn1f = sbuf(nc, c, "xn1_f", [128, 1024])
                xn1_b, b_xn1b = sbuf(nc, c, "xn1_b", [128, 1024], BF16)
                xn1n, b_xn1n = sbuf(nc, c, "xn1n", [128, 1024])
                xa_b, b_xab = sbuf(nc, c, "xa_b", [128, 1024], BF16)
                xa_fm, b_xafm = sbuf(nc, c, "xa_fm", [128, 8 * S], BF16)
                xnfm_t, b_xnfmt = sbuf(nc, c, "xnfm_t", [128, 8 * 128], BF16)
                ev_sb, b_ev = sbuf(nc, c, "ev_sb", [128, 512], BF16)
                v_sb, b_vsb = sbuf(nc, c, "v_sb", [128, 64], BF16)
                ps_m = []; b_psm = []
                for i in range(2):
                    t = c.enter_context(nc.psum_tensor(f"ps_m{i}", [128, 512], F32))
                    ps_m.append(t); b_psm.append(Buf(f"ps_m{i}"))
                ps_t = []; b_pst = []
                for i in range(2):
                    t = c.enter_context(nc.psum_tensor(f"ps_t{i}", [128, 128], BF16))
                    ps_t.append(t); b_pst.append(Buf(f"ps_t{i}"))
                ps_b = c.enter_context(nc.psum_tensor("ps_b", [128, 1024], F32)); b_psb = Buf("ps_b")
                p.emit("sync", lambda e: e.dma_start(out=nw_row[:], in_=norm_w[None, :]), writes=[b_nwr], dma="n")
                p.emit("sync", lambda e: e.dma_start(out=nb_row[:], in_=norm_b[None, :]), writes=[b_nbr], dma="n2")
                p.emit("gpsimd", lambda e: e.dma_start(out=wq_sb[:].rearrange("p (t d) -> p t d", d=512),
                                                       in_=Wq.rearrange("(t p) d -> p t d", p=128)),
                       writes=[b_wq], dma="wq")
                p.emit("gpsimd", lambda e: e.dma_start(out=wkv_sb[:].rearrange("p (t d) -> p t d", d=128),
                                                       in_=Wkv.rearrange("(t p) d -> p t d", p=128)),
                       writes=[b_wkv], dma="wkv")
                for eh in range(2):
                    p.emit("tensor", lambda e, eh=eh: e.matmul(ps_b[:, eh*512:(eh+1)*512], ones1[:],
                                                               nw_row[0:1, eh*512:(eh+1)*512], start=True, stop=True),
                           reads=[b_on, b_nwr], writes=[b_psb])
                p.emit("vector", lambda e: e.tensor_copy(nwb[:], ps_b[:]), reads=[b_psb], writes=[b_nwb])
                for eh in range(2):
                    p.emit("tensor", lambda e, eh=eh: e.matmul(ps_b[:, eh*512:(eh+1)*512], ones1[:],
                                                               nb_row[0:1, eh*512:(eh+1)*512], start=True, stop=True),
                           reads=[b_on, b_nbr], writes=[b_psb])
                p.emit("vector", lambda e: e.tensor_copy(nbb[:], ps_b[:]), reads=[b_psb], writes=[b_nbb])
                for t in range(32):
                    p.emit("sync", lambda e, t=t: e.dma_start(out=xt[:], in_=x1_dr[t*128:(t+1)*128, :]),
                           writes=[b_xt], dma="xB")
                    emit_ln(p, xt[:], xn1_f[:], st[:], sq[:], b_xt, b_xn1f, b_st, b_sq, 1024)
                    p.emit("scalar", lambda e: e.activation(xn1_b[:], xn1_f[:], AF.Copy), reads=[b_xn1f], writes=[b_xn1b])
                    emit_ln(p, xn1_f[:], xn1n[:], st[:], sq[:], b_xn1f, b_xn1n, b_st, b_sq, 1024)
                    p.emit("vector", lambda e: e.tensor_mul(xn1n[:], xn1n[:], nwb[:]), reads=[b_xn1n, b_nwb], writes=[b_xn1n])
                    p.emit("vector", lambda e: e.tensor_add(xa_b[:], xn1n[:], nbb[:]), reads=[b_xn1n, b_nbb], writes=[b_xab])
                    for cc in range(8):
                        j = cc % 2
                        p.emit("tensor", lambda e, cc=cc, j=j: e.transpose(ps_t[j][:], xa_b[:, cc*128:(cc+1)*128], ident[:]),
                               reads=[b_xab, b_id], writes=[b_pst[j]])
                        p.emit("vector", lambda e, cc=cc, j=j, t=t: e.tensor_copy(xa_fm[:, cc*S + t*128 : cc*S + (t+1)*128], ps_t[j][:]),
                               reads=[b_pst[j]], writes=[b_xafm])
                    for cc in range(8):
                        j = cc % 2
                        p.emit("tensor", lambda e, cc=cc, j=j: e.transpose(ps_t[j][:], xn1_b[:, cc*128:(cc+1)*128], ident[:]),
                               reads=[b_xn1b, b_id], writes=[b_pst[j]])
                        p.emit("vector", lambda e, cc=cc, j=j: e.tensor_copy(xnfm_t[:, cc*128:(cc+1)*128], ps_t[j][:]),
                               reads=[b_pst[j]], writes=[b_xnfmt])
                    p.emit("sync", lambda e, t=t: e.dma_start(
                            out=xn1_dr[:, :].rearrange("p (sb dc s) -> p sb dc s", dc=8, s=256)[:, t//2, :, (t % 2)*128:(t % 2)*128+128],
                            in_=xnfm_t[:].rearrange("p (dc s) -> p dc s", s=128)),
                           reads=[b_xnfmt], dma="xnw")
                for ic in range(4):
                    for sc in range(8):
                        j = (ic * 8 + sc) % 2
                        for dc in range(8):
                            p.emit("tensor", lambda e, ic=ic, sc=sc, dc=dc, j=j:
                                   e.matmul(ps_m[j][:], wq_sb[:, dc*512 + ic*128 : dc*512 + (ic+1)*128],
                                            xa_fm[:, dc*S + sc*512 : dc*S + (sc+1)*512],
                                            start=(dc == 0), stop=(dc == 7)),
                                   reads=[b_wq, b_xafm], writes=[b_psm[j]])
                        for hh in range(2):
                            p.emit("vector", lambda e, hh=hh, j=j: e.tensor_copy(ev_sb[0:64, :], ps_m[j][hh*64:hh*64+64, :]),
                                   reads=[b_psm[j]], writes=[b_ev])
                            hf, lc = sc // 4, sc % 4
                            p.emit("sync", lambda e, ic=ic, hh=hh, hf=hf, lc=lc:
                                   e.dma_start(out=q1_dr[:, hf*16384 + (2*ic+hh)*2048 + lc*512 : hf*16384 + (2*ic+hh)*2048 + (lc+1)*512],
                                               in_=ev_sb[0:64, :]),
                                   reads=[b_ev], dma="q1w")
                for sc in range(8):
                    j = sc % 2
                    for dc in range(8):
                        p.emit("tensor", lambda e, sc=sc, dc=dc, j=j:
                               e.matmul(ps_m[j][0:64, :], wkv_sb[:, dc*128 : dc*128 + 64],
                                        xa_fm[:, dc*S + sc*512 : dc*S + (sc+1)*512],
                                        start=(dc == 0), stop=(dc == 7)),
                               reads=[b_wkv, b_xafm], writes=[b_psm[j]])
                    p.emit("vector", lambda e, sc=sc, j=j: e.tensor_copy(k1_sb[0:64, sc*512:(sc+1)*512], ps_m[j][0:64, :]),
                           reads=[b_psm[j]], writes=[b_k1])
                for t in range(32):
                    j = t % 2
                    for dc in range(8):
                        p.emit("tensor", lambda e, t=t, dc=dc, j=j:
                               e.matmul(ps_m[j][:, 0:64], xa_fm[:, dc*S + t*128 : dc*S + (t+1)*128],
                                        wkv_sb[:, dc*128 + 64 : dc*128 + 128],
                                        start=(dc == 0), stop=(dc == 7)),
                               reads=[b_wkv, b_xafm], writes=[b_psm[j]])
                    p.emit("vector", lambda e, t=t, j=j: e.tensor_copy(v1_sb[:, t*64:(t+1)*64], ps_m[j][:, 0:64]),
                           reads=[b_psm[j]], writes=[b_v1])
                p.full_barrier()

        _phase3()

        emit_b2_core(nc, p, ctx, dict(
            ident=ident, b_id=b_id, identf=identf, b_idf=b_idf, ones1=ones1, b_on=b_on,
            k1_sb=k1_sb, b_k1=b_k1, v1_sb=v1_sb, b_v1=b_v1,
            q1_dr=q1_dr, xn1_dr=xn1_dr, x1_dr=x1_dr, attn1_dr=attn1_dr, fin_o=fin_o,
            idx_xn=idx_xn, idx_q=idx_q, idx_x1=idx_x1,
            Wo=Wo, onw=onw, onb=onb, gate_w=gate_w, gate_b=gate_b, expert_bias=expert_bias,
            w1=w1, b1=b1, w2=w2, b2=b2))
        p.play(block)
    return nc


def emit_b2_core(nc, p, ctx, io):
    T, S = 2048, 4096
    ident, b_id = io["ident"], io["b_id"]
    identf, b_idf = io["identf"], io["b_idf"]
    ones1, b_on = io["ones1"], io["b_on"]
    k_sb, b_k = io["k1_sb"], io["b_k1"]
    v_sb, b_v = io["v1_sb"], io["b_v1"]
    q1_fl = io["q1_dr"].rearrange("p (b x) -> (p b) x", x=16384)
    xn1_fl = io["xn1_dr"].rearrange("p (s x) -> (p s) x", x=2048)
    x1_fl = io["x1_dr"]
    attn1_dr = io["attn1_dr"]; fin_o = io["fin_o"]
    Wo, onw, onb = io["Wo"], io["onw"], io["onb"]
    gate_w, gate_b, expert_bias = io["gate_w"], io["gate_b"], io["expert_bias"]
    w1, b1, w2, b2 = io["w1"], io["b1"], io["w2"], io["b2"]

    oh_tm, b_oht = sbuf(nc, ctx, "oh_tm", [128, 16 * 8])
    oh_dr = nc.dram_tensor("oh_dr", [8, 16 * 128], BF16).ap()
    moe_sb, b_moe = sbuf(nc, ctx, "moe_sb", [128, 16 * 1024], BF16)
    ixn_sb, b_ixn = sbuf(nc, ctx, "ixn_sb", [128, 8], I32)
    ixq_sb, b_ixq = sbuf(nc, ctx, "ixq_sb", [64, 1], I32)
    ix1_sb, b_ix1 = sbuf(nc, ctx, "ix1_sb", [128, 16], I32)
    p.emit("sync", lambda e: e.dma_start(out=ixn_sb[:], in_=io["idx_xn"]), writes=[b_ixn], dma="ix")
    p.emit("sync", lambda e: e.dma_start(out=ixq_sb[:], in_=io["idx_q"]), writes=[b_ixq], dma="ix")
    p.emit("sync", lambda e: e.dma_start(out=ix1_sb[:], in_=io["idx_x1"]), writes=[b_ix1], dma="ix")

    with ExitStack() as c1:
        q_sb, b_q = sbuf(nc, c1, "q_sb", [64, 8 * T], BF16)
        wo_sb, b_wo = sbuf(nc, c1, "wo_sb", [128, 4 * 1024], BF16)
        onwb, b_onwb = sbuf(nc, c1, "onwb", [128, 1024])
        onbb, b_onbb = sbuf(nc, c1, "onbb", [128, 1024])
        row_sb, b_row = sbuf(nc, c1, "row_sb", [1, 1024])
        sc_sb = []; b_sc = []
        pr_sb = []; b_pr = []
        for i in range(2):
            t, b = sbuf(nc, c1, f"sc_sbB{i}", [128, S]); sc_sb.append(t); b_sc.append(b)
            t, b = sbuf(nc, c1, f"pr_sbB{i}", [128, S], BF16); pr_sb.append(t); b_pr.append(b)
        pt_sb, b_pt = sbuf(nc, c1, "pt_sb", [128, 32 * 512], BF16)
        st, b_st = sbuf(nc, c1, "st", [128, 8])
        o1_sb, b_o1 = sbuf(nc, c1, "o1_sb", [128, 4 * T], BF16)
        z_sb, b_z = sbuf(nc, c1, "z_sb", [128, 1024])
        sq, b_sq = sbuf(nc, c1, "sq", [128, 1024])
        a1_sb, b_a1 = sbuf(nc, c1, "a1_sb", [128, 1024], BF16)
        ps_s = []; b_ps = []
        for i in range(2):
            t = c1.enter_context(nc.psum_tensor(f"ps_s{i}", [128, 512], F32))
            ps_s.append(t); b_ps.append(Buf(f"ps_s{i}"))
        ps_t = []; b_pst = []
        for i in range(2):
            t = c1.enter_context(nc.psum_tensor(f"ps_t{i}", [128, 128], BF16))
            ps_t.append(t); b_pst.append(Buf(f"ps_t{i}"))
        ps_o = c1.enter_context(nc.psum_tensor("ps_o", [128, 512], F32)); b_pso = Buf("ps_o")
        ps_z = c1.enter_context(nc.psum_tensor("ps_z", [128, 1024], F32)); b_psz = Buf("ps_z")

        p.emit("gpsimd", lambda e: e.indirect_dma_start(out=q_sb[:], out_offset=None,
               in_=q1_fl, in_offset=IndirectOffsetOnAxis(ixq_sb[:], axis=0)),
               reads=[b_ixq], writes=[b_q], dma="qI")
        p.emit("gpsimd", lambda e: e.dma_start(out=wo_sb[:].rearrange("p (t d) -> p t d", d=1024),
                                               in_=Wo.rearrange("(t p) d -> p t d", p=128)),
               writes=[b_wo], dma="wo")
        for row_b, dst, b_dst in ((onw, onwb, b_onwb), (onb, onbb, b_onbb)):
            p.emit("sync", lambda e, row_b=row_b: e.dma_start(out=row_sb[:], in_=row_b[None, :]),
                   writes=[b_row], dma="nr")
            for eh in range(2):
                p.emit("tensor", lambda e, eh=eh: e.matmul(ps_z[:, eh*512:(eh+1)*512], ones1[:],
                                                           row_sb[0:1, eh*512:(eh+1)*512], start=True, stop=True),
                       reads=[b_on, b_row], writes=[b_psz])
            p.emit("vector", lambda e, dst=dst, b_dst=b_dst: e.tensor_copy(dst[:], ps_z[:]),
                   reads=[b_psz], writes=[b_dst])

        def b2_scores(h, qt, sci):
            for kc in range(8):
                j = kc % 2
                p.emit("tensor", lambda e, h=h, qt=qt, kc=kc, j=j:
                       e.matmul(ps_s[j][:], q_sb[:, h*T + qt*128 : h*T + (qt+1)*128],
                                k_sb[:, kc*512:(kc+1)*512], start=True, stop=True),
                       reads=[b_q, b_k], writes=[b_ps[j]])
                p.emit("vector", lambda e, kc=kc, j=j, sci=sci: e.tensor_copy(sc_sb[sci][:, kc*512:(kc+1)*512], ps_s[j][:]),
                       reads=[b_ps[j]], writes=[b_sc[sci]])
            p.emit("vector", lambda e, sci=sci: e.reduce_max(st[:, 4*sci:4*sci+1], sc_sb[sci][:], axis=AX.X, negate=True),
                   reads=[b_sc[sci]], writes=[b_st])
            p.emit("vector", lambda e, sci=sci: e.tensor_scalar_mul(st[:, 4*sci+1:4*sci+2], st[:, 4*sci:4*sci+1], 0.125),
                   reads=[b_st], writes=[b_st])
            p.emit("scalar", lambda e, sci=sci: e.activation(pr_sb[sci][:], sc_sb[sci][:], AF.Exp, bias=st[:, 4*sci+1:4*sci+2],
                                                    scale=0.125, accum_out=st[:, 4*sci+2:4*sci+3]),
                   reads=[b_sc[sci], b_st], writes=[b_pr[sci], b_st])
            p.emit("vector", lambda e, sci=sci: e.reciprocal(st[:, 4*sci+3:4*sci+4], st[:, 4*sci+2:4*sci+3]),
                   reads=[b_st], writes=[b_st])
            p.emit("vector", lambda e, sci=sci: e.tensor_scalar(pr_sb[sci][:], pr_sb[sci][:], st[:, 4*sci+3:4*sci+4], None, op0=AL.mult),
                   reads=[b_pr[sci], b_st], writes=[b_pr[sci]])

        def b2_trans(q4, sci):
            for kt in range(32):
                j = kt % 2
                p.emit("tensor", lambda e, kt=kt, j=j, sci=sci: e.transpose(ps_t[j][:], pr_sb[sci][:, kt*128:(kt+1)*128], ident[:]),
                       reads=[b_pr[sci], b_id], writes=[b_pst[j]])
                p.emit("vector", lambda e, kt=kt, j=j, q4=q4: e.tensor_copy(pt_sb[:, kt*512 + q4*128 : kt*512 + (q4+1)*128], ps_t[j][:]),
                       reads=[b_pst[j]], writes=[b_pt])

        def b2_av(h, qg):
            hb = 64 * (h % 2)
            for kt in range(32):
                p.emit("tensor", lambda e, kt=kt:
                       e.matmul(ps_o[0:64, :], v_sb[:, kt*64:(kt+1)*64], pt_sb[:, kt*512:(kt+1)*512],
                                start=(kt == 0), stop=(kt == 31)),
                       reads=[b_v, b_pt], writes=[b_pso])
            p.emit("vector", lambda e, h=h, hb=hb, qg=qg:
                   e.tensor_copy(o1_sb[hb:hb+64, (h//2)*T + qg*512 : (h//2)*T + (qg+1)*512], ps_o[0:64, :]),
                   reads=[b_pso], writes=[b_o1])

        seq = [(h, qg, q4) for h in range(8) for qg in range(4) for q4 in range(4)]
        prev = None
        for n, (h, qg, q4) in enumerate(seq):
            b2_scores(h, qg * 4 + q4, n % 2)
            if prev is not None:
                ph, pqg, pq4 = prev
                b2_trans(pq4, (n - 1) % 2)
                if pq4 == 3:
                    b2_av(ph, pqg)
            prev = (h, qg, q4)
        b2_trans(3, (len(seq) - 1) % 2)
        b2_av(7, 3)
        for qt in range(16):
            for eh in range(2):
                for ic in range(4):
                    p.emit("tensor", lambda e, qt=qt, eh=eh, ic=ic:
                           e.matmul(ps_z[:, eh*512:(eh+1)*512], o1_sb[:, ic*T + qt*128 : ic*T + (qt+1)*128],
                                    wo_sb[:, ic*1024 + eh*512 : ic*1024 + (eh+1)*512],
                                    start=(ic == 0), stop=(ic == 3)),
                           reads=[b_o1, b_wo], writes=[b_psz])
            p.emit("vector", lambda e: e.tensor_copy(z_sb[:], ps_z[:]), reads=[b_psz], writes=[b_z])
            emit_ln(p, z_sb[:], z_sb[:], st[:], sq[:], b_z, b_z, b_st, b_sq, 1024)
            p.emit("vector", lambda e: e.tensor_mul(z_sb[:], z_sb[:], onwb[:]), reads=[b_z, b_onwb], writes=[b_z])
            p.emit("vector", lambda e: e.tensor_add(a1_sb[:], z_sb[:], onbb[:]), reads=[b_z, b_onbb], writes=[b_a1])
            p.emit("sync", lambda e, qt=qt: e.dma_start(out=attn1_dr[qt*128:(qt+1)*128, :], in_=a1_sb[:]),
                   reads=[b_a1], dma="a1")
        p.full_barrier()

    with ExitStack() as c2:
        xn_str = []; b_xns = []
        for i in range(2):
            t, b = sbuf(nc, c2, f"xn_str{i}", [128, 8 * 256], BF16); xn_str.append(t); b_xns.append(b)
        ps_g = c2.enter_context(nc.psum_tensor("ps_g", [128, 512], F32)); b_psg = Buf("ps_g")
        ps_h = []; b_psh = []
        for i in range(4):
            t = c2.enter_context(nc.psum_tensor(f"ps_h{i}", [128, 512], F32))
            ps_h.append(t); b_psh.append(Buf(f"ps_h{i}"))
        ps_y = []; b_psy = []
        for i in range(3):
            t = c2.enter_context(nc.psum_tensor(f"ps_y{i}", [128, 512], F32))
            ps_y.append(t); b_psy.append(Buf(f"ps_y{i}"))

        with ExitStack() as c2g:
            oh_em, b_ohe = sbuf(nc, c2g, "oh_em", [8, 16 * 128], BF16)
            gw_row, b_gwr = sbuf(nc, c2g, "gw_row", [8, 1024])
            gwT, b_gw = sbuf(nc, c2g, "gwT", [128, 64], BF16)
            gb_col, b_gb = sbuf(nc, c2g, "gb_col", [8, 2])
            lg_sb, b_lg = sbuf(nc, c2g, "lg_sb", [8, T])
            stm, b_stm = sbuf(nc, c2g, "stm", [128, 4])
            p.emit("sync", lambda e: e.dma_start(out=gw_row[:], in_=gate_w), writes=[b_gwr], dma="gw")
            for dc in range(8):
                j = dc % 2
                p.emit("tensor", lambda e, dc=dc, j=j: e.transpose(ps_h[j][0:128, 0:8], gw_row[0:8, dc*128:(dc+1)*128], identf[0:8, 0:8]),
                       reads=[b_gwr, b_idf], writes=[b_psh[j]])
                p.emit("vector", lambda e, dc=dc, j=j: e.tensor_copy(gwT[:, dc*8:(dc+1)*8], ps_h[j][0:128, 0:8]),
                       reads=[b_psh[j]], writes=[b_gw])
            p.emit("sync", lambda e: e.dma_start(out=gb_col[:, 0:1], in_=gate_b[:, None]), writes=[b_gb], dma="gb")
            p.emit("sync", lambda e: e.dma_start(out=gb_col[:, 1:2], in_=expert_bias[:, None]), writes=[b_gb], dma="gb2")
            p.emit("vector", lambda e: e.tensor_add(gb_col[:, 0:1], gb_col[:, 0:1], gb_col[:, 1:2]),
                   reads=[b_gb], writes=[b_gb])
            for sb in range(8):
                i = sb % 2
                p.emit("gpsimd", lambda e, sb=sb, i=i: e.indirect_dma_start(out=xn_str[i][:], out_offset=None,
                       in_=xn1_fl, in_offset=IndirectOffsetOnAxis(ixn_sb[:, sb:sb+1], axis=0)),
                       reads=[b_ixn], writes=[b_xns[i]], dma=f"xsI{i}")
                for dc in range(8):
                    p.emit("tensor", lambda e, sb=sb, dc=dc, i=i:
                           e.matmul(ps_g[0:8, 0:256], gwT[:, dc*8:(dc+1)*8], xn_str[i][:, dc*256:(dc+1)*256],
                                    start=(dc == 0), stop=(dc == 7)),
                           reads=[b_gw, b_xns[i]], writes=[b_psg])
                p.emit("vector", lambda e, sb=sb: e.tensor_scalar(lg_sb[0:8, sb*256:(sb+1)*256], ps_g[0:8, 0:256],
                                                                  gb_col[0:8, 0:1], None, op0=AL.add),
                       reads=[b_psg, b_gb], writes=[b_lg])
            for t in range(16):
                j = t % 2
                p.emit("tensor", lambda e, t=t, j=j: e.transpose(ps_h[j][0:128, 0:8], lg_sb[0:8, t*128:(t+1)*128], identf[0:8, 0:8]),
                       reads=[b_lg, b_idf], writes=[b_psh[j]])
                p.emit("vector", lambda e, t=t, j=j: e.tensor_copy(oh_tm[:, t*8:(t+1)*8], ps_h[j][0:128, 0:8]),
                       reads=[b_psh[j]], writes=[b_oht])
                p.emit("vector", lambda e, t=t: e.reduce_max(stm[:, 0:1], oh_tm[:, t*8:(t+1)*8], axis=AX.X),
                       reads=[b_oht], writes=[b_stm])
                p.emit("vector", lambda e, t=t: e.tensor_scalar(oh_tm[:, t*8:(t+1)*8], oh_tm[:, t*8:(t+1)*8],
                                                                stm[:, 0:1], None, op0=AL.is_ge),
                       reads=[b_oht, b_stm], writes=[b_oht])
                p.emit("tensor", lambda e, t=t, j=j: e.transpose(ps_h[j][0:8, 0:128], oh_tm[:, t*8:(t+1)*8], identf[:]),
                       reads=[b_oht, b_idf], writes=[b_psh[j]])
                p.emit("vector", lambda e, t=t, j=j: e.tensor_copy(oh_em[0:8, t*128:(t+1)*128], ps_h[j][0:8, 0:128]),
                       reads=[b_psh[j]], writes=[b_ohe])
            p.emit("sync", lambda e: e.dma_start(out=oh_dr, in_=oh_em[:]), reads=[b_ohe], dma="ohw")
            p.full_barrier()

        w1e, b_w1e = sbuf(nc, c2, "w1e", [128, 8 * 4096], BF16)
        w2e, b_w2e = sbuf(nc, c2, "w2e", [128, 32 * 1024], BF16)
        b1row, b_b1r = sbuf(nc, c2, "b1row", [1, 4096], BF16)
        onesr, b_onr = sbuf(nc, c2, "onesr", [1, 512], BF16)
        h_sb, b_h = sbuf(nc, c2, "h_sb", [128, 32 * 256], BF16)
        p.emit("vector", lambda e: e.memset(onesr[:], 1.0), writes=[b_onr])
        p.emit("vector", lambda e: e.memset(moe_sb[:], 0.0), writes=[b_moe])
        for ex in range(8):
            for dcl in range(8):
                p.emit("gpsimd", lambda e, ex=ex, dcl=dcl: e.dma_start(
                        out=w1e[:, dcl*4096:(dcl+1)*4096],
                        in_=w1[ex, dcl*128:(dcl+1)*128, :]),
                       writes=[b_w1e], dma="w1")
            p.emit("gpsimd", lambda e, ex=ex: e.dma_start(out=b1row[:], in_=b1[ex][None, :]),
                   writes=[b_b1r], dma="b1")
            for fcl in range(8):
                p.emit("gpsimd", lambda e, ex=ex, fcl=fcl: e.dma_start(
                        out=w2e[:].rearrange("p (t d) -> p t d", d=1024)[:, fcl*4:(fcl+1)*4, :],
                        in_=w2[ex].rearrange("(t p) d -> p t d", p=128)[:, fcl*4:(fcl+1)*4, :]),
                       writes=[b_w2e], dma="w2")
            for sb in range(8):
                i = sb % 2
                p.emit("gpsimd", lambda e, sb=sb, i=i: e.indirect_dma_start(out=xn_str[i][:], out_offset=None,
                       in_=xn1_fl, in_offset=IndirectOffsetOnAxis(ixn_sb[:, sb:sb+1], axis=0)),
                       reads=[b_ixn], writes=[b_xns[i]], dma=f"xsI{i}")
                for fc in range(32):
                    j = fc % 4
                    for dc in range(8):
                        p.emit("tensor", lambda e, fc=fc, dc=dc, j=j, i=i:
                               e.matmul(ps_h[j][:, 0:256], w1e[:, dc*4096 + fc*128 : dc*4096 + (fc+1)*128],
                                        xn_str[i][:, dc*256:(dc+1)*256],
                                        start=(dc == 0), stop=False),
                               reads=[b_w1e, b_xns[i]], writes=[b_psh[j]])
                    p.emit("tensor", lambda e, fc=fc, j=j:
                           e.matmul(ps_h[j][:, 0:256], b1row[0:1, fc*128:(fc+1)*128], onesr[0:1, 0:256],
                                    start=False, stop=True),
                           reads=[b_b1r, b_onr], writes=[b_psh[j]])
                    p.emit("scalar", lambda e, fc=fc, j=j: e.activation(h_sb[:, fc*256:(fc+1)*256], ps_h[j][:, 0:256], AF.Relu),
                           reads=[b_psh[j]], writes=[b_h])
                for st2 in range(2):
                    for dh in range(2):
                        j2 = (st2 * 2 + dh) % 3
                        for fc in range(32):
                            p.emit("tensor", lambda e, st2=st2, dh=dh, fc=fc, j2=j2:
                                   e.matmul(ps_y[j2][:],
                                            h_sb[:, fc*256 + st2*128 : fc*256 + (st2+1)*128],
                                            w2e[:, fc*1024 + dh*512 : fc*1024 + (dh+1)*512],
                                            start=(fc == 0), stop=(fc == 31)),
                                   reads=[b_h, b_w2e], writes=[b_psy[j2]])
                        tt = sb * 2 + st2
                        p.emit("vector", lambda e, tt=tt, dh=dh, j2=j2, ex=ex:
                               e.scalar_tensor_tensor(moe_sb[:, tt*1024 + dh*512 : tt*1024 + (dh+1)*512], ps_y[j2][:],
                                                      oh_tm[:, tt*8 + ex : tt*8 + ex + 1],
                                                      moe_sb[:, tt*1024 + dh*512 : tt*1024 + (dh+1)*512], op0=AL.mult, op1=AL.add),
                               reads=[b_psy[j2], b_oht, b_moe], writes=[b_moe])
        p.full_barrier()

    with ExitStack() as c3:
        b2_sb, b_b2 = sbuf(nc, c3, "b2_sb", [8, 1024], BF16)
        oh_em3, b_ohe3 = sbuf(nc, c3, "oh_em2", [8, 16 * 128], BF16)
        p.emit("sync", lambda e: e.dma_start(out=oh_em3[:], in_=oh_dr), writes=[b_ohe3], dma="ohr")
        xt, b_xt = sbuf(nc, c3, "xt", [128, 1024])
        a1t, b_a1t = sbuf(nc, c3, "a1t", [128, 1024], BF16)
        f_sb, b_f = sbuf(nc, c3, "f_sb", [128, 1024])
        ps_f = c3.enter_context(nc.psum_tensor("ps_f", [128, 1024], F32)); b_psf = Buf("ps_f")
        p.emit("gpsimd", lambda e: e.dma_start(out=b2_sb[:], in_=b2), writes=[b_b2], dma="b2l")
        for t in range(16):
            p.emit("gpsimd", lambda e, t=t: e.indirect_dma_start(out=xt[:], out_offset=None,
                   in_=x1_fl, in_offset=IndirectOffsetOnAxis(ix1_sb[:, t:t+1], axis=0)),
                   reads=[b_ix1], writes=[b_xt], dma="x1I")
            p.emit("sync", lambda e, t=t: e.dma_start(out=a1t[:], in_=attn1_dr[t*128:(t+1)*128, :]),
                   writes=[b_a1t], dma="a1l")
            for eh in range(2):
                p.emit("tensor", lambda e, t=t, eh=eh:
                       e.matmul(ps_f[:, eh*512:(eh+1)*512], oh_em3[0:8, t*128:(t+1)*128],
                                b2_sb[0:8, eh*512:(eh+1)*512], start=True, stop=True),
                       reads=[b_ohe3, b_b2], writes=[b_psf])
            p.emit("vector", lambda e: e.tensor_add(f_sb[:], xt[:], ps_f[:]), reads=[b_xt, b_psf], writes=[b_f])
            p.emit("vector", lambda e: e.tensor_add(f_sb[:], f_sb[:], a1t[:]), reads=[b_f, b_a1t], writes=[b_f])
            p.emit("vector", lambda e, t=t: e.tensor_add(f_sb[:], f_sb[:], moe_sb[:, t*1024:(t+1)*1024]),
                   reads=[b_f, b_moe], writes=[b_f])
            p.emit("sync", lambda e, t=t: e.dma_start(out=fin_o[t*128:(t+1)*128, :], in_=f_sb[:]),
                   reads=[b_f], dma="fo")
        s_, c_ = p.dma_sems["fo"]
        p.barrier("sync", [Tok(s_, c_)])
import numpy as np

_NCS = {}

def _get_nc(name, fn):
    if name not in _NCS:
        _NCS[name] = fn()
    return _NCS[name]


def _idx_tables(h):
    p64 = np.arange(64, dtype=np.int32)
    p128 = np.arange(128, dtype=np.int32)
    idx_q = (2 * p64 + h)[:, None].astype(np.int32)
    idx_xn = (16 * p128[:, None] + 8 * h + np.arange(8, dtype=np.int32)[None, :]).astype(np.int32)
    idx_x1 = (2048 * h + 128 * np.arange(16, dtype=np.int32)[None, :] + p128[:, None]).astype(np.int32)
    return idx_q, idx_xn, idx_x1


def kernel(L0_attn_x, L0_attn_w_qkv, L0_attn_w_proj, L0_attn_b_proj,
           L0_ffn_w_router, L0_ffn_w_block,
           L1_attn_Wq, L1_attn_Wkv, L1_attn_Wo, L1_attn_norm_w, L1_attn_norm_b,
           L1_attn_out_norm_w, L1_attn_out_norm_b,
           L1_ffn_gate_w, L1_ffn_gate_b, L1_ffn_expert_bias,
           L1_ffn_w1, L1_ffn_b1, L1_ffn_w2, L1_ffn_b2):
    x = np.ascontiguousarray(L0_attn_x)
    B, S, D = x.shape
    T = 2048
    cores = list(range(8))
    nc = _get_nc("fused", build_fused)
    maps = []
    for c in cores:
        b, h = c // 2, c % 2
        idx_q, idx_xn, idx_x1 = _idx_tables(h)
        maps.append(dict(
            x=np.ascontiguousarray(x[b]),
            w_qkv=L0_attn_w_qkv, w_router=L0_ffn_w_router,
            w_proj=L0_attn_w_proj, b_proj=L0_attn_b_proj, w_block=L0_ffn_w_block,
            norm_w=L1_attn_norm_w, norm_b=L1_attn_norm_b,
            Wq=L1_attn_Wq, Wkv=L1_attn_Wkv, Wo=L1_attn_Wo,
            onw=L1_attn_out_norm_w, onb=L1_attn_out_norm_b,
            gate_w=L1_ffn_gate_w, gate_b=L1_ffn_gate_b, expert_bias=L1_ffn_expert_bias,
            w1=L1_ffn_w1, b1=L1_ffn_b1, w2=L1_ffn_w2, b2=L1_ffn_b2,
            idx_q=idx_q, idx_xn=idx_xn, idx_x1=idx_x1))
    res = run_bass_kernel_spmd(nc, maps, core_ids=cores).results
    out = np.empty((B, S, D), np.float32)
    for c in cores:
        b, h = c // 2, c % 2
        out[b, T*h:T*h+T] = res[c]["fin_o"]
    return out


# version 06
# speedup vs baseline: 2.2218x; 2.2218x over previous
import numpy as np
from contextlib import ExitStack
import concourse.bass as bass
import concourse.mybir as mybir
from concourse.bass import IndirectOffsetOnAxis
from concourse.bass_utils import run_bass_kernel_spmd

F32 = mybir.dt.float32
BF16 = mybir.dt.bfloat16
I32 = mybir.dt.int32
AF = mybir.ActivationFunctionType
AL = mybir.AluOpType
AX = mybir.AxisListType

ENGS = ("sync", "scalar", "vector", "tensor", "gpsimd")


class Tok:
    __slots__ = ("item", "sem", "val")
    def __init__(self, a=None, b=None, c=None):
        if isinstance(a, Item):
            self.item = a; self.sem = b; self.val = c
        else:
            self.item = None
            if c is not None:
                self.sem = b; self.val = c
            else:
                self.sem = a; self.val = b


class Item:
    __slots__ = ("eng", "waits", "fn", "marked", "val", "dma")
    def __init__(self, eng, waits, fn, dma):
        self.eng = eng; self.waits = waits; self.fn = fn
        self.marked = False; self.val = None; self.dma = dma


class Buf:
    __slots__ = ("name", "writer", "readers")
    def __init__(self, name=""):
        self.name = name; self.writer = None; self.readers = []


class Prog:
    def __init__(self, nc):
        self.nc = nc
        self.items = {e: [] for e in ENGS}
        self.chain = {}
        self.dma_sems = {}
        self._ctx = None
        self.extra_waits = {e: [] for e in ENGS}

    def setup(self, ctx):
        self._ctx = ctx
        for e in ENGS:
            self.chain[e] = ctx.enter_context(self.nc.semaphore(f"ch_{e}"))

    def dma_sem(self, name):
        if name not in self.dma_sems:
            self.dma_sems[name] = [self._ctx.enter_context(self.nc.semaphore(f"ds_{name}")), 0]
        return self.dma_sems[name]

    def emit(self, eng, fn, reads=(), writes=(), waits=(), dma=None):
        wl = []
        for b in reads:
            if b.writer: wl.append(b.writer)
        for b in writes:
            if b.writer: wl.append(b.writer)
            wl.extend(b.readers)
        wl.extend(waits)
        item = Item(eng, wl, fn, dma)
        if dma is not None:
            ds = self.dma_sem(dma)
            ds[1] += 16
            tok = Tok(None, ds[0], ds[1])
            item.marked = True
            item.val = (ds[0], ds[1])
        else:
            tok = Tok(item)
        self.items[eng].append(item)
        for b in reads: b.readers.append(tok)
        for b in writes:
            b.writer = tok; b.readers = []
        return tok

    def barrier(self, eng, toks):
        item = Item(eng, list(toks), None, None)
        self.items[eng].append(item)

    def full_barrier(self):
        toks = []
        for e in ENGS:
            for it in reversed(self.items[e]):
                if it.fn is not None and it.dma is None:
                    toks.append(Tok(it)); break
        toks += [Tok(None, s, c) for (s, c) in self.dma_sems.values() if c > 0]
        for e in ENGS:
            self.barrier(e, toks)

    def play(self, block):
        for e in ENGS:
            for it in self.items[e]:
                for t in it.waits:
                    if t.item is not None:
                        t.item.marked = True
        for e in ENGS:
            cnt = 0
            for it in self.items[e]:
                if it.fn is not None and it.dma is None and it.marked:
                    cnt += 1
                    it.val = (self.chain[e], cnt)
        def mk(eng_name):
            waited = {}
            def body(e):
                for it in self.items[eng_name]:
                    byse = {}
                    for t in it.waits:
                        if t.item is not None:
                            sem, val = t.item.val[0], t.item.val[1]
                        else:
                            sem, val = t.sem, t.val
                        kk = id(sem)
                        if kk not in byse or byse[kk][1] < val:
                            byse[kk] = (sem, val)
                    for kk, (sem, val) in byse.items():
                        if waited.get(kk, -1) < val:
                            e.wait_ge(sem, val)
                            waited[kk] = val
                    if it.fn is not None:
                        inst = it.fn(e)
                        if it.dma is not None:
                            inst.then_inc(it.val[0], 16)
                        elif it.marked:
                            inst.then_inc(it.val[0], 1)
            return body
        for e in ENGS:
            if self.items[e]:
                getattr(block, e)(mk(e))


def emit_group(p, eng, fns, reads=(), writes=(), waits=(), ):
    def run(e):
        last = None
        for f in fns:
            last = f(e)
        return last
    return p.emit(eng, run, reads=reads, writes=writes, waits=waits)


_UNIQ = [0]

def _uniquify(nc):
    orig_sb, orig_ps = nc.sbuf_tensor, nc.psum_tensor
    def sb(name, shape, dt, **kw):
        _UNIQ[0] += 1
        return orig_sb(f"{name}_u{_UNIQ[0]}", shape, dt, **kw)
    def ps(name, shape, dt=F32, **kw):
        _UNIQ[0] += 1
        return orig_ps(f"{name}_u{_UNIQ[0]}", shape, dt, **kw)
    nc.sbuf_tensor = sb
    nc.psum_tensor = ps

def sbuf(nc, ctx, name, shape, dt=F32):
    return ctx.enter_context(nc.sbuf_tensor(name, shape, dt)), Buf(name)


def psum(nc, ctx, name, shape, dt=F32):
    return ctx.enter_context(nc.psum_tensor(name, shape, dt)), Buf(name)


def emit_identity(nc, p, b_id, ap):
    p.emit("gpsimd", lambda g: g.memset(ap, 0.0), writes=[b_id])
    p.emit("gpsimd", lambda g: g.affine_select(out=ap, in_=ap,
        compare_op=AL.not_equal, fill=1.0, base=0,
        pattern=[[-1, ap.shape[0]]], channel_multiplier=1),
        reads=[b_id], writes=[b_id])


def emit_ln(p, x_ap, out_ap, st_ap, sq_ap, b_x, b_out, b_st, b_sq, D, out2_ap=None, b_out2=None):
    p.emit("vector", lambda e: e.reduce_sum(st_ap[:, 0:1], x_ap, axis=AX.X), reads=[b_x], writes=[b_st])
    p.emit("vector", lambda e: e.tensor_scalar_mul(st_ap[:, 1:2], st_ap[:, 0:1], -1.0 / D), reads=[b_st], writes=[b_st])
    p.emit("scalar", lambda e: e.activation(sq_ap, x_ap, AF.Square, bias=st_ap[:, 1:2], scale=1.0,
                                            accum_out=st_ap[:, 2:3]), reads=[b_x, b_st], writes=[b_sq, b_st])
    p.emit("vector", lambda e: e.tensor_scalar(st_ap[:, 3:4], st_ap[:, 2:3], 1.0 / D, 1e-5, op0=AL.mult, op1=AL.add), reads=[b_st], writes=[b_st])
    p.emit("vector", lambda e: e.reciprocal(st_ap[:, 2:3], st_ap[:, 3:4]), reads=[b_st], writes=[b_st])
    p.emit("scalar", lambda e: e.activation(st_ap[:, 3:4], st_ap[:, 2:3], AF.Sqrt), reads=[b_st], writes=[b_st])
    p.emit("vector", lambda e: e.tensor_scalar(out_ap, x_ap, st_ap[:, 1:2], st_ap[:, 3:4],
                                               op0=AL.add, op1=AL.mult), reads=[b_x, b_st], writes=[b_out])
    if out2_ap is not None:
        p.emit("scalar", lambda e: e.activation(out2_ap, out_ap, AF.Copy), reads=[b_out], writes=[b_out2])


def build_a1():
    nc = bass.Bass("TRN2", target_bir_lowering=False)
    p = Prog(nc)
    T = 2048
    x = nc.dram_tensor("x", [T, 1024], F32, kind="ExternalInput").ap()
    w_qkv = nc.dram_tensor("w_qkv", [3072, 1024], F32, kind="ExternalInput").ap()
    w_router = nc.dram_tensor("w_router", [1024], F32, kind="ExternalInput").ap()
    q_o = nc.dram_tensor("q_o", [128, 8 * T], BF16, kind="ExternalOutput").ap()
    k_o = nc.dram_tensor("k_o", [128, 8 * T], BF16, kind="ExternalOutput").ap()
    v_o = nc.dram_tensor("v_o", [T, 1024], BF16, kind="ExternalOutput").ap()
    r_o = nc.dram_tensor("r_o", [1, T], F32, kind="ExternalOutput").ap()
    ctx = ExitStack()
    with ctx:
        p.setup(ctx)
        ident, b_id = sbuf(nc, ctx, "ident", [128, 128], BF16)
        xt = []; b_xt = []
        for i in range(2):
            t, b = sbuf(nc, ctx, f"xt{i}", [128, 1024]); xt.append(t); b_xt.append(b)
        st, b_st = sbuf(nc, ctx, "st", [128, 8])
        sq, b_sq = sbuf(nc, ctx, "sq", [128, 1024])
        lnb, b_lnb = sbuf(nc, ctx, "lnb", [128, 1024], BF16)
        xn_fm, b_xn = sbuf(nc, ctx, "xn_fm", [128, 8 * T], BF16)
        wn, b_wn = sbuf(nc, ctx, "wn", [128, 2 * 1024], BF16)
        wT, b_wT = sbuf(nc, ctx, "wT", [128, 8 * 3072], BF16)
        wr_row, b_wrr = sbuf(nc, ctx, "wr_row", [1, 1024], BF16)
        wr, b_wr = sbuf(nc, ctx, "wr", [128, 8], BF16)
        qk_sb, b_qk = sbuf(nc, ctx, "qk_sb", [128, 512], BF16)
        v_sb, b_vsb = sbuf(nc, ctx, "v_sb", [128, 1024], BF16)
        r_sb, b_rsb = sbuf(nc, ctx, "r_sb", [1, T])
        ps_t = []; b_pst = []
        for i in range(2):
            t = ctx.enter_context(nc.psum_tensor(f"ps_t{i}", [128, 128], BF16))
            ps_t.append(t); b_pst.append(Buf(f"ps_t{i}"))
        ps_m = []; b_psm = []
        for i in range(2):
            t = ctx.enter_context(nc.psum_tensor(f"ps_m{i}", [128, 512], F32))
            ps_m.append(t); b_psm.append(Buf(f"ps_m{i}"))
        block = ctx.enter_context(nc.Block())

        emit_identity(nc, p, b_id, ident[:])

        for t in range(16):
            i = t % 2
            p.emit("sync", lambda e, t=t, i=i: e.dma_start(out=xt[i][:], in_=x[t*128:(t+1)*128, :]),
                   writes=[b_xt[i]], dma=f"x{i}")
            emit_ln(p, xt[i][:], lnb[:], st[:], sq[:], b_xt[i], b_lnb, b_st, b_sq, 1024)
            for c in range(8):
                j = c % 2
                p.emit("tensor", lambda e, c=c, j=j: e.transpose(ps_t[j][:], lnb[:, c*128:(c+1)*128], ident[:]),
                       reads=[b_lnb, b_id], writes=[b_pst[j]])
                p.emit("vector", lambda e, c=c, j=j, t=t: e.tensor_copy(xn_fm[:, c*T + t*128 : c*T + (t+1)*128], ps_t[j][:]),
                       reads=[b_pst[j]], writes=[b_xn])

        for t in range(24):
            i = t % 2
            p.emit("gpsimd", lambda e, t=t, i=i: e.dma_start(out=wn[:, i*1024:(i+1)*1024], in_=w_qkv[t*128:(t+1)*128, :]),
                   writes=[b_wn], dma=f"w{i}")
            for c in range(8):
                j = c % 2
                p.emit("tensor", lambda e, c=c, i=i, j=j: e.transpose(ps_t[j][:], wn[:, i*1024 + c*128 : i*1024 + (c+1)*128], ident[:]),
                       reads=[b_wn, b_id], writes=[b_pst[j]])
                p.emit("vector", lambda e, c=c, j=j, t=t: e.tensor_copy(wT[:, c*3072 + t*128 : c*3072 + (t+1)*128], ps_t[j][:]),
                       reads=[b_pst[j]], writes=[b_wT])

        p.emit("gpsimd", lambda e: e.dma_start(out=wr_row[:], in_=w_router[None, :]), writes=[b_wrr], dma="wr")
        for c in range(8):
            j = c % 2
            p.emit("tensor", lambda e, c=c, j=j: e.transpose(ps_t[j][0:128, 0:1], wr_row[0:1, c*128:(c+1)*128], ident[0:1, 0:1]),
                   reads=[b_wrr, b_id], writes=[b_pst[j]])
            p.emit("vector", lambda e, c=c, j=j: e.tensor_copy(wr[:, c:c+1], ps_t[j][0:128, 0:1]),
                   reads=[b_pst[j]], writes=[b_wr])

        for which, off, out_t in (("q", 0, q_o), ("k", 1024, k_o)):
            for f in range(8):
                for sc in range(4):
                    j = (f * 4 + sc) % 2
                    for ec in range(8):
                        p.emit("tensor", lambda e, f=f, sc=sc, ec=ec, j=j, off=off:
                               e.matmul(ps_m[j][:], wT[:, ec*3072 + off + f*128 : ec*3072 + off + (f+1)*128],
                                        xn_fm[:, ec*T + sc*512 : ec*T + (sc+1)*512],
                                        start=(ec == 0), stop=(ec == 7)),
                               reads=[b_wT, b_xn], writes=[b_psm[j]])
                    p.emit("vector", lambda e, j=j: e.tensor_copy(qk_sb[:], ps_m[j][:]),
                           reads=[b_psm[j]], writes=[b_qk])
                    p.emit("sync", lambda e, f=f, sc=sc, out_t=out_t:
                           e.dma_start(out=out_t[:, f*T + sc*512 : f*T + (sc+1)*512], in_=qk_sb[:]),
                           reads=[b_qk], dma="qk_o")

        for t in range(16):
            for vh in range(2):
                j = (t * 2 + vh) % 2
                for ec in range(8):
                    p.emit("tensor", lambda e, t=t, vh=vh, ec=ec, j=j:
                           e.matmul(ps_m[j][:], xn_fm[:, ec*T + t*128 : ec*T + (t+1)*128],
                                    wT[:, ec*3072 + 2048 + vh*512 : ec*3072 + 2048 + (vh+1)*512],
                                    start=(ec == 0), stop=(ec == 7)),
                           reads=[b_wT, b_xn], writes=[b_psm[j]])
                p.emit("vector", lambda e, vh=vh, j=j: e.tensor_copy(v_sb[:, vh*512:(vh+1)*512], ps_m[j][:]),
                       reads=[b_psm[j]], writes=[b_vsb])
            p.emit("sync", lambda e, t=t: e.dma_start(out=v_o[t*128:(t+1)*128, :], in_=v_sb[:]),
                   reads=[b_vsb], dma="v_o")

        for sc in range(4):
            j = sc % 2
            for ec in range(8):
                p.emit("tensor", lambda e, sc=sc, ec=ec, j=j:
                       e.matmul(ps_m[j][0:1, :], wr[:, ec:ec+1],
                                xn_fm[:, ec*T + sc*512 : ec*T + (sc+1)*512],
                                start=(ec == 0), stop=(ec == 7)),
                       reads=[b_wr, b_xn], writes=[b_psm[j]])
            p.emit("vector", lambda e, sc=sc, j=j: e.tensor_copy(r_sb[0:1, sc*512:(sc+1)*512], ps_m[j][0:1, :]),
                   reads=[b_psm[j]], writes=[b_rsb])
        p.emit("sync", lambda e: e.dma_start(out=r_o, in_=r_sb[:]), reads=[b_rsb], dma="r_o")

        fin = []
        for n in ("qk_o", "v_o", "r_o"):
            s, c = p.dma_sems[n]
            fin.append(Tok(s, c))
        p.barrier("sync", fin)
        p.play(block)
    return nc


def build_a2a():
    nc = bass.Bass("TRN2", target_bir_lowering=False)
    p = Prog(nc)
    T, S = 2048, 4096
    q = nc.dram_tensor("q", [128, 8 * T], BF16, kind="ExternalInput").ap()
    k = nc.dram_tensor("k", [128, 8 * S], BF16, kind="ExternalInput").ap()
    v = nc.dram_tensor("v", [S, 1024], BF16, kind="ExternalInput").ap()
    o = nc.dram_tensor("o", [T, 1024], BF16, kind="ExternalOutput").ap()
    ctx = ExitStack()
    with ctx:
        p.setup(ctx)
        ident, b_id = sbuf(nc, ctx, "ident", [128, 128], BF16)
        q_sb, b_q = sbuf(nc, ctx, "q_sb", [128, 8 * T], BF16)
        k_sb, b_k = sbuf(nc, ctx, "k_sb", [128, 8 * S], BF16)
        v_sb, b_v = sbuf(nc, ctx, "v_sb", [128, 32 * 1024], BF16)
        sc_sb, b_sc = sbuf(nc, ctx, "sc_sb", [128, S])
        pr_sb, b_pr = sbuf(nc, ctx, "pr_sb", [128, S], BF16)
        pt_sb, b_pt = sbuf(nc, ctx, "pt_sb", [128, 32 * 128], BF16)
        st, b_st = sbuf(nc, ctx, "st", [128, 8])
        o_sb = []; b_o = []
        for i in range(2):
            t, b = sbuf(nc, ctx, f"o_sb{i}", [128, 1024], BF16); o_sb.append(t); b_o.append(b)
        ps_s = []; b_ps = []
        for i in range(2):
            t = ctx.enter_context(nc.psum_tensor(f"ps_s{i}", [128, 512], F32))
            ps_s.append(t); b_ps.append(Buf(f"ps_s{i}"))
        ps_t = []; b_pst = []
        for i in range(2):
            t = ctx.enter_context(nc.psum_tensor(f"ps_t{i}", [128, 128], BF16))
            ps_t.append(t); b_pst.append(Buf(f"ps_t{i}"))
        ps_av = []; b_pav = []
        for i in range(2):
            t = ctx.enter_context(nc.psum_tensor(f"ps_av{i}", [128, 512], F32))
            ps_av.append(t); b_pav.append(Buf(f"ps_av{i}"))
        block = ctx.enter_context(nc.Block())

        emit_identity(nc, p, b_id, ident[:])
        p.emit("sync", lambda e: e.dma_start(out=q_sb[:], in_=q), writes=[b_q], dma="q")
        p.emit("sync", lambda e: e.dma_start(out=k_sb[:], in_=k), writes=[b_k], dma="k")
        p.emit("sync", lambda e: e.dma_start(out=v_sb[:].rearrange("p (t d) -> p t d", d=1024),
                                             in_=v.rearrange("(t p) d -> p t d", p=128)),
               writes=[b_v], dma="v")

        for qt in range(16):
            for kc in range(8):
                j = kc % 2
                for qf in range(8):
                    p.emit("tensor", lambda e, qt=qt, kc=kc, qf=qf, j=j:
                           e.matmul(ps_s[j][:], q_sb[:, qf*T + qt*128 : qf*T + (qt+1)*128],
                                    k_sb[:, qf*S + kc*512 : qf*S + (kc+1)*512],
                                    start=(qf == 0), stop=(qf == 7)),
                           reads=[b_q, b_k], writes=[b_ps[j]])
                p.emit("vector", lambda e, kc=kc, j=j: e.tensor_copy(sc_sb[:, kc*512:(kc+1)*512], ps_s[j][:]),
                       reads=[b_ps[j]], writes=[b_sc])
            p.emit("vector", lambda e: e.reduce_max(st[:, 0:1], sc_sb[:], axis=AX.X, negate=True),
                   reads=[b_sc], writes=[b_st])
            p.emit("vector", lambda e: e.tensor_scalar_mul(st[:, 1:2], st[:, 0:1], 0.125),
                   reads=[b_st], writes=[b_st])
            p.emit("scalar", lambda e: e.activation(pr_sb[:], sc_sb[:], AF.Exp, bias=st[:, 1:2],
                                                    scale=0.125, accum_out=st[:, 2:3]),
                   reads=[b_sc, b_st], writes=[b_pr, b_st])
            p.emit("vector", lambda e: e.reciprocal(st[:, 3:4], st[:, 2:3]), reads=[b_st], writes=[b_st])
            for kt in range(32):
                j = kt % 2
                p.emit("tensor", lambda e, kt=kt, j=j: e.transpose(ps_t[j][:], pr_sb[:, kt*128:(kt+1)*128], ident[:]),
                       reads=[b_pr, b_id], writes=[b_pst[j]])
                p.emit("vector", lambda e, kt=kt, j=j: e.tensor_copy(pt_sb[:, kt*128:(kt+1)*128], ps_t[j][:]),
                       reads=[b_pst[j]], writes=[b_pt])
            ob = qt % 2
            for vh in range(2):
                j = vh
                for kt in range(32):
                    p.emit("tensor", lambda e, kt=kt, vh=vh, j=j:
                           e.matmul(ps_av[j][:], pt_sb[:, kt*128:(kt+1)*128],
                                    v_sb[:, kt*1024 + vh*512 : kt*1024 + (vh+1)*512],
                                    start=(kt == 0), stop=(kt == 31)),
                           reads=[b_pt, b_v], writes=[b_pav[j]])
                p.emit("vector", lambda e, vh=vh, j=j, ob=ob:
                       e.tensor_scalar(o_sb[ob][:, vh*512:(vh+1)*512], ps_av[j][:], st[:, 3:4], None, op0=AL.mult),
                       reads=[b_pav[j], b_st], writes=[b_o[ob]])
            p.emit("sync", lambda e, qt=qt, ob=ob: e.dma_start(out=o[qt*128:(qt+1)*128, :], in_=o_sb[ob][:]),
                   reads=[b_o[ob]], dma="o")
        s_, c_ = p.dma_sems["o"]
        p.barrier("sync", [Tok(s_, c_)])
        p.play(block)
    return nc


def build_a2b(TOPK=3072, NIT=36):
    nc = bass.Bass("TRN2", target_bir_lowering=False)
    p = Prog(nc)
    T = 2048
    og = nc.dram_tensor("og", [T, 1024], BF16, kind="ExternalInput").ap()
    x_sig = nc.dram_tensor("x_sig", [T, 1024], F32, kind="ExternalInput").ap()
    rsc = nc.dram_tensor("rsc", [4096], F32, kind="ExternalInput").ap()
    rsc_sig = nc.dram_tensor("rsc_sig", [2048], F32, kind="ExternalInput").ap()
    w_proj = nc.dram_tensor("w_proj", [1024, 1024], F32, kind="ExternalInput").ap()
    b_proj = nc.dram_tensor("b_proj", [1024], F32, kind="ExternalInput").ap()
    w_block = nc.dram_tensor("w_block", [1024, 1024], F32, kind="ExternalInput").ap()
    x1_o = nc.dram_tensor("x1_o", [T, 1024], F32, kind="ExternalOutput").ap()
    ctx = ExitStack()
    with ctx:
        p.setup(ctx)
        ident, b_id = sbuf(nc, ctx, "ident", [128, 128], BF16)
        identf, b_idf = sbuf(nc, ctx, "identf", [128, 128])
        ones128, b_o128 = sbuf(nc, ctx, "ones128", [128, 128])
        og_sb, b_og = sbuf(nc, ctx, "og_sb", [128, 16 * 1024], BF16)
        wpn, b_wpn = sbuf(nc, ctx, "wpn", [128, 2 * 1024], BF16)
        wpT, b_wpT = sbuf(nc, ctx, "wpT", [128, 8 * 1024], BF16)
        wb_sb, b_wb = sbuf(nc, ctx, "wb_sb", [128, 8 * 1024], BF16)
        bp_row, b_bpr = sbuf(nc, ctx, "bp_row", [1, 1024])
        bpb, b_bpb = sbuf(nc, ctx, "bpb", [128, 1024])
        rs_sb, b_rs = sbuf(nc, ctx, "rs_sb", [128, 32])
        rsig_sb, b_rsig = sbuf(nc, ctx, "rsig_sb", [1, 2048])
        cmp_sb, b_cmp = sbuf(nc, ctx, "cmp_sb", [128, 32])
        thr, b_thr = sbuf(nc, ctx, "thr", [128, 1])
        sgn, b_sgn = sbuf(nc, ctx, "sgn", [128, 1])
        step_tab, b_stp = sbuf(nc, ctx, "step_tab", [128, NIT])
        st4, b_st4 = sbuf(nc, ctx, "st4", [128, 8])
        row_sc, b_rsc2 = sbuf(nc, ctx, "row_sc", [1, 128])
        mcol, b_mc = sbuf(nc, ctx, "mcol", [128, 2])
        xt, b_xt = sbuf(nc, ctx, "xt", [128, 1024])
        st, b_st = sbuf(nc, ctx, "st", [128, 8])
        sq, b_sq = sbuf(nc, ctx, "sq", [128, 1024])
        xn_f, b_xnf = sbuf(nc, ctx, "xn_f", [128, 1024])
        xn_b, b_xnb = sbuf(nc, ctx, "xn_b", [128, 1024], BF16)
        xnfm, b_xnfm = sbuf(nc, ctx, "xnfm", [128, 8 * 128], BF16)
        y_sb, b_y = sbuf(nc, ctx, "y_sb", [128, 1024])
        a3_sb, b_a3 = sbuf(nc, ctx, "a3_sb", [128, 1024])
        d_sb, b_d = sbuf(nc, ctx, "d_sb", [128, 1024])
        x1_sb, b_x1 = sbuf(nc, ctx, "x1_sb", [128, 1024])
        ps_y = ctx.enter_context(nc.psum_tensor("ps_y", [128, 1024], F32)); b_psy = Buf("ps_y")
        ps_p = ctx.enter_context(nc.psum_tensor("ps_p", [128, 1024], F32)); b_psp = Buf("ps_p")
        ps_c = ctx.enter_context(nc.psum_tensor("ps_c", [128, 128], F32)); b_psc = Buf("ps_c")
        ps_t = []; b_pst = []
        for i in range(2):
            t = ctx.enter_context(nc.psum_tensor(f"ps_t{i}", [128, 128], BF16))
            ps_t.append(t); b_pst.append(Buf(f"ps_t{i}"))
        block = ctx.enter_context(nc.Block())

        emit_identity(nc, p, b_id, ident[:])
        p.emit("gpsimd", lambda g: g.memset(identf[:], 0.0), writes=[b_idf])
        p.emit("gpsimd", lambda g: g.affine_select(out=identf[:], in_=identf[:],
            compare_op=AL.not_equal, fill=1.0, base=0, pattern=[[-1, 128]], channel_multiplier=1),
            reads=[b_idf], writes=[b_idf])
        p.emit("vector", lambda e: e.memset(ones128[:], 1.0), writes=[b_o128])

        p.emit("sync", lambda e: e.dma_start(out=og_sb[:].rearrange("p (t d) -> p t d", d=1024),
                                             in_=og.rearrange("(t p) d -> p t d", p=128)),
               writes=[b_og], dma="og")
        p.emit("gpsimd", lambda e: e.dma_start(out=wb_sb[:].rearrange("p (t d) -> p t d", d=1024),
                                               in_=w_block.rearrange("(t p) d -> p t d", p=128)),
               writes=[b_wb], dma="wb")
        p.emit("sync", lambda e: e.dma_start(out=rs_sb[:], in_=rsc.rearrange("(q j) -> q j", j=32)),
               writes=[b_rs], dma="rs")
        p.emit("sync", lambda e: e.dma_start(out=rsig_sb[:], in_=rsc_sig[None, :]), writes=[b_rsig], dma="rs")
        p.emit("sync", lambda e: e.dma_start(out=bp_row[:], in_=b_proj[None, :]), writes=[b_bpr], dma="rs")
        for t in range(8):
            i = t % 2
            p.emit("gpsimd", lambda e, t=t, i=i: e.dma_start(out=wpn[:, i*1024:(i+1)*1024],
                                                             in_=w_proj[t*128:(t+1)*128, :]),
                   writes=[b_wpn], dma=f"wp{i}")
            for c in range(8):
                j = c % 2
                p.emit("tensor", lambda e, c=c, i=i, j=j: e.transpose(ps_t[j][:], wpn[:, i*1024 + c*128 : i*1024 + (c+1)*128], ident[:]),
                       reads=[b_wpn, b_id], writes=[b_pst[j]])
                p.emit("vector", lambda e, c=c, j=j, t=t: e.tensor_copy(wpT[:, c*1024 + t*128 : c*1024 + (t+1)*128], ps_t[j][:]),
                       reads=[b_pst[j]], writes=[b_wpT])
        for eh in range(2):
            p.emit("tensor", lambda e, eh=eh: e.matmul(ps_p[:, eh*512:(eh+1)*512], ones128[0:1, :],
                                                       bp_row[0:1, eh*512:(eh+1)*512], start=True, stop=True),
                   reads=[b_o128, b_bpr], writes=[b_psp])
        p.emit("vector", lambda e: e.tensor_copy(bpb[:], ps_p[:]), reads=[b_psp], writes=[b_bpb])

        p.emit("vector", lambda e: e.reduce_max(st4[:, 0:1], rs_sb[:], axis=AX.X, apply_absolute_value=True),
               reads=[b_rs], writes=[b_st4])
        p.emit("tensor", lambda e: e.transpose(ps_c[0:1, 0:128], st4[:, 0:1], identf[:]),
               reads=[b_st4, b_idf], writes=[b_psc])
        p.emit("vector", lambda e: e.tensor_copy(row_sc[:], ps_c[0:1, 0:128]), reads=[b_psc], writes=[b_rsc2])
        p.emit("vector", lambda e: e.reduce_max(row_sc[0:1, 0:1], row_sc[:], axis=AX.X),
               reads=[b_rsc2], writes=[b_rsc2])
        p.emit("tensor", lambda e: e.matmul(ps_c[:, 0:1], ones128[0:1, :], row_sc[0:1, 0:1], start=True, stop=True),
               reads=[b_o128, b_rsc2], writes=[b_psc])
        p.emit("vector", lambda e: e.tensor_copy(st4[:, 1:2], ps_c[:, 0:1]), reads=[b_psc], writes=[b_st4])
        for i in range(NIT):
            p.emit("vector", lambda e, i=i: e.tensor_scalar_mul(step_tab[:, i:i+1], st4[:, 1:2], 2.0 ** (-i)),
                   reads=[b_st4], writes=[b_stp])
        p.emit("vector", lambda e: e.memset(thr[:], 0.0), writes=[b_thr])
        for i in range(NIT):
            p.emit("vector", lambda e: e.tensor_scalar(cmp_sb[:], rs_sb[:], thr[:, 0:1], None, op0=AL.is_gt),
                   reads=[b_rs, b_thr], writes=[b_cmp])
            p.emit("vector", lambda e: e.reduce_sum(st4[:, 2:3], cmp_sb[:], axis=AX.X),
                   reads=[b_cmp], writes=[b_st4])
            p.emit("tensor", lambda e: e.matmul(ps_c[:, 1:2], ones128[:], st4[:, 2:3], start=True, stop=True),
                   reads=[b_o128, b_st4], writes=[b_psc])
            p.emit("vector", lambda e: e.tensor_scalar(st4[:, 3:4], ps_c[:, 1:2], float(-TOPK), None, op0=AL.add),
                   reads=[b_psc], writes=[b_st4])
            p.emit("scalar", lambda e: e.activation(sgn[:], st4[:, 3:4], AF.Sign),
                   reads=[b_st4], writes=[b_sgn])
            p.emit("vector", lambda e, i=i: e.scalar_tensor_tensor(thr[:], sgn[:], step_tab[:, i:i+1], thr[:],
                                                                   op0=AL.mult, op1=AL.add),
                   reads=[b_sgn, b_stp, b_thr], writes=[b_thr])

        for t in range(16):
            gg, hc = t // 8, t % 8
            p.emit("sync", lambda e, t=t: e.dma_start(out=xt[:], in_=x_sig[t*128:(t+1)*128, :]),
                   writes=[b_xt], dma="xs")
            emit_ln(p, xt[:], xn_f[:], st[:], sq[:], b_xt, b_xnf, b_st, b_sq, 1024)
            p.emit("scalar", lambda e: e.activation(xn_b[:], xn_f[:], AF.Copy), reads=[b_xnf], writes=[b_xnb])
            for c in range(8):
                j = c % 2
                p.emit("tensor", lambda e, c=c, j=j: e.transpose(ps_t[j][:], xn_b[:, c*128:(c+1)*128], ident[:]),
                       reads=[b_xnb, b_id], writes=[b_pst[j]])
                p.emit("vector", lambda e, c=c, j=j: e.tensor_copy(xnfm[:, c*128:(c+1)*128], ps_t[j][:]),
                       reads=[b_pst[j]], writes=[b_xnfm])
            for eh in range(2):
                for dc in range(8):
                    p.emit("tensor", lambda e, eh=eh, dc=dc:
                           e.matmul(ps_y[:, eh*512:(eh+1)*512], xnfm[:, dc*128:(dc+1)*128],
                                    wb_sb[:, dc*1024 + eh*512 : dc*1024 + (eh+1)*512],
                                    start=(dc == 0), stop=(dc == 7)),
                           reads=[b_xnfm, b_wb], writes=[b_psy])
            p.emit("scalar", lambda e: e.activation(y_sb[:], ps_y[:], AF.Gelu_apprx_tanh),
                   reads=[b_psy], writes=[b_y])
            for eh in range(2):
                for jc in range(8):
                    p.emit("tensor", lambda e, eh=eh, jc=jc, gg=gg, hc=hc:
                           e.matmul(ps_p[:, eh*512:(eh+1)*512],
                                    og_sb[:, (8*gg + jc)*1024 + hc*128 : (8*gg + jc)*1024 + (hc+1)*128],
                                    wpT[:, jc*1024 + eh*512 : jc*1024 + (eh+1)*512],
                                    start=(jc == 0), stop=(jc == 7)),
                           reads=[b_og, b_wpT], writes=[b_psp])
            p.emit("vector", lambda e: e.tensor_add(a3_sb[:], ps_p[:], bpb[:]), reads=[b_psp, b_bpb], writes=[b_a3])
            p.emit("tensor", lambda e, t=t: e.transpose(ps_c[0:128, 2:3], rsig_sb[0:1, t*128:(t+1)*128], identf[0:1, 0:1]),
                   reads=[b_rsig, b_idf], writes=[b_psc])
            p.emit("vector", lambda e: e.tensor_copy(mcol[:, 0:1], ps_c[0:128, 2:3]), reads=[b_psc], writes=[b_mc])
            p.emit("vector", lambda e: e.tensor_scalar(mcol[:, 1:2], mcol[:, 0:1], thr[:, 0:1], None, op0=AL.is_gt),
                   reads=[b_mc, b_thr], writes=[b_mc])
            p.emit("vector", lambda e: e.tensor_sub(d_sb[:], y_sb[:], xn_f[:]), reads=[b_y, b_xnf], writes=[b_d])
            p.emit("vector", lambda e: e.scalar_tensor_tensor(d_sb[:], d_sb[:], mcol[:, 1:2], xn_f[:],
                                                              op0=AL.mult, op1=AL.add),
                   reads=[b_d, b_mc, b_xnf], writes=[b_d])
            p.emit("vector", lambda e: e.tensor_add(x1_sb[:], xt[:], a3_sb[:]), reads=[b_xt, b_a3], writes=[b_x1])
            p.emit("vector", lambda e: e.tensor_add(x1_sb[:], x1_sb[:], d_sb[:]), reads=[b_x1, b_d], writes=[b_x1])
            p.emit("sync", lambda e, t=t: e.dma_start(out=x1_o[t*128:(t+1)*128, :], in_=x1_sb[:]),
                   reads=[b_x1], dma="x1")
        s_, c_ = p.dma_sems["x1"]
        p.barrier("sync", [Tok(s_, c_)])
        p.play(block)
    return nc


def build_b1():
    nc = bass.Bass("TRN2", target_bir_lowering=False)
    p = Prog(nc)
    T = 2048
    x1 = nc.dram_tensor("x1", [T, 1024], F32, kind="ExternalInput").ap()
    norm_w = nc.dram_tensor("norm_w", [1024], F32, kind="ExternalInput").ap()
    norm_b = nc.dram_tensor("norm_b", [1024], F32, kind="ExternalInput").ap()
    Wq = nc.dram_tensor("Wq", [1024, 512], F32, kind="ExternalInput").ap()
    Wkv = nc.dram_tensor("Wkv", [1024, 128], F32, kind="ExternalInput").ap()
    xn1_tm_o = nc.dram_tensor("xn1_tm_o", [T, 1024], BF16, kind="ExternalOutput").ap()
    xn1_fm_o = nc.dram_tensor("xn1_fm_o", [128, 8 * T], BF16, kind="ExternalOutput").ap()
    q_o = nc.dram_tensor("q_o", [64, 8 * T], BF16, kind="ExternalOutput").ap()
    k_o = nc.dram_tensor("k_o", [64, T], BF16, kind="ExternalOutput").ap()
    v_o = nc.dram_tensor("v_o", [T, 64], BF16, kind="ExternalOutput").ap()
    ctx = ExitStack()
    with ctx:
        p.setup(ctx)
        ident, b_id = sbuf(nc, ctx, "ident", [128, 128], BF16)
        ones1, b_on = sbuf(nc, ctx, "ones1", [1, 128])
        nw_row, b_nwr = sbuf(nc, ctx, "nw_row", [1, 1024])
        nb_row, b_nbr = sbuf(nc, ctx, "nb_row", [1, 1024])
        nwb, b_nwb = sbuf(nc, ctx, "nwb", [128, 1024])
        nbb, b_nbb = sbuf(nc, ctx, "nbb", [128, 1024])
        wq_sb, b_wq = sbuf(nc, ctx, "wq_sb", [128, 8 * 512], BF16)
        wkv_sb, b_wkv = sbuf(nc, ctx, "wkv_sb", [128, 8 * 128], BF16)
        xt, b_xt = sbuf(nc, ctx, "xt", [128, 1024])
        st, b_st = sbuf(nc, ctx, "st", [128, 8])
        sq, b_sq = sbuf(nc, ctx, "sq", [128, 1024])
        xn1_f, b_xn1f = sbuf(nc, ctx, "xn1_f", [128, 1024])
        xn1_b, b_xn1b = sbuf(nc, ctx, "xn1_b", [128, 1024], BF16)
        xn1n, b_xn1n = sbuf(nc, ctx, "xn1n", [128, 1024])
        xa_b, b_xab = sbuf(nc, ctx, "xa_b", [128, 1024], BF16)
        xa_fm, b_xafm = sbuf(nc, ctx, "xa_fm", [128, 8 * T], BF16)
        xn1_fm, b_xn1fm = sbuf(nc, ctx, "xn1_fm", [128, 8 * T], BF16)
        ev_sb, b_ev = sbuf(nc, ctx, "ev_sb", [128, 512], BF16)
        v_sb, b_vsb = sbuf(nc, ctx, "v_sb", [128, 64], BF16)
        ps_m = []; b_psm = []
        for i in range(2):
            t = ctx.enter_context(nc.psum_tensor(f"ps_m{i}", [128, 512], F32))
            ps_m.append(t); b_psm.append(Buf(f"ps_m{i}"))
        ps_t = []; b_pst = []
        for i in range(2):
            t = ctx.enter_context(nc.psum_tensor(f"ps_t{i}", [128, 128], BF16))
            ps_t.append(t); b_pst.append(Buf(f"ps_t{i}"))
        ps_b = ctx.enter_context(nc.psum_tensor("ps_b", [128, 1024], F32)); b_psb = Buf("ps_b")
        block = ctx.enter_context(nc.Block())

        emit_identity(nc, p, b_id, ident[:])
        p.emit("vector", lambda e: e.memset(ones1[:], 1.0), writes=[b_on])
        p.emit("sync", lambda e: e.dma_start(out=nw_row[:], in_=norm_w[None, :]), writes=[b_nwr], dma="n")
        p.emit("sync", lambda e: e.dma_start(out=nb_row[:], in_=norm_b[None, :]), writes=[b_nbr], dma="n2")
        p.emit("gpsimd", lambda e: e.dma_start(out=wq_sb[:].rearrange("p (t d) -> p t d", d=512),
                                               in_=Wq.rearrange("(t p) d -> p t d", p=128)),
               writes=[b_wq], dma="wq")
        p.emit("gpsimd", lambda e: e.dma_start(out=wkv_sb[:].rearrange("p (t d) -> p t d", d=128),
                                               in_=Wkv.rearrange("(t p) d -> p t d", p=128)),
               writes=[b_wkv], dma="wkv")
        for eh in range(2):
            p.emit("tensor", lambda e, eh=eh: e.matmul(ps_b[:, eh*512:(eh+1)*512], ones1[:],
                                                       nw_row[0:1, eh*512:(eh+1)*512], start=True, stop=True),
                   reads=[b_on, b_nwr], writes=[b_psb])
        p.emit("vector", lambda e: e.tensor_copy(nwb[:], ps_b[:]), reads=[b_psb], writes=[b_nwb])
        for eh in range(2):
            p.emit("tensor", lambda e, eh=eh: e.matmul(ps_b[:, eh*512:(eh+1)*512], ones1[:],
                                                       nb_row[0:1, eh*512:(eh+1)*512], start=True, stop=True),
                   reads=[b_on, b_nbr], writes=[b_psb])
        p.emit("vector", lambda e: e.tensor_copy(nbb[:], ps_b[:]), reads=[b_psb], writes=[b_nbb])

        for t in range(16):
            p.emit("sync", lambda e, t=t: e.dma_start(out=xt[:], in_=x1[t*128:(t+1)*128, :]),
                   writes=[b_xt], dma="x")
            emit_ln(p, xt[:], xn1_f[:], st[:], sq[:], b_xt, b_xn1f, b_st, b_sq, 1024)
            p.emit("scalar", lambda e: e.activation(xn1_b[:], xn1_f[:], AF.Copy), reads=[b_xn1f], writes=[b_xn1b])
            p.emit("sync", lambda e, t=t: e.dma_start(out=xn1_tm_o[t*128:(t+1)*128, :], in_=xn1_b[:]),
                   reads=[b_xn1b], dma="xo")
            emit_ln(p, xn1_f[:], xn1n[:], st[:], sq[:], b_xn1f, b_xn1n, b_st, b_sq, 1024)
            p.emit("vector", lambda e: e.tensor_mul(xn1n[:], xn1n[:], nwb[:]), reads=[b_xn1n, b_nwb], writes=[b_xn1n])
            p.emit("vector", lambda e: e.tensor_add(xa_b[:], xn1n[:], nbb[:]), reads=[b_xn1n, b_nbb], writes=[b_xab])
            for c in range(8):
                j = c % 2
                p.emit("tensor", lambda e, c=c, j=j: e.transpose(ps_t[j][:], xa_b[:, c*128:(c+1)*128], ident[:]),
                       reads=[b_xab, b_id], writes=[b_pst[j]])
                p.emit("vector", lambda e, c=c, j=j, t=t: e.tensor_copy(xa_fm[:, c*T + t*128 : c*T + (t+1)*128], ps_t[j][:]),
                       reads=[b_pst[j]], writes=[b_xafm])
            for c in range(8):
                j = c % 2
                p.emit("tensor", lambda e, c=c, j=j: e.transpose(ps_t[j][:], xn1_b[:, c*128:(c+1)*128], ident[:]),
                       reads=[b_xn1b, b_id], writes=[b_pst[j]])
                p.emit("vector", lambda e, c=c, j=j, t=t: e.tensor_copy(xn1_fm[:, c*T + t*128 : c*T + (t+1)*128], ps_t[j][:]),
                       reads=[b_pst[j]], writes=[b_xn1fm])
        p.emit("sync", lambda e: e.dma_start(out=xn1_fm_o, in_=xn1_fm[:]), reads=[b_xn1fm], dma="xo")

        for ic in range(4):
            for sc in range(4):
                j = (ic * 4 + sc) % 2
                for dc in range(8):
                    p.emit("tensor", lambda e, ic=ic, sc=sc, dc=dc, j=j:
                           e.matmul(ps_m[j][:], wq_sb[:, dc*512 + ic*128 : dc*512 + (ic+1)*128],
                                    xa_fm[:, dc*T + sc*512 : dc*T + (sc+1)*512],
                                    start=(dc == 0), stop=(dc == 7)),
                           reads=[b_wq, b_xafm], writes=[b_psm[j]])
                for hh in range(2):
                    p.emit("vector", lambda e, hh=hh, j=j: e.tensor_copy(ev_sb[0:64, :], ps_m[j][hh*64:hh*64+64, :]),
                           reads=[b_psm[j]], writes=[b_ev])
                    p.emit("sync", lambda e, ic=ic, sc=sc, hh=hh:
                           e.dma_start(out=q_o[:, (2*ic+hh)*T + sc*512 : (2*ic+hh)*T + (sc+1)*512], in_=ev_sb[0:64, :]),
                           reads=[b_ev], dma="qo")
        for sc in range(4):
            j = sc % 2
            for dc in range(8):
                p.emit("tensor", lambda e, sc=sc, dc=dc, j=j:
                       e.matmul(ps_m[j][0:64, :], wkv_sb[:, dc*128 : dc*128 + 64],
                                xa_fm[:, dc*T + sc*512 : dc*T + (sc+1)*512],
                                start=(dc == 0), stop=(dc == 7)),
                       reads=[b_wkv, b_xafm], writes=[b_psm[j]])
            p.emit("vector", lambda e, j=j: e.tensor_copy(ev_sb[0:64, :], ps_m[j][0:64, :]),
                   reads=[b_psm[j]], writes=[b_ev])
            p.emit("sync", lambda e, sc=sc: e.dma_start(out=k_o[:, sc*512:(sc+1)*512], in_=ev_sb[0:64, :]),
                   reads=[b_ev], dma="ko")
        for t in range(16):
            j = t % 2
            for dc in range(8):
                p.emit("tensor", lambda e, t=t, dc=dc, j=j:
                       e.matmul(ps_m[j][:, 0:64], xa_fm[:, dc*T + t*128 : dc*T + (t+1)*128],
                                wkv_sb[:, dc*128 + 64 : dc*128 + 128],
                                start=(dc == 0), stop=(dc == 7)),
                       reads=[b_wkv, b_xafm], writes=[b_psm[j]])
            p.emit("vector", lambda e, j=j: e.tensor_copy(v_sb[:], ps_m[j][:, 0:64]),
                   reads=[b_psm[j]], writes=[b_vsb])
            p.emit("sync", lambda e, t=t: e.dma_start(out=v_o[t*128:(t+1)*128, :], in_=v_sb[:]),
                   reads=[b_vsb], dma="vo")
        fin = []
        for n in ("xo", "qo", "ko", "vo"):
            s_, c_ = p.dma_sems[n]
            fin.append(Tok(s_, c_))
        p.barrier("sync", fin)
        p.play(block)
    return nc


def build_b2(debug=False):
    nc = bass.Bass("TRN2", target_bir_lowering=False)
    p = Prog(nc)
    T, S = 2048, 4096
    q = nc.dram_tensor("q", [64, 8 * T], BF16, kind="ExternalInput").ap()
    k = nc.dram_tensor("k", [64, S], BF16, kind="ExternalInput").ap()
    v = nc.dram_tensor("v", [S, 64], BF16, kind="ExternalInput").ap()
    xn1_fm = nc.dram_tensor("xn1_fm", [128, 8 * T], BF16, kind="ExternalInput").ap()
    x1 = nc.dram_tensor("x1", [T, 1024], F32, kind="ExternalInput").ap()
    Wo = nc.dram_tensor("Wo", [512, 1024], F32, kind="ExternalInput").ap()
    onw = nc.dram_tensor("onw", [1024], F32, kind="ExternalInput").ap()
    onb = nc.dram_tensor("onb", [1024], F32, kind="ExternalInput").ap()
    gate_w = nc.dram_tensor("gate_w", [8, 1024], F32, kind="ExternalInput").ap()
    gate_b = nc.dram_tensor("gate_b", [8], F32, kind="ExternalInput").ap()
    expert_bias = nc.dram_tensor("expert_bias", [8], F32, kind="ExternalInput").ap()
    w1 = nc.dram_tensor("w1", [8, 1024, 4096], F32, kind="ExternalInput").ap()
    b1 = nc.dram_tensor("b1", [8, 4096], F32, kind="ExternalInput").ap()
    w2 = nc.dram_tensor("w2", [8, 4096, 1024], F32, kind="ExternalInput").ap()
    b2 = nc.dram_tensor("b2", [8, 1024], F32, kind="ExternalInput").ap()
    fin_o = nc.dram_tensor("fin_o", [T, 1024], F32, kind="ExternalOutput").ap()
    attn1_dr = nc.dram_tensor("attn1_dr", [T, 1024], BF16,
                              kind="ExternalOutput" if debug else "Internal").ap()
    if debug:
        moe_o = nc.dram_tensor("moe_o", [128, 16 * 1024], BF16, kind="ExternalOutput").ap()
        oh_o = nc.dram_tensor("oh_o", [128, 16 * 8], F32, kind="ExternalOutput").ap()
        o1_o = nc.dram_tensor("o1_o", [128, 4 * 2048], BF16, kind="ExternalOutput").ap()
    ctx = ExitStack()
    with ctx:
        p.setup(ctx)
        ident, b_id = sbuf(nc, ctx, "ident", [128, 128], BF16)
        identf, b_idf = sbuf(nc, ctx, "identf", [128, 128])
        ones1, b_on = sbuf(nc, ctx, "ones1", [1, 128])
        oh_tm, b_oht = sbuf(nc, ctx, "oh_tm", [128, 16 * 8])
        oh_em, b_ohe = sbuf(nc, ctx, "oh_em", [8, 16 * 128], BF16)
        moe_sb, b_moe = sbuf(nc, ctx, "moe_sb", [128, 16 * 1024], BF16)
        block = ctx.enter_context(nc.Block())
        emit_identity(nc, p, b_id, ident[:])
        p.emit("gpsimd", lambda g: g.memset(identf[:], 0.0), writes=[b_idf])
        p.emit("gpsimd", lambda g: g.affine_select(out=identf[:], in_=identf[:],
            compare_op=AL.not_equal, fill=1.0, base=0, pattern=[[-1, 128]], channel_multiplier=1),
            reads=[b_idf], writes=[b_idf])
        p.emit("vector", lambda e: e.memset(ones1[:], 1.0), writes=[b_on])

        with ExitStack() as c1:
            q_sb, b_q = sbuf(nc, c1, "q_sb", [64, 8 * T], BF16)
            k_sb, b_k = sbuf(nc, c1, "k_sb", [64, S], BF16)
            v_sb, b_v = sbuf(nc, c1, "v_sb", [128, 32 * 64], BF16)
            wo_sb, b_wo = sbuf(nc, c1, "wo_sb", [128, 4 * 1024], BF16)
            onwb, b_onwb = sbuf(nc, c1, "onwb", [128, 1024])
            onbb, b_onbb = sbuf(nc, c1, "onbb", [128, 1024])
            row_sb, b_row = sbuf(nc, c1, "row_sb", [1, 1024])
            sc_sb, b_sc = sbuf(nc, c1, "sc_sb", [128, S])
            pr_sb, b_pr = sbuf(nc, c1, "pr_sb", [128, S], BF16)
            pt_sb, b_pt = sbuf(nc, c1, "pt_sb", [128, 32 * 512], BF16)
            st, b_st = sbuf(nc, c1, "st", [128, 8])
            o1_sb, b_o1 = sbuf(nc, c1, "o1_sb", [128, 4 * T], BF16)
            z_sb, b_z = sbuf(nc, c1, "z_sb", [128, 1024])
            sq, b_sq = sbuf(nc, c1, "sq", [128, 1024])
            a1_sb, b_a1 = sbuf(nc, c1, "a1_sb", [128, 1024], BF16)
            ps_s = []; b_ps = []
            for i in range(2):
                t = c1.enter_context(nc.psum_tensor(f"ps_s{i}", [128, 512], F32))
                ps_s.append(t); b_ps.append(Buf(f"ps_s{i}"))
            ps_t = []; b_pst = []
            for i in range(2):
                t = c1.enter_context(nc.psum_tensor(f"ps_t{i}", [128, 128], BF16))
                ps_t.append(t); b_pst.append(Buf(f"ps_t{i}"))
            ps_o = c1.enter_context(nc.psum_tensor("ps_o", [128, 512], F32)); b_pso = Buf("ps_o")
            ps_z = c1.enter_context(nc.psum_tensor("ps_z", [128, 1024], F32)); b_psz = Buf("ps_z")

            p.emit("sync", lambda e: e.dma_start(out=q_sb[:], in_=q), writes=[b_q], dma="q")
            p.emit("sync", lambda e: e.dma_start(out=k_sb[:], in_=k), writes=[b_k], dma="kk")
            p.emit("sync", lambda e: e.dma_start(out=v_sb[:].rearrange("p (t d) -> p t d", d=64),
                                                 in_=v.rearrange("(t p) d -> p t d", p=128)),
                   writes=[b_v], dma="vv")
            p.emit("gpsimd", lambda e: e.dma_start(out=wo_sb[:].rearrange("p (t d) -> p t d", d=1024),
                                                   in_=Wo.rearrange("(t p) d -> p t d", p=128)),
                   writes=[b_wo], dma="wo")
            for nm, row_b, dst, b_dst in (("onw", onw, onwb, b_onwb), ("onb", onb, onbb, b_onbb)):
                p.emit("sync", lambda e, row_b=row_b: e.dma_start(out=row_sb[:], in_=row_b[None, :]),
                       writes=[b_row], dma="nr")
                for eh in range(2):
                    p.emit("tensor", lambda e, eh=eh: e.matmul(ps_z[:, eh*512:(eh+1)*512], ones1[:],
                                                               row_sb[0:1, eh*512:(eh+1)*512], start=True, stop=True),
                           reads=[b_on, b_row], writes=[b_psz])
                p.emit("vector", lambda e, dst=dst: e.tensor_copy(dst[:], ps_z[:]), reads=[b_psz], writes=[b_dst])

            for h in range(8):
                hb = 64 * (h % 2)
                for qg in range(4):
                    for q4 in range(4):
                        qt = qg * 4 + q4
                        for kc in range(8):
                            j = kc % 2
                            p.emit("tensor", lambda e, h=h, hb=hb, qt=qt, kc=kc, j=j:
                                   e.matmul(ps_s[j][:], q_sb[:, h*T + qt*128 : h*T + (qt+1)*128],
                                            k_sb[:, kc*512:(kc+1)*512], start=True, stop=True),
                                   reads=[b_q, b_k], writes=[b_ps[j]])
                            p.emit("vector", lambda e, kc=kc, j=j: e.tensor_copy(sc_sb[:, kc*512:(kc+1)*512], ps_s[j][:]),
                                   reads=[b_ps[j]], writes=[b_sc])
                        p.emit("vector", lambda e: e.reduce_max(st[:, 0:1], sc_sb[:], axis=AX.X, negate=True),
                               reads=[b_sc], writes=[b_st])
                        p.emit("vector", lambda e: e.tensor_scalar_mul(st[:, 1:2], st[:, 0:1], 0.125),
                               reads=[b_st], writes=[b_st])
                        p.emit("scalar", lambda e: e.activation(pr_sb[:], sc_sb[:], AF.Exp, bias=st[:, 1:2], scale=0.125,
                                                                accum_out=st[:, 2:3]),
                               reads=[b_sc, b_st], writes=[b_pr, b_st])
                        p.emit("vector", lambda e: e.reciprocal(st[:, 3:4], st[:, 2:3]), reads=[b_st], writes=[b_st])
                        p.emit("vector", lambda e: e.tensor_scalar(pr_sb[:], pr_sb[:], st[:, 3:4], None, op0=AL.mult),
                               reads=[b_pr, b_st], writes=[b_pr])
                        for kt in range(32):
                            j = kt % 2
                            p.emit("tensor", lambda e, kt=kt, j=j: e.transpose(ps_t[j][:], pr_sb[:, kt*128:(kt+1)*128], ident[:]),
                                   reads=[b_pr, b_id], writes=[b_pst[j]])
                            p.emit("vector", lambda e, kt=kt, j=j, q4=q4: e.tensor_copy(pt_sb[:, kt*512 + q4*128 : kt*512 + (q4+1)*128], ps_t[j][:]),
                                   reads=[b_pst[j]], writes=[b_pt])
                    for kt in range(32):
                        p.emit("tensor", lambda e, kt=kt:
                               e.matmul(ps_o[0:64, :], v_sb[:, kt*64:(kt+1)*64], pt_sb[:, kt*512:(kt+1)*512],
                                        start=(kt == 0), stop=(kt == 31)),
                               reads=[b_v, b_pt], writes=[b_pso])
                    p.emit("vector", lambda e, h=h, hb=hb, qg=qg:
                           e.tensor_copy(o1_sb[hb:hb+64, (h//2)*T + qg*512 : (h//2)*T + (qg+1)*512], ps_o[0:64, :]),
                           reads=[b_pso], writes=[b_o1])
            if debug:
                p.emit("sync", lambda e: e.dma_start(out=o1_o, in_=o1_sb[:]), reads=[b_o1], dma="dbg")
            for qt in range(16):
                for eh in range(2):
                    for ic in range(4):
                        p.emit("tensor", lambda e, qt=qt, eh=eh, ic=ic:
                               e.matmul(ps_z[:, eh*512:(eh+1)*512], o1_sb[:, ic*T + qt*128 : ic*T + (qt+1)*128],
                                        wo_sb[:, ic*1024 + eh*512 : ic*1024 + (eh+1)*512],
                                        start=(ic == 0), stop=(ic == 3)),
                               reads=[b_o1, b_wo], writes=[b_psz])
                p.emit("vector", lambda e: e.tensor_copy(z_sb[:], ps_z[:]), reads=[b_psz], writes=[b_z])
                emit_ln(p, z_sb[:], z_sb[:], st[:], sq[:], b_z, b_z, b_st, b_sq, 1024)
                p.emit("vector", lambda e: e.tensor_mul(z_sb[:], z_sb[:], onwb[:]), reads=[b_z, b_onwb], writes=[b_z])
                p.emit("vector", lambda e: e.tensor_add(a1_sb[:], z_sb[:], onbb[:]), reads=[b_z, b_onbb], writes=[b_a1])
                p.emit("sync", lambda e, qt=qt: e.dma_start(out=attn1_dr[qt*128:(qt+1)*128, :], in_=a1_sb[:]),
                       reads=[b_a1], dma="a1")
            p.full_barrier()

        with ExitStack() as c2:
            T2 = T
            xn_str = []; b_xns = []
            for i in range(2):
                t, b = sbuf(nc, c2, f"xn_str{i}", [128, 8 * 256], BF16); xn_str.append(t); b_xns.append(b)
            ps_g = c2.enter_context(nc.psum_tensor("ps_g", [128, 512], F32)); b_psg = Buf("ps_g")
            ps_h = []; b_psh = []
            for i in range(2):
                t = c2.enter_context(nc.psum_tensor(f"ps_h{i}", [128, 256], F32))
                ps_h.append(t); b_psh.append(Buf(f"ps_h{i}"))
            ps_y = []; b_psy = []
            for i in range(2):
                t = c2.enter_context(nc.psum_tensor(f"ps_y{i}", [128, 1024], F32))
                ps_y.append(t); b_psy.append(Buf(f"ps_y{i}"))

            with ExitStack() as c2g:
                gw_row, b_gwr = sbuf(nc, c2g, "gw_row", [8, 1024])
                gwT, b_gw = sbuf(nc, c2g, "gwT", [128, 64], BF16)
                gb_col, b_gb = sbuf(nc, c2g, "gb_col", [8, 2])
                lg_sb, b_lg = sbuf(nc, c2g, "lg_sb", [8, T])
                stm, b_stm = sbuf(nc, c2g, "stm", [128, 4])
                p.emit("sync", lambda e: e.dma_start(out=gw_row[:], in_=gate_w), writes=[b_gwr], dma="gw")
                for dc in range(8):
                    j = dc % 2
                    p.emit("tensor", lambda e, dc=dc, j=j: e.transpose(ps_h[j][0:128, 0:8], gw_row[0:8, dc*128:(dc+1)*128], identf[0:8, 0:8]),
                           reads=[b_gwr, b_idf], writes=[b_psh[j]])
                    p.emit("vector", lambda e, dc=dc, j=j: e.tensor_copy(gwT[:, dc*8:(dc+1)*8], ps_h[j][0:128, 0:8]),
                           reads=[b_psh[j]], writes=[b_gw])
                p.emit("sync", lambda e: e.dma_start(out=gb_col[:, 0:1], in_=gate_b[:, None]), writes=[b_gb], dma="gb")
                p.emit("sync", lambda e: e.dma_start(out=gb_col[:, 1:2], in_=expert_bias[:, None]), writes=[b_gb], dma="gb2")
                p.emit("vector", lambda e: e.tensor_add(gb_col[:, 0:1], gb_col[:, 0:1], gb_col[:, 1:2]),
                       reads=[b_gb], writes=[b_gb])
                for sc in range(4):
                    i = sc % 2
                    p.emit("sync", lambda e, sc=sc, i=i: e.dma_start(
                            out=xn_str[i][:].rearrange("p (dc s) -> p dc s", s=512),
                            in_=xn1_fm.rearrange("p (dc s) -> p dc s", s=T)[:, :, sc*512:(sc+1)*512]),
                           writes=[b_xns[i]], dma=f"xs{i}")
                    for dc in range(8):
                        p.emit("tensor", lambda e, sc=sc, dc=dc, i=i:
                               e.matmul(ps_g[0:8, :], gwT[:, dc*8:(dc+1)*8], xn_str[i][:, dc*512:(dc+1)*512],
                                        start=(dc == 0), stop=(dc == 7)),
                               reads=[b_gw, b_xns[i]], writes=[b_psg])
                    p.emit("vector", lambda e, sc=sc: e.tensor_scalar(lg_sb[0:8, sc*512:(sc+1)*512], ps_g[0:8, :],
                                                                      gb_col[0:8, 0:1], None, op0=AL.add),
                           reads=[b_psg, b_gb], writes=[b_lg])
                for t in range(16):
                    j = t % 2
                    p.emit("tensor", lambda e, t=t, j=j: e.transpose(ps_h[j][0:128, 0:8], lg_sb[0:8, t*128:(t+1)*128], identf[0:8, 0:8]),
                           reads=[b_lg, b_idf], writes=[b_psh[j]])
                    p.emit("vector", lambda e, t=t, j=j: e.tensor_copy(oh_tm[:, t*8:(t+1)*8], ps_h[j][0:128, 0:8]),
                           reads=[b_psh[j]], writes=[b_oht])
                    p.emit("vector", lambda e, t=t: e.reduce_max(stm[:, 0:1], oh_tm[:, t*8:(t+1)*8], axis=AX.X),
                           reads=[b_oht], writes=[b_stm])
                    p.emit("vector", lambda e, t=t: e.tensor_scalar(oh_tm[:, t*8:(t+1)*8], oh_tm[:, t*8:(t+1)*8],
                                                                    stm[:, 0:1], None, op0=AL.is_ge),
                           reads=[b_oht, b_stm], writes=[b_oht])
                    p.emit("tensor", lambda e, t=t, j=j: e.transpose(ps_h[j][0:8, 0:128], oh_tm[:, t*8:(t+1)*8], identf[:]),
                           reads=[b_oht, b_idf], writes=[b_psh[j]])
                    p.emit("vector", lambda e, t=t, j=j: e.tensor_copy(oh_em[0:8, t*128:(t+1)*128], ps_h[j][0:8, 0:128]),
                           reads=[b_psh[j]], writes=[b_ohe])
                p.full_barrier()

            w1e, b_w1e = sbuf(nc, c2, "w1e", [128, 8 * 4096], BF16)
            w2e, b_w2e = sbuf(nc, c2, "w2e", [128, 32 * 1024], BF16)
            b1row, b_b1r = sbuf(nc, c2, "b1row", [1, 4096], BF16)
            onesr, b_onr = sbuf(nc, c2, "onesr", [1, 512], BF16)
            h_sb, b_h = sbuf(nc, c2, "h_sb", [128, 32 * 256], BF16)
            p.emit("vector", lambda e: e.memset(onesr[:], 1.0), writes=[b_onr])
            p.emit("vector", lambda e: e.memset(moe_sb[:], 0.0), writes=[b_moe])
            for ex in range(8):
                p.emit("gpsimd", lambda e, ex=ex: e.dma_start(
                        out=w1e[:].rearrange("p (t d) -> p t d", d=4096),
                        in_=w1[ex].rearrange("(t p) d -> p t d", p=128)),
                       writes=[b_w1e], dma="w1")
                p.emit("gpsimd", lambda e, ex=ex: e.dma_start(
                        out=w2e[:].rearrange("p (t d) -> p t d", d=1024),
                        in_=w2[ex].rearrange("(t p) d -> p t d", p=128)),
                       writes=[b_w2e], dma="w2")
                p.emit("gpsimd", lambda e, ex=ex: e.dma_start(out=b1row[:], in_=b1[ex][None, :]),
                       writes=[b_b1r], dma="b1")
                for sb in range(8):
                    i = sb % 2
                    p.emit("sync", lambda e, sb=sb, i=i: e.dma_start(
                            out=xn_str[i][:, 0:8*256].rearrange("p (dc s) -> p dc s", s=256),
                            in_=xn1_fm.rearrange("p (dc s) -> p dc s", s=T)[:, :, sb*256:(sb+1)*256]),
                           writes=[b_xns[i]], dma=f"xs{i}")
                    for fc in range(32):
                        j = fc % 2
                        for dc in range(8):
                            p.emit("tensor", lambda e, sb=sb, fc=fc, dc=dc, j=j, i=i:
                                   e.matmul(ps_h[j][:], w1e[:, dc*4096 + fc*128 : dc*4096 + (fc+1)*128],
                                            xn_str[i][:, dc*256:(dc+1)*256],
                                            start=(dc == 0), stop=False),
                                   reads=[b_w1e, b_xns[i]], writes=[b_psh[j]])
                        p.emit("tensor", lambda e, fc=fc, j=j:
                               e.matmul(ps_h[j][:], b1row[0:1, fc*128:(fc+1)*128], onesr[0:1, 0:256],
                                        start=False, stop=True),
                               reads=[b_b1r, b_onr], writes=[b_psh[j]])
                        p.emit("scalar", lambda e, fc=fc, j=j: e.activation(h_sb[:, fc*256:(fc+1)*256], ps_h[j][:], AF.Relu),
                               reads=[b_psh[j]], writes=[b_h])
                    for st2 in range(2):
                        for fc in range(32):
                            for eh in range(2):
                                p.emit("tensor", lambda e, sb=sb, st2=st2, fc=fc, eh=eh:
                                       e.matmul(ps_y[st2][:, eh*512:(eh+1)*512],
                                                h_sb[:, fc*256 + st2*128 : fc*256 + (st2+1)*128],
                                                w2e[:, fc*1024 + eh*512 : fc*1024 + (eh+1)*512],
                                                start=(fc == 0), stop=(fc == 31)),
                                       reads=[b_h, b_w2e], writes=[b_psy[st2]])
                        tt = sb * 2 + st2
                        p.emit("vector", lambda e, tt=tt, st2=st2, ex=ex:
                               e.scalar_tensor_tensor(moe_sb[:, tt*1024:(tt+1)*1024], ps_y[st2][:],
                                                      oh_tm[:, tt*8 + ex : tt*8 + ex + 1],
                                                      moe_sb[:, tt*1024:(tt+1)*1024], op0=AL.mult, op1=AL.add),
                               reads=[b_psy[st2], b_oht, b_moe], writes=[b_moe])
            p.full_barrier()

        if debug:
            p.emit("sync", lambda e: e.dma_start(out=moe_o, in_=moe_sb[:]), reads=[b_moe], dma="dbg")
            p.emit("sync", lambda e: e.dma_start(out=oh_o, in_=oh_tm[:]), reads=[b_oht], dma="dbg")
        with ExitStack() as c3:
            b2_sb, b_b2 = sbuf(nc, c3, "b2_sb", [8, 1024], BF16)
            xt, b_xt = sbuf(nc, c3, "xt", [128, 1024])
            a1t, b_a1t = sbuf(nc, c3, "a1t", [128, 1024], BF16)
            f_sb, b_f = sbuf(nc, c3, "f_sb", [128, 1024])
            ps_f = c3.enter_context(nc.psum_tensor("ps_f", [128, 1024], F32)); b_psf = Buf("ps_f")
            p.emit("gpsimd", lambda e: e.dma_start(out=b2_sb[:], in_=b2), writes=[b_b2], dma="b2")
            for t in range(16):
                p.emit("sync", lambda e, t=t: e.dma_start(out=xt[:], in_=x1[t*128:(t+1)*128, :]),
                       writes=[b_xt], dma="x1")
                p.emit("sync", lambda e, t=t: e.dma_start(out=a1t[:], in_=attn1_dr[t*128:(t+1)*128, :]),
                       writes=[b_a1t], dma="a1l")
                for eh in range(2):
                    p.emit("tensor", lambda e, t=t, eh=eh:
                           e.matmul(ps_f[:, eh*512:(eh+1)*512], oh_em[0:8, t*128:(t+1)*128],
                                    b2_sb[0:8, eh*512:(eh+1)*512], start=True, stop=True),
                           reads=[b_ohe, b_b2], writes=[b_psf])
                p.emit("vector", lambda e: e.tensor_add(f_sb[:], xt[:], ps_f[:]), reads=[b_xt, b_psf], writes=[b_f])
                p.emit("vector", lambda e: e.tensor_add(f_sb[:], f_sb[:], a1t[:]), reads=[b_f, b_a1t], writes=[b_f])
                p.emit("vector", lambda e, t=t: e.tensor_add(f_sb[:], f_sb[:], moe_sb[:, t*1024:(t+1)*1024]),
                       reads=[b_f, b_moe], writes=[b_f])
                p.emit("sync", lambda e, t=t: e.dma_start(out=fin_o[t*128:(t+1)*128, :], in_=f_sb[:]),
                       reads=[b_f], dma="fo")
            s_, c_ = p.dma_sems["fo"]
            p.barrier("sync", [Tok(s_, c_)])
        p.play(block)
    return nc


def build_fused():
    nc = bass.Bass("TRN2", target_bir_lowering=False)
    _uniquify(nc)
    p = Prog(nc)
    S, T, NIT, TOPK = 4096, 2048, 36, 3072
    def ein(name, shape, dt=F32):
        return nc.dram_tensor(name, shape, dt, kind="ExternalInput").ap()
    x = ein("x", [S, 1024])
    w_qkv = ein("w_qkv", [3072, 1024]); w_router = ein("w_router", [1024])
    w_proj = ein("w_proj", [1024, 1024]); b_proj = ein("b_proj", [1024])
    w_block = ein("w_block", [1024, 1024])
    norm_w = ein("norm_w", [1024]); norm_b = ein("norm_b", [1024])
    Wq = ein("Wq", [1024, 512]); Wkv = ein("Wkv", [1024, 128]); Wo = ein("Wo", [512, 1024])
    onw = ein("onw", [1024]); onb = ein("onb", [1024])
    gate_w = ein("gate_w", [8, 1024]); gate_b = ein("gate_b", [8]); expert_bias = ein("expert_bias", [8])
    w1 = ein("w1", [8, 1024, 4096]); b1 = ein("b1", [8, 4096])
    w2 = ein("w2", [8, 4096, 1024]); b2 = ein("b2", [8, 1024])
    idx_xn = ein("idx_xn", [128, 8], I32)
    idx_q = ein("idx_q", [64, 1], I32)
    idx_x1 = ein("idx_x1", [128, 16], I32)
    fin_o = nc.dram_tensor("fin_o", [T, 1024], F32, kind="ExternalOutput").ap()
    q_dr = nc.dram_tensor("q_dr", [128, 8 * S], BF16).ap()
    k_dr = nc.dram_tensor("k_dr", [128, 8 * S], BF16).ap()
    v_dr = nc.dram_tensor("v_dr", [S, 1024], BF16).ap()
    rsc_dr = nc.dram_tensor("rsc_dr", [4096], F32).ap()
    out_dr = nc.dram_tensor("out_dr", [S, 1024], BF16).ap()
    x1_dr = nc.dram_tensor("x1_dr", [S, 1024], F32).ap()
    xn1_dr = nc.dram_tensor("xn1_dr", [128, 16 * 2048], BF16).ap()
    q1_dr = nc.dram_tensor("q1_dr", [64, 2 * 8 * 2048], BF16).ap()
    attn1_dr = nc.dram_tensor("attn1_dr", [T, 1024], BF16).ap()

    ctx = ExitStack()
    with ctx:
        p.setup(ctx)
        ident, b_id = sbuf(nc, ctx, "ident", [128, 128], BF16)
        identf, b_idf = sbuf(nc, ctx, "identf", [128, 128])
        ones1, b_on = sbuf(nc, ctx, "ones1", [1, 128])
        block = ctx.enter_context(nc.Block())
        emit_identity(nc, p, b_id, ident[:])
        p.emit("gpsimd", lambda g: g.memset(identf[:], 0.0), writes=[b_idf])
        p.emit("gpsimd", lambda g: g.affine_select(out=identf[:], in_=identf[:],
            compare_op=AL.not_equal, fill=1.0, base=0, pattern=[[-1, 128]], channel_multiplier=1),
            reads=[b_idf], writes=[b_idf])
        p.emit("vector", lambda e: e.memset(ones1[:], 1.0), writes=[b_on])

        def _phase0():
            with ExitStack() as c:
                xt = []; b_xt = []
                for i in range(2):
                    t, b = sbuf(nc, c, f"xt{i}", [128, 1024]); xt.append(t); b_xt.append(b)
                st, b_st = sbuf(nc, c, "st", [128, 8])
                sq, b_sq = sbuf(nc, c, "sq", [128, 1024])
                lnb, b_lnb = sbuf(nc, c, "lnb", [128, 1024], BF16)
                xn_fm, b_xn = sbuf(nc, c, "xn_fm", [128, 8 * S], BF16)
                wn, b_wn = sbuf(nc, c, "wn", [128, 2 * 1024], BF16)
                wT, b_wT = sbuf(nc, c, "wT", [128, 8 * 3072], BF16)
                wr_row, b_wrr = sbuf(nc, c, "wr_row", [1, 1024], BF16)
                wr, b_wr = sbuf(nc, c, "wr", [128, 8], BF16)
                qk_sb, b_qk = sbuf(nc, c, "qk_sb", [128, 512], BF16)
                v_sb, b_vsb = sbuf(nc, c, "v_sb", [128, 1024], BF16)
                r_sb, b_rsb = sbuf(nc, c, "r_sb", [1, S])
                ps_t = []; b_pst = []
                for i in range(2):
                    t = c.enter_context(nc.psum_tensor(f"ps_t{i}", [128, 1024], BF16))
                    ps_t.append(t); b_pst.append(Buf(f"ps_t{i}"))
                ps_m = []; b_psm = []
                for i in range(2):
                    t = c.enter_context(nc.psum_tensor(f"ps_m{i}", [128, 512], F32))
                    ps_m.append(t); b_psm.append(Buf(f"ps_m{i}"))

                for t in range(32):
                    i = t % 2
                    p.emit("sync", lambda e, t=t, i=i: e.dma_start(out=xt[i][:], in_=x[t*128:(t+1)*128, :]),
                           writes=[b_xt[i]], dma=f"x{i}")
                    emit_ln(p, xt[i][:], lnb[:], st[:], sq[:], b_xt[i], b_lnb, b_st, b_sq, 1024)
                    j = t % 2
                    emit_group(p, "tensor",
                               [lambda e, cc=cc, j=j: e.transpose(ps_t[j][:, cc*128:(cc+1)*128], lnb[:, cc*128:(cc+1)*128], ident[:])
                                for cc in range(8)],
                               reads=[b_lnb, b_id], writes=[b_pst[j]])
                    p.emit("vector", lambda e, j=j, t=t: e.tensor_copy(
                               xn_fm[:].rearrange("p (c s) -> p c s", s=S)[:, :, t*128:(t+1)*128],
                               ps_t[j][:].rearrange("p (c s) -> p c s", s=128)),
                           reads=[b_pst[j]], writes=[b_xn])
                for t in range(24):
                    i = t % 2
                    p.emit("gpsimd", lambda e, t=t, i=i: e.dma_start(out=wn[:, i*1024:(i+1)*1024], in_=w_qkv[t*128:(t+1)*128, :]),
                           writes=[b_wn], dma=f"w{i}")
                    j = t % 2
                    emit_group(p, "tensor",
                               [lambda e, cc=cc, i=i, j=j: e.transpose(ps_t[j][:, cc*128:(cc+1)*128], wn[:, i*1024 + cc*128 : i*1024 + (cc+1)*128], ident[:])
                                for cc in range(8)],
                               reads=[b_wn, b_id], writes=[b_pst[j]])
                    p.emit("vector", lambda e, j=j, t=t: e.tensor_copy(
                               wT[:].rearrange("p (c s) -> p c s", s=3072)[:, :, t*128:(t+1)*128],
                               ps_t[j][:].rearrange("p (c s) -> p c s", s=128)),
                           reads=[b_pst[j]], writes=[b_wT])
                p.emit("gpsimd", lambda e: e.dma_start(out=wr_row[:], in_=w_router[None, :]), writes=[b_wrr], dma="wr")
                for cc in range(8):
                    j = cc % 2
                    p.emit("tensor", lambda e, cc=cc, j=j: e.transpose(ps_t[j][0:128, 0:1], wr_row[0:1, cc*128:(cc+1)*128], ident[0:1, 0:1]),
                           reads=[b_wrr, b_id], writes=[b_pst[j]])
                    p.emit("vector", lambda e, cc=cc, j=j: e.tensor_copy(wr[:, cc:cc+1], ps_t[j][0:128, 0:1]),
                           reads=[b_pst[j]], writes=[b_wr])
                for which, off, out_t in (("q", 0, q_dr), ("k", 1024, k_dr)):
                    for f in range(8):
                        for sc in range(8):
                            j = (f * 8 + sc) % 2
                            emit_group(p, "tensor",
                                       [lambda e, f=f, sc=sc, ec=ec, j=j, off=off:
                                        e.matmul(ps_m[j][:], wT[:, ec*3072 + off + f*128 : ec*3072 + off + (f+1)*128],
                                                 xn_fm[:, ec*S + sc*512 : ec*S + (sc+1)*512],
                                                 start=(ec == 0), stop=(ec == 7)) for ec in range(8)],
                                       reads=[b_wT, b_xn], writes=[b_psm[j]])
                            p.emit("vector", lambda e, j=j: e.tensor_copy(qk_sb[:], ps_m[j][:]),
                                   reads=[b_psm[j]], writes=[b_qk])
                            p.emit("sync", lambda e, f=f, sc=sc, out_t=out_t:
                                   e.dma_start(out=out_t[:, f*S + sc*512 : f*S + (sc+1)*512], in_=qk_sb[:]),
                                   reads=[b_qk], dma="qk_o")
                for t in range(32):
                    for vh in range(2):
                        j = (t * 2 + vh) % 2
                        emit_group(p, "tensor",
                                   [lambda e, t=t, vh=vh, ec=ec, j=j:
                                    e.matmul(ps_m[j][:], xn_fm[:, ec*S + t*128 : ec*S + (t+1)*128],
                                             wT[:, ec*3072 + 2048 + vh*512 : ec*3072 + 2048 + (vh+1)*512],
                                             start=(ec == 0), stop=(ec == 7)) for ec in range(8)],
                                   reads=[b_wT, b_xn], writes=[b_psm[j]])
                        p.emit("vector", lambda e, vh=vh, j=j: e.tensor_copy(v_sb[:, vh*512:(vh+1)*512], ps_m[j][:]),
                               reads=[b_psm[j]], writes=[b_vsb])
                    p.emit("sync", lambda e, t=t: e.dma_start(out=v_dr[t*128:(t+1)*128, :], in_=v_sb[:]),
                           reads=[b_vsb], dma="v_o")
                for sc in range(8):
                    j = sc % 2
                    emit_group(p, "tensor",
                               [lambda e, sc=sc, ec=ec, j=j:
                                e.matmul(ps_m[j][0:1, :], wr[:, ec:ec+1],
                                         xn_fm[:, ec*S + sc*512 : ec*S + (sc+1)*512],
                                         start=(ec == 0), stop=(ec == 7)) for ec in range(8)],
                               reads=[b_wr, b_xn], writes=[b_psm[j]])
                    p.emit("vector", lambda e, sc=sc, j=j: e.tensor_copy(r_sb[0:1, sc*512:(sc+1)*512], ps_m[j][0:1, :]),
                           reads=[b_psm[j]], writes=[b_rsb])
                p.emit("sync", lambda e: e.dma_start(out=rsc_dr[None, :], in_=r_sb[:]), reads=[b_rsb], dma="r_o")
                p.full_barrier()

        _phase0()

        def _phase1():
            with ExitStack() as c:
                qg_sb, b_q = sbuf(nc, c, "qg_sb", [128, 8 * 1024], BF16)
                k_sb, b_k = sbuf(nc, c, "k_sb", [128, 8 * S], BF16)
                v_sb2, b_v = sbuf(nc, c, "v_sb2", [128, 32 * 1024], BF16)
                sc_sb = []; b_sc = []
                pr_sb = []; b_pr = []
                for i in range(2):
                    t, b = sbuf(nc, c, f"sc_sb{i}", [128, S]); sc_sb.append(t); b_sc.append(b)
                    t, b = sbuf(nc, c, f"pr_sb{i}", [128, S], BF16); pr_sb.append(t); b_pr.append(b)
                pt_sb, b_pt = sbuf(nc, c, "pt_sb", [128, 32 * 128], BF16)
                st, b_st = sbuf(nc, c, "st", [128, 8])
                o_sb = []; b_o = []
                for i in range(2):
                    t, b = sbuf(nc, c, f"o_sb{i}", [128, 1024], BF16); o_sb.append(t); b_o.append(b)
                ps_s = []; b_ps = []
                for i in range(2):
                    t = c.enter_context(nc.psum_tensor(f"ps_s{i}", [128, 1024], F32))
                    ps_s.append(t); b_ps.append(Buf(f"ps_s{i}"))
                ps_t = []; b_pst = []
                for i in range(2):
                    t = c.enter_context(nc.psum_tensor(f"ps_t{i}", [128, 1024], BF16))
                    ps_t.append(t); b_pst.append(Buf(f"ps_t{i}"))
                ps_av = []; b_pav = []
                for i in range(2):
                    t = c.enter_context(nc.psum_tensor(f"ps_av{i}", [128, 512], F32))
                    ps_av.append(t); b_pav.append(Buf(f"ps_av{i}"))
                p.emit("sync", lambda e: e.dma_start(out=k_sb[:], in_=k_dr), writes=[b_k], dma="kL")
                p.emit("sync", lambda e: e.dma_start(out=v_sb2[:].rearrange("p (t d) -> p t d", d=1024),
                                                     in_=v_dr.rearrange("(t p) d -> p t d", p=128)),
                       writes=[b_v], dma="vL")
                def emit_scores(qt, sci):
                    for kp in range(4):
                        j = kp % 2
                        fns = []
                        for kh in range(2):
                            kc = kp * 2 + kh
                            for qf in range(8):
                                fns.append(lambda e, qt=qt, kc=kc, kh=kh, qf=qf, j=j:
                                       e.matmul(ps_s[j][:, kh*512:(kh+1)*512],
                                                qg_sb[:, qf*1024 + (qt % 8)*128 : qf*1024 + (qt % 8 + 1)*128],
                                                k_sb[:, qf*S + kc*512 : qf*S + (kc+1)*512],
                                                start=(qf == 0), stop=(qf == 7)))
                        emit_group(p, "tensor", fns, reads=[b_q, b_k], writes=[b_ps[j]])
                        p.emit("vector", lambda e, kp=kp, j=j, sci=sci: e.tensor_copy(sc_sb[sci][:, kp*1024:(kp+1)*1024], ps_s[j][:]),
                               reads=[b_ps[j]], writes=[b_sc[sci]])
                    p.emit("vector", lambda e, sci=sci: e.reduce_max(st[:, 4*sci:4*sci+1], sc_sb[sci][:], axis=AX.X, negate=True),
                           reads=[b_sc[sci]], writes=[b_st])
                    p.emit("vector", lambda e, sci=sci: e.tensor_scalar_mul(st[:, 4*sci+1:4*sci+2], st[:, 4*sci:4*sci+1], 0.125),
                           reads=[b_st], writes=[b_st])
                    p.emit("scalar", lambda e, sci=sci: e.activation(pr_sb[sci][:], sc_sb[sci][:], AF.Exp, bias=st[:, 4*sci+1:4*sci+2],
                                                            scale=0.125, accum_out=st[:, 4*sci+2:4*sci+3]),
                           reads=[b_sc[sci], b_st], writes=[b_pr[sci], b_st])
                    p.emit("vector", lambda e, sci=sci: e.reciprocal(st[:, 4*sci+3:4*sci+4], st[:, 4*sci+2:4*sci+3]),
                           reads=[b_st], writes=[b_st])

                def emit_av(qt, sci):
                    for kg in range(4):
                        j = kg % 2
                        emit_group(p, "tensor",
                                   [lambda e, kt=kt, j=j, sci=sci:
                                    e.transpose(ps_t[j][:, (kt % 8)*128:(kt % 8 + 1)*128], pr_sb[sci][:, kt*128:(kt+1)*128], ident[:])
                                    for kt in range(kg*8, kg*8+8)],
                                   reads=[b_pr[sci], b_id], writes=[b_pst[j]])
                        p.emit("vector", lambda e, kg=kg, j=j: e.tensor_copy(pt_sb[:, kg*1024:(kg+1)*1024], ps_t[j][:]),
                               reads=[b_pst[j]], writes=[b_pt])
                    ob = qt % 2
                    for vh in range(2):
                        j = vh
                        emit_group(p, "tensor",
                                   [lambda e, kt=kt, vh=vh, j=j:
                                    e.matmul(ps_av[j][:], pt_sb[:, kt*128:(kt+1)*128],
                                             v_sb2[:, kt*1024 + vh*512 : kt*1024 + (vh+1)*512],
                                             start=(kt == 0), stop=(kt == 31)) for kt in range(32)],
                                   reads=[b_pt, b_v], writes=[b_pav[j]])
                        p.emit("vector", lambda e, vh=vh, j=j, ob=ob, sci=sci:
                               e.tensor_scalar(o_sb[ob][:, vh*512:(vh+1)*512], ps_av[j][:], st[:, 4*sci+3:4*sci+4], None, op0=AL.mult),
                               reads=[b_pav[j], b_st], writes=[b_o[ob]])
                    p.emit("sync", lambda e, qt=qt, ob=ob: e.dma_start(out=out_dr[qt*128:(qt+1)*128, :], in_=o_sb[ob][:]),
                           reads=[b_o[ob]], dma="o_o")

                prev = None
                for g in range(4):
                    p.emit("sync", lambda e, g=g: e.dma_start(
                            out=qg_sb[:].rearrange("p (f s) -> p f s", s=1024),
                            in_=q_dr.rearrange("p (f s) -> p f s", s=S)[:, :, g*1024:(g+1)*1024]),
                           writes=[b_q], dma="qL")
                    for q4 in range(8):
                        qt = g * 8 + q4
                        emit_scores(qt, qt % 2)
                        if prev is not None:
                            emit_av(prev, prev % 2)
                        prev = qt
                emit_av(prev, prev % 2)
                p.full_barrier()

        _phase1()

        def _phase2():
            with ExitStack() as c:
                ogg_sb, b_og = sbuf(nc, c, "ogg_sb", [128, 8 * 1024], BF16)
                wpn, b_wpn = sbuf(nc, c, "wpn", [128, 2 * 1024], BF16)
                wpT, b_wpT = sbuf(nc, c, "wpT", [128, 8 * 1024], BF16)
                wb_sb, b_wb = sbuf(nc, c, "wb_sb", [128, 8 * 1024], BF16)
                bp_row, b_bpr = sbuf(nc, c, "bp_row", [1, 1024])
                bpb, b_bpb = sbuf(nc, c, "bpb", [128, 1024])
                rs_sb, b_rs = sbuf(nc, c, "rs_sb", [128, 32])
                rsig_sb, b_rsig = sbuf(nc, c, "rsig_sb", [1, 4096])
                cmp_sb, b_cmp = sbuf(nc, c, "cmp_sb", [128, 32])
                ones128, b_o128 = sbuf(nc, c, "ones128", [128, 128])
                thr, b_thr = sbuf(nc, c, "thr", [128, 1])
                sgn, b_sgn = sbuf(nc, c, "sgn", [128, 1])
                step_tab, b_stp = sbuf(nc, c, "step_tab", [128, NIT])
                st4, b_st4 = sbuf(nc, c, "st4", [128, 8])
                row_sc, b_rsc2 = sbuf(nc, c, "row_sc", [1, 128])
                mcol, b_mc = sbuf(nc, c, "mcol", [128, 2])
                xt, b_xt = sbuf(nc, c, "xt", [128, 1024])
                st, b_st = sbuf(nc, c, "st", [128, 8])
                sq, b_sq = sbuf(nc, c, "sq", [128, 1024])
                xn_f, b_xnf = sbuf(nc, c, "xn_f", [128, 1024])
                xn_b, b_xnb = sbuf(nc, c, "xn_b", [128, 1024], BF16)
                xnfm, b_xnfm = sbuf(nc, c, "xnfm", [128, 8 * 128], BF16)
                y_sb, b_y = sbuf(nc, c, "y_sb", [128, 1024])
                a3_sb, b_a3 = sbuf(nc, c, "a3_sb", [128, 1024])
                d_sb, b_d = sbuf(nc, c, "d_sb", [128, 1024])
                x1_sb, b_x1 = sbuf(nc, c, "x1_sb", [128, 1024])
                ps_y = c.enter_context(nc.psum_tensor("ps_y", [128, 1024], F32)); b_psy = Buf("ps_y")
                ps_p = c.enter_context(nc.psum_tensor("ps_p", [128, 1024], F32)); b_psp = Buf("ps_p")
                ps_c = c.enter_context(nc.psum_tensor("ps_c", [128, 128], F32)); b_psc = Buf("ps_c")
                ps_t = []; b_pst = []
                for i in range(2):
                    t = c.enter_context(nc.psum_tensor(f"ps_t{i}", [128, 1024], BF16))
                    ps_t.append(t); b_pst.append(Buf(f"ps_t{i}"))
                p.emit("vector", lambda e: e.memset(ones128[:], 1.0), writes=[b_o128])
                p.emit("gpsimd", lambda e: e.dma_start(out=wb_sb[:].rearrange("p (t d) -> p t d", d=1024),
                                                       in_=w_block.rearrange("(t p) d -> p t d", p=128)),
                       writes=[b_wb], dma="wb")
                p.emit("sync", lambda e: e.dma_start(out=rs_sb[:], in_=rsc_dr.rearrange("(q j) -> q j", j=32)),
                       writes=[b_rs], dma="rs", waits=[Tok(p.dma_sems["r_o"][0], p.dma_sems["r_o"][1])])
                p.emit("sync", lambda e: e.dma_start(out=rsig_sb[:], in_=rsc_dr[None, :]), writes=[b_rsig], dma="rs")
                p.emit("sync", lambda e: e.dma_start(out=bp_row[:], in_=b_proj[None, :]), writes=[b_bpr], dma="rs")
                for t in range(8):
                    i = t % 2
                    p.emit("gpsimd", lambda e, t=t, i=i: e.dma_start(out=wpn[:, i*1024:(i+1)*1024],
                                                                     in_=w_proj[t*128:(t+1)*128, :]),
                           writes=[b_wpn], dma=f"wp{i}")
                    j = t % 2
                    emit_group(p, "tensor",
                               [lambda e, cc=cc, i=i, j=j: e.transpose(ps_t[j][:, cc*128:(cc+1)*128], wpn[:, i*1024 + cc*128 : i*1024 + (cc+1)*128], ident[:])
                                for cc in range(8)],
                               reads=[b_wpn, b_id], writes=[b_pst[j]])
                    p.emit("vector", lambda e, j=j, t=t: e.tensor_copy(
                               wpT[:].rearrange("p (c s) -> p c s", s=1024)[:, :, t*128:(t+1)*128],
                               ps_t[j][:].rearrange("p (c s) -> p c s", s=128)),
                           reads=[b_pst[j]], writes=[b_wpT])
                for eh in range(2):
                    p.emit("tensor", lambda e, eh=eh: e.matmul(ps_p[:, eh*512:(eh+1)*512], ones1[:],
                                                               bp_row[0:1, eh*512:(eh+1)*512], start=True, stop=True),
                           reads=[b_on, b_bpr], writes=[b_psp])
                p.emit("vector", lambda e: e.tensor_copy(bpb[:], ps_p[:]), reads=[b_psp], writes=[b_bpb])
                p.emit("vector", lambda e: e.reduce_max(st4[:, 0:1], rs_sb[:], axis=AX.X, apply_absolute_value=True),
                       reads=[b_rs], writes=[b_st4])
                p.emit("tensor", lambda e: e.transpose(ps_c[0:1, 0:128], st4[:, 0:1], identf[:]),
                       reads=[b_st4, b_idf], writes=[b_psc])
                p.emit("vector", lambda e: e.tensor_copy(row_sc[:], ps_c[0:1, 0:128]), reads=[b_psc], writes=[b_rsc2])
                p.emit("vector", lambda e: e.reduce_max(row_sc[0:1, 0:1], row_sc[:], axis=AX.X),
                       reads=[b_rsc2], writes=[b_rsc2])
                p.emit("tensor", lambda e: e.matmul(ps_c[:, 0:1], ones1[:], row_sc[0:1, 0:1], start=True, stop=True),
                       reads=[b_on, b_rsc2], writes=[b_psc])
                p.emit("vector", lambda e: e.tensor_copy(st4[:, 1:2], ps_c[:, 0:1]), reads=[b_psc], writes=[b_st4])
                for i in range(NIT):
                    p.emit("vector", lambda e, i=i: e.tensor_scalar_mul(step_tab[:, i:i+1], st4[:, 1:2], 2.0 ** (-i)),
                           reads=[b_st4], writes=[b_stp])
                p.emit("vector", lambda e: e.memset(thr[:], 0.0), writes=[b_thr])
                for i in range(NIT):
                    p.emit("vector", lambda e: e.tensor_scalar(cmp_sb[:], rs_sb[:], thr[:, 0:1], None, op0=AL.is_gt),
                           reads=[b_rs, b_thr], writes=[b_cmp])
                    p.emit("vector", lambda e: e.reduce_sum(st4[:, 2:3], cmp_sb[:], axis=AX.X),
                           reads=[b_cmp], writes=[b_st4])
                    p.emit("tensor", lambda e: e.matmul(ps_c[:, 1:2], ones128[:], st4[:, 2:3], start=True, stop=True),
                           reads=[b_o128, b_st4], writes=[b_psc])
                    p.emit("vector", lambda e: e.tensor_scalar(st4[:, 3:4], ps_c[:, 1:2], float(-TOPK), None, op0=AL.add),
                           reads=[b_psc], writes=[b_st4])
                    p.emit("scalar", lambda e: e.activation(sgn[:], st4[:, 3:4], AF.Sign),
                           reads=[b_st4], writes=[b_sgn])
                    p.emit("vector", lambda e, i=i: e.scalar_tensor_tensor(thr[:], sgn[:], step_tab[:, i:i+1], thr[:],
                                                                           op0=AL.mult, op1=AL.add),
                           reads=[b_sgn, b_stp, b_thr], writes=[b_thr])
                for tt in range(32):
                    gg, hc = tt // 8, tt % 8
                    p.emit("sync", lambda e, gg=gg, hc=hc: e.dma_start(
                            out=xt[:], in_=x.rearrange("(a f) d -> f a d", f=4)[gg, hc*128:(hc+1)*128, :]),
                           writes=[b_xt], dma="xs")
                    emit_ln(p, xt[:], xn_f[:], st[:], sq[:], b_xt, b_xnf, b_st, b_sq, 1024)
                    p.emit("scalar", lambda e: e.activation(xn_b[:], xn_f[:], AF.Copy), reads=[b_xnf], writes=[b_xnb])
                    j = tt % 2
                    emit_group(p, "tensor",
                               [lambda e, cc=cc, j=j: e.transpose(ps_t[j][:, cc*128:(cc+1)*128], xn_b[:, cc*128:(cc+1)*128], ident[:])
                                for cc in range(8)],
                               reads=[b_xnb, b_id], writes=[b_pst[j]])
                    p.emit("vector", lambda e, j=j: e.tensor_copy(xnfm[:], ps_t[j][:]),
                           reads=[b_pst[j]], writes=[b_xnfm])
                    emit_group(p, "tensor",
                               [lambda e, eh=eh, dc=dc:
                                e.matmul(ps_y[:, eh*512:(eh+1)*512], xnfm[:, dc*128:(dc+1)*128],
                                         wb_sb[:, dc*1024 + eh*512 : dc*1024 + (eh+1)*512],
                                         start=(dc == 0), stop=(dc == 7))
                                for eh in range(2) for dc in range(8)],
                               reads=[b_xnfm, b_wb], writes=[b_psy])
                    p.emit("scalar", lambda e: e.activation(y_sb[:], ps_y[:], AF.Gelu_apprx_tanh),
                           reads=[b_psy], writes=[b_y])
                    if hc == 0:
                        p.emit("sync", lambda e, gg=gg: e.dma_start(
                                out=ogg_sb[:].rearrange("p (t d) -> p t d", d=1024),
                                in_=out_dr.rearrange("(t p) d -> p t d", p=128)[:, 8*gg:8*gg+8, :]),
                               writes=[b_og], dma="og")
                    for eh in range(2):
                        for jc in range(8):
                            p.emit("tensor", lambda e, eh=eh, jc=jc, hc=hc:
                                   e.matmul(ps_p[:, eh*512:(eh+1)*512],
                                            ogg_sb[:, jc*1024 + hc*128 : jc*1024 + (hc+1)*128],
                                            wpT[:, jc*1024 + eh*512 : jc*1024 + (eh+1)*512],
                                            start=(jc == 0), stop=(jc == 7)),
                                   reads=[b_og, b_wpT], writes=[b_psp])
                    p.emit("vector", lambda e: e.tensor_add(a3_sb[:], ps_p[:], bpb[:]), reads=[b_psp, b_bpb], writes=[b_a3])
                    p.emit("tensor", lambda e, gg=gg, hc=hc: e.transpose(ps_c[0:128, 2:3],
                            rsig_sb[0:1, :].rearrange("o (a f) -> o f a", f=4)[:, gg, hc*128:(hc+1)*128],
                            identf[0:1, 0:1]),
                           reads=[b_rsig, b_idf], writes=[b_psc])
                    p.emit("vector", lambda e: e.tensor_copy(mcol[:, 0:1], ps_c[0:128, 2:3]), reads=[b_psc], writes=[b_mc])
                    p.emit("vector", lambda e: e.tensor_scalar(mcol[:, 1:2], mcol[:, 0:1], thr[:, 0:1], None, op0=AL.is_gt),
                           reads=[b_mc, b_thr], writes=[b_mc])
                    p.emit("vector", lambda e: e.tensor_sub(d_sb[:], y_sb[:], xn_f[:]), reads=[b_y, b_xnf], writes=[b_d])
                    p.emit("vector", lambda e: e.scalar_tensor_tensor(d_sb[:], d_sb[:], mcol[:, 1:2], xn_f[:],
                                                                      op0=AL.mult, op1=AL.add),
                           reads=[b_d, b_mc, b_xnf], writes=[b_d])
                    p.emit("vector", lambda e: e.tensor_add(x1_sb[:], xt[:], a3_sb[:]), reads=[b_xt, b_a3], writes=[b_x1])
                    p.emit("vector", lambda e: e.tensor_add(x1_sb[:], x1_sb[:], d_sb[:]), reads=[b_x1, b_d], writes=[b_x1])
                    p.emit("sync", lambda e, gg=gg, hc=hc: e.dma_start(
                            out=x1_dr.rearrange("(a f) d -> f a d", f=4)[gg, hc*128:(hc+1)*128, :], in_=x1_sb[:]),
                           reads=[b_x1], dma="x1w")
                p.full_barrier()

        _phase2()

        k1_sb, b_k1 = sbuf(nc, ctx, "k1_sb", [64, S], BF16)
        v1_sb, b_v1 = sbuf(nc, ctx, "v1_sb", [128, 32 * 64], BF16)
        def _phase3():
            with ExitStack() as c:
                nw_row, b_nwr = sbuf(nc, c, "nw_row", [1, 1024])
                nb_row, b_nbr = sbuf(nc, c, "nb_row", [1, 1024])
                nwb, b_nwb = sbuf(nc, c, "nwb", [128, 1024])
                nbb, b_nbb = sbuf(nc, c, "nbb", [128, 1024])
                wq_sb, b_wq = sbuf(nc, c, "wq_sb", [128, 8 * 512], BF16)
                wkv_sb, b_wkv = sbuf(nc, c, "wkv_sb", [128, 8 * 128], BF16)
                xt, b_xt = sbuf(nc, c, "xt", [128, 1024])
                st, b_st = sbuf(nc, c, "st", [128, 8])
                sq, b_sq = sbuf(nc, c, "sq", [128, 1024])
                xn1_f, b_xn1f = sbuf(nc, c, "xn1_f", [128, 1024])
                xn1_b, b_xn1b = sbuf(nc, c, "xn1_b", [128, 1024], BF16)
                xn1n, b_xn1n = sbuf(nc, c, "xn1n", [128, 1024])
                xa_b, b_xab = sbuf(nc, c, "xa_b", [128, 1024], BF16)
                xa_fm, b_xafm = sbuf(nc, c, "xa_fm", [128, 8 * S], BF16)
                xnfm_t, b_xnfmt = sbuf(nc, c, "xnfm_t", [128, 8 * 128], BF16)
                ev_sb, b_ev = sbuf(nc, c, "ev_sb", [128, 512], BF16)
                v_sb, b_vsb = sbuf(nc, c, "v_sb", [128, 64], BF16)
                ps_m = []; b_psm = []
                for i in range(2):
                    t = c.enter_context(nc.psum_tensor(f"ps_m{i}", [128, 512], F32))
                    ps_m.append(t); b_psm.append(Buf(f"ps_m{i}"))
                ps_t = []; b_pst = []
                for i in range(2):
                    t = c.enter_context(nc.psum_tensor(f"ps_t{i}", [128, 1024], BF16))
                    ps_t.append(t); b_pst.append(Buf(f"ps_t{i}"))
                ps_b = c.enter_context(nc.psum_tensor("ps_b", [128, 1024], F32)); b_psb = Buf("ps_b")
                p.emit("sync", lambda e: e.dma_start(out=nw_row[:], in_=norm_w[None, :]), writes=[b_nwr], dma="n")
                p.emit("sync", lambda e: e.dma_start(out=nb_row[:], in_=norm_b[None, :]), writes=[b_nbr], dma="n2")
                p.emit("gpsimd", lambda e: e.dma_start(out=wq_sb[:].rearrange("p (t d) -> p t d", d=512),
                                                       in_=Wq.rearrange("(t p) d -> p t d", p=128)),
                       writes=[b_wq], dma="wq")
                p.emit("gpsimd", lambda e: e.dma_start(out=wkv_sb[:].rearrange("p (t d) -> p t d", d=128),
                                                       in_=Wkv.rearrange("(t p) d -> p t d", p=128)),
                       writes=[b_wkv], dma="wkv")
                for eh in range(2):
                    p.emit("tensor", lambda e, eh=eh: e.matmul(ps_b[:, eh*512:(eh+1)*512], ones1[:],
                                                               nw_row[0:1, eh*512:(eh+1)*512], start=True, stop=True),
                           reads=[b_on, b_nwr], writes=[b_psb])
                p.emit("vector", lambda e: e.tensor_copy(nwb[:], ps_b[:]), reads=[b_psb], writes=[b_nwb])
                for eh in range(2):
                    p.emit("tensor", lambda e, eh=eh: e.matmul(ps_b[:, eh*512:(eh+1)*512], ones1[:],
                                                               nb_row[0:1, eh*512:(eh+1)*512], start=True, stop=True),
                           reads=[b_on, b_nbr], writes=[b_psb])
                p.emit("vector", lambda e: e.tensor_copy(nbb[:], ps_b[:]), reads=[b_psb], writes=[b_nbb])
                for t in range(32):
                    p.emit("sync", lambda e, t=t: e.dma_start(out=xt[:], in_=x1_dr[t*128:(t+1)*128, :]),
                           writes=[b_xt], dma="xB")
                    emit_ln(p, xt[:], xn1_f[:], st[:], sq[:], b_xt, b_xn1f, b_st, b_sq, 1024)
                    p.emit("scalar", lambda e: e.activation(xn1_b[:], xn1_f[:], AF.Copy), reads=[b_xn1f], writes=[b_xn1b])
                    emit_ln(p, xn1_f[:], xn1n[:], st[:], sq[:], b_xn1f, b_xn1n, b_st, b_sq, 1024)
                    p.emit("vector", lambda e: e.tensor_mul(xn1n[:], xn1n[:], nwb[:]), reads=[b_xn1n, b_nwb], writes=[b_xn1n])
                    p.emit("vector", lambda e: e.tensor_add(xa_b[:], xn1n[:], nbb[:]), reads=[b_xn1n, b_nbb], writes=[b_xab])
                    emit_group(p, "tensor",
                               [lambda e, cc=cc: e.transpose(ps_t[0][:, cc*128:(cc+1)*128], xa_b[:, cc*128:(cc+1)*128], ident[:])
                                for cc in range(8)],
                               reads=[b_xab, b_id], writes=[b_pst[0]])
                    p.emit("vector", lambda e, t=t: e.tensor_copy(
                               xa_fm[:].rearrange("p (c s) -> p c s", s=S)[:, :, t*128:(t+1)*128],
                               ps_t[0][:].rearrange("p (c s) -> p c s", s=128)),
                           reads=[b_pst[0]], writes=[b_xafm])
                    emit_group(p, "tensor",
                               [lambda e, cc=cc: e.transpose(ps_t[1][:, cc*128:(cc+1)*128], xn1_b[:, cc*128:(cc+1)*128], ident[:])
                                for cc in range(8)],
                               reads=[b_xn1b, b_id], writes=[b_pst[1]])
                    p.emit("vector", lambda e: e.tensor_copy(xnfm_t[:], ps_t[1][:]),
                           reads=[b_pst[1]], writes=[b_xnfmt])
                    p.emit("sync", lambda e, t=t: e.dma_start(
                            out=xn1_dr[:, :].rearrange("p (sb dc s) -> p sb dc s", dc=8, s=256)[:, t//2, :, (t % 2)*128:(t % 2)*128+128],
                            in_=xnfm_t[:].rearrange("p (dc s) -> p dc s", s=128)),
                           reads=[b_xnfmt], dma="xnw")
                for ic in range(4):
                    for sc in range(8):
                        j = (ic * 8 + sc) % 2
                        emit_group(p, "tensor",
                                   [lambda e, ic=ic, sc=sc, dc=dc, j=j:
                                    e.matmul(ps_m[j][:], wq_sb[:, dc*512 + ic*128 : dc*512 + (ic+1)*128],
                                             xa_fm[:, dc*S + sc*512 : dc*S + (sc+1)*512],
                                             start=(dc == 0), stop=(dc == 7)) for dc in range(8)],
                                   reads=[b_wq, b_xafm], writes=[b_psm[j]])
                        for hh in range(2):
                            p.emit("vector", lambda e, hh=hh, j=j: e.tensor_copy(ev_sb[0:64, :], ps_m[j][hh*64:hh*64+64, :]),
                                   reads=[b_psm[j]], writes=[b_ev])
                            hf, lc = sc // 4, sc % 4
                            p.emit("sync", lambda e, ic=ic, hh=hh, hf=hf, lc=lc:
                                   e.dma_start(out=q1_dr[:, hf*16384 + (2*ic+hh)*2048 + lc*512 : hf*16384 + (2*ic+hh)*2048 + (lc+1)*512],
                                               in_=ev_sb[0:64, :]),
                                   reads=[b_ev], dma="q1w")
                for sc in range(8):
                    j = sc % 2
                    emit_group(p, "tensor",
                               [lambda e, sc=sc, dc=dc, j=j:
                                e.matmul(ps_m[j][0:64, :], wkv_sb[:, dc*128 : dc*128 + 64],
                                         xa_fm[:, dc*S + sc*512 : dc*S + (sc+1)*512],
                                         start=(dc == 0), stop=(dc == 7)) for dc in range(8)],
                               reads=[b_wkv, b_xafm], writes=[b_psm[j]])
                    p.emit("vector", lambda e, sc=sc, j=j: e.tensor_copy(k1_sb[0:64, sc*512:(sc+1)*512], ps_m[j][0:64, :]),
                           reads=[b_psm[j]], writes=[b_k1])
                for t in range(32):
                    j = t % 2
                    emit_group(p, "tensor",
                               [lambda e, t=t, dc=dc, j=j:
                                e.matmul(ps_m[j][:, 0:64], xa_fm[:, dc*S + t*128 : dc*S + (t+1)*128],
                                         wkv_sb[:, dc*128 + 64 : dc*128 + 128],
                                         start=(dc == 0), stop=(dc == 7)) for dc in range(8)],
                               reads=[b_wkv, b_xafm], writes=[b_psm[j]])
                    p.emit("vector", lambda e, t=t, j=j: e.tensor_copy(v1_sb[:, t*64:(t+1)*64], ps_m[j][:, 0:64]),
                           reads=[b_psm[j]], writes=[b_v1])
                p.full_barrier()

        _phase3()

        emit_b2_core(nc, p, ctx, dict(
            ident=ident, b_id=b_id, identf=identf, b_idf=b_idf, ones1=ones1, b_on=b_on,
            k1_sb=k1_sb, b_k1=b_k1, v1_sb=v1_sb, b_v1=b_v1,
            q1_dr=q1_dr, xn1_dr=xn1_dr, x1_dr=x1_dr, attn1_dr=attn1_dr, fin_o=fin_o,
            idx_xn=idx_xn, idx_q=idx_q, idx_x1=idx_x1,
            Wo=Wo, onw=onw, onb=onb, gate_w=gate_w, gate_b=gate_b, expert_bias=expert_bias,
            w1=w1, b1=b1, w2=w2, b2=b2))
        p.play(block)
    return nc


def emit_b2_core(nc, p, ctx, io):
    T, S = 2048, 4096
    ident, b_id = io["ident"], io["b_id"]
    identf, b_idf = io["identf"], io["b_idf"]
    ones1, b_on = io["ones1"], io["b_on"]
    k_sb, b_k = io["k1_sb"], io["b_k1"]
    v_sb, b_v = io["v1_sb"], io["b_v1"]
    q1_fl = io["q1_dr"].rearrange("p (b x) -> (p b) x", x=16384)
    xn1_fl = io["xn1_dr"].rearrange("p (s x) -> (p s) x", x=2048)
    x1_fl = io["x1_dr"]
    attn1_dr = io["attn1_dr"]; fin_o = io["fin_o"]
    Wo, onw, onb = io["Wo"], io["onw"], io["onb"]
    gate_w, gate_b, expert_bias = io["gate_w"], io["gate_b"], io["expert_bias"]
    w1, b1, w2, b2 = io["w1"], io["b1"], io["w2"], io["b2"]

    oh_tm, b_oht = sbuf(nc, ctx, "oh_tm", [128, 16 * 8])
    oh_dr = nc.dram_tensor("oh_dr", [8, 16 * 128], BF16).ap()
    moe_sb, b_moe = sbuf(nc, ctx, "moe_sb", [128, 16 * 1024], BF16)
    ixn_sb, b_ixn = sbuf(nc, ctx, "ixn_sb", [128, 8], I32)
    ixq_sb, b_ixq = sbuf(nc, ctx, "ixq_sb", [64, 1], I32)
    ix1_sb, b_ix1 = sbuf(nc, ctx, "ix1_sb", [128, 16], I32)
    p.emit("sync", lambda e: e.dma_start(out=ixn_sb[:], in_=io["idx_xn"]), writes=[b_ixn], dma="ix")
    p.emit("sync", lambda e: e.dma_start(out=ixq_sb[:], in_=io["idx_q"]), writes=[b_ixq], dma="ix")
    p.emit("sync", lambda e: e.dma_start(out=ix1_sb[:], in_=io["idx_x1"]), writes=[b_ix1], dma="ix")

    with ExitStack() as c1:
        q_sb, b_q = sbuf(nc, c1, "q_sb", [64, 8 * T], BF16)
        wo_sb, b_wo = sbuf(nc, c1, "wo_sb", [128, 4 * 1024], BF16)
        onwb, b_onwb = sbuf(nc, c1, "onwb", [128, 1024])
        onbb, b_onbb = sbuf(nc, c1, "onbb", [128, 1024])
        row_sb, b_row = sbuf(nc, c1, "row_sb", [1, 1024])
        sc_sb = []; b_sc = []
        pr_sb = []; b_pr = []
        for i in range(2):
            t, b = sbuf(nc, c1, f"sc_sbB{i}", [128, S]); sc_sb.append(t); b_sc.append(b)
            t, b = sbuf(nc, c1, f"pr_sbB{i}", [128, S], BF16); pr_sb.append(t); b_pr.append(b)
        pt_sb, b_pt = sbuf(nc, c1, "pt_sb", [128, 32 * 512], BF16)
        st, b_st = sbuf(nc, c1, "st", [128, 8])
        o1_sb, b_o1 = sbuf(nc, c1, "o1_sb", [128, 4 * T], BF16)
        z_sb, b_z = sbuf(nc, c1, "z_sb", [128, 1024])
        sq, b_sq = sbuf(nc, c1, "sq", [128, 1024])
        a1_sb, b_a1 = sbuf(nc, c1, "a1_sb", [128, 1024], BF16)
        ps_s = []; b_ps = []
        for i in range(2):
            t = c1.enter_context(nc.psum_tensor(f"ps_s{i}", [128, 1024], F32))
            ps_s.append(t); b_ps.append(Buf(f"ps_s{i}"))
        ps_t = []; b_pst = []
        for i in range(2):
            t = c1.enter_context(nc.psum_tensor(f"ps_t{i}", [128, 1024], BF16))
            ps_t.append(t); b_pst.append(Buf(f"ps_t{i}"))
        ps_o = c1.enter_context(nc.psum_tensor("ps_o", [128, 512], F32)); b_pso = Buf("ps_o")
        ps_z = ps_s[0]; b_psz = b_ps[0]

        p.emit("gpsimd", lambda e: e.indirect_dma_start(out=q_sb[:], out_offset=None,
               in_=q1_fl, in_offset=IndirectOffsetOnAxis(ixq_sb[:], axis=0)),
               reads=[b_ixq], writes=[b_q], dma="qI")
        p.emit("gpsimd", lambda e: e.dma_start(out=wo_sb[:].rearrange("p (t d) -> p t d", d=1024),
                                               in_=Wo.rearrange("(t p) d -> p t d", p=128)),
               writes=[b_wo], dma="wo")
        for row_b, dst, b_dst in ((onw, onwb, b_onwb), (onb, onbb, b_onbb)):
            p.emit("sync", lambda e, row_b=row_b: e.dma_start(out=row_sb[:], in_=row_b[None, :]),
                   writes=[b_row], dma="nr")
            for eh in range(2):
                p.emit("tensor", lambda e, eh=eh: e.matmul(ps_z[:, eh*512:(eh+1)*512], ones1[:],
                                                           row_sb[0:1, eh*512:(eh+1)*512], start=True, stop=True),
                       reads=[b_on, b_row], writes=[b_psz])
            p.emit("vector", lambda e, dst=dst, b_dst=b_dst: e.tensor_copy(dst[:], ps_z[:]),
                   reads=[b_psz], writes=[b_dst])

        def b2_scores(h, qt, sci):
            for kp in range(4):
                j = kp % 2
                emit_group(p, "tensor",
                           [lambda e, h=h, qt=qt, kc=kc, kh=kh, j=j:
                            e.matmul(ps_s[j][:, kh*512:(kh+1)*512], q_sb[:, h*T + qt*128 : h*T + (qt+1)*128],
                                     k_sb[:, kc*512:(kc+1)*512], start=True, stop=True)
                            for kh, kc in ((0, kp*2), (1, kp*2+1))],
                           reads=[b_q, b_k], writes=[b_ps[j]])
                p.emit("vector", lambda e, kp=kp, j=j, sci=sci: e.tensor_copy(sc_sb[sci][:, kp*1024:(kp+1)*1024], ps_s[j][:]),
                       reads=[b_ps[j]], writes=[b_sc[sci]])
            p.emit("vector", lambda e, sci=sci: e.reduce_max(st[:, 4*sci:4*sci+1], sc_sb[sci][:], axis=AX.X, negate=True),
                   reads=[b_sc[sci]], writes=[b_st])
            p.emit("vector", lambda e, sci=sci: e.tensor_scalar_mul(st[:, 4*sci+1:4*sci+2], st[:, 4*sci:4*sci+1], 0.125),
                   reads=[b_st], writes=[b_st])
            p.emit("scalar", lambda e, sci=sci: e.activation(pr_sb[sci][:], sc_sb[sci][:], AF.Exp, bias=st[:, 4*sci+1:4*sci+2],
                                                    scale=0.125, accum_out=st[:, 4*sci+2:4*sci+3]),
                   reads=[b_sc[sci], b_st], writes=[b_pr[sci], b_st])
            p.emit("vector", lambda e, sci=sci: e.reciprocal(st[:, 4*sci+3:4*sci+4], st[:, 4*sci+2:4*sci+3]),
                   reads=[b_st], writes=[b_st])
            p.emit("vector", lambda e, sci=sci: e.tensor_scalar(pr_sb[sci][:], pr_sb[sci][:], st[:, 4*sci+3:4*sci+4], None, op0=AL.mult),
                   reads=[b_pr[sci], b_st], writes=[b_pr[sci]])

        def b2_trans(q4, sci):
            for kg in range(4):
                j = kg % 2
                emit_group(p, "tensor",
                           [lambda e, kt=kt, j=j, sci=sci:
                            e.transpose(ps_t[j][:, (kt % 8)*128:(kt % 8 + 1)*128], pr_sb[sci][:, kt*128:(kt+1)*128], ident[:])
                            for kt in range(kg*8, kg*8+8)],
                           reads=[b_pr[sci], b_id], writes=[b_pst[j]])
                p.emit("vector", lambda e, kg=kg, j=j, q4=q4: e.tensor_copy(
                           pt_sb[:].rearrange("p (kt s) -> p kt s", s=512)[:, kg*8:(kg+1)*8, q4*128:(q4+1)*128],
                           ps_t[j][:].rearrange("p (kt s) -> p kt s", s=128)),
                       reads=[b_pst[j]], writes=[b_pt])

        def b2_av(h, qg):
            hb = 64 * (h % 2)
            emit_group(p, "tensor",
                       [lambda e, kt=kt:
                        e.matmul(ps_o[0:64, :], v_sb[:, kt*64:(kt+1)*64], pt_sb[:, kt*512:(kt+1)*512],
                                 start=(kt == 0), stop=(kt == 31)) for kt in range(32)],
                       reads=[b_v, b_pt], writes=[b_pso])
            p.emit("vector", lambda e, h=h, hb=hb, qg=qg:
                   e.tensor_copy(o1_sb[hb:hb+64, (h//2)*T + qg*512 : (h//2)*T + (qg+1)*512], ps_o[0:64, :]),
                   reads=[b_pso], writes=[b_o1])

        seq = [(h, qg, q4) for h in range(8) for qg in range(4) for q4 in range(4)]
        prev = None
        for n, (h, qg, q4) in enumerate(seq):
            b2_scores(h, qg * 4 + q4, n % 2)
            if prev is not None:
                ph, pqg, pq4 = prev
                b2_trans(pq4, (n - 1) % 2)
                if pq4 == 3:
                    b2_av(ph, pqg)
            prev = (h, qg, q4)
        b2_trans(3, (len(seq) - 1) % 2)
        b2_av(7, 3)
        for qt in range(16):
            emit_group(p, "tensor",
                       [lambda e, qt=qt, eh=eh, ic=ic:
                        e.matmul(ps_z[:, eh*512:(eh+1)*512], o1_sb[:, ic*T + qt*128 : ic*T + (qt+1)*128],
                                 wo_sb[:, ic*1024 + eh*512 : ic*1024 + (eh+1)*512],
                                 start=(ic == 0), stop=(ic == 3))
                        for eh in range(2) for ic in range(4)],
                       reads=[b_o1, b_wo], writes=[b_psz])
            p.emit("vector", lambda e: e.tensor_copy(z_sb[:], ps_z[:]), reads=[b_psz], writes=[b_z])
            emit_ln(p, z_sb[:], z_sb[:], st[:], sq[:], b_z, b_z, b_st, b_sq, 1024)
            p.emit("vector", lambda e: e.tensor_mul(z_sb[:], z_sb[:], onwb[:]), reads=[b_z, b_onwb], writes=[b_z])
            p.emit("vector", lambda e: e.tensor_add(a1_sb[:], z_sb[:], onbb[:]), reads=[b_z, b_onbb], writes=[b_a1])
            p.emit("sync", lambda e, qt=qt: e.dma_start(out=attn1_dr[qt*128:(qt+1)*128, :], in_=a1_sb[:]),
                   reads=[b_a1], dma="a1")
        p.full_barrier()

    with ExitStack() as c2:
        xn_str = []; b_xns = []
        for i in range(2):
            t, b = sbuf(nc, c2, f"xn_str{i}", [128, 8 * 256], BF16); xn_str.append(t); b_xns.append(b)
        ps_g = c2.enter_context(nc.psum_tensor("ps_g", [128, 512], F32)); b_psg = Buf("ps_g")
        ps_h = []; b_psh = []
        for i in range(4):
            t = c2.enter_context(nc.psum_tensor(f"ps_h{i}", [128, 512], F32))
            ps_h.append(t); b_psh.append(Buf(f"ps_h{i}"))
        ps_y = []; b_psy = []
        for i in range(3):
            t = c2.enter_context(nc.psum_tensor(f"ps_y{i}", [128, 512], F32))
            ps_y.append(t); b_psy.append(Buf(f"ps_y{i}"))

        with ExitStack() as c2g:
            oh_em, b_ohe = sbuf(nc, c2g, "oh_em", [8, 16 * 128], BF16)
            gw_row, b_gwr = sbuf(nc, c2g, "gw_row", [8, 1024])
            gwT, b_gw = sbuf(nc, c2g, "gwT", [128, 64], BF16)
            gb_col, b_gb = sbuf(nc, c2g, "gb_col", [8, 2])
            lg_sb, b_lg = sbuf(nc, c2g, "lg_sb", [8, T])
            stm, b_stm = sbuf(nc, c2g, "stm", [128, 4])
            p.emit("sync", lambda e: e.dma_start(out=gw_row[:], in_=gate_w), writes=[b_gwr], dma="gw")
            for dc in range(8):
                j = dc % 2
                p.emit("tensor", lambda e, dc=dc, j=j: e.transpose(ps_h[j][0:128, 0:8], gw_row[0:8, dc*128:(dc+1)*128], identf[0:8, 0:8]),
                       reads=[b_gwr, b_idf], writes=[b_psh[j]])
                p.emit("vector", lambda e, dc=dc, j=j: e.tensor_copy(gwT[:, dc*8:(dc+1)*8], ps_h[j][0:128, 0:8]),
                       reads=[b_psh[j]], writes=[b_gw])
            p.emit("sync", lambda e: e.dma_start(out=gb_col[:, 0:1], in_=gate_b[:, None]), writes=[b_gb], dma="gb")
            p.emit("sync", lambda e: e.dma_start(out=gb_col[:, 1:2], in_=expert_bias[:, None]), writes=[b_gb], dma="gb2")
            p.emit("vector", lambda e: e.tensor_add(gb_col[:, 0:1], gb_col[:, 0:1], gb_col[:, 1:2]),
                   reads=[b_gb], writes=[b_gb])
            for sb in range(8):
                i = sb % 2
                p.emit("gpsimd", lambda e, sb=sb, i=i: e.indirect_dma_start(out=xn_str[i][:], out_offset=None,
                       in_=xn1_fl, in_offset=IndirectOffsetOnAxis(ixn_sb[:, sb:sb+1], axis=0)),
                       reads=[b_ixn], writes=[b_xns[i]], dma=f"xsI{i}")
                emit_group(p, "tensor",
                           [lambda e, sb=sb, dc=dc, i=i:
                            e.matmul(ps_g[0:8, 0:256], gwT[:, dc*8:(dc+1)*8], xn_str[i][:, dc*256:(dc+1)*256],
                                     start=(dc == 0), stop=(dc == 7)) for dc in range(8)],
                           reads=[b_gw, b_xns[i]], writes=[b_psg])
                p.emit("vector", lambda e, sb=sb: e.tensor_scalar(lg_sb[0:8, sb*256:(sb+1)*256], ps_g[0:8, 0:256],
                                                                  gb_col[0:8, 0:1], None, op0=AL.add),
                       reads=[b_psg, b_gb], writes=[b_lg])
            for t in range(16):
                j = t % 2
                p.emit("tensor", lambda e, t=t, j=j: e.transpose(ps_h[j][0:128, 0:8], lg_sb[0:8, t*128:(t+1)*128], identf[0:8, 0:8]),
                       reads=[b_lg, b_idf], writes=[b_psh[j]])
                p.emit("vector", lambda e, t=t, j=j: e.tensor_copy(oh_tm[:, t*8:(t+1)*8], ps_h[j][0:128, 0:8]),
                       reads=[b_psh[j]], writes=[b_oht])
                p.emit("vector", lambda e, t=t: e.reduce_max(stm[:, 0:1], oh_tm[:, t*8:(t+1)*8], axis=AX.X),
                       reads=[b_oht], writes=[b_stm])
                p.emit("vector", lambda e, t=t: e.tensor_scalar(oh_tm[:, t*8:(t+1)*8], oh_tm[:, t*8:(t+1)*8],
                                                                stm[:, 0:1], None, op0=AL.is_ge),
                       reads=[b_oht, b_stm], writes=[b_oht])
                p.emit("tensor", lambda e, t=t, j=j: e.transpose(ps_h[j][0:8, 0:128], oh_tm[:, t*8:(t+1)*8], identf[:]),
                       reads=[b_oht, b_idf], writes=[b_psh[j]])
                p.emit("vector", lambda e, t=t, j=j: e.tensor_copy(oh_em[0:8, t*128:(t+1)*128], ps_h[j][0:8, 0:128]),
                       reads=[b_psh[j]], writes=[b_ohe])
            p.emit("sync", lambda e: e.dma_start(out=oh_dr, in_=oh_em[:]), reads=[b_ohe], dma="ohw")
            p.full_barrier()

        w1e, b_w1e = sbuf(nc, c2, "w1e", [128, 8 * 4096], BF16)
        w2e, b_w2e = sbuf(nc, c2, "w2e", [128, 32 * 1024], BF16)
        b1row, b_b1r = sbuf(nc, c2, "b1row", [1, 4096], BF16)
        onesr, b_onr = sbuf(nc, c2, "onesr", [1, 512], BF16)
        h_sb, b_h = sbuf(nc, c2, "h_sb", [128, 32 * 256], BF16)
        p.emit("vector", lambda e: e.memset(onesr[:], 1.0), writes=[b_onr])
        p.emit("vector", lambda e: e.memset(moe_sb[:], 0.0), writes=[b_moe])
        for ex in range(8):
            for dcl in range(8):
                p.emit("gpsimd", lambda e, ex=ex, dcl=dcl: e.dma_start(
                        out=w1e[:, dcl*4096:(dcl+1)*4096],
                        in_=w1[ex, dcl*128:(dcl+1)*128, :]),
                       writes=[b_w1e], dma="w1")
            p.emit("gpsimd", lambda e, ex=ex: e.dma_start(out=b1row[:], in_=b1[ex][None, :]),
                   writes=[b_b1r], dma="b1")
            for fcl in range(8):
                p.emit("gpsimd", lambda e, ex=ex, fcl=fcl: e.dma_start(
                        out=w2e[:].rearrange("p (t d) -> p t d", d=1024)[:, fcl*4:(fcl+1)*4, :],
                        in_=w2[ex].rearrange("(t p) d -> p t d", p=128)[:, fcl*4:(fcl+1)*4, :]),
                       writes=[b_w2e], dma="w2")
            for sb in range(8):
                i = sb % 2
                p.emit("gpsimd", lambda e, sb=sb, i=i: e.indirect_dma_start(out=xn_str[i][:], out_offset=None,
                       in_=xn1_fl, in_offset=IndirectOffsetOnAxis(ixn_sb[:, sb:sb+1], axis=0)),
                       reads=[b_ixn], writes=[b_xns[i]], dma=f"xsI{i}")
                for fc in range(32):
                    j = fc % 4
                    fns = [lambda e, fc=fc, dc=dc, j=j, i=i:
                           e.matmul(ps_h[j][:, 0:256], w1e[:, dc*4096 + fc*128 : dc*4096 + (fc+1)*128],
                                    xn_str[i][:, dc*256:(dc+1)*256],
                                    start=(dc == 0), stop=False) for dc in range(8)]
                    fns.append(lambda e, fc=fc, j=j:
                           e.matmul(ps_h[j][:, 0:256], b1row[0:1, fc*128:(fc+1)*128], onesr[0:1, 0:256],
                                    start=False, stop=True))
                    emit_group(p, "tensor", fns, reads=[b_w1e, b_xns[i], b_b1r, b_onr], writes=[b_psh[j]])
                    p.emit("scalar", lambda e, fc=fc, j=j: e.activation(h_sb[:, fc*256:(fc+1)*256], ps_h[j][:, 0:256], AF.Relu),
                           reads=[b_psh[j]], writes=[b_h])
                for st2 in range(2):
                    for dh in range(2):
                        j2 = (st2 * 2 + dh) % 3
                        emit_group(p, "tensor",
                                   [lambda e, st2=st2, dh=dh, fc=fc, j2=j2:
                                    e.matmul(ps_y[j2][:],
                                             h_sb[:, fc*256 + st2*128 : fc*256 + (st2+1)*128],
                                             w2e[:, fc*1024 + dh*512 : fc*1024 + (dh+1)*512],
                                             start=(fc == 0), stop=(fc == 31)) for fc in range(32)],
                                   reads=[b_h, b_w2e], writes=[b_psy[j2]])
                        tt = sb * 2 + st2
                        p.emit("vector", lambda e, tt=tt, dh=dh, j2=j2, ex=ex:
                               e.scalar_tensor_tensor(moe_sb[:, tt*1024 + dh*512 : tt*1024 + (dh+1)*512], ps_y[j2][:],
                                                      oh_tm[:, tt*8 + ex : tt*8 + ex + 1],
                                                      moe_sb[:, tt*1024 + dh*512 : tt*1024 + (dh+1)*512], op0=AL.mult, op1=AL.add),
                               reads=[b_psy[j2], b_oht, b_moe], writes=[b_moe])
        p.full_barrier()

    with ExitStack() as c3:
        b2_sb, b_b2 = sbuf(nc, c3, "b2_sb", [8, 1024], BF16)
        oh_em3, b_ohe3 = sbuf(nc, c3, "oh_em2", [8, 16 * 128], BF16)
        p.emit("sync", lambda e: e.dma_start(out=oh_em3[:], in_=oh_dr), writes=[b_ohe3], dma="ohr")
        xt, b_xt = sbuf(nc, c3, "xt", [128, 1024])
        a1t, b_a1t = sbuf(nc, c3, "a1t", [128, 1024], BF16)
        f_sb, b_f = sbuf(nc, c3, "f_sb", [128, 1024])
        ps_f = c3.enter_context(nc.psum_tensor("ps_f", [128, 1024], F32)); b_psf = Buf("ps_f")
        p.emit("gpsimd", lambda e: e.dma_start(out=b2_sb[:], in_=b2), writes=[b_b2], dma="b2l")
        for t in range(16):
            p.emit("gpsimd", lambda e, t=t: e.indirect_dma_start(out=xt[:], out_offset=None,
                   in_=x1_fl, in_offset=IndirectOffsetOnAxis(ix1_sb[:, t:t+1], axis=0)),
                   reads=[b_ix1], writes=[b_xt], dma="x1I")
            p.emit("sync", lambda e, t=t: e.dma_start(out=a1t[:], in_=attn1_dr[t*128:(t+1)*128, :]),
                   writes=[b_a1t], dma="a1l")
            for eh in range(2):
                p.emit("tensor", lambda e, t=t, eh=eh:
                       e.matmul(ps_f[:, eh*512:(eh+1)*512], oh_em3[0:8, t*128:(t+1)*128],
                                b2_sb[0:8, eh*512:(eh+1)*512], start=True, stop=True),
                       reads=[b_ohe3, b_b2], writes=[b_psf])
            p.emit("vector", lambda e: e.tensor_add(f_sb[:], xt[:], ps_f[:]), reads=[b_xt, b_psf], writes=[b_f])
            p.emit("vector", lambda e: e.tensor_add(f_sb[:], f_sb[:], a1t[:]), reads=[b_f, b_a1t], writes=[b_f])
            p.emit("vector", lambda e, t=t: e.tensor_add(f_sb[:], f_sb[:], moe_sb[:, t*1024:(t+1)*1024]),
                   reads=[b_f, b_moe], writes=[b_f])
            p.emit("sync", lambda e, t=t: e.dma_start(out=fin_o[t*128:(t+1)*128, :], in_=f_sb[:]),
                   reads=[b_f], dma="fo")
        s_, c_ = p.dma_sems["fo"]
        p.barrier("sync", [Tok(s_, c_)])
import numpy as np

_NCS = {}

def _get_nc(name, fn):
    if name not in _NCS:
        _NCS[name] = fn()
    return _NCS[name]


def _idx_tables(h):
    p64 = np.arange(64, dtype=np.int32)
    p128 = np.arange(128, dtype=np.int32)
    idx_q = (2 * p64 + h)[:, None].astype(np.int32)
    idx_xn = (16 * p128[:, None] + 8 * h + np.arange(8, dtype=np.int32)[None, :]).astype(np.int32)
    idx_x1 = (2048 * h + 128 * np.arange(16, dtype=np.int32)[None, :] + p128[:, None]).astype(np.int32)
    return idx_q, idx_xn, idx_x1


def kernel(L0_attn_x, L0_attn_w_qkv, L0_attn_w_proj, L0_attn_b_proj,
           L0_ffn_w_router, L0_ffn_w_block,
           L1_attn_Wq, L1_attn_Wkv, L1_attn_Wo, L1_attn_norm_w, L1_attn_norm_b,
           L1_attn_out_norm_w, L1_attn_out_norm_b,
           L1_ffn_gate_w, L1_ffn_gate_b, L1_ffn_expert_bias,
           L1_ffn_w1, L1_ffn_b1, L1_ffn_w2, L1_ffn_b2):
    x = np.ascontiguousarray(L0_attn_x)
    B, S, D = x.shape
    T = 2048
    cores = list(range(8))
    nc = _get_nc("fused", build_fused)
    maps = []
    for c in cores:
        b, h = c // 2, c % 2
        idx_q, idx_xn, idx_x1 = _idx_tables(h)
        maps.append(dict(
            x=np.ascontiguousarray(x[b]),
            w_qkv=L0_attn_w_qkv, w_router=L0_ffn_w_router,
            w_proj=L0_attn_w_proj, b_proj=L0_attn_b_proj, w_block=L0_ffn_w_block,
            norm_w=L1_attn_norm_w, norm_b=L1_attn_norm_b,
            Wq=L1_attn_Wq, Wkv=L1_attn_Wkv, Wo=L1_attn_Wo,
            onw=L1_attn_out_norm_w, onb=L1_attn_out_norm_b,
            gate_w=L1_ffn_gate_w, gate_b=L1_ffn_gate_b, expert_bias=L1_ffn_expert_bias,
            w1=L1_ffn_w1, b1=L1_ffn_b1, w2=L1_ffn_w2, b2=L1_ffn_b2,
            idx_q=idx_q, idx_xn=idx_xn, idx_x1=idx_x1))
    res = run_bass_kernel_spmd(nc, maps, core_ids=cores).results
    out = np.empty((B, S, D), np.float32)
    for c in cores:
        b, h = c // 2, c % 2
        out[b, T*h:T*h+T] = res[c]["fin_o"]
    return out


# version 07 (final)
# speedup vs baseline: 2.3970x; 1.0788x over previous
import numpy as np
from contextlib import ExitStack
import concourse.bass as bass
import concourse.mybir as mybir
from concourse.bass import IndirectOffsetOnAxis
from concourse.bass_utils import run_bass_kernel_spmd

F32 = mybir.dt.float32
BF16 = mybir.dt.bfloat16
I32 = mybir.dt.int32
AF = mybir.ActivationFunctionType
AL = mybir.AluOpType
AX = mybir.AxisListType

ENGS = ("sync", "scalar", "vector", "tensor", "gpsimd")


class Tok:
    __slots__ = ("item", "sem", "val")
    def __init__(self, a=None, b=None, c=None):
        if isinstance(a, Item):
            self.item = a; self.sem = b; self.val = c
        else:
            self.item = None
            if c is not None:
                self.sem = b; self.val = c
            else:
                self.sem = a; self.val = b


class Item:
    __slots__ = ("eng", "waits", "fn", "marked", "val", "dma")
    def __init__(self, eng, waits, fn, dma):
        self.eng = eng; self.waits = waits; self.fn = fn
        self.marked = False; self.val = None; self.dma = dma


class Buf:
    __slots__ = ("name", "writer", "readers")
    def __init__(self, name=""):
        self.name = name; self.writer = None; self.readers = []


class Prog:
    def __init__(self, nc):
        self.nc = nc
        self.items = {e: [] for e in ENGS}
        self.chain = {}
        self.dma_sems = {}
        self._ctx = None
        self.extra_waits = {e: [] for e in ENGS}

    def setup(self, ctx):
        self._ctx = ctx
        for e in ENGS:
            self.chain[e] = ctx.enter_context(self.nc.semaphore(f"ch_{e}"))

    def dma_sem(self, name):
        if name not in self.dma_sems:
            self.dma_sems[name] = [self._ctx.enter_context(self.nc.semaphore(f"ds_{name}")), 0]
        return self.dma_sems[name]

    def emit(self, eng, fn, reads=(), writes=(), waits=(), dma=None):
        wl = []
        for b in reads:
            if b.writer: wl.append(b.writer)
        for b in writes:
            if b.writer: wl.append(b.writer)
            wl.extend(b.readers)
        wl.extend(waits)
        item = Item(eng, wl, fn, dma)
        if dma is not None:
            ds = self.dma_sem(dma)
            ds[1] += 16
            tok = Tok(None, ds[0], ds[1])
            item.marked = True
            item.val = (ds[0], ds[1])
        else:
            tok = Tok(item)
        self.items[eng].append(item)
        for b in reads: b.readers.append(tok)
        for b in writes:
            b.writer = tok; b.readers = []
        return tok

    def barrier(self, eng, toks):
        item = Item(eng, list(toks), None, None)
        self.items[eng].append(item)

    def full_barrier(self):
        toks = []
        for e in ENGS:
            for it in reversed(self.items[e]):
                if it.fn is not None and it.dma is None:
                    toks.append(Tok(it)); break
        toks += [Tok(None, s, c) for (s, c) in self.dma_sems.values() if c > 0]
        for e in ENGS:
            self.barrier(e, toks)

    def play(self, block):
        for e in ENGS:
            for it in self.items[e]:
                for t in it.waits:
                    if t.item is not None:
                        t.item.marked = True
        for e in ENGS:
            cnt = 0
            for it in self.items[e]:
                if it.fn is not None and it.dma is None and it.marked:
                    cnt += 1
                    it.val = (self.chain[e], cnt)
        def mk(eng_name):
            waited = {}
            def body(e):
                for it in self.items[eng_name]:
                    byse = {}
                    for t in it.waits:
                        if t.item is not None:
                            sem, val = t.item.val[0], t.item.val[1]
                        else:
                            sem, val = t.sem, t.val
                        kk = id(sem)
                        if kk not in byse or byse[kk][1] < val:
                            byse[kk] = (sem, val)
                    for kk, (sem, val) in byse.items():
                        if waited.get(kk, -1) < val:
                            e.wait_ge(sem, val)
                            waited[kk] = val
                    if it.fn is not None:
                        inst = it.fn(e)
                        if it.dma is not None:
                            inst.then_inc(it.val[0], 16)
                        elif it.marked:
                            inst.then_inc(it.val[0], 1)
            return body
        for e in ENGS:
            if self.items[e]:
                getattr(block, e)(mk(e))


def emit_group(p, eng, fns, reads=(), writes=(), waits=(), ):
    def run(e):
        last = None
        for f in fns:
            last = f(e)
        return last
    return p.emit(eng, run, reads=reads, writes=writes, waits=waits)


_UNIQ = [0]

def _uniquify(nc):
    orig_sb, orig_ps = nc.sbuf_tensor, nc.psum_tensor
    def sb(name, shape, dt, **kw):
        _UNIQ[0] += 1
        return orig_sb(f"{name}_u{_UNIQ[0]}", shape, dt, **kw)
    def ps(name, shape, dt=F32, **kw):
        _UNIQ[0] += 1
        return orig_ps(f"{name}_u{_UNIQ[0]}", shape, dt, **kw)
    nc.sbuf_tensor = sb
    nc.psum_tensor = ps

def sbuf(nc, ctx, name, shape, dt=F32):
    return ctx.enter_context(nc.sbuf_tensor(name, shape, dt)), Buf(name)


def psum(nc, ctx, name, shape, dt=F32):
    return ctx.enter_context(nc.psum_tensor(name, shape, dt)), Buf(name)


def emit_identity(nc, p, b_id, ap):
    p.emit("gpsimd", lambda g: g.memset(ap, 0.0), writes=[b_id])
    p.emit("gpsimd", lambda g: g.affine_select(out=ap, in_=ap,
        compare_op=AL.not_equal, fill=1.0, base=0,
        pattern=[[-1, ap.shape[0]]], channel_multiplier=1),
        reads=[b_id], writes=[b_id])


def emit_ln(p, x_ap, out_ap, st_ap, sq_ap, b_x, b_out, b_st, b_sq, D, out2_ap=None, b_out2=None):
    p.emit("vector", lambda e: e.reduce_sum(st_ap[:, 0:1], x_ap, axis=AX.X), reads=[b_x], writes=[b_st])
    p.emit("vector", lambda e: e.tensor_scalar_mul(st_ap[:, 1:2], st_ap[:, 0:1], -1.0 / D), reads=[b_st], writes=[b_st])
    p.emit("scalar", lambda e: e.activation(sq_ap, x_ap, AF.Square, bias=st_ap[:, 1:2], scale=1.0,
                                            accum_out=st_ap[:, 2:3]), reads=[b_x, b_st], writes=[b_sq, b_st])
    p.emit("vector", lambda e: e.tensor_scalar(st_ap[:, 3:4], st_ap[:, 2:3], 1.0 / D, 1e-5, op0=AL.mult, op1=AL.add), reads=[b_st], writes=[b_st])
    p.emit("vector", lambda e: e.reciprocal(st_ap[:, 2:3], st_ap[:, 3:4]), reads=[b_st], writes=[b_st])
    p.emit("scalar", lambda e: e.activation(st_ap[:, 3:4], st_ap[:, 2:3], AF.Sqrt), reads=[b_st], writes=[b_st])
    p.emit("vector", lambda e: e.tensor_scalar(out_ap, x_ap, st_ap[:, 1:2], st_ap[:, 3:4],
                                               op0=AL.add, op1=AL.mult), reads=[b_x, b_st], writes=[b_out])
    if out2_ap is not None:
        p.emit("scalar", lambda e: e.activation(out2_ap, out_ap, AF.Copy), reads=[b_out], writes=[b_out2])


def build_a1():
    nc = bass.Bass("TRN2", target_bir_lowering=False)
    p = Prog(nc)
    T = 2048
    x = nc.dram_tensor("x", [T, 1024], F32, kind="ExternalInput").ap()
    w_qkv = nc.dram_tensor("w_qkv", [3072, 1024], F32, kind="ExternalInput").ap()
    w_router = nc.dram_tensor("w_router", [1024], F32, kind="ExternalInput").ap()
    q_o = nc.dram_tensor("q_o", [128, 8 * T], BF16, kind="ExternalOutput").ap()
    k_o = nc.dram_tensor("k_o", [128, 8 * T], BF16, kind="ExternalOutput").ap()
    v_o = nc.dram_tensor("v_o", [T, 1024], BF16, kind="ExternalOutput").ap()
    r_o = nc.dram_tensor("r_o", [1, T], F32, kind="ExternalOutput").ap()
    ctx = ExitStack()
    with ctx:
        p.setup(ctx)
        ident, b_id = sbuf(nc, ctx, "ident", [128, 128], BF16)
        xt = []; b_xt = []
        for i in range(2):
            t, b = sbuf(nc, ctx, f"xt{i}", [128, 1024]); xt.append(t); b_xt.append(b)
        st, b_st = sbuf(nc, ctx, "st", [128, 8])
        sq, b_sq = sbuf(nc, ctx, "sq", [128, 1024])
        lnb, b_lnb = sbuf(nc, ctx, "lnb", [128, 1024], BF16)
        xn_fm, b_xn = sbuf(nc, ctx, "xn_fm", [128, 8 * T], BF16)
        wn, b_wn = sbuf(nc, ctx, "wn", [128, 2 * 1024], BF16)
        wT, b_wT = sbuf(nc, ctx, "wT", [128, 8 * 3072], BF16)
        wr_row, b_wrr = sbuf(nc, ctx, "wr_row", [1, 1024], BF16)
        wr, b_wr = sbuf(nc, ctx, "wr", [128, 8], BF16)
        qk_sb, b_qk = sbuf(nc, ctx, "qk_sb", [128, 512], BF16)
        v_sb, b_vsb = sbuf(nc, ctx, "v_sb", [128, 1024], BF16)
        r_sb, b_rsb = sbuf(nc, ctx, "r_sb", [1, T])
        ps_t = []; b_pst = []
        for i in range(2):
            t = ctx.enter_context(nc.psum_tensor(f"ps_t{i}", [128, 128], BF16))
            ps_t.append(t); b_pst.append(Buf(f"ps_t{i}"))
        ps_m = []; b_psm = []
        for i in range(2):
            t = ctx.enter_context(nc.psum_tensor(f"ps_m{i}", [128, 512], F32))
            ps_m.append(t); b_psm.append(Buf(f"ps_m{i}"))
        block = ctx.enter_context(nc.Block())

        emit_identity(nc, p, b_id, ident[:])

        for t in range(16):
            i = t % 2
            p.emit("sync", lambda e, t=t, i=i: e.dma_start(out=xt[i][:], in_=x[t*128:(t+1)*128, :]),
                   writes=[b_xt[i]], dma=f"x{i}")
            emit_ln(p, xt[i][:], lnb[:], st[:], sq[:], b_xt[i], b_lnb, b_st, b_sq, 1024)
            for c in range(8):
                j = c % 2
                p.emit("tensor", lambda e, c=c, j=j: e.transpose(ps_t[j][:], lnb[:, c*128:(c+1)*128], ident[:]),
                       reads=[b_lnb, b_id], writes=[b_pst[j]])
                p.emit("vector", lambda e, c=c, j=j, t=t: e.tensor_copy(xn_fm[:, c*T + t*128 : c*T + (t+1)*128], ps_t[j][:]),
                       reads=[b_pst[j]], writes=[b_xn])

        for t in range(24):
            i = t % 2
            p.emit("gpsimd", lambda e, t=t, i=i: e.dma_start(out=wn[:, i*1024:(i+1)*1024], in_=w_qkv[t*128:(t+1)*128, :]),
                   writes=[b_wn], dma=f"w{i}")
            for c in range(8):
                j = c % 2
                p.emit("tensor", lambda e, c=c, i=i, j=j: e.transpose(ps_t[j][:], wn[:, i*1024 + c*128 : i*1024 + (c+1)*128], ident[:]),
                       reads=[b_wn, b_id], writes=[b_pst[j]])
                p.emit("vector", lambda e, c=c, j=j, t=t: e.tensor_copy(wT[:, c*3072 + t*128 : c*3072 + (t+1)*128], ps_t[j][:]),
                       reads=[b_pst[j]], writes=[b_wT])

        p.emit("gpsimd", lambda e: e.dma_start(out=wr_row[:], in_=w_router[None, :]), writes=[b_wrr], dma="wr")
        for c in range(8):
            j = c % 2
            p.emit("tensor", lambda e, c=c, j=j: e.transpose(ps_t[j][0:128, 0:1], wr_row[0:1, c*128:(c+1)*128], ident[0:1, 0:1]),
                   reads=[b_wrr, b_id], writes=[b_pst[j]])
            p.emit("vector", lambda e, c=c, j=j: e.tensor_copy(wr[:, c:c+1], ps_t[j][0:128, 0:1]),
                   reads=[b_pst[j]], writes=[b_wr])

        for which, off, out_t in (("q", 0, q_o), ("k", 1024, k_o)):
            for f in range(8):
                for sc in range(4):
                    j = (f * 4 + sc) % 2
                    for ec in range(8):
                        p.emit("tensor", lambda e, f=f, sc=sc, ec=ec, j=j, off=off:
                               e.matmul(ps_m[j][:], wT[:, ec*3072 + off + f*128 : ec*3072 + off + (f+1)*128],
                                        xn_fm[:, ec*T + sc*512 : ec*T + (sc+1)*512],
                                        start=(ec == 0), stop=(ec == 7)),
                               reads=[b_wT, b_xn], writes=[b_psm[j]])
                    p.emit("vector", lambda e, j=j: e.tensor_copy(qk_sb[:], ps_m[j][:]),
                           reads=[b_psm[j]], writes=[b_qk])
                    p.emit("sync", lambda e, f=f, sc=sc, out_t=out_t:
                           e.dma_start(out=out_t[:, f*T + sc*512 : f*T + (sc+1)*512], in_=qk_sb[:]),
                           reads=[b_qk], dma="qk_o")

        for t in range(16):
            for vh in range(2):
                j = (t * 2 + vh) % 2
                for ec in range(8):
                    p.emit("tensor", lambda e, t=t, vh=vh, ec=ec, j=j:
                           e.matmul(ps_m[j][:], xn_fm[:, ec*T + t*128 : ec*T + (t+1)*128],
                                    wT[:, ec*3072 + 2048 + vh*512 : ec*3072 + 2048 + (vh+1)*512],
                                    start=(ec == 0), stop=(ec == 7)),
                           reads=[b_wT, b_xn], writes=[b_psm[j]])
                p.emit("vector", lambda e, vh=vh, j=j: e.tensor_copy(v_sb[:, vh*512:(vh+1)*512], ps_m[j][:]),
                       reads=[b_psm[j]], writes=[b_vsb])
            p.emit("sync", lambda e, t=t: e.dma_start(out=v_o[t*128:(t+1)*128, :], in_=v_sb[:]),
                   reads=[b_vsb], dma="v_o")

        for sc in range(4):
            j = sc % 2
            for ec in range(8):
                p.emit("tensor", lambda e, sc=sc, ec=ec, j=j:
                       e.matmul(ps_m[j][0:1, :], wr[:, ec:ec+1],
                                xn_fm[:, ec*T + sc*512 : ec*T + (sc+1)*512],
                                start=(ec == 0), stop=(ec == 7)),
                       reads=[b_wr, b_xn], writes=[b_psm[j]])
            p.emit("vector", lambda e, sc=sc, j=j: e.tensor_copy(r_sb[0:1, sc*512:(sc+1)*512], ps_m[j][0:1, :]),
                   reads=[b_psm[j]], writes=[b_rsb])
        p.emit("sync", lambda e: e.dma_start(out=r_o, in_=r_sb[:]), reads=[b_rsb], dma="r_o")

        fin = []
        for n in ("qk_o", "v_o", "r_o"):
            s, c = p.dma_sems[n]
            fin.append(Tok(s, c))
        p.barrier("sync", fin)
        p.play(block)
    return nc


def build_a2a():
    nc = bass.Bass("TRN2", target_bir_lowering=False)
    p = Prog(nc)
    T, S = 2048, 4096
    q = nc.dram_tensor("q", [128, 8 * T], BF16, kind="ExternalInput").ap()
    k = nc.dram_tensor("k", [128, 8 * S], BF16, kind="ExternalInput").ap()
    v = nc.dram_tensor("v", [S, 1024], BF16, kind="ExternalInput").ap()
    o = nc.dram_tensor("o", [T, 1024], BF16, kind="ExternalOutput").ap()
    ctx = ExitStack()
    with ctx:
        p.setup(ctx)
        ident, b_id = sbuf(nc, ctx, "ident", [128, 128], BF16)
        q_sb, b_q = sbuf(nc, ctx, "q_sb", [128, 8 * T], BF16)
        k_sb, b_k = sbuf(nc, ctx, "k_sb", [128, 8 * S], BF16)
        v_sb, b_v = sbuf(nc, ctx, "v_sb", [128, 32 * 1024], BF16)
        sc_sb, b_sc = sbuf(nc, ctx, "sc_sb", [128, S])
        pr_sb, b_pr = sbuf(nc, ctx, "pr_sb", [128, S], BF16)
        pt_sb, b_pt = sbuf(nc, ctx, "pt_sb", [128, 32 * 128], BF16)
        st, b_st = sbuf(nc, ctx, "st", [128, 8])
        o_sb = []; b_o = []
        for i in range(2):
            t, b = sbuf(nc, ctx, f"o_sb{i}", [128, 1024], BF16); o_sb.append(t); b_o.append(b)
        ps_s = []; b_ps = []
        for i in range(2):
            t = ctx.enter_context(nc.psum_tensor(f"ps_s{i}", [128, 512], F32))
            ps_s.append(t); b_ps.append(Buf(f"ps_s{i}"))
        ps_t = []; b_pst = []
        for i in range(2):
            t = ctx.enter_context(nc.psum_tensor(f"ps_t{i}", [128, 128], BF16))
            ps_t.append(t); b_pst.append(Buf(f"ps_t{i}"))
        ps_av = []; b_pav = []
        for i in range(2):
            t = ctx.enter_context(nc.psum_tensor(f"ps_av{i}", [128, 512], F32))
            ps_av.append(t); b_pav.append(Buf(f"ps_av{i}"))
        block = ctx.enter_context(nc.Block())

        emit_identity(nc, p, b_id, ident[:])
        p.emit("sync", lambda e: e.dma_start(out=q_sb[:], in_=q), writes=[b_q], dma="q")
        p.emit("sync", lambda e: e.dma_start(out=k_sb[:], in_=k), writes=[b_k], dma="k")
        p.emit("sync", lambda e: e.dma_start(out=v_sb[:].rearrange("p (t d) -> p t d", d=1024),
                                             in_=v.rearrange("(t p) d -> p t d", p=128)),
               writes=[b_v], dma="v")

        for qt in range(16):
            for kc in range(8):
                j = kc % 2
                for qf in range(8):
                    p.emit("tensor", lambda e, qt=qt, kc=kc, qf=qf, j=j:
                           e.matmul(ps_s[j][:], q_sb[:, qf*T + qt*128 : qf*T + (qt+1)*128],
                                    k_sb[:, qf*S + kc*512 : qf*S + (kc+1)*512],
                                    start=(qf == 0), stop=(qf == 7)),
                           reads=[b_q, b_k], writes=[b_ps[j]])
                p.emit("vector", lambda e, kc=kc, j=j: e.tensor_copy(sc_sb[:, kc*512:(kc+1)*512], ps_s[j][:]),
                       reads=[b_ps[j]], writes=[b_sc])
            p.emit("vector", lambda e: e.reduce_max(st[:, 0:1], sc_sb[:], axis=AX.X, negate=True),
                   reads=[b_sc], writes=[b_st])
            p.emit("vector", lambda e: e.tensor_scalar_mul(st[:, 1:2], st[:, 0:1], 0.125),
                   reads=[b_st], writes=[b_st])
            p.emit("scalar", lambda e: e.activation(pr_sb[:], sc_sb[:], AF.Exp, bias=st[:, 1:2],
                                                    scale=0.125, accum_out=st[:, 2:3]),
                   reads=[b_sc, b_st], writes=[b_pr, b_st])
            p.emit("vector", lambda e: e.reciprocal(st[:, 3:4], st[:, 2:3]), reads=[b_st], writes=[b_st])
            for kt in range(32):
                j = kt % 2
                p.emit("tensor", lambda e, kt=kt, j=j: e.transpose(ps_t[j][:], pr_sb[:, kt*128:(kt+1)*128], ident[:]),
                       reads=[b_pr, b_id], writes=[b_pst[j]])
                p.emit("vector", lambda e, kt=kt, j=j: e.tensor_copy(pt_sb[:, kt*128:(kt+1)*128], ps_t[j][:]),
                       reads=[b_pst[j]], writes=[b_pt])
            ob = qt % 2
            for vh in range(2):
                j = vh
                for kt in range(32):
                    p.emit("tensor", lambda e, kt=kt, vh=vh, j=j:
                           e.matmul(ps_av[j][:], pt_sb[:, kt*128:(kt+1)*128],
                                    v_sb[:, kt*1024 + vh*512 : kt*1024 + (vh+1)*512],
                                    start=(kt == 0), stop=(kt == 31)),
                           reads=[b_pt, b_v], writes=[b_pav[j]])
                p.emit("vector", lambda e, vh=vh, j=j, ob=ob:
                       e.tensor_scalar(o_sb[ob][:, vh*512:(vh+1)*512], ps_av[j][:], st[:, 3:4], None, op0=AL.mult),
                       reads=[b_pav[j], b_st], writes=[b_o[ob]])
            p.emit("sync", lambda e, qt=qt, ob=ob: e.dma_start(out=o[qt*128:(qt+1)*128, :], in_=o_sb[ob][:]),
                   reads=[b_o[ob]], dma="o")
        s_, c_ = p.dma_sems["o"]
        p.barrier("sync", [Tok(s_, c_)])
        p.play(block)
    return nc


def build_a2b(TOPK=3072, NIT=36):
    nc = bass.Bass("TRN2", target_bir_lowering=False)
    p = Prog(nc)
    T = 2048
    og = nc.dram_tensor("og", [T, 1024], BF16, kind="ExternalInput").ap()
    x_sig = nc.dram_tensor("x_sig", [T, 1024], F32, kind="ExternalInput").ap()
    rsc = nc.dram_tensor("rsc", [4096], F32, kind="ExternalInput").ap()
    rsc_sig = nc.dram_tensor("rsc_sig", [2048], F32, kind="ExternalInput").ap()
    w_proj = nc.dram_tensor("w_proj", [1024, 1024], F32, kind="ExternalInput").ap()
    b_proj = nc.dram_tensor("b_proj", [1024], F32, kind="ExternalInput").ap()
    w_block = nc.dram_tensor("w_block", [1024, 1024], F32, kind="ExternalInput").ap()
    x1_o = nc.dram_tensor("x1_o", [T, 1024], F32, kind="ExternalOutput").ap()
    ctx = ExitStack()
    with ctx:
        p.setup(ctx)
        ident, b_id = sbuf(nc, ctx, "ident", [128, 128], BF16)
        identf, b_idf = sbuf(nc, ctx, "identf", [128, 128])
        ones128, b_o128 = sbuf(nc, ctx, "ones128", [128, 128])
        og_sb, b_og = sbuf(nc, ctx, "og_sb", [128, 16 * 1024], BF16)
        wpn, b_wpn = sbuf(nc, ctx, "wpn", [128, 2 * 1024], BF16)
        wpT, b_wpT = sbuf(nc, ctx, "wpT", [128, 8 * 1024], BF16)
        wb_sb, b_wb = sbuf(nc, ctx, "wb_sb", [128, 8 * 1024], BF16)
        bp_row, b_bpr = sbuf(nc, ctx, "bp_row", [1, 1024])
        bpb, b_bpb = sbuf(nc, ctx, "bpb", [128, 1024])
        rs_sb, b_rs = sbuf(nc, ctx, "rs_sb", [128, 32])
        rsig_sb, b_rsig = sbuf(nc, ctx, "rsig_sb", [1, 2048])
        cmp_sb, b_cmp = sbuf(nc, ctx, "cmp_sb", [128, 32])
        thr, b_thr = sbuf(nc, ctx, "thr", [128, 1])
        sgn, b_sgn = sbuf(nc, ctx, "sgn", [128, 1])
        step_tab, b_stp = sbuf(nc, ctx, "step_tab", [128, NIT])
        st4, b_st4 = sbuf(nc, ctx, "st4", [128, 8])
        row_sc, b_rsc2 = sbuf(nc, ctx, "row_sc", [1, 128])
        mcol, b_mc = sbuf(nc, ctx, "mcol", [128, 2])
        xt, b_xt = sbuf(nc, ctx, "xt", [128, 1024])
        st, b_st = sbuf(nc, ctx, "st", [128, 8])
        sq, b_sq = sbuf(nc, ctx, "sq", [128, 1024])
        xn_f, b_xnf = sbuf(nc, ctx, "xn_f", [128, 1024])
        xn_b, b_xnb = sbuf(nc, ctx, "xn_b", [128, 1024], BF16)
        xnfm, b_xnfm = sbuf(nc, ctx, "xnfm", [128, 8 * 128], BF16)
        y_sb, b_y = sbuf(nc, ctx, "y_sb", [128, 1024])
        a3_sb, b_a3 = sbuf(nc, ctx, "a3_sb", [128, 1024])
        d_sb, b_d = sbuf(nc, ctx, "d_sb", [128, 1024])
        x1_sb, b_x1 = sbuf(nc, ctx, "x1_sb", [128, 1024])
        ps_y = ctx.enter_context(nc.psum_tensor("ps_y", [128, 1024], F32)); b_psy = Buf("ps_y")
        ps_p = ctx.enter_context(nc.psum_tensor("ps_p", [128, 1024], F32)); b_psp = Buf("ps_p")
        ps_c = ctx.enter_context(nc.psum_tensor("ps_c", [128, 128], F32)); b_psc = Buf("ps_c")
        ps_t = []; b_pst = []
        for i in range(2):
            t = ctx.enter_context(nc.psum_tensor(f"ps_t{i}", [128, 128], BF16))
            ps_t.append(t); b_pst.append(Buf(f"ps_t{i}"))
        block = ctx.enter_context(nc.Block())

        emit_identity(nc, p, b_id, ident[:])
        p.emit("gpsimd", lambda g: g.memset(identf[:], 0.0), writes=[b_idf])
        p.emit("gpsimd", lambda g: g.affine_select(out=identf[:], in_=identf[:],
            compare_op=AL.not_equal, fill=1.0, base=0, pattern=[[-1, 128]], channel_multiplier=1),
            reads=[b_idf], writes=[b_idf])
        p.emit("vector", lambda e: e.memset(ones128[:], 1.0), writes=[b_o128])

        p.emit("sync", lambda e: e.dma_start(out=og_sb[:].rearrange("p (t d) -> p t d", d=1024),
                                             in_=og.rearrange("(t p) d -> p t d", p=128)),
               writes=[b_og], dma="og")
        p.emit("gpsimd", lambda e: e.dma_start(out=wb_sb[:].rearrange("p (t d) -> p t d", d=1024),
                                               in_=w_block.rearrange("(t p) d -> p t d", p=128)),
               writes=[b_wb], dma="wb")
        p.emit("sync", lambda e: e.dma_start(out=rs_sb[:], in_=rsc.rearrange("(q j) -> q j", j=32)),
               writes=[b_rs], dma="rs")
        p.emit("sync", lambda e: e.dma_start(out=rsig_sb[:], in_=rsc_sig[None, :]), writes=[b_rsig], dma="rs")
        p.emit("sync", lambda e: e.dma_start(out=bp_row[:], in_=b_proj[None, :]), writes=[b_bpr], dma="rs")
        for t in range(8):
            i = t % 2
            p.emit("gpsimd", lambda e, t=t, i=i: e.dma_start(out=wpn[:, i*1024:(i+1)*1024],
                                                             in_=w_proj[t*128:(t+1)*128, :]),
                   writes=[b_wpn], dma=f"wp{i}")
            for c in range(8):
                j = c % 2
                p.emit("tensor", lambda e, c=c, i=i, j=j: e.transpose(ps_t[j][:], wpn[:, i*1024 + c*128 : i*1024 + (c+1)*128], ident[:]),
                       reads=[b_wpn, b_id], writes=[b_pst[j]])
                p.emit("vector", lambda e, c=c, j=j, t=t: e.tensor_copy(wpT[:, c*1024 + t*128 : c*1024 + (t+1)*128], ps_t[j][:]),
                       reads=[b_pst[j]], writes=[b_wpT])
        for eh in range(2):
            p.emit("tensor", lambda e, eh=eh: e.matmul(ps_p[:, eh*512:(eh+1)*512], ones128[0:1, :],
                                                       bp_row[0:1, eh*512:(eh+1)*512], start=True, stop=True),
                   reads=[b_o128, b_bpr], writes=[b_psp])
        p.emit("vector", lambda e: e.tensor_copy(bpb[:], ps_p[:]), reads=[b_psp], writes=[b_bpb])

        p.emit("vector", lambda e: e.reduce_max(st4[:, 0:1], rs_sb[:], axis=AX.X, apply_absolute_value=True),
               reads=[b_rs], writes=[b_st4])
        p.emit("tensor", lambda e: e.transpose(ps_c[0:1, 0:128], st4[:, 0:1], identf[:]),
               reads=[b_st4, b_idf], writes=[b_psc])
        p.emit("vector", lambda e: e.tensor_copy(row_sc[:], ps_c[0:1, 0:128]), reads=[b_psc], writes=[b_rsc2])
        p.emit("vector", lambda e: e.reduce_max(row_sc[0:1, 0:1], row_sc[:], axis=AX.X),
               reads=[b_rsc2], writes=[b_rsc2])
        p.emit("tensor", lambda e: e.matmul(ps_c[:, 0:1], ones128[0:1, :], row_sc[0:1, 0:1], start=True, stop=True),
               reads=[b_o128, b_rsc2], writes=[b_psc])
        p.emit("vector", lambda e: e.tensor_copy(st4[:, 1:2], ps_c[:, 0:1]), reads=[b_psc], writes=[b_st4])
        for i in range(NIT):
            p.emit("vector", lambda e, i=i: e.tensor_scalar_mul(step_tab[:, i:i+1], st4[:, 1:2], 2.0 ** (-i)),
                   reads=[b_st4], writes=[b_stp])
        p.emit("vector", lambda e: e.memset(thr[:], 0.0), writes=[b_thr])
        for i in range(NIT):
            p.emit("vector", lambda e: e.tensor_scalar(cmp_sb[:], rs_sb[:], thr[:, 0:1], None, op0=AL.is_gt),
                   reads=[b_rs, b_thr], writes=[b_cmp])
            p.emit("vector", lambda e: e.reduce_sum(st4[:, 2:3], cmp_sb[:], axis=AX.X),
                   reads=[b_cmp], writes=[b_st4])
            p.emit("tensor", lambda e: e.matmul(ps_c[:, 1:2], ones128[:], st4[:, 2:3], start=True, stop=True),
                   reads=[b_o128, b_st4], writes=[b_psc])
            p.emit("vector", lambda e: e.tensor_scalar(st4[:, 3:4], ps_c[:, 1:2], float(-TOPK), None, op0=AL.add),
                   reads=[b_psc], writes=[b_st4])
            p.emit("scalar", lambda e: e.activation(sgn[:], st4[:, 3:4], AF.Sign),
                   reads=[b_st4], writes=[b_sgn])
            p.emit("vector", lambda e, i=i: e.scalar_tensor_tensor(thr[:], sgn[:], step_tab[:, i:i+1], thr[:],
                                                                   op0=AL.mult, op1=AL.add),
                   reads=[b_sgn, b_stp, b_thr], writes=[b_thr])

        for t in range(16):
            gg, hc = t // 8, t % 8
            p.emit("sync", lambda e, t=t: e.dma_start(out=xt[:], in_=x_sig[t*128:(t+1)*128, :]),
                   writes=[b_xt], dma="xs")
            emit_ln(p, xt[:], xn_f[:], st[:], sq[:], b_xt, b_xnf, b_st, b_sq, 1024)
            p.emit("scalar", lambda e: e.activation(xn_b[:], xn_f[:], AF.Copy), reads=[b_xnf], writes=[b_xnb])
            for c in range(8):
                j = c % 2
                p.emit("tensor", lambda e, c=c, j=j: e.transpose(ps_t[j][:], xn_b[:, c*128:(c+1)*128], ident[:]),
                       reads=[b_xnb, b_id], writes=[b_pst[j]])
                p.emit("vector", lambda e, c=c, j=j: e.tensor_copy(xnfm[:, c*128:(c+1)*128], ps_t[j][:]),
                       reads=[b_pst[j]], writes=[b_xnfm])
            for eh in range(2):
                for dc in range(8):
                    p.emit("tensor", lambda e, eh=eh, dc=dc:
                           e.matmul(ps_y[:, eh*512:(eh+1)*512], xnfm[:, dc*128:(dc+1)*128],
                                    wb_sb[:, dc*1024 + eh*512 : dc*1024 + (eh+1)*512],
                                    start=(dc == 0), stop=(dc == 7)),
                           reads=[b_xnfm, b_wb], writes=[b_psy])
            p.emit("scalar", lambda e: e.activation(y_sb[:], ps_y[:], AF.Gelu_apprx_tanh),
                   reads=[b_psy], writes=[b_y])
            for eh in range(2):
                for jc in range(8):
                    p.emit("tensor", lambda e, eh=eh, jc=jc, gg=gg, hc=hc:
                           e.matmul(ps_p[:, eh*512:(eh+1)*512],
                                    og_sb[:, (8*gg + jc)*1024 + hc*128 : (8*gg + jc)*1024 + (hc+1)*128],
                                    wpT[:, jc*1024 + eh*512 : jc*1024 + (eh+1)*512],
                                    start=(jc == 0), stop=(jc == 7)),
                           reads=[b_og, b_wpT], writes=[b_psp])
            p.emit("vector", lambda e: e.tensor_add(a3_sb[:], ps_p[:], bpb[:]), reads=[b_psp, b_bpb], writes=[b_a3])
            p.emit("tensor", lambda e, t=t: e.transpose(ps_c[0:128, 2:3], rsig_sb[0:1, t*128:(t+1)*128], identf[0:1, 0:1]),
                   reads=[b_rsig, b_idf], writes=[b_psc])
            p.emit("vector", lambda e: e.tensor_copy(mcol[:, 0:1], ps_c[0:128, 2:3]), reads=[b_psc], writes=[b_mc])
            p.emit("vector", lambda e: e.tensor_scalar(mcol[:, 1:2], mcol[:, 0:1], thr[:, 0:1], None, op0=AL.is_gt),
                   reads=[b_mc, b_thr], writes=[b_mc])
            p.emit("vector", lambda e: e.tensor_sub(d_sb[:], y_sb[:], xn_f[:]), reads=[b_y, b_xnf], writes=[b_d])
            p.emit("vector", lambda e: e.scalar_tensor_tensor(d_sb[:], d_sb[:], mcol[:, 1:2], xn_f[:],
                                                              op0=AL.mult, op1=AL.add),
                   reads=[b_d, b_mc, b_xnf], writes=[b_d])
            p.emit("vector", lambda e: e.tensor_add(x1_sb[:], xt[:], a3_sb[:]), reads=[b_xt, b_a3], writes=[b_x1])
            p.emit("vector", lambda e: e.tensor_add(x1_sb[:], x1_sb[:], d_sb[:]), reads=[b_x1, b_d], writes=[b_x1])
            p.emit("sync", lambda e, t=t: e.dma_start(out=x1_o[t*128:(t+1)*128, :], in_=x1_sb[:]),
                   reads=[b_x1], dma="x1")
        s_, c_ = p.dma_sems["x1"]
        p.barrier("sync", [Tok(s_, c_)])
        p.play(block)
    return nc


def build_b1():
    nc = bass.Bass("TRN2", target_bir_lowering=False)
    p = Prog(nc)
    T = 2048
    x1 = nc.dram_tensor("x1", [T, 1024], F32, kind="ExternalInput").ap()
    norm_w = nc.dram_tensor("norm_w", [1024], F32, kind="ExternalInput").ap()
    norm_b = nc.dram_tensor("norm_b", [1024], F32, kind="ExternalInput").ap()
    Wq = nc.dram_tensor("Wq", [1024, 512], F32, kind="ExternalInput").ap()
    Wkv = nc.dram_tensor("Wkv", [1024, 128], F32, kind="ExternalInput").ap()
    xn1_tm_o = nc.dram_tensor("xn1_tm_o", [T, 1024], BF16, kind="ExternalOutput").ap()
    xn1_fm_o = nc.dram_tensor("xn1_fm_o", [128, 8 * T], BF16, kind="ExternalOutput").ap()
    q_o = nc.dram_tensor("q_o", [64, 8 * T], BF16, kind="ExternalOutput").ap()
    k_o = nc.dram_tensor("k_o", [64, T], BF16, kind="ExternalOutput").ap()
    v_o = nc.dram_tensor("v_o", [T, 64], BF16, kind="ExternalOutput").ap()
    ctx = ExitStack()
    with ctx:
        p.setup(ctx)
        ident, b_id = sbuf(nc, ctx, "ident", [128, 128], BF16)
        ones1, b_on = sbuf(nc, ctx, "ones1", [1, 128])
        nw_row, b_nwr = sbuf(nc, ctx, "nw_row", [1, 1024])
        nb_row, b_nbr = sbuf(nc, ctx, "nb_row", [1, 1024])
        nwb, b_nwb = sbuf(nc, ctx, "nwb", [128, 1024])
        nbb, b_nbb = sbuf(nc, ctx, "nbb", [128, 1024])
        wq_sb, b_wq = sbuf(nc, ctx, "wq_sb", [128, 8 * 512], BF16)
        wkv_sb, b_wkv = sbuf(nc, ctx, "wkv_sb", [128, 8 * 128], BF16)
        xt, b_xt = sbuf(nc, ctx, "xt", [128, 1024])
        st, b_st = sbuf(nc, ctx, "st", [128, 8])
        sq, b_sq = sbuf(nc, ctx, "sq", [128, 1024])
        xn1_f, b_xn1f = sbuf(nc, ctx, "xn1_f", [128, 1024])
        xn1_b, b_xn1b = sbuf(nc, ctx, "xn1_b", [128, 1024], BF16)
        xn1n, b_xn1n = sbuf(nc, ctx, "xn1n", [128, 1024])
        xa_b, b_xab = sbuf(nc, ctx, "xa_b", [128, 1024], BF16)
        xa_fm, b_xafm = sbuf(nc, ctx, "xa_fm", [128, 8 * T], BF16)
        xn1_fm, b_xn1fm = sbuf(nc, ctx, "xn1_fm", [128, 8 * T], BF16)
        ev_sb, b_ev = sbuf(nc, ctx, "ev_sb", [128, 512], BF16)
        v_sb, b_vsb = sbuf(nc, ctx, "v_sb", [128, 64], BF16)
        ps_m = []; b_psm = []
        for i in range(2):
            t = ctx.enter_context(nc.psum_tensor(f"ps_m{i}", [128, 512], F32))
            ps_m.append(t); b_psm.append(Buf(f"ps_m{i}"))
        ps_t = []; b_pst = []
        for i in range(2):
            t = ctx.enter_context(nc.psum_tensor(f"ps_t{i}", [128, 128], BF16))
            ps_t.append(t); b_pst.append(Buf(f"ps_t{i}"))
        ps_b = ctx.enter_context(nc.psum_tensor("ps_b", [128, 1024], F32)); b_psb = Buf("ps_b")
        block = ctx.enter_context(nc.Block())

        emit_identity(nc, p, b_id, ident[:])
        p.emit("vector", lambda e: e.memset(ones1[:], 1.0), writes=[b_on])
        p.emit("sync", lambda e: e.dma_start(out=nw_row[:], in_=norm_w[None, :]), writes=[b_nwr], dma="n")
        p.emit("sync", lambda e: e.dma_start(out=nb_row[:], in_=norm_b[None, :]), writes=[b_nbr], dma="n2")
        p.emit("gpsimd", lambda e: e.dma_start(out=wq_sb[:].rearrange("p (t d) -> p t d", d=512),
                                               in_=Wq.rearrange("(t p) d -> p t d", p=128)),
               writes=[b_wq], dma="wq")
        p.emit("gpsimd", lambda e: e.dma_start(out=wkv_sb[:].rearrange("p (t d) -> p t d", d=128),
                                               in_=Wkv.rearrange("(t p) d -> p t d", p=128)),
               writes=[b_wkv], dma="wkv")
        for eh in range(2):
            p.emit("tensor", lambda e, eh=eh: e.matmul(ps_b[:, eh*512:(eh+1)*512], ones1[:],
                                                       nw_row[0:1, eh*512:(eh+1)*512], start=True, stop=True),
                   reads=[b_on, b_nwr], writes=[b_psb])
        p.emit("vector", lambda e: e.tensor_copy(nwb[:], ps_b[:]), reads=[b_psb], writes=[b_nwb])
        for eh in range(2):
            p.emit("tensor", lambda e, eh=eh: e.matmul(ps_b[:, eh*512:(eh+1)*512], ones1[:],
                                                       nb_row[0:1, eh*512:(eh+1)*512], start=True, stop=True),
                   reads=[b_on, b_nbr], writes=[b_psb])
        p.emit("vector", lambda e: e.tensor_copy(nbb[:], ps_b[:]), reads=[b_psb], writes=[b_nbb])

        for t in range(16):
            p.emit("sync", lambda e, t=t: e.dma_start(out=xt[:], in_=x1[t*128:(t+1)*128, :]),
                   writes=[b_xt], dma="x")
            emit_ln(p, xt[:], xn1_f[:], st[:], sq[:], b_xt, b_xn1f, b_st, b_sq, 1024)
            p.emit("scalar", lambda e: e.activation(xn1_b[:], xn1_f[:], AF.Copy), reads=[b_xn1f], writes=[b_xn1b])
            p.emit("sync", lambda e, t=t: e.dma_start(out=xn1_tm_o[t*128:(t+1)*128, :], in_=xn1_b[:]),
                   reads=[b_xn1b], dma="xo")
            emit_ln(p, xn1_f[:], xn1n[:], st[:], sq[:], b_xn1f, b_xn1n, b_st, b_sq, 1024)
            p.emit("vector", lambda e: e.tensor_mul(xn1n[:], xn1n[:], nwb[:]), reads=[b_xn1n, b_nwb], writes=[b_xn1n])
            p.emit("vector", lambda e: e.tensor_add(xa_b[:], xn1n[:], nbb[:]), reads=[b_xn1n, b_nbb], writes=[b_xab])
            for c in range(8):
                j = c % 2
                p.emit("tensor", lambda e, c=c, j=j: e.transpose(ps_t[j][:], xa_b[:, c*128:(c+1)*128], ident[:]),
                       reads=[b_xab, b_id], writes=[b_pst[j]])
                p.emit("vector", lambda e, c=c, j=j, t=t: e.tensor_copy(xa_fm[:, c*T + t*128 : c*T + (t+1)*128], ps_t[j][:]),
                       reads=[b_pst[j]], writes=[b_xafm])
            for c in range(8):
                j = c % 2
                p.emit("tensor", lambda e, c=c, j=j: e.transpose(ps_t[j][:], xn1_b[:, c*128:(c+1)*128], ident[:]),
                       reads=[b_xn1b, b_id], writes=[b_pst[j]])
                p.emit("vector", lambda e, c=c, j=j, t=t: e.tensor_copy(xn1_fm[:, c*T + t*128 : c*T + (t+1)*128], ps_t[j][:]),
                       reads=[b_pst[j]], writes=[b_xn1fm])
        p.emit("sync", lambda e: e.dma_start(out=xn1_fm_o, in_=xn1_fm[:]), reads=[b_xn1fm], dma="xo")

        for ic in range(4):
            for sc in range(4):
                j = (ic * 4 + sc) % 2
                for dc in range(8):
                    p.emit("tensor", lambda e, ic=ic, sc=sc, dc=dc, j=j:
                           e.matmul(ps_m[j][:], wq_sb[:, dc*512 + ic*128 : dc*512 + (ic+1)*128],
                                    xa_fm[:, dc*T + sc*512 : dc*T + (sc+1)*512],
                                    start=(dc == 0), stop=(dc == 7)),
                           reads=[b_wq, b_xafm], writes=[b_psm[j]])
                for hh in range(2):
                    p.emit("vector", lambda e, hh=hh, j=j: e.tensor_copy(ev_sb[0:64, :], ps_m[j][hh*64:hh*64+64, :]),
                           reads=[b_psm[j]], writes=[b_ev])
                    p.emit("sync", lambda e, ic=ic, sc=sc, hh=hh:
                           e.dma_start(out=q_o[:, (2*ic+hh)*T + sc*512 : (2*ic+hh)*T + (sc+1)*512], in_=ev_sb[0:64, :]),
                           reads=[b_ev], dma="qo")
        for sc in range(4):
            j = sc % 2
            for dc in range(8):
                p.emit("tensor", lambda e, sc=sc, dc=dc, j=j:
                       e.matmul(ps_m[j][0:64, :], wkv_sb[:, dc*128 : dc*128 + 64],
                                xa_fm[:, dc*T + sc*512 : dc*T + (sc+1)*512],
                                start=(dc == 0), stop=(dc == 7)),
                       reads=[b_wkv, b_xafm], writes=[b_psm[j]])
            p.emit("vector", lambda e, j=j: e.tensor_copy(ev_sb[0:64, :], ps_m[j][0:64, :]),
                   reads=[b_psm[j]], writes=[b_ev])
            p.emit("sync", lambda e, sc=sc: e.dma_start(out=k_o[:, sc*512:(sc+1)*512], in_=ev_sb[0:64, :]),
                   reads=[b_ev], dma="ko")
        for t in range(16):
            j = t % 2
            for dc in range(8):
                p.emit("tensor", lambda e, t=t, dc=dc, j=j:
                       e.matmul(ps_m[j][:, 0:64], xa_fm[:, dc*T + t*128 : dc*T + (t+1)*128],
                                wkv_sb[:, dc*128 + 64 : dc*128 + 128],
                                start=(dc == 0), stop=(dc == 7)),
                       reads=[b_wkv, b_xafm], writes=[b_psm[j]])
            p.emit("vector", lambda e, j=j: e.tensor_copy(v_sb[:], ps_m[j][:, 0:64]),
                   reads=[b_psm[j]], writes=[b_vsb])
            p.emit("sync", lambda e, t=t: e.dma_start(out=v_o[t*128:(t+1)*128, :], in_=v_sb[:]),
                   reads=[b_vsb], dma="vo")
        fin = []
        for n in ("xo", "qo", "ko", "vo"):
            s_, c_ = p.dma_sems[n]
            fin.append(Tok(s_, c_))
        p.barrier("sync", fin)
        p.play(block)
    return nc


def build_b2(debug=False):
    nc = bass.Bass("TRN2", target_bir_lowering=False)
    p = Prog(nc)
    T, S = 2048, 4096
    q = nc.dram_tensor("q", [64, 8 * T], BF16, kind="ExternalInput").ap()
    k = nc.dram_tensor("k", [64, S], BF16, kind="ExternalInput").ap()
    v = nc.dram_tensor("v", [S, 64], BF16, kind="ExternalInput").ap()
    xn1_fm = nc.dram_tensor("xn1_fm", [128, 8 * T], BF16, kind="ExternalInput").ap()
    x1 = nc.dram_tensor("x1", [T, 1024], F32, kind="ExternalInput").ap()
    Wo = nc.dram_tensor("Wo", [512, 1024], F32, kind="ExternalInput").ap()
    onw = nc.dram_tensor("onw", [1024], F32, kind="ExternalInput").ap()
    onb = nc.dram_tensor("onb", [1024], F32, kind="ExternalInput").ap()
    gate_w = nc.dram_tensor("gate_w", [8, 1024], F32, kind="ExternalInput").ap()
    gate_b = nc.dram_tensor("gate_b", [8], F32, kind="ExternalInput").ap()
    expert_bias = nc.dram_tensor("expert_bias", [8], F32, kind="ExternalInput").ap()
    w1 = nc.dram_tensor("w1", [8, 1024, 4096], F32, kind="ExternalInput").ap()
    b1 = nc.dram_tensor("b1", [8, 4096], F32, kind="ExternalInput").ap()
    w2 = nc.dram_tensor("w2", [8, 4096, 1024], F32, kind="ExternalInput").ap()
    b2 = nc.dram_tensor("b2", [8, 1024], F32, kind="ExternalInput").ap()
    fin_o = nc.dram_tensor("fin_o", [T, 1024], F32, kind="ExternalOutput").ap()
    attn1_dr = nc.dram_tensor("attn1_dr", [T, 1024], BF16,
                              kind="ExternalOutput" if debug else "Internal").ap()
    if debug:
        moe_o = nc.dram_tensor("moe_o", [128, 16 * 1024], BF16, kind="ExternalOutput").ap()
        oh_o = nc.dram_tensor("oh_o", [128, 16 * 8], F32, kind="ExternalOutput").ap()
        o1_o = nc.dram_tensor("o1_o", [128, 4 * 2048], BF16, kind="ExternalOutput").ap()
    ctx = ExitStack()
    with ctx:
        p.setup(ctx)
        ident, b_id = sbuf(nc, ctx, "ident", [128, 128], BF16)
        identf, b_idf = sbuf(nc, ctx, "identf", [128, 128])
        ones1, b_on = sbuf(nc, ctx, "ones1", [1, 128])
        oh_tm, b_oht = sbuf(nc, ctx, "oh_tm", [128, 16 * 8])
        oh_em, b_ohe = sbuf(nc, ctx, "oh_em", [8, 16 * 128], BF16)
        moe_sb, b_moe = sbuf(nc, ctx, "moe_sb", [128, 16 * 1024], BF16)
        block = ctx.enter_context(nc.Block())
        emit_identity(nc, p, b_id, ident[:])
        p.emit("gpsimd", lambda g: g.memset(identf[:], 0.0), writes=[b_idf])
        p.emit("gpsimd", lambda g: g.affine_select(out=identf[:], in_=identf[:],
            compare_op=AL.not_equal, fill=1.0, base=0, pattern=[[-1, 128]], channel_multiplier=1),
            reads=[b_idf], writes=[b_idf])
        p.emit("vector", lambda e: e.memset(ones1[:], 1.0), writes=[b_on])

        with ExitStack() as c1:
            q_sb, b_q = sbuf(nc, c1, "q_sb", [64, 8 * T], BF16)
            k_sb, b_k = sbuf(nc, c1, "k_sb", [64, S], BF16)
            v_sb, b_v = sbuf(nc, c1, "v_sb", [128, 32 * 64], BF16)
            wo_sb, b_wo = sbuf(nc, c1, "wo_sb", [128, 4 * 1024], BF16)
            onwb, b_onwb = sbuf(nc, c1, "onwb", [128, 1024])
            onbb, b_onbb = sbuf(nc, c1, "onbb", [128, 1024])
            row_sb, b_row = sbuf(nc, c1, "row_sb", [1, 1024])
            sc_sb, b_sc = sbuf(nc, c1, "sc_sb", [128, S])
            pr_sb, b_pr = sbuf(nc, c1, "pr_sb", [128, S], BF16)
            pt_sb, b_pt = sbuf(nc, c1, "pt_sb", [128, 32 * 512], BF16)
            st, b_st = sbuf(nc, c1, "st", [128, 8])
            o1_sb, b_o1 = sbuf(nc, c1, "o1_sb", [128, 4 * T], BF16)
            z_sb, b_z = sbuf(nc, c1, "z_sb", [128, 1024])
            sq, b_sq = sbuf(nc, c1, "sq", [128, 1024])
            a1_sb, b_a1 = sbuf(nc, c1, "a1_sb", [128, 1024], BF16)
            ps_s = []; b_ps = []
            for i in range(2):
                t = c1.enter_context(nc.psum_tensor(f"ps_s{i}", [128, 512], F32))
                ps_s.append(t); b_ps.append(Buf(f"ps_s{i}"))
            ps_t = []; b_pst = []
            for i in range(2):
                t = c1.enter_context(nc.psum_tensor(f"ps_t{i}", [128, 128], BF16))
                ps_t.append(t); b_pst.append(Buf(f"ps_t{i}"))
            ps_o = c1.enter_context(nc.psum_tensor("ps_o", [128, 512], F32)); b_pso = Buf("ps_o")
            ps_z = c1.enter_context(nc.psum_tensor("ps_z", [128, 1024], F32)); b_psz = Buf("ps_z")

            p.emit("sync", lambda e: e.dma_start(out=q_sb[:], in_=q), writes=[b_q], dma="q")
            p.emit("sync", lambda e: e.dma_start(out=k_sb[:], in_=k), writes=[b_k], dma="kk")
            p.emit("sync", lambda e: e.dma_start(out=v_sb[:].rearrange("p (t d) -> p t d", d=64),
                                                 in_=v.rearrange("(t p) d -> p t d", p=128)),
                   writes=[b_v], dma="vv")
            p.emit("gpsimd", lambda e: e.dma_start(out=wo_sb[:].rearrange("p (t d) -> p t d", d=1024),
                                                   in_=Wo.rearrange("(t p) d -> p t d", p=128)),
                   writes=[b_wo], dma="wo")
            for nm, row_b, dst, b_dst in (("onw", onw, onwb, b_onwb), ("onb", onb, onbb, b_onbb)):
                p.emit("sync", lambda e, row_b=row_b: e.dma_start(out=row_sb[:], in_=row_b[None, :]),
                       writes=[b_row], dma="nr")
                for eh in range(2):
                    p.emit("tensor", lambda e, eh=eh: e.matmul(ps_z[:, eh*512:(eh+1)*512], ones1[:],
                                                               row_sb[0:1, eh*512:(eh+1)*512], start=True, stop=True),
                           reads=[b_on, b_row], writes=[b_psz])
                p.emit("vector", lambda e, dst=dst: e.tensor_copy(dst[:], ps_z[:]), reads=[b_psz], writes=[b_dst])

            for h in range(8):
                hb = 64 * (h % 2)
                for qg in range(4):
                    for q4 in range(4):
                        qt = qg * 4 + q4
                        for kc in range(8):
                            j = kc % 2
                            p.emit("tensor", lambda e, h=h, hb=hb, qt=qt, kc=kc, j=j:
                                   e.matmul(ps_s[j][:], q_sb[:, h*T + qt*128 : h*T + (qt+1)*128],
                                            k_sb[:, kc*512:(kc+1)*512], start=True, stop=True),
                                   reads=[b_q, b_k], writes=[b_ps[j]])
                            p.emit("vector", lambda e, kc=kc, j=j: e.tensor_copy(sc_sb[:, kc*512:(kc+1)*512], ps_s[j][:]),
                                   reads=[b_ps[j]], writes=[b_sc])
                        p.emit("vector", lambda e: e.reduce_max(st[:, 0:1], sc_sb[:], axis=AX.X, negate=True),
                               reads=[b_sc], writes=[b_st])
                        p.emit("vector", lambda e: e.tensor_scalar_mul(st[:, 1:2], st[:, 0:1], 0.125),
                               reads=[b_st], writes=[b_st])
                        p.emit("scalar", lambda e: e.activation(pr_sb[:], sc_sb[:], AF.Exp, bias=st[:, 1:2], scale=0.125,
                                                                accum_out=st[:, 2:3]),
                               reads=[b_sc, b_st], writes=[b_pr, b_st])
                        p.emit("vector", lambda e: e.reciprocal(st[:, 3:4], st[:, 2:3]), reads=[b_st], writes=[b_st])
                        p.emit("vector", lambda e: e.tensor_scalar(pr_sb[:], pr_sb[:], st[:, 3:4], None, op0=AL.mult),
                               reads=[b_pr, b_st], writes=[b_pr])
                        for kt in range(32):
                            j = kt % 2
                            p.emit("tensor", lambda e, kt=kt, j=j: e.transpose(ps_t[j][:], pr_sb[:, kt*128:(kt+1)*128], ident[:]),
                                   reads=[b_pr, b_id], writes=[b_pst[j]])
                            p.emit("vector", lambda e, kt=kt, j=j, q4=q4: e.tensor_copy(pt_sb[:, kt*512 + q4*128 : kt*512 + (q4+1)*128], ps_t[j][:]),
                                   reads=[b_pst[j]], writes=[b_pt])
                    for kt in range(32):
                        p.emit("tensor", lambda e, kt=kt:
                               e.matmul(ps_o[0:64, :], v_sb[:, kt*64:(kt+1)*64], pt_sb[:, kt*512:(kt+1)*512],
                                        start=(kt == 0), stop=(kt == 31)),
                               reads=[b_v, b_pt], writes=[b_pso])
                    p.emit("vector", lambda e, h=h, hb=hb, qg=qg:
                           e.tensor_copy(o1_sb[hb:hb+64, (h//2)*T + qg*512 : (h//2)*T + (qg+1)*512], ps_o[0:64, :]),
                           reads=[b_pso], writes=[b_o1])
            if debug:
                p.emit("sync", lambda e: e.dma_start(out=o1_o, in_=o1_sb[:]), reads=[b_o1], dma="dbg")
            for qt in range(16):
                for eh in range(2):
                    for ic in range(4):
                        p.emit("tensor", lambda e, qt=qt, eh=eh, ic=ic:
                               e.matmul(ps_z[:, eh*512:(eh+1)*512], o1_sb[:, ic*T + qt*128 : ic*T + (qt+1)*128],
                                        wo_sb[:, ic*1024 + eh*512 : ic*1024 + (eh+1)*512],
                                        start=(ic == 0), stop=(ic == 3)),
                               reads=[b_o1, b_wo], writes=[b_psz])
                p.emit("vector", lambda e: e.tensor_copy(z_sb[:], ps_z[:]), reads=[b_psz], writes=[b_z])
                emit_ln(p, z_sb[:], z_sb[:], st[:], sq[:], b_z, b_z, b_st, b_sq, 1024)
                p.emit("vector", lambda e: e.tensor_mul(z_sb[:], z_sb[:], onwb[:]), reads=[b_z, b_onwb], writes=[b_z])
                p.emit("vector", lambda e: e.tensor_add(a1_sb[:], z_sb[:], onbb[:]), reads=[b_z, b_onbb], writes=[b_a1])
                p.emit("sync", lambda e, qt=qt: e.dma_start(out=attn1_dr[qt*128:(qt+1)*128, :], in_=a1_sb[:]),
                       reads=[b_a1], dma="a1")
            p.full_barrier()

        with ExitStack() as c2:
            T2 = T
            xn_str = []; b_xns = []
            for i in range(2):
                t, b = sbuf(nc, c2, f"xn_str{i}", [128, 8 * 256], BF16); xn_str.append(t); b_xns.append(b)
            ps_g = c2.enter_context(nc.psum_tensor("ps_g", [128, 512], F32)); b_psg = Buf("ps_g")
            ps_h = []; b_psh = []
            for i in range(2):
                t = c2.enter_context(nc.psum_tensor(f"ps_h{i}", [128, 256], F32))
                ps_h.append(t); b_psh.append(Buf(f"ps_h{i}"))
            ps_y = []; b_psy = []
            for i in range(2):
                t = c2.enter_context(nc.psum_tensor(f"ps_y{i}", [128, 1024], F32))
                ps_y.append(t); b_psy.append(Buf(f"ps_y{i}"))

            with ExitStack() as c2g:
                gw_row, b_gwr = sbuf(nc, c2g, "gw_row", [8, 1024])
                gwT, b_gw = sbuf(nc, c2g, "gwT", [128, 64], BF16)
                gb_col, b_gb = sbuf(nc, c2g, "gb_col", [8, 2])
                lg_sb, b_lg = sbuf(nc, c2g, "lg_sb", [8, T])
                stm, b_stm = sbuf(nc, c2g, "stm", [128, 4])
                p.emit("sync", lambda e: e.dma_start(out=gw_row[:], in_=gate_w), writes=[b_gwr], dma="gw")
                for dc in range(8):
                    j = dc % 2
                    p.emit("tensor", lambda e, dc=dc, j=j: e.transpose(ps_h[j][0:128, 0:8], gw_row[0:8, dc*128:(dc+1)*128], identf[0:8, 0:8]),
                           reads=[b_gwr, b_idf], writes=[b_psh[j]])
                    p.emit("vector", lambda e, dc=dc, j=j: e.tensor_copy(gwT[:, dc*8:(dc+1)*8], ps_h[j][0:128, 0:8]),
                           reads=[b_psh[j]], writes=[b_gw])
                p.emit("sync", lambda e: e.dma_start(out=gb_col[:, 0:1], in_=gate_b[:, None]), writes=[b_gb], dma="gb")
                p.emit("sync", lambda e: e.dma_start(out=gb_col[:, 1:2], in_=expert_bias[:, None]), writes=[b_gb], dma="gb2")
                p.emit("vector", lambda e: e.tensor_add(gb_col[:, 0:1], gb_col[:, 0:1], gb_col[:, 1:2]),
                       reads=[b_gb], writes=[b_gb])
                for sc in range(4):
                    i = sc % 2
                    p.emit("sync", lambda e, sc=sc, i=i: e.dma_start(
                            out=xn_str[i][:].rearrange("p (dc s) -> p dc s", s=512),
                            in_=xn1_fm.rearrange("p (dc s) -> p dc s", s=T)[:, :, sc*512:(sc+1)*512]),
                           writes=[b_xns[i]], dma=f"xs{i}")
                    for dc in range(8):
                        p.emit("tensor", lambda e, sc=sc, dc=dc, i=i:
                               e.matmul(ps_g[0:8, :], gwT[:, dc*8:(dc+1)*8], xn_str[i][:, dc*512:(dc+1)*512],
                                        start=(dc == 0), stop=(dc == 7)),
                               reads=[b_gw, b_xns[i]], writes=[b_psg])
                    p.emit("vector", lambda e, sc=sc: e.tensor_scalar(lg_sb[0:8, sc*512:(sc+1)*512], ps_g[0:8, :],
                                                                      gb_col[0:8, 0:1], None, op0=AL.add),
                           reads=[b_psg, b_gb], writes=[b_lg])
                for t in range(16):
                    j = t % 2
                    p.emit("tensor", lambda e, t=t, j=j: e.transpose(ps_h[j][0:128, 0:8], lg_sb[0:8, t*128:(t+1)*128], identf[0:8, 0:8]),
                           reads=[b_lg, b_idf], writes=[b_psh[j]])
                    p.emit("vector", lambda e, t=t, j=j: e.tensor_copy(oh_tm[:, t*8:(t+1)*8], ps_h[j][0:128, 0:8]),
                           reads=[b_psh[j]], writes=[b_oht])
                    p.emit("vector", lambda e, t=t: e.reduce_max(stm[:, 0:1], oh_tm[:, t*8:(t+1)*8], axis=AX.X),
                           reads=[b_oht], writes=[b_stm])
                    p.emit("vector", lambda e, t=t: e.tensor_scalar(oh_tm[:, t*8:(t+1)*8], oh_tm[:, t*8:(t+1)*8],
                                                                    stm[:, 0:1], None, op0=AL.is_ge),
                           reads=[b_oht, b_stm], writes=[b_oht])
                    p.emit("tensor", lambda e, t=t, j=j: e.transpose(ps_h[j][0:8, 0:128], oh_tm[:, t*8:(t+1)*8], identf[:]),
                           reads=[b_oht, b_idf], writes=[b_psh[j]])
                    p.emit("vector", lambda e, t=t, j=j: e.tensor_copy(oh_em[0:8, t*128:(t+1)*128], ps_h[j][0:8, 0:128]),
                           reads=[b_psh[j]], writes=[b_ohe])
                p.full_barrier()

            w1e, b_w1e = sbuf(nc, c2, "w1e", [128, 8 * 4096], BF16)
            w2e, b_w2e = sbuf(nc, c2, "w2e", [128, 32 * 1024], BF16)
            b1row, b_b1r = sbuf(nc, c2, "b1row", [1, 4096], BF16)
            onesr, b_onr = sbuf(nc, c2, "onesr", [1, 512], BF16)
            h_sb, b_h = sbuf(nc, c2, "h_sb", [128, 32 * 256], BF16)
            p.emit("vector", lambda e: e.memset(onesr[:], 1.0), writes=[b_onr])
            p.emit("vector", lambda e: e.memset(moe_sb[:], 0.0), writes=[b_moe])
            for ex in range(8):
                p.emit("gpsimd", lambda e, ex=ex: e.dma_start(
                        out=w1e[:].rearrange("p (t d) -> p t d", d=4096),
                        in_=w1[ex].rearrange("(t p) d -> p t d", p=128)),
                       writes=[b_w1e], dma="w1")
                p.emit("gpsimd", lambda e, ex=ex: e.dma_start(
                        out=w2e[:].rearrange("p (t d) -> p t d", d=1024),
                        in_=w2[ex].rearrange("(t p) d -> p t d", p=128)),
                       writes=[b_w2e], dma="w2")
                p.emit("gpsimd", lambda e, ex=ex: e.dma_start(out=b1row[:], in_=b1[ex][None, :]),
                       writes=[b_b1r], dma="b1")
                for sb in range(8):
                    i = sb % 2
                    p.emit("sync", lambda e, sb=sb, i=i: e.dma_start(
                            out=xn_str[i][:, 0:8*256].rearrange("p (dc s) -> p dc s", s=256),
                            in_=xn1_fm.rearrange("p (dc s) -> p dc s", s=T)[:, :, sb*256:(sb+1)*256]),
                           writes=[b_xns[i]], dma=f"xs{i}")
                    for fc in range(32):
                        j = fc % 2
                        for dc in range(8):
                            p.emit("tensor", lambda e, sb=sb, fc=fc, dc=dc, j=j, i=i:
                                   e.matmul(ps_h[j][:], w1e[:, dc*4096 + fc*128 : dc*4096 + (fc+1)*128],
                                            xn_str[i][:, dc*256:(dc+1)*256],
                                            start=(dc == 0), stop=False),
                                   reads=[b_w1e, b_xns[i]], writes=[b_psh[j]])
                        p.emit("tensor", lambda e, fc=fc, j=j:
                               e.matmul(ps_h[j][:], b1row[0:1, fc*128:(fc+1)*128], onesr[0:1, 0:256],
                                        start=False, stop=True),
                               reads=[b_b1r, b_onr], writes=[b_psh[j]])
                        p.emit("scalar", lambda e, fc=fc, j=j: e.activation(h_sb[:, fc*256:(fc+1)*256], ps_h[j][:], AF.Relu),
                               reads=[b_psh[j]], writes=[b_h])
                    for st2 in range(2):
                        for fc in range(32):
                            for eh in range(2):
                                p.emit("tensor", lambda e, sb=sb, st2=st2, fc=fc, eh=eh:
                                       e.matmul(ps_y[st2][:, eh*512:(eh+1)*512],
                                                h_sb[:, fc*256 + st2*128 : fc*256 + (st2+1)*128],
                                                w2e[:, fc*1024 + eh*512 : fc*1024 + (eh+1)*512],
                                                start=(fc == 0), stop=(fc == 31)),
                                       reads=[b_h, b_w2e], writes=[b_psy[st2]])
                        tt = sb * 2 + st2
                        p.emit("vector", lambda e, tt=tt, st2=st2, ex=ex:
                               e.scalar_tensor_tensor(moe_sb[:, tt*1024:(tt+1)*1024], ps_y[st2][:],
                                                      oh_tm[:, tt*8 + ex : tt*8 + ex + 1],
                                                      moe_sb[:, tt*1024:(tt+1)*1024], op0=AL.mult, op1=AL.add),
                               reads=[b_psy[st2], b_oht, b_moe], writes=[b_moe])
            p.full_barrier()

        if debug:
            p.emit("sync", lambda e: e.dma_start(out=moe_o, in_=moe_sb[:]), reads=[b_moe], dma="dbg")
            p.emit("sync", lambda e: e.dma_start(out=oh_o, in_=oh_tm[:]), reads=[b_oht], dma="dbg")
        with ExitStack() as c3:
            b2_sb, b_b2 = sbuf(nc, c3, "b2_sb", [8, 1024], BF16)
            xt, b_xt = sbuf(nc, c3, "xt", [128, 1024])
            a1t, b_a1t = sbuf(nc, c3, "a1t", [128, 1024], BF16)
            f_sb, b_f = sbuf(nc, c3, "f_sb", [128, 1024])
            ps_f = c3.enter_context(nc.psum_tensor("ps_f", [128, 1024], F32)); b_psf = Buf("ps_f")
            p.emit("gpsimd", lambda e: e.dma_start(out=b2_sb[:], in_=b2), writes=[b_b2], dma="b2")
            for t in range(16):
                p.emit("sync", lambda e, t=t: e.dma_start(out=xt[:], in_=x1[t*128:(t+1)*128, :]),
                       writes=[b_xt], dma="x1")
                p.emit("sync", lambda e, t=t: e.dma_start(out=a1t[:], in_=attn1_dr[t*128:(t+1)*128, :]),
                       writes=[b_a1t], dma="a1l")
                for eh in range(2):
                    p.emit("tensor", lambda e, t=t, eh=eh:
                           e.matmul(ps_f[:, eh*512:(eh+1)*512], oh_em[0:8, t*128:(t+1)*128],
                                    b2_sb[0:8, eh*512:(eh+1)*512], start=True, stop=True),
                           reads=[b_ohe, b_b2], writes=[b_psf])
                p.emit("vector", lambda e: e.tensor_add(f_sb[:], xt[:], ps_f[:]), reads=[b_xt, b_psf], writes=[b_f])
                p.emit("vector", lambda e: e.tensor_add(f_sb[:], f_sb[:], a1t[:]), reads=[b_f, b_a1t], writes=[b_f])
                p.emit("vector", lambda e, t=t: e.tensor_add(f_sb[:], f_sb[:], moe_sb[:, t*1024:(t+1)*1024]),
                       reads=[b_f, b_moe], writes=[b_f])
                p.emit("sync", lambda e, t=t: e.dma_start(out=fin_o[t*128:(t+1)*128, :], in_=f_sb[:]),
                       reads=[b_f], dma="fo")
            s_, c_ = p.dma_sems["fo"]
            p.barrier("sync", [Tok(s_, c_)])
        p.play(block)
    return nc


def build_fused():
    nc = bass.Bass("TRN2", target_bir_lowering=False)
    _uniquify(nc)
    p = Prog(nc)
    S, T, NIT, TOPK = 4096, 2048, 36, 3072
    def ein(name, shape, dt=F32):
        return nc.dram_tensor(name, shape, dt, kind="ExternalInput").ap()
    x = ein("x", [S, 1024])
    w_qkv = ein("w_qkv", [3072, 1024]); w_router = ein("w_router", [1024])
    w_proj = ein("w_proj", [1024, 1024]); b_proj = ein("b_proj", [1024])
    w_block = ein("w_block", [1024, 1024])
    norm_w = ein("norm_w", [1024]); norm_b = ein("norm_b", [1024])
    Wq = ein("Wq", [1024, 512]); Wkv = ein("Wkv", [1024, 128]); Wo = ein("Wo", [512, 1024])
    onw = ein("onw", [1024]); onb = ein("onb", [1024])
    gate_w = ein("gate_w", [8, 1024]); gate_b = ein("gate_b", [8]); expert_bias = ein("expert_bias", [8])
    w1 = ein("w1", [8, 1024, 4096]); b1 = ein("b1", [8, 4096])
    w2 = ein("w2", [8, 4096, 1024]); b2 = ein("b2", [8, 1024])
    idx_xn = ein("idx_xn", [128, 8], I32)
    idx_q = ein("idx_q", [64, 1], I32)
    idx_x1 = ein("idx_x1", [128, 16], I32)
    fin_o = nc.dram_tensor("fin_o", [T, 1024], F32, kind="ExternalOutput").ap()
    q_dr = nc.dram_tensor("q_dr", [128, 8 * S], BF16).ap()
    k_dr = nc.dram_tensor("k_dr", [128, 8 * S], BF16).ap()
    v_dr = nc.dram_tensor("v_dr", [S, 1024], BF16).ap()
    rsc_dr = nc.dram_tensor("rsc_dr", [4096], F32).ap()
    out_dr = nc.dram_tensor("out_dr", [S, 1024], BF16).ap()
    x1_dr = nc.dram_tensor("x1_dr", [S, 1024], F32).ap()
    xn1_dr = nc.dram_tensor("xn1_dr", [128, 16 * 2048], BF16).ap()
    q1_dr = nc.dram_tensor("q1_dr", [64, 2 * 8 * 2048], BF16).ap()
    attn1_dr = nc.dram_tensor("attn1_dr", [T, 1024], BF16).ap()

    ctx = ExitStack()
    with ctx:
        p.setup(ctx)
        ident, b_id = sbuf(nc, ctx, "ident", [128, 128], BF16)
        identf, b_idf = sbuf(nc, ctx, "identf", [128, 128])
        ones1, b_on = sbuf(nc, ctx, "ones1", [1, 128])
        block = ctx.enter_context(nc.Block())
        emit_identity(nc, p, b_id, ident[:])
        p.emit("gpsimd", lambda g: g.memset(identf[:], 0.0), writes=[b_idf])
        p.emit("gpsimd", lambda g: g.affine_select(out=identf[:], in_=identf[:],
            compare_op=AL.not_equal, fill=1.0, base=0, pattern=[[-1, 128]], channel_multiplier=1),
            reads=[b_idf], writes=[b_idf])
        p.emit("vector", lambda e: e.memset(ones1[:], 1.0), writes=[b_on])

        def _phase0():
            with ExitStack() as c:
                xt = []; b_xt = []
                for i in range(2):
                    t, b = sbuf(nc, c, f"xt{i}", [128, 1024]); xt.append(t); b_xt.append(b)
                st, b_st = sbuf(nc, c, "st", [128, 8])
                sq, b_sq = sbuf(nc, c, "sq", [128, 1024])
                lnb, b_lnb = sbuf(nc, c, "lnb", [128, 1024], BF16)
                xn_fm, b_xn = sbuf(nc, c, "xn_fm", [128, 8 * S], BF16)
                wn, b_wn = sbuf(nc, c, "wn", [128, 2 * 1024], BF16)
                wT, b_wT = sbuf(nc, c, "wT", [128, 8 * 3072], BF16)
                wr_row, b_wrr = sbuf(nc, c, "wr_row", [1, 1024], BF16)
                wr, b_wr = sbuf(nc, c, "wr", [128, 8], BF16)
                qk_sb, b_qk = sbuf(nc, c, "qk_sb", [128, 512], BF16)
                v_sb, b_vsb = sbuf(nc, c, "v_sb", [128, 1024], BF16)
                r_sb, b_rsb = sbuf(nc, c, "r_sb", [1, S])
                ps_t = []; b_pst = []
                for i in range(2):
                    t = c.enter_context(nc.psum_tensor(f"ps_t{i}", [128, 1024], BF16))
                    ps_t.append(t); b_pst.append(Buf(f"ps_t{i}"))
                ps_m = []; b_psm = []
                for i in range(2):
                    t = c.enter_context(nc.psum_tensor(f"ps_m{i}", [128, 512], F32))
                    ps_m.append(t); b_psm.append(Buf(f"ps_m{i}"))

                for t in range(32):
                    i = t % 2
                    p.emit("sync", lambda e, t=t, i=i: e.dma_start(out=xt[i][:], in_=x[t*128:(t+1)*128, :]),
                           writes=[b_xt[i]], dma=f"x{i}")
                    emit_ln(p, xt[i][:], lnb[:], st[:], sq[:], b_xt[i], b_lnb, b_st, b_sq, 1024)
                    j = t % 2
                    emit_group(p, "tensor",
                               [lambda e, cc=cc, j=j: e.transpose(ps_t[j][:, cc*128:(cc+1)*128], lnb[:, cc*128:(cc+1)*128], ident[:])
                                for cc in range(8)],
                               reads=[b_lnb, b_id], writes=[b_pst[j]])
                    p.emit("vector", lambda e, j=j, t=t: e.tensor_copy(
                               xn_fm[:].rearrange("p (c s) -> p c s", s=S)[:, :, t*128:(t+1)*128],
                               ps_t[j][:].rearrange("p (c s) -> p c s", s=128)),
                           reads=[b_pst[j]], writes=[b_xn])
                for t in range(24):
                    i = t % 2
                    p.emit("gpsimd", lambda e, t=t, i=i: e.dma_start(out=wn[:, i*1024:(i+1)*1024], in_=w_qkv[t*128:(t+1)*128, :]),
                           writes=[b_wn], dma=f"w{i}")
                    j = t % 2
                    emit_group(p, "tensor",
                               [lambda e, cc=cc, i=i, j=j: e.transpose(ps_t[j][:, cc*128:(cc+1)*128], wn[:, i*1024 + cc*128 : i*1024 + (cc+1)*128], ident[:])
                                for cc in range(8)],
                               reads=[b_wn, b_id], writes=[b_pst[j]])
                    p.emit("vector", lambda e, j=j, t=t: e.tensor_copy(
                               wT[:].rearrange("p (c s) -> p c s", s=3072)[:, :, t*128:(t+1)*128],
                               ps_t[j][:].rearrange("p (c s) -> p c s", s=128)),
                           reads=[b_pst[j]], writes=[b_wT])
                p.emit("gpsimd", lambda e: e.dma_start(out=wr_row[:], in_=w_router[None, :]), writes=[b_wrr], dma="wr")
                for cc in range(8):
                    j = cc % 2
                    p.emit("tensor", lambda e, cc=cc, j=j: e.transpose(ps_t[j][0:128, 0:1], wr_row[0:1, cc*128:(cc+1)*128], ident[0:1, 0:1]),
                           reads=[b_wrr, b_id], writes=[b_pst[j]])
                    p.emit("vector", lambda e, cc=cc, j=j: e.tensor_copy(wr[:, cc:cc+1], ps_t[j][0:128, 0:1]),
                           reads=[b_pst[j]], writes=[b_wr])
                for which, off, out_t in (("q", 0, q_dr), ("k", 1024, k_dr)):
                    for f in range(8):
                        for sc in range(8):
                            j = (f * 8 + sc) % 2
                            emit_group(p, "tensor",
                                       [lambda e, f=f, sc=sc, ec=ec, j=j, off=off:
                                        e.matmul(ps_m[j][:], wT[:, ec*3072 + off + f*128 : ec*3072 + off + (f+1)*128],
                                                 xn_fm[:, ec*S + sc*512 : ec*S + (sc+1)*512],
                                                 start=(ec == 0), stop=(ec == 7)) for ec in range(8)],
                                       reads=[b_wT, b_xn], writes=[b_psm[j]])
                            p.emit("vector", lambda e, j=j: e.tensor_copy(qk_sb[:], ps_m[j][:]),
                                   reads=[b_psm[j]], writes=[b_qk])
                            p.emit("sync", lambda e, f=f, sc=sc, out_t=out_t:
                                   e.dma_start(out=out_t[:, f*S + sc*512 : f*S + (sc+1)*512], in_=qk_sb[:]),
                                   reads=[b_qk], dma="qk_o")
                for t in range(32):
                    for vh in range(2):
                        j = (t * 2 + vh) % 2
                        emit_group(p, "tensor",
                                   [lambda e, t=t, vh=vh, ec=ec, j=j:
                                    e.matmul(ps_m[j][:], xn_fm[:, ec*S + t*128 : ec*S + (t+1)*128],
                                             wT[:, ec*3072 + 2048 + vh*512 : ec*3072 + 2048 + (vh+1)*512],
                                             start=(ec == 0), stop=(ec == 7)) for ec in range(8)],
                                   reads=[b_wT, b_xn], writes=[b_psm[j]])
                        p.emit("vector", lambda e, vh=vh, j=j: e.tensor_copy(v_sb[:, vh*512:(vh+1)*512], ps_m[j][:]),
                               reads=[b_psm[j]], writes=[b_vsb])
                    p.emit("sync", lambda e, t=t: e.dma_start(out=v_dr[t*128:(t+1)*128, :], in_=v_sb[:]),
                           reads=[b_vsb], dma="v_o")
                for sc in range(8):
                    j = sc % 2
                    emit_group(p, "tensor",
                               [lambda e, sc=sc, ec=ec, j=j:
                                e.matmul(ps_m[j][0:1, :], wr[:, ec:ec+1],
                                         xn_fm[:, ec*S + sc*512 : ec*S + (sc+1)*512],
                                         start=(ec == 0), stop=(ec == 7)) for ec in range(8)],
                               reads=[b_wr, b_xn], writes=[b_psm[j]])
                    p.emit("vector", lambda e, sc=sc, j=j: e.tensor_copy(r_sb[0:1, sc*512:(sc+1)*512], ps_m[j][0:1, :]),
                           reads=[b_psm[j]], writes=[b_rsb])
                p.emit("sync", lambda e: e.dma_start(out=rsc_dr[None, :], in_=r_sb[:]), reads=[b_rsb], dma="r_o")
                p.full_barrier()

        _phase0()

        def _phase1():
            with ExitStack() as c:
                qg_sb, b_q = sbuf(nc, c, "qg_sb", [128, 8 * 1024], BF16)
                k_sb, b_k = sbuf(nc, c, "k_sb", [128, 8 * S], BF16)
                v_sb2, b_v = sbuf(nc, c, "v_sb2", [128, 32 * 1024], BF16)
                sc_sb = []; b_sc = []
                pr_sb = []; b_pr = []
                for i in range(2):
                    t, b = sbuf(nc, c, f"sc_sb{i}", [128, S]); sc_sb.append(t); b_sc.append(b)
                    t, b = sbuf(nc, c, f"pr_sb{i}", [128, S], BF16); pr_sb.append(t); b_pr.append(b)
                pt_sb, b_pt = sbuf(nc, c, "pt_sb", [128, 32 * 128], BF16)
                st, b_st = sbuf(nc, c, "st", [128, 8])
                o_sb = []; b_o = []
                for i in range(2):
                    t, b = sbuf(nc, c, f"o_sb{i}", [128, 1024], BF16); o_sb.append(t); b_o.append(b)
                ps_s = []; b_ps = []
                for i in range(2):
                    t = c.enter_context(nc.psum_tensor(f"ps_s{i}", [128, 1024], F32))
                    ps_s.append(t); b_ps.append(Buf(f"ps_s{i}"))
                ps_t = []; b_pst = []
                for i in range(2):
                    t = c.enter_context(nc.psum_tensor(f"ps_t{i}", [128, 1024], BF16))
                    ps_t.append(t); b_pst.append(Buf(f"ps_t{i}"))
                ps_av = []; b_pav = []
                for i in range(2):
                    t = c.enter_context(nc.psum_tensor(f"ps_av{i}", [128, 512], F32))
                    ps_av.append(t); b_pav.append(Buf(f"ps_av{i}"))
                p.emit("sync", lambda e: e.dma_start(out=k_sb[:], in_=k_dr), writes=[b_k], dma="kL")
                p.emit("sync", lambda e: e.dma_start(out=v_sb2[:].rearrange("p (t d) -> p t d", d=1024),
                                                     in_=v_dr.rearrange("(t p) d -> p t d", p=128)),
                       writes=[b_v], dma="vL")
                def emit_scores(qt, sci):
                    for kp in range(4):
                        j = kp % 2
                        fns = []
                        for kh in range(2):
                            kc = kp * 2 + kh
                            for qf in range(8):
                                fns.append(lambda e, qt=qt, kc=kc, kh=kh, qf=qf, j=j:
                                       e.matmul(ps_s[j][:, kh*512:(kh+1)*512],
                                                qg_sb[:, qf*1024 + (qt % 8)*128 : qf*1024 + (qt % 8 + 1)*128],
                                                k_sb[:, qf*S + kc*512 : qf*S + (kc+1)*512],
                                                start=(qf == 0), stop=(qf == 7)))
                        emit_group(p, "tensor", fns, reads=[b_q, b_k], writes=[b_ps[j]])
                        p.emit("vector", lambda e, kp=kp, j=j, sci=sci: e.tensor_copy(sc_sb[sci][:, kp*1024:(kp+1)*1024], ps_s[j][:]),
                               reads=[b_ps[j]], writes=[b_sc[sci]])
                    p.emit("vector", lambda e, sci=sci: e.reduce_max(st[:, 4*sci:4*sci+1], sc_sb[sci][:], axis=AX.X, negate=True),
                           reads=[b_sc[sci]], writes=[b_st])
                    p.emit("vector", lambda e, sci=sci: e.tensor_scalar_mul(st[:, 4*sci+1:4*sci+2], st[:, 4*sci:4*sci+1], 0.125),
                           reads=[b_st], writes=[b_st])
                    p.emit("scalar", lambda e, sci=sci: e.activation(pr_sb[sci][:], sc_sb[sci][:], AF.Exp, bias=st[:, 4*sci+1:4*sci+2],
                                                            scale=0.125, accum_out=st[:, 4*sci+2:4*sci+3]),
                           reads=[b_sc[sci], b_st], writes=[b_pr[sci], b_st])
                    p.emit("vector", lambda e, sci=sci: e.reciprocal(st[:, 4*sci+3:4*sci+4], st[:, 4*sci+2:4*sci+3]),
                           reads=[b_st], writes=[b_st])

                def emit_av(qt, sci):
                    for kg in range(4):
                        j = kg % 2
                        emit_group(p, "tensor",
                                   [lambda e, kt=kt, j=j, sci=sci:
                                    e.transpose(ps_t[j][:, (kt % 8)*128:(kt % 8 + 1)*128], pr_sb[sci][:, kt*128:(kt+1)*128], ident[:])
                                    for kt in range(kg*8, kg*8+8)],
                                   reads=[b_pr[sci], b_id], writes=[b_pst[j]])
                        p.emit("vector", lambda e, kg=kg, j=j: e.tensor_copy(pt_sb[:, kg*1024:(kg+1)*1024], ps_t[j][:]),
                               reads=[b_pst[j]], writes=[b_pt])
                    ob = qt % 2
                    for vh in range(2):
                        j = vh
                        emit_group(p, "tensor",
                                   [lambda e, kt=kt, vh=vh, j=j:
                                    e.matmul(ps_av[j][:], pt_sb[:, kt*128:(kt+1)*128],
                                             v_sb2[:, kt*1024 + vh*512 : kt*1024 + (vh+1)*512],
                                             start=(kt == 0), stop=(kt == 31)) for kt in range(32)],
                                   reads=[b_pt, b_v], writes=[b_pav[j]])
                        p.emit("vector", lambda e, vh=vh, j=j, ob=ob, sci=sci:
                               e.tensor_scalar(o_sb[ob][:, vh*512:(vh+1)*512], ps_av[j][:], st[:, 4*sci+3:4*sci+4], None, op0=AL.mult),
                               reads=[b_pav[j], b_st], writes=[b_o[ob]])
                    p.emit("sync", lambda e, qt=qt, ob=ob: e.dma_start(out=out_dr[qt*128:(qt+1)*128, :], in_=o_sb[ob][:]),
                           reads=[b_o[ob]], dma="o_o")

                prev = None
                for g in range(4):
                    p.emit("sync", lambda e, g=g: e.dma_start(
                            out=qg_sb[:].rearrange("p (f s) -> p f s", s=1024),
                            in_=q_dr.rearrange("p (f s) -> p f s", s=S)[:, :, g*1024:(g+1)*1024]),
                           writes=[b_q], dma="qL")
                    for q4 in range(8):
                        qt = g * 8 + q4
                        emit_scores(qt, qt % 2)
                        if prev is not None:
                            emit_av(prev, prev % 2)
                        prev = qt
                emit_av(prev, prev % 2)
                p.full_barrier()

        _phase1()

        def _phase2():
            with ExitStack() as c:
                ogg_sb, b_og = sbuf(nc, c, "ogg_sb", [128, 8 * 1024], BF16)
                wpn, b_wpn = sbuf(nc, c, "wpn", [128, 2 * 1024], BF16)
                wpT, b_wpT = sbuf(nc, c, "wpT", [128, 8 * 1024], BF16)
                wb_sb, b_wb = sbuf(nc, c, "wb_sb", [128, 8 * 1024], BF16)
                bp_row, b_bpr = sbuf(nc, c, "bp_row", [1, 1024])
                bpb, b_bpb = sbuf(nc, c, "bpb", [128, 1024])
                rs_sb, b_rs = sbuf(nc, c, "rs_sb", [128, 32])
                rsig_sb, b_rsig = sbuf(nc, c, "rsig_sb", [1, 4096])
                cmp_sb, b_cmp = sbuf(nc, c, "cmp_sb", [128, 32])
                ones128, b_o128 = sbuf(nc, c, "ones128", [128, 128])
                thr, b_thr = sbuf(nc, c, "thr", [128, 1])
                sgn, b_sgn = sbuf(nc, c, "sgn", [128, 1])
                step_tab, b_stp = sbuf(nc, c, "step_tab", [128, NIT])
                st4, b_st4 = sbuf(nc, c, "st4", [128, 8])
                row_sc, b_rsc2 = sbuf(nc, c, "row_sc", [1, 128])
                mcol, b_mc = sbuf(nc, c, "mcol", [128, 2])
                xt, b_xt = sbuf(nc, c, "xt", [128, 1024])
                st, b_st = sbuf(nc, c, "st", [128, 8])
                sq, b_sq = sbuf(nc, c, "sq", [128, 1024])
                xn_f, b_xnf = sbuf(nc, c, "xn_f", [128, 1024])
                xn_b, b_xnb = sbuf(nc, c, "xn_b", [128, 1024], BF16)
                xnfm, b_xnfm = sbuf(nc, c, "xnfm", [128, 8 * 128], BF16)
                y_sb, b_y = sbuf(nc, c, "y_sb", [128, 1024])
                a3_sb, b_a3 = sbuf(nc, c, "a3_sb", [128, 1024])
                d_sb, b_d = sbuf(nc, c, "d_sb", [128, 1024])
                x1_sb, b_x1 = sbuf(nc, c, "x1_sb", [128, 1024])
                ps_y = c.enter_context(nc.psum_tensor("ps_y", [128, 1024], F32)); b_psy = Buf("ps_y")
                ps_p = c.enter_context(nc.psum_tensor("ps_p", [128, 1024], F32)); b_psp = Buf("ps_p")
                ps_c = c.enter_context(nc.psum_tensor("ps_c", [128, 128], F32)); b_psc = Buf("ps_c")
                ps_t = []; b_pst = []
                for i in range(2):
                    t = c.enter_context(nc.psum_tensor(f"ps_t{i}", [128, 1024], BF16))
                    ps_t.append(t); b_pst.append(Buf(f"ps_t{i}"))
                p.emit("vector", lambda e: e.memset(ones128[:], 1.0), writes=[b_o128])
                p.emit("gpsimd", lambda e: e.dma_start(out=wb_sb[:].rearrange("p (t d) -> p t d", d=1024),
                                                       in_=w_block.rearrange("(t p) d -> p t d", p=128)),
                       writes=[b_wb], dma="wb")
                p.emit("sync", lambda e: e.dma_start(out=rs_sb[:], in_=rsc_dr.rearrange("(q j) -> q j", j=32)),
                       writes=[b_rs], dma="rs", waits=[Tok(p.dma_sems["r_o"][0], p.dma_sems["r_o"][1])])
                p.emit("sync", lambda e: e.dma_start(out=rsig_sb[:], in_=rsc_dr[None, :]), writes=[b_rsig], dma="rs")
                p.emit("sync", lambda e: e.dma_start(out=bp_row[:], in_=b_proj[None, :]), writes=[b_bpr], dma="rs")
                for t in range(8):
                    i = t % 2
                    p.emit("gpsimd", lambda e, t=t, i=i: e.dma_start(out=wpn[:, i*1024:(i+1)*1024],
                                                                     in_=w_proj[t*128:(t+1)*128, :]),
                           writes=[b_wpn], dma=f"wp{i}")
                    j = t % 2
                    emit_group(p, "tensor",
                               [lambda e, cc=cc, i=i, j=j: e.transpose(ps_t[j][:, cc*128:(cc+1)*128], wpn[:, i*1024 + cc*128 : i*1024 + (cc+1)*128], ident[:])
                                for cc in range(8)],
                               reads=[b_wpn, b_id], writes=[b_pst[j]])
                    p.emit("vector", lambda e, j=j, t=t: e.tensor_copy(
                               wpT[:].rearrange("p (c s) -> p c s", s=1024)[:, :, t*128:(t+1)*128],
                               ps_t[j][:].rearrange("p (c s) -> p c s", s=128)),
                           reads=[b_pst[j]], writes=[b_wpT])
                for eh in range(2):
                    p.emit("tensor", lambda e, eh=eh: e.matmul(ps_p[:, eh*512:(eh+1)*512], ones1[:],
                                                               bp_row[0:1, eh*512:(eh+1)*512], start=True, stop=True),
                           reads=[b_on, b_bpr], writes=[b_psp])
                p.emit("vector", lambda e: e.tensor_copy(bpb[:], ps_p[:]), reads=[b_psp], writes=[b_bpb])
                p.emit("vector", lambda e: e.reduce_max(st4[:, 0:1], rs_sb[:], axis=AX.X, apply_absolute_value=True),
                       reads=[b_rs], writes=[b_st4])
                p.emit("tensor", lambda e: e.transpose(ps_c[0:1, 0:128], st4[:, 0:1], identf[:]),
                       reads=[b_st4, b_idf], writes=[b_psc])
                p.emit("vector", lambda e: e.tensor_copy(row_sc[:], ps_c[0:1, 0:128]), reads=[b_psc], writes=[b_rsc2])
                p.emit("vector", lambda e: e.reduce_max(row_sc[0:1, 0:1], row_sc[:], axis=AX.X),
                       reads=[b_rsc2], writes=[b_rsc2])
                p.emit("tensor", lambda e: e.matmul(ps_c[:, 0:1], ones1[:], row_sc[0:1, 0:1], start=True, stop=True),
                       reads=[b_on, b_rsc2], writes=[b_psc])
                p.emit("vector", lambda e: e.tensor_copy(st4[:, 1:2], ps_c[:, 0:1]), reads=[b_psc], writes=[b_st4])
                for i in range(NIT):
                    p.emit("vector", lambda e, i=i: e.tensor_scalar_mul(step_tab[:, i:i+1], st4[:, 1:2], 2.0 ** (-i)),
                           reads=[b_st4], writes=[b_stp])
                p.emit("vector", lambda e: e.memset(thr[:], 0.0), writes=[b_thr])
                for i in range(NIT):
                    p.emit("vector", lambda e: e.tensor_scalar(cmp_sb[:], rs_sb[:], thr[:, 0:1], None, op0=AL.is_gt),
                           reads=[b_rs, b_thr], writes=[b_cmp])
                    p.emit("vector", lambda e: e.reduce_sum(st4[:, 2:3], cmp_sb[:], axis=AX.X),
                           reads=[b_cmp], writes=[b_st4])
                    p.emit("tensor", lambda e: e.matmul(ps_c[:, 1:2], ones128[:], st4[:, 2:3], start=True, stop=True),
                           reads=[b_o128, b_st4], writes=[b_psc])
                    p.emit("vector", lambda e: e.tensor_scalar(st4[:, 3:4], ps_c[:, 1:2], float(-TOPK), None, op0=AL.add),
                           reads=[b_psc], writes=[b_st4])
                    p.emit("scalar", lambda e: e.activation(sgn[:], st4[:, 3:4], AF.Sign),
                           reads=[b_st4], writes=[b_sgn])
                    p.emit("vector", lambda e, i=i: e.scalar_tensor_tensor(thr[:], sgn[:], step_tab[:, i:i+1], thr[:],
                                                                           op0=AL.mult, op1=AL.add),
                           reads=[b_sgn, b_stp, b_thr], writes=[b_thr])
                for tt in range(32):
                    gg, hc = tt // 8, tt % 8
                    p.emit("sync", lambda e, gg=gg, hc=hc: e.dma_start(
                            out=xt[:], in_=x.rearrange("(a f) d -> f a d", f=4)[gg, hc*128:(hc+1)*128, :]),
                           writes=[b_xt], dma="xs")
                    emit_ln(p, xt[:], xn_f[:], st[:], sq[:], b_xt, b_xnf, b_st, b_sq, 1024)
                    p.emit("scalar", lambda e: e.activation(xn_b[:], xn_f[:], AF.Copy), reads=[b_xnf], writes=[b_xnb])
                    j = tt % 2
                    emit_group(p, "tensor",
                               [lambda e, cc=cc, j=j: e.transpose(ps_t[j][:, cc*128:(cc+1)*128], xn_b[:, cc*128:(cc+1)*128], ident[:])
                                for cc in range(8)],
                               reads=[b_xnb, b_id], writes=[b_pst[j]])
                    p.emit("vector", lambda e, j=j: e.tensor_copy(xnfm[:], ps_t[j][:]),
                           reads=[b_pst[j]], writes=[b_xnfm])
                    emit_group(p, "tensor",
                               [lambda e, eh=eh, dc=dc:
                                e.matmul(ps_y[:, eh*512:(eh+1)*512], xnfm[:, dc*128:(dc+1)*128],
                                         wb_sb[:, dc*1024 + eh*512 : dc*1024 + (eh+1)*512],
                                         start=(dc == 0), stop=(dc == 7))
                                for eh in range(2) for dc in range(8)],
                               reads=[b_xnfm, b_wb], writes=[b_psy])
                    p.emit("scalar", lambda e: e.activation(y_sb[:], ps_y[:], AF.Gelu_apprx_tanh),
                           reads=[b_psy], writes=[b_y])
                    if hc == 0:
                        p.emit("sync", lambda e, gg=gg: e.dma_start(
                                out=ogg_sb[:].rearrange("p (t d) -> p t d", d=1024),
                                in_=out_dr.rearrange("(t p) d -> p t d", p=128)[:, 8*gg:8*gg+8, :]),
                               writes=[b_og], dma="og")
                    for eh in range(2):
                        for jc in range(8):
                            p.emit("tensor", lambda e, eh=eh, jc=jc, hc=hc:
                                   e.matmul(ps_p[:, eh*512:(eh+1)*512],
                                            ogg_sb[:, jc*1024 + hc*128 : jc*1024 + (hc+1)*128],
                                            wpT[:, jc*1024 + eh*512 : jc*1024 + (eh+1)*512],
                                            start=(jc == 0), stop=(jc == 7)),
                                   reads=[b_og, b_wpT], writes=[b_psp])
                    p.emit("vector", lambda e: e.tensor_add(a3_sb[:], ps_p[:], bpb[:]), reads=[b_psp, b_bpb], writes=[b_a3])
                    p.emit("tensor", lambda e, gg=gg, hc=hc: e.transpose(ps_c[0:128, 2:3],
                            rsig_sb[0:1, :].rearrange("o (a f) -> o f a", f=4)[:, gg, hc*128:(hc+1)*128],
                            identf[0:1, 0:1]),
                           reads=[b_rsig, b_idf], writes=[b_psc])
                    p.emit("vector", lambda e: e.tensor_copy(mcol[:, 0:1], ps_c[0:128, 2:3]), reads=[b_psc], writes=[b_mc])
                    p.emit("vector", lambda e: e.tensor_scalar(mcol[:, 1:2], mcol[:, 0:1], thr[:, 0:1], None, op0=AL.is_gt),
                           reads=[b_mc, b_thr], writes=[b_mc])
                    p.emit("vector", lambda e: e.tensor_sub(d_sb[:], y_sb[:], xn_f[:]), reads=[b_y, b_xnf], writes=[b_d])
                    p.emit("vector", lambda e: e.scalar_tensor_tensor(d_sb[:], d_sb[:], mcol[:, 1:2], xn_f[:],
                                                                      op0=AL.mult, op1=AL.add),
                           reads=[b_d, b_mc, b_xnf], writes=[b_d])
                    p.emit("vector", lambda e: e.tensor_add(x1_sb[:], xt[:], a3_sb[:]), reads=[b_xt, b_a3], writes=[b_x1])
                    p.emit("vector", lambda e: e.tensor_add(x1_sb[:], x1_sb[:], d_sb[:]), reads=[b_x1, b_d], writes=[b_x1])
                    p.emit("sync", lambda e, gg=gg, hc=hc: e.dma_start(
                            out=x1_dr.rearrange("(a f) d -> f a d", f=4)[gg, hc*128:(hc+1)*128, :], in_=x1_sb[:]),
                           reads=[b_x1], dma="x1w")
                p.full_barrier()

        _phase2()

        k1_sb, b_k1 = sbuf(nc, ctx, "k1_sb", [64, S], BF16)
        v1_sb, b_v1 = sbuf(nc, ctx, "v1_sb", [128, 32 * 64], BF16)
        def _phase3():
            with ExitStack() as c:
                nw_row, b_nwr = sbuf(nc, c, "nw_row", [1, 1024])
                nb_row, b_nbr = sbuf(nc, c, "nb_row", [1, 1024])
                nwb, b_nwb = sbuf(nc, c, "nwb", [128, 1024])
                nbb, b_nbb = sbuf(nc, c, "nbb", [128, 1024])
                wq_sb, b_wq = sbuf(nc, c, "wq_sb", [128, 8 * 512], BF16)
                wkv_sb, b_wkv = sbuf(nc, c, "wkv_sb", [128, 8 * 128], BF16)
                xt, b_xt = sbuf(nc, c, "xt", [128, 1024])
                st, b_st = sbuf(nc, c, "st", [128, 8])
                sq, b_sq = sbuf(nc, c, "sq", [128, 1024])
                xn1_f, b_xn1f = sbuf(nc, c, "xn1_f", [128, 1024])
                xn1_b, b_xn1b = sbuf(nc, c, "xn1_b", [128, 1024], BF16)
                xn1n, b_xn1n = sbuf(nc, c, "xn1n", [128, 1024])
                xa_b, b_xab = sbuf(nc, c, "xa_b", [128, 1024], BF16)
                xa_fm, b_xafm = sbuf(nc, c, "xa_fm", [128, 8 * S], BF16)
                xnfm_t, b_xnfmt = sbuf(nc, c, "xnfm_t", [128, 8 * 128], BF16)
                ev_sb, b_ev = sbuf(nc, c, "ev_sb", [128, 512], BF16)
                v_sb, b_vsb = sbuf(nc, c, "v_sb", [128, 64], BF16)
                ps_m = []; b_psm = []
                for i in range(2):
                    t = c.enter_context(nc.psum_tensor(f"ps_m{i}", [128, 512], F32))
                    ps_m.append(t); b_psm.append(Buf(f"ps_m{i}"))
                ps_t = []; b_pst = []
                for i in range(2):
                    t = c.enter_context(nc.psum_tensor(f"ps_t{i}", [128, 1024], BF16))
                    ps_t.append(t); b_pst.append(Buf(f"ps_t{i}"))
                ps_b = c.enter_context(nc.psum_tensor("ps_b", [128, 1024], F32)); b_psb = Buf("ps_b")
                p.emit("sync", lambda e: e.dma_start(out=nw_row[:], in_=norm_w[None, :]), writes=[b_nwr], dma="n")
                p.emit("sync", lambda e: e.dma_start(out=nb_row[:], in_=norm_b[None, :]), writes=[b_nbr], dma="n2")
                p.emit("gpsimd", lambda e: e.dma_start(out=wq_sb[:].rearrange("p (t d) -> p t d", d=512),
                                                       in_=Wq.rearrange("(t p) d -> p t d", p=128)),
                       writes=[b_wq], dma="wq")
                p.emit("gpsimd", lambda e: e.dma_start(out=wkv_sb[:].rearrange("p (t d) -> p t d", d=128),
                                                       in_=Wkv.rearrange("(t p) d -> p t d", p=128)),
                       writes=[b_wkv], dma="wkv")
                for eh in range(2):
                    p.emit("tensor", lambda e, eh=eh: e.matmul(ps_b[:, eh*512:(eh+1)*512], ones1[:],
                                                               nw_row[0:1, eh*512:(eh+1)*512], start=True, stop=True),
                           reads=[b_on, b_nwr], writes=[b_psb])
                p.emit("vector", lambda e: e.tensor_copy(nwb[:], ps_b[:]), reads=[b_psb], writes=[b_nwb])
                for eh in range(2):
                    p.emit("tensor", lambda e, eh=eh: e.matmul(ps_b[:, eh*512:(eh+1)*512], ones1[:],
                                                               nb_row[0:1, eh*512:(eh+1)*512], start=True, stop=True),
                           reads=[b_on, b_nbr], writes=[b_psb])
                p.emit("vector", lambda e: e.tensor_copy(nbb[:], ps_b[:]), reads=[b_psb], writes=[b_nbb])
                for t in range(32):
                    p.emit("sync", lambda e, t=t: e.dma_start(out=xt[:], in_=x1_dr[t*128:(t+1)*128, :]),
                           writes=[b_xt], dma="xB")
                    emit_ln(p, xt[:], xn1_f[:], st[:], sq[:], b_xt, b_xn1f, b_st, b_sq, 1024)
                    p.emit("scalar", lambda e: e.activation(xn1_b[:], xn1_f[:], AF.Copy), reads=[b_xn1f], writes=[b_xn1b])
                    emit_ln(p, xn1_f[:], xn1n[:], st[:], sq[:], b_xn1f, b_xn1n, b_st, b_sq, 1024)
                    p.emit("vector", lambda e: e.tensor_mul(xn1n[:], xn1n[:], nwb[:]), reads=[b_xn1n, b_nwb], writes=[b_xn1n])
                    p.emit("vector", lambda e: e.tensor_add(xa_b[:], xn1n[:], nbb[:]), reads=[b_xn1n, b_nbb], writes=[b_xab])
                    emit_group(p, "tensor",
                               [lambda e, cc=cc: e.transpose(ps_t[0][:, cc*128:(cc+1)*128], xa_b[:, cc*128:(cc+1)*128], ident[:])
                                for cc in range(8)],
                               reads=[b_xab, b_id], writes=[b_pst[0]])
                    p.emit("vector", lambda e, t=t: e.tensor_copy(
                               xa_fm[:].rearrange("p (c s) -> p c s", s=S)[:, :, t*128:(t+1)*128],
                               ps_t[0][:].rearrange("p (c s) -> p c s", s=128)),
                           reads=[b_pst[0]], writes=[b_xafm])
                    emit_group(p, "tensor",
                               [lambda e, cc=cc: e.transpose(ps_t[1][:, cc*128:(cc+1)*128], xn1_b[:, cc*128:(cc+1)*128], ident[:])
                                for cc in range(8)],
                               reads=[b_xn1b, b_id], writes=[b_pst[1]])
                    p.emit("vector", lambda e: e.tensor_copy(xnfm_t[:], ps_t[1][:]),
                           reads=[b_pst[1]], writes=[b_xnfmt])
                    p.emit("sync", lambda e, t=t: e.dma_start(
                            out=xn1_dr[:, :].rearrange("p (sb dc s) -> p sb dc s", dc=8, s=256)[:, t//2, :, (t % 2)*128:(t % 2)*128+128],
                            in_=xnfm_t[:].rearrange("p (dc s) -> p dc s", s=128)),
                           reads=[b_xnfmt], dma="xnw")
                for ic in range(4):
                    for sc in range(8):
                        j = (ic * 8 + sc) % 2
                        emit_group(p, "tensor",
                                   [lambda e, ic=ic, sc=sc, dc=dc, j=j:
                                    e.matmul(ps_m[j][:], wq_sb[:, dc*512 + ic*128 : dc*512 + (ic+1)*128],
                                             xa_fm[:, dc*S + sc*512 : dc*S + (sc+1)*512],
                                             start=(dc == 0), stop=(dc == 7)) for dc in range(8)],
                                   reads=[b_wq, b_xafm], writes=[b_psm[j]])
                        for hh in range(2):
                            p.emit("vector", lambda e, hh=hh, j=j: e.tensor_copy(ev_sb[0:64, :], ps_m[j][hh*64:hh*64+64, :]),
                                   reads=[b_psm[j]], writes=[b_ev])
                            hf, lc = sc // 4, sc % 4
                            p.emit("sync", lambda e, ic=ic, hh=hh, hf=hf, lc=lc:
                                   e.dma_start(out=q1_dr[:, hf*16384 + (2*ic+hh)*2048 + lc*512 : hf*16384 + (2*ic+hh)*2048 + (lc+1)*512],
                                               in_=ev_sb[0:64, :]),
                                   reads=[b_ev], dma="q1w")
                for sc in range(8):
                    j = sc % 2
                    emit_group(p, "tensor",
                               [lambda e, sc=sc, dc=dc, j=j:
                                e.matmul(ps_m[j][0:64, :], wkv_sb[:, dc*128 : dc*128 + 64],
                                         xa_fm[:, dc*S + sc*512 : dc*S + (sc+1)*512],
                                         start=(dc == 0), stop=(dc == 7)) for dc in range(8)],
                               reads=[b_wkv, b_xafm], writes=[b_psm[j]])
                    p.emit("vector", lambda e, sc=sc, j=j: e.tensor_copy(k1_sb[0:64, sc*512:(sc+1)*512], ps_m[j][0:64, :]),
                           reads=[b_psm[j]], writes=[b_k1])
                for t in range(32):
                    j = t % 2
                    emit_group(p, "tensor",
                               [lambda e, t=t, dc=dc, j=j:
                                e.matmul(ps_m[j][:, 0:64], xa_fm[:, dc*S + t*128 : dc*S + (t+1)*128],
                                         wkv_sb[:, dc*128 + 64 : dc*128 + 128],
                                         start=(dc == 0), stop=(dc == 7)) for dc in range(8)],
                               reads=[b_wkv, b_xafm], writes=[b_psm[j]])
                    p.emit("vector", lambda e, t=t, j=j: e.tensor_copy(v1_sb[:, t*64:(t+1)*64], ps_m[j][:, 0:64]),
                           reads=[b_psm[j]], writes=[b_v1])
                p.full_barrier()

        _phase3()

        emit_b2_core(nc, p, ctx, dict(
            ident=ident, b_id=b_id, identf=identf, b_idf=b_idf, ones1=ones1, b_on=b_on,
            k1_sb=k1_sb, b_k1=b_k1, v1_sb=v1_sb, b_v1=b_v1,
            q1_dr=q1_dr, xn1_dr=xn1_dr, x1_dr=x1_dr, attn1_dr=attn1_dr, fin_o=fin_o,
            idx_xn=idx_xn, idx_q=idx_q, idx_x1=idx_x1,
            Wo=Wo, onw=onw, onb=onb, gate_w=gate_w, gate_b=gate_b, expert_bias=expert_bias,
            w1=w1, b1=b1, w2=w2, b2=b2))
        p.play(block)
    return nc


def emit_b2_core(nc, p, ctx, io):
    T, S = 2048, 4096
    ident, b_id = io["ident"], io["b_id"]
    identf, b_idf = io["identf"], io["b_idf"]
    ones1, b_on = io["ones1"], io["b_on"]
    k_sb, b_k = io["k1_sb"], io["b_k1"]
    v_sb, b_v = io["v1_sb"], io["b_v1"]
    q1_fl = io["q1_dr"].rearrange("p (b x) -> (p b) x", x=16384)
    xn1_fl = io["xn1_dr"].rearrange("p (s x) -> (p s) x", x=2048)
    x1_fl = io["x1_dr"]
    attn1_dr = io["attn1_dr"]; fin_o = io["fin_o"]
    Wo, onw, onb = io["Wo"], io["onw"], io["onb"]
    gate_w, gate_b, expert_bias = io["gate_w"], io["gate_b"], io["expert_bias"]
    w1, b1, w2, b2 = io["w1"], io["b1"], io["w2"], io["b2"]

    oh_tm, b_oht = sbuf(nc, ctx, "oh_tm", [128, 16 * 8])
    oh_dr = nc.dram_tensor("oh_dr", [8, 16 * 128], BF16).ap()
    moe_sb, b_moe = sbuf(nc, ctx, "moe_sb", [128, 16 * 1024], BF16)
    ixn_sb, b_ixn = sbuf(nc, ctx, "ixn_sb", [128, 8], I32)
    ixq_sb, b_ixq = sbuf(nc, ctx, "ixq_sb", [64, 1], I32)
    ix1_sb, b_ix1 = sbuf(nc, ctx, "ix1_sb", [128, 16], I32)
    p.emit("sync", lambda e: e.dma_start(out=ixn_sb[:], in_=io["idx_xn"]), writes=[b_ixn], dma="ix")
    p.emit("sync", lambda e: e.dma_start(out=ixq_sb[:], in_=io["idx_q"]), writes=[b_ixq], dma="ix")
    p.emit("sync", lambda e: e.dma_start(out=ix1_sb[:], in_=io["idx_x1"]), writes=[b_ix1], dma="ix")

    with ExitStack() as c1:
        q_sb, b_q = sbuf(nc, c1, "q_sb", [64, 8 * T], BF16)
        wo_sb, b_wo = sbuf(nc, c1, "wo_sb", [128, 4 * 1024], BF16)
        onwb, b_onwb = sbuf(nc, c1, "onwb", [128, 1024])
        onbb, b_onbb = sbuf(nc, c1, "onbb", [128, 1024])
        row_sb, b_row = sbuf(nc, c1, "row_sb", [1, 1024])
        sc_sb = []; b_sc = []
        pr_sb = []; b_pr = []
        for i in range(2):
            t, b = sbuf(nc, c1, f"sc_sbB{i}", [128, S]); sc_sb.append(t); b_sc.append(b)
            t, b = sbuf(nc, c1, f"pr_sbB{i}", [128, S], BF16); pr_sb.append(t); b_pr.append(b)
        pt_sb, b_pt = sbuf(nc, c1, "pt_sb", [128, 32 * 512], BF16)
        diag_sb, _bd = sbuf(nc, c1, "diag_sb", [128, 2 * 128], BF16)
        b_diag = [Buf("diag0"), Buf("diag1")]
        st, b_st = sbuf(nc, c1, "st", [128, 8])
        o1_sb, b_o1 = sbuf(nc, c1, "o1_sb", [128, 4 * T], BF16)
        z_sb, b_z = sbuf(nc, c1, "z_sb", [128, 1024])
        sq, b_sq = sbuf(nc, c1, "sq", [128, 1024])
        a1_sb, b_a1 = sbuf(nc, c1, "a1_sb", [128, 1024], BF16)
        ps_s = []; b_ps = []
        for i in range(2):
            t = c1.enter_context(nc.psum_tensor(f"ps_s{i}", [128, 1024], F32))
            ps_s.append(t); b_ps.append(Buf(f"ps_s{i}"))
        ps_t = []; b_pst = []
        for i in range(2):
            t = c1.enter_context(nc.psum_tensor(f"ps_t{i}", [128, 512], F32))
            ps_t.append(t); b_pst.append(Buf(f"ps_t{i}"))
        ps_o = c1.enter_context(nc.psum_tensor("ps_o", [128, 512], F32)); b_pso = Buf("ps_o")
        ps_z = ps_s[0]; b_psz = b_ps[0]

        p.emit("gpsimd", lambda e: e.indirect_dma_start(out=q_sb[:], out_offset=None,
               in_=q1_fl, in_offset=IndirectOffsetOnAxis(ixq_sb[:], axis=0)),
               reads=[b_ixq], writes=[b_q], dma="qI")
        p.emit("gpsimd", lambda e: e.dma_start(out=wo_sb[:].rearrange("p (t d) -> p t d", d=1024),
                                               in_=Wo.rearrange("(t p) d -> p t d", p=128)),
               writes=[b_wo], dma="wo")
        for row_b, dst, b_dst in ((onw, onwb, b_onwb), (onb, onbb, b_onbb)):
            p.emit("sync", lambda e, row_b=row_b: e.dma_start(out=row_sb[:], in_=row_b[None, :]),
                   writes=[b_row], dma="nr")
            for eh in range(2):
                p.emit("tensor", lambda e, eh=eh: e.matmul(ps_z[:, eh*512:(eh+1)*512], ones1[:],
                                                           row_sb[0:1, eh*512:(eh+1)*512], start=True, stop=True),
                       reads=[b_on, b_row], writes=[b_psz])
            p.emit("vector", lambda e, dst=dst, b_dst=b_dst: e.tensor_copy(dst[:], ps_z[:]),
                   reads=[b_psz], writes=[b_dst])

        def b2_scores(h, qt, sci):
            for kp in range(4):
                j = kp % 2
                emit_group(p, "tensor",
                           [lambda e, h=h, qt=qt, kc=kc, kh=kh, j=j:
                            e.matmul(ps_s[j][:, kh*512:(kh+1)*512], q_sb[:, h*T + qt*128 : h*T + (qt+1)*128],
                                     k_sb[:, kc*512:(kc+1)*512], start=True, stop=True)
                            for kh, kc in ((0, kp*2), (1, kp*2+1))],
                           reads=[b_q, b_k], writes=[b_ps[j]])
                if kp % 2 == 0:
                    p.emit("vector", lambda e, kp=kp, j=j, sci=sci: e.tensor_copy(sc_sb[sci][:, kp*1024:(kp+1)*1024], ps_s[j][:]),
                           reads=[b_ps[j]], writes=[b_sc[sci]])
                else:
                    p.emit("scalar", lambda e, kp=kp, j=j, sci=sci: e.activation(sc_sb[sci][:, kp*1024:(kp+1)*1024], ps_s[j][:], AF.Copy),
                           reads=[b_ps[j]], writes=[b_sc[sci]])
            p.emit("vector", lambda e, sci=sci: e.reduce_max(st[:, 4*sci:4*sci+1], sc_sb[sci][:], axis=AX.X, negate=True),
                   reads=[b_sc[sci]], writes=[b_st])
            p.emit("vector", lambda e, sci=sci: e.tensor_scalar_mul(st[:, 4*sci+1:4*sci+2], st[:, 4*sci:4*sci+1], 0.125),
                   reads=[b_st], writes=[b_st])
            p.emit("scalar", lambda e, sci=sci: e.activation(pr_sb[sci][:], sc_sb[sci][:], AF.Exp, bias=st[:, 4*sci+1:4*sci+2],
                                                    scale=0.125, accum_out=st[:, 4*sci+2:4*sci+3]),
                   reads=[b_sc[sci], b_st], writes=[b_pr[sci], b_st])
            p.emit("vector", lambda e, sci=sci: e.reciprocal(st[:, 4*sci+3:4*sci+4], st[:, 4*sci+2:4*sci+3]),
                   reads=[b_st], writes=[b_st])
            p.emit("vector", lambda e, sci=sci: e.tensor_scalar(diag_sb[:, sci*128:(sci+1)*128], identf[:],
                                                                st[:, 4*sci+3:4*sci+4], None, op0=AL.mult),
                   reads=[b_idf, b_st], writes=[b_diag[sci]])

        def b2_trans(q4, sci):
            for kg in range(8):
                j = kg % 2
                emit_group(p, "tensor",
                           [lambda e, kt=kt, i=i, j=j, sci=sci:
                            e.matmul(ps_t[j][:, i*128:(i+1)*128], pr_sb[sci][:, kt*128:(kt+1)*128],
                                     diag_sb[:, sci*128:(sci+1)*128], start=True, stop=True)
                            for i, kt in enumerate(range(kg*4, kg*4+4))],
                           reads=[b_pr[sci], b_diag[sci]], writes=[b_pst[j]])
                if kg % 2 == 0:
                    p.emit("vector", lambda e, kg=kg, j=j, q4=q4: e.tensor_copy(
                               pt_sb[:].rearrange("p (kt s) -> p kt s", s=512)[:, kg*4:(kg+1)*4, q4*128:(q4+1)*128],
                               ps_t[j][:].rearrange("p (kt s) -> p kt s", s=128)),
                           reads=[b_pst[j]], writes=[b_pt])
                else:
                    p.emit("scalar", lambda e, kg=kg, j=j, q4=q4: e.activation(
                               pt_sb[:].rearrange("p (kt s) -> p kt s", s=512)[:, kg*4:(kg+1)*4, q4*128:(q4+1)*128],
                               ps_t[j][:].rearrange("p (kt s) -> p kt s", s=128), AF.Copy),
                           reads=[b_pst[j]], writes=[b_pt])

        def b2_av(h, qg):
            hb = 64 * (h % 2)
            emit_group(p, "tensor",
                       [lambda e, kt=kt:
                        e.matmul(ps_o[0:64, :], v_sb[:, kt*64:(kt+1)*64], pt_sb[:, kt*512:(kt+1)*512],
                                 start=(kt == 0), stop=(kt == 31)) for kt in range(32)],
                       reads=[b_v, b_pt], writes=[b_pso])
            p.emit("vector", lambda e, h=h, hb=hb, qg=qg:
                   e.tensor_copy(o1_sb[hb:hb+64, (h//2)*T + qg*512 : (h//2)*T + (qg+1)*512], ps_o[0:64, :]),
                   reads=[b_pso], writes=[b_o1])

        seq = [(h, qg, q4) for h in range(8) for qg in range(4) for q4 in range(4)]
        prev = None
        for n, (h, qg, q4) in enumerate(seq):
            b2_scores(h, qg * 4 + q4, n % 2)
            if prev is not None:
                ph, pqg, pq4 = prev
                b2_trans(pq4, (n - 1) % 2)
                if pq4 == 3:
                    b2_av(ph, pqg)
            prev = (h, qg, q4)
        b2_trans(3, (len(seq) - 1) % 2)
        b2_av(7, 3)
        for qt in range(16):
            emit_group(p, "tensor",
                       [lambda e, qt=qt, eh=eh, ic=ic:
                        e.matmul(ps_z[:, eh*512:(eh+1)*512], o1_sb[:, ic*T + qt*128 : ic*T + (qt+1)*128],
                                 wo_sb[:, ic*1024 + eh*512 : ic*1024 + (eh+1)*512],
                                 start=(ic == 0), stop=(ic == 3))
                        for eh in range(2) for ic in range(4)],
                       reads=[b_o1, b_wo], writes=[b_psz])
            p.emit("vector", lambda e: e.tensor_copy(z_sb[:], ps_z[:]), reads=[b_psz], writes=[b_z])
            emit_ln(p, z_sb[:], z_sb[:], st[:], sq[:], b_z, b_z, b_st, b_sq, 1024)
            p.emit("vector", lambda e: e.tensor_mul(z_sb[:], z_sb[:], onwb[:]), reads=[b_z, b_onwb], writes=[b_z])
            p.emit("vector", lambda e: e.tensor_add(a1_sb[:], z_sb[:], onbb[:]), reads=[b_z, b_onbb], writes=[b_a1])
            p.emit("sync", lambda e, qt=qt: e.dma_start(out=attn1_dr[qt*128:(qt+1)*128, :], in_=a1_sb[:]),
                   reads=[b_a1], dma="a1")
        p.full_barrier()

    with ExitStack() as c2:
        xn_str = []; b_xns = []
        for i in range(2):
            t, b = sbuf(nc, c2, f"xn_str{i}", [128, 8 * 256], BF16); xn_str.append(t); b_xns.append(b)
        ps_g = c2.enter_context(nc.psum_tensor("ps_g", [128, 512], F32)); b_psg = Buf("ps_g")
        ps_h = []; b_psh = []
        for i in range(4):
            t = c2.enter_context(nc.psum_tensor(f"ps_h{i}", [128, 512], F32))
            ps_h.append(t); b_psh.append(Buf(f"ps_h{i}"))
        ps_y = []; b_psy = []
        for i in range(3):
            t = c2.enter_context(nc.psum_tensor(f"ps_y{i}", [128, 512], F32))
            ps_y.append(t); b_psy.append(Buf(f"ps_y{i}"))

        with ExitStack() as c2g:
            oh_em, b_ohe = sbuf(nc, c2g, "oh_em", [8, 16 * 128], BF16)
            gw_row, b_gwr = sbuf(nc, c2g, "gw_row", [8, 1024])
            gwT, b_gw = sbuf(nc, c2g, "gwT", [128, 64], BF16)
            gb_col, b_gb = sbuf(nc, c2g, "gb_col", [8, 2])
            lg_sb, b_lg = sbuf(nc, c2g, "lg_sb", [8, T])
            stm, b_stm = sbuf(nc, c2g, "stm", [128, 4])
            p.emit("sync", lambda e: e.dma_start(out=gw_row[:], in_=gate_w), writes=[b_gwr], dma="gw")
            for dc in range(8):
                j = dc % 2
                p.emit("tensor", lambda e, dc=dc, j=j: e.transpose(ps_h[j][0:128, 0:8], gw_row[0:8, dc*128:(dc+1)*128], identf[0:8, 0:8]),
                       reads=[b_gwr, b_idf], writes=[b_psh[j]])
                p.emit("vector", lambda e, dc=dc, j=j: e.tensor_copy(gwT[:, dc*8:(dc+1)*8], ps_h[j][0:128, 0:8]),
                       reads=[b_psh[j]], writes=[b_gw])
            p.emit("sync", lambda e: e.dma_start(out=gb_col[:, 0:1], in_=gate_b[:, None]), writes=[b_gb], dma="gb")
            p.emit("sync", lambda e: e.dma_start(out=gb_col[:, 1:2], in_=expert_bias[:, None]), writes=[b_gb], dma="gb2")
            p.emit("vector", lambda e: e.tensor_add(gb_col[:, 0:1], gb_col[:, 0:1], gb_col[:, 1:2]),
                   reads=[b_gb], writes=[b_gb])
            for sb in range(8):
                i = sb % 2
                p.emit("gpsimd", lambda e, sb=sb, i=i: e.indirect_dma_start(out=xn_str[i][:], out_offset=None,
                       in_=xn1_fl, in_offset=IndirectOffsetOnAxis(ixn_sb[:, sb:sb+1], axis=0)),
                       reads=[b_ixn], writes=[b_xns[i]], dma=f"xsI{i}")
                emit_group(p, "tensor",
                           [lambda e, sb=sb, dc=dc, i=i:
                            e.matmul(ps_g[0:8, 0:256], gwT[:, dc*8:(dc+1)*8], xn_str[i][:, dc*256:(dc+1)*256],
                                     start=(dc == 0), stop=(dc == 7)) for dc in range(8)],
                           reads=[b_gw, b_xns[i]], writes=[b_psg])
                p.emit("vector", lambda e, sb=sb: e.tensor_scalar(lg_sb[0:8, sb*256:(sb+1)*256], ps_g[0:8, 0:256],
                                                                  gb_col[0:8, 0:1], None, op0=AL.add),
                       reads=[b_psg, b_gb], writes=[b_lg])
            for t in range(16):
                j = t % 2
                p.emit("tensor", lambda e, t=t, j=j: e.transpose(ps_h[j][0:128, 0:8], lg_sb[0:8, t*128:(t+1)*128], identf[0:8, 0:8]),
                       reads=[b_lg, b_idf], writes=[b_psh[j]])
                p.emit("vector", lambda e, t=t, j=j: e.tensor_copy(oh_tm[:, t*8:(t+1)*8], ps_h[j][0:128, 0:8]),
                       reads=[b_psh[j]], writes=[b_oht])
                p.emit("vector", lambda e, t=t: e.reduce_max(stm[:, 0:1], oh_tm[:, t*8:(t+1)*8], axis=AX.X),
                       reads=[b_oht], writes=[b_stm])
                p.emit("vector", lambda e, t=t: e.tensor_scalar(oh_tm[:, t*8:(t+1)*8], oh_tm[:, t*8:(t+1)*8],
                                                                stm[:, 0:1], None, op0=AL.is_ge),
                       reads=[b_oht, b_stm], writes=[b_oht])
                p.emit("tensor", lambda e, t=t, j=j: e.transpose(ps_h[j][0:8, 0:128], oh_tm[:, t*8:(t+1)*8], identf[:]),
                       reads=[b_oht, b_idf], writes=[b_psh[j]])
                p.emit("vector", lambda e, t=t, j=j: e.tensor_copy(oh_em[0:8, t*128:(t+1)*128], ps_h[j][0:8, 0:128]),
                       reads=[b_psh[j]], writes=[b_ohe])
            p.emit("sync", lambda e: e.dma_start(out=oh_dr, in_=oh_em[:]), reads=[b_ohe], dma="ohw")
            p.full_barrier()

        w1e, _bw1 = sbuf(nc, c2, "w1e", [128, 8 * 4096], BF16)
        b_w1h = [Buf("w1a"), Buf("w1b")]
        w2e, _bw2 = sbuf(nc, c2, "w2e", [128, 32 * 1024], BF16)
        b_w2h = [Buf("w2a"), Buf("w2b")]
        b1raw, b_b1raw = sbuf(nc, c2, "b1raw", [32, 128])
        b1T, _bb1 = sbuf(nc, c2, "b1T", [128, 2 * 32])
        b_b1T = [Buf("b1T0"), Buf("b1T1")]
        h_sb, b_h = sbuf(nc, c2, "h_sb", [128, 32 * 256], BF16)
        p.emit("vector", lambda e: e.memset(moe_sb[:], 0.0), writes=[b_moe])
        for ex in range(8):
            p.emit("gpsimd", lambda e, ex=ex: e.dma_start(
                    out=w1e[:].rearrange("p (dc f) -> p dc f", f=4096)[:, 0:4, :],
                    in_=w1[ex].rearrange("(dc p) f -> p dc f", p=128)[:, 0:4, :]),
                   writes=[b_w1h[0]], dma="w1a")
            p.emit("gpsimd", lambda e, ex=ex: e.dma_start(
                    out=w1e[:].rearrange("p (dc f) -> p dc f", f=4096)[:, 4:8, :],
                    in_=w1[ex].rearrange("(dc p) f -> p dc f", p=128)[:, 4:8, :]),
                   writes=[b_w1h[1]], dma="w1b")
            p.emit("gpsimd", lambda e, ex=ex: e.dma_start(
                    out=w2e[:].rearrange("p (t d) -> p t d", d=1024)[:, 0:16, :],
                    in_=w2[ex].rearrange("(t p) d -> p t d", p=128)[:, 0:16, :]),
                   writes=[b_w2h[0]], dma="w2a")
            p.emit("gpsimd", lambda e, ex=ex: e.dma_start(
                    out=w2e[:].rearrange("p (t d) -> p t d", d=1024)[:, 16:32, :],
                    in_=w2[ex].rearrange("(t p) d -> p t d", p=128)[:, 16:32, :]),
                   writes=[b_w2h[1]], dma="w2b")
            ei = ex % 2
            p.emit("gpsimd", lambda e, ex=ex: e.dma_start(out=b1raw[:], in_=b1[ex].rearrange("(fc p) -> fc p", p=128)),
                   writes=[b_b1raw], dma="b1")
            p.emit("tensor", lambda e: e.transpose(ps_h[3][:, 0:32], b1raw[:], identf[0:32, 0:32]),
                   reads=[b_b1raw, b_idf], writes=[b_psh[3]])
            p.emit("vector", lambda e, ei=ei: e.tensor_copy(b1T[:, ei*32:(ei+1)*32], ps_h[3][:, 0:32]),
                   reads=[b_psh[3]], writes=[b_b1T[ei]])
            for sb in range(8):
                i = sb % 2
                p.emit("gpsimd", lambda e, sb=sb, i=i: e.indirect_dma_start(out=xn_str[i][:], out_offset=None,
                       in_=xn1_fl, in_offset=IndirectOffsetOnAxis(ixn_sb[:, sb:sb+1], axis=0)),
                       reads=[b_ixn], writes=[b_xns[i]], dma=f"xsI{i}")
                for fc in range(32):
                    j = fc % 3
                    fns = [lambda e, fc=fc, dc=dc, j=j, i=i:
                           e.matmul(ps_h[j][:, 0:256], w1e[:, dc*4096 + fc*128 : dc*4096 + (fc+1)*128],
                                    xn_str[i][:, dc*256:(dc+1)*256],
                                    start=(dc == 0), stop=(dc == 7)) for dc in range(8)]
                    emit_group(p, "tensor", fns, reads=[b_w1h[0], b_w1h[1], b_xns[i]], writes=[b_psh[j]])
                    p.emit("scalar", lambda e, fc=fc, j=j, ei=ei: e.activation(h_sb[:, fc*256:(fc+1)*256], ps_h[j][:, 0:256],
                                                                               AF.Relu, bias=b1T[:, ei*32+fc:ei*32+fc+1]),
                           reads=[b_psh[j], b_b1T[ei]], writes=[b_h])
                for st2 in range(2):
                    for dh in range(2):
                        j2 = (st2 * 2 + dh) % 3
                        emit_group(p, "tensor",
                                   [lambda e, st2=st2, dh=dh, fc=fc, j2=j2:
                                    e.matmul(ps_y[j2][:],
                                             h_sb[:, fc*256 + st2*128 : fc*256 + (st2+1)*128],
                                             w2e[:, fc*1024 + dh*512 : fc*1024 + (dh+1)*512],
                                             start=(fc == 0), stop=(fc == 31)) for fc in range(32)],
                                   reads=[b_h, b_w2h[0], b_w2h[1]], writes=[b_psy[j2]])
                        tt = sb * 2 + st2
                        p.emit("vector", lambda e, tt=tt, dh=dh, j2=j2, ex=ex:
                               e.scalar_tensor_tensor(moe_sb[:, tt*1024 + dh*512 : tt*1024 + (dh+1)*512], ps_y[j2][:],
                                                      oh_tm[:, tt*8 + ex : tt*8 + ex + 1],
                                                      moe_sb[:, tt*1024 + dh*512 : tt*1024 + (dh+1)*512], op0=AL.mult, op1=AL.add),
                               reads=[b_psy[j2], b_oht, b_moe], writes=[b_moe])
        p.full_barrier()

    with ExitStack() as c3:
        b2_sb, b_b2 = sbuf(nc, c3, "b2_sb", [8, 1024], BF16)
        oh_em3, b_ohe3 = sbuf(nc, c3, "oh_em2", [8, 16 * 128], BF16)
        p.emit("sync", lambda e: e.dma_start(out=oh_em3[:], in_=oh_dr), writes=[b_ohe3], dma="ohr")
        xt, b_xt = sbuf(nc, c3, "xt", [128, 1024])
        a1t, b_a1t = sbuf(nc, c3, "a1t", [128, 1024], BF16)
        f_sb, b_f = sbuf(nc, c3, "f_sb", [128, 1024])
        ps_f = c3.enter_context(nc.psum_tensor("ps_f", [128, 1024], F32)); b_psf = Buf("ps_f")
        p.emit("gpsimd", lambda e: e.dma_start(out=b2_sb[:], in_=b2), writes=[b_b2], dma="b2l")
        for t in range(16):
            p.emit("gpsimd", lambda e, t=t: e.indirect_dma_start(out=xt[:], out_offset=None,
                   in_=x1_fl, in_offset=IndirectOffsetOnAxis(ix1_sb[:, t:t+1], axis=0)),
                   reads=[b_ix1], writes=[b_xt], dma="x1I")
            p.emit("sync", lambda e, t=t: e.dma_start(out=a1t[:], in_=attn1_dr[t*128:(t+1)*128, :]),
                   writes=[b_a1t], dma="a1l")
            for eh in range(2):
                p.emit("tensor", lambda e, t=t, eh=eh:
                       e.matmul(ps_f[:, eh*512:(eh+1)*512], oh_em3[0:8, t*128:(t+1)*128],
                                b2_sb[0:8, eh*512:(eh+1)*512], start=True, stop=True),
                       reads=[b_ohe3, b_b2], writes=[b_psf])
            p.emit("vector", lambda e: e.tensor_add(f_sb[:], xt[:], ps_f[:]), reads=[b_xt, b_psf], writes=[b_f])
            p.emit("vector", lambda e: e.tensor_add(f_sb[:], f_sb[:], a1t[:]), reads=[b_f, b_a1t], writes=[b_f])
            p.emit("vector", lambda e, t=t: e.tensor_add(f_sb[:], f_sb[:], moe_sb[:, t*1024:(t+1)*1024]),
                   reads=[b_f, b_moe], writes=[b_f])
            p.emit("sync", lambda e, t=t: e.dma_start(out=fin_o[t*128:(t+1)*128, :], in_=f_sb[:]),
                   reads=[b_f], dma="fo")
        s_, c_ = p.dma_sems["fo"]
        p.barrier("sync", [Tok(s_, c_)])
import numpy as np

_NCS = {}

def _get_nc(name, fn):
    if name not in _NCS:
        _NCS[name] = fn()
    return _NCS[name]


def _idx_tables(h):
    p64 = np.arange(64, dtype=np.int32)
    p128 = np.arange(128, dtype=np.int32)
    idx_q = (2 * p64 + h)[:, None].astype(np.int32)
    idx_xn = (16 * p128[:, None] + 8 * h + np.arange(8, dtype=np.int32)[None, :]).astype(np.int32)
    idx_x1 = (2048 * h + 128 * np.arange(16, dtype=np.int32)[None, :] + p128[:, None]).astype(np.int32)
    return idx_q, idx_xn, idx_x1


def kernel(L0_attn_x, L0_attn_w_qkv, L0_attn_w_proj, L0_attn_b_proj,
           L0_ffn_w_router, L0_ffn_w_block,
           L1_attn_Wq, L1_attn_Wkv, L1_attn_Wo, L1_attn_norm_w, L1_attn_norm_b,
           L1_attn_out_norm_w, L1_attn_out_norm_b,
           L1_ffn_gate_w, L1_ffn_gate_b, L1_ffn_expert_bias,
           L1_ffn_w1, L1_ffn_b1, L1_ffn_w2, L1_ffn_b2):
    x = np.ascontiguousarray(L0_attn_x)
    B, S, D = x.shape
    T = 2048
    cores = list(range(8))
    nc = _get_nc("fused", build_fused)
    maps = []
    for c in cores:
        b, h = c // 2, c % 2
        idx_q, idx_xn, idx_x1 = _idx_tables(h)
        maps.append(dict(
            x=np.ascontiguousarray(x[b]),
            w_qkv=L0_attn_w_qkv, w_router=L0_ffn_w_router,
            w_proj=L0_attn_w_proj, b_proj=L0_attn_b_proj, w_block=L0_ffn_w_block,
            norm_w=L1_attn_norm_w, norm_b=L1_attn_norm_b,
            Wq=L1_attn_Wq, Wkv=L1_attn_Wkv, Wo=L1_attn_Wo,
            onw=L1_attn_out_norm_w, onb=L1_attn_out_norm_b,
            gate_w=L1_ffn_gate_w, gate_b=L1_ffn_gate_b, expert_bias=L1_ffn_expert_bias,
            w1=L1_ffn_w1, b1=L1_ffn_b1, w2=L1_ffn_w2, b2=L1_ffn_b2,
            idx_q=idx_q, idx_xn=idx_xn, idx_x1=idx_x1))
    res = run_bass_kernel_spmd(nc, maps, core_ids=cores).results
    out = np.empty((B, S, D), np.float32)
    for c in cores:
        b, h = c // 2, c % 2
        out[b, T*h:T*h+T] = res[c]["fin_o"]
    return out
